# version 20
# baseline (speedup 1.0000x reference)
import os
from contextlib import ExitStack as ExitStack2
import numpy as np
import concourse.bass as bass
import concourse.mybir as mybir
from concourse.alu_op_type import AluOpType as OP

F32 = mybir.dt.float32
BF16 = mybir.dt.bfloat16
AF = mybir.ActivationFunctionType
AX = mybir.AxisListType

T = 4096
D = 1024
KD = D // 128
NC = 512
NCH = T // NC
NH = 16
DH = 64
SHIFT = 12.0
LN_EPS = 1e-5
RMS_EPS = 1e-6

ENGS = ("sync", "tensor", "vector", "scalar", "gpsimd")


class _Op:
    __slots__ = ("eng", "fn", "reads", "writes", "is_async", "weight", "idx",
                 "deps", "marked", "cum", "vc", "marked_fence", "stream")

    def __init__(self, eng, fn, reads, writes, is_async, weight, idx):
        self.eng = eng
        self.fn = fn
        self.reads = reads
        self.writes = writes
        self.is_async = is_async
        self.weight = weight
        self.idx = idx
        self.deps = []
        self.marked = False
        self.cum = 0
        self.vc = None
        self.marked_fence = False
        self.stream = None


class Auto:

    def __init__(self, nc, es):
        self.nc = nc
        self.es = es
        self.ops = []
        self.tok = {}
        self.last_on = {e: None for e in ENGS}

    def op(self, eng, fn, reads=(), writes=(), is_async=False, weight=1, _fence=False):
        o = _Op(eng, fn, tuple(reads), tuple(writes), is_async, weight, len(self.ops))
        deps = set()
        for t in o.reads:
            st = self.tok.get(t)
            if st and st["w"] is not None:
                deps.add(st["w"])
        for t in o.writes:
            st = self.tok.get(t)
            if st:
                if st["w"] is not None:
                    deps.add(st["w"])
                for r in st["r"]:
                    deps.add(r)
        for d in deps:
            if d.eng != eng or d.is_async or is_async or eng != "tensor":
                o.deps.append(d)
        for t in o.reads:
            self.tok.setdefault(t, {"w": None, "r": []})["r"].append(o)
        for t in o.writes:
            self.tok[t] = {"w": o, "r": []}
        self.ops.append(o)
        if not _fence:
            self.last_on[eng] = o
        return o

    def dma(self, eng, out, in_, reads=(), writes=(), stream=None, **kw):
        fn = lambda e, o=out, i=in_, kw=kw: e.dma_start(out=o, in_=i, **kw)
        o = self.op(eng, fn, reads=reads, writes=writes, is_async=True, weight=16)
        o.stream = stream if stream is not None else (o.writes[0] if o.writes else ("anon",))
        return o

    def barrier(self):
        prior = [o for o in (self.last_on[e] for e in ENGS) if o is not None]
        outstanding = [o for o in self.ops if o.is_async and not o.marked_fence]
        for e in ENGS:
            o = self.op(e, lambda eng: eng.nop(), _fence=True)
            o.deps = [d for d in set(prior + outstanding) if d is not o]
        for o in outstanding:
            o.marked_fence = True

    def finalize(self, block):
        def chan(o):
            if o.is_async:
                return ("d", o.eng, o.stream)
            return ("e", o.eng)

        chans = []
        seen = set()
        for o in self.ops:
            c = chan(o)
            if c not in seen:
                seen.add(c)
                chans.append(c)
        for e in ENGS:
            if ("e", e) not in seen:
                seen.add(("e", e))
                chans.append(("e", e))
        sems = {c: self.es.enter_context(self.nc.semaphore(f"pg{i}"))
                for i, c in enumerate(chans)}
        for o in self.ops:
            for d in o.deps:
                d.marked = True
        cum = {c: 0 for c in chans}
        for o in self.ops:
            c = chan(o)
            if o.marked:
                cum[c] += o.weight
            o.cum = cum[c]
        eng_vc = {e: {c: 0 for c in chans} for e in ENGS}
        op_vc = {}
        emit = {e: [] for e in ENGS}
        for o in self.ops:
            need = {}
            for d in o.deps:
                dc = chan(d)
                if d.cum > need.get(dc, 0):
                    need[dc] = d.cum
                dvc = op_vc.get(d.idx)
                if dvc:
                    for c2, v2 in dvc.items():
                        if v2 > need.get(c2, 0):
                            need[c2] = v2
            mycl = eng_vc[o.eng]
            own = ("e", o.eng)
            for c2, v in sorted(need.items(), key=str):
                if c2 == own and o.eng == "tensor" and not o.is_async:
                    continue
                if v > mycl[c2]:
                    emit[o.eng].append(("wait", c2, v))
                    mycl[c2] = v
            emit[o.eng].append(("op", o))
            oc = chan(o)
            if not o.is_async and o.eng == "tensor":
                mycl[oc] = o.cum
            vc = dict(mycl)
            if o.is_async:
                vc[oc] = o.cum
            op_vc[o.idx] = vc

        def run(e, eng_name, seq=emit, sems=sems):
            for item in seq[eng_name]:
                if item[0] == "wait":
                    e.wait_ge(sems[item[1]], item[2])
                else:
                    o = item[1]
                    inst = o.fn(e)
                    if o.marked:
                        s = sems[chan(o)]
                        if inst is None:
                            e.sem_inc(s, o.weight)
                        else:
                            inst.then_inc(s, o.weight)
        for eng_name in ENGS:
            getattr(block, eng_name)(lambda e, en=eng_name: run(e, en))
        self.n_chans = len(chans)
        return emit

    def stats(self):
        n_wait = 0
        return {"ops": len(self.ops), "marked": sum(o.marked for o in self.ops)}



def kpt(w_t):
    fin, fout = w_t.shape
    return np.ascontiguousarray(w_t.reshape(fin // 128, 128, fout))


def pvec(v):
    n = v.shape[0]
    return np.ascontiguousarray(v.reshape(n // 128, 128).T)


def rope_perm():
    perm = np.zeros(D, dtype=np.int64)
    for h in range(NH):
        for k in range(32):
            perm[h * 64 + k] = h * 64 + 2 * k
            perm[h * 64 + 32 + k] = h * 64 + 2 * k + 1
    return perm


def rope_tables():
    inv_freq = 1.0 / 10000.0 ** (np.arange(0, DH, 2, dtype=np.float64) / DH)
    ang = np.outer(inv_freq, np.arange(T, dtype=np.float64))
    cos32, sin32 = np.cos(ang), np.sin(ang)
    cos128 = np.tile(cos32, (4, 1)).astype(np.float32)
    sin128 = np.tile(sin32, (4, 1)).astype(np.float32)
    sgn = np.ones((128, 1), np.float32)
    for b in (0, 2):
        sgn[b * 32:(b + 1) * 32] = -1.0
    return cos128, sin128 * sgn


def causal_masks():
    p = np.arange(128)[:, None]
    f = np.arange(NC)[None, :]
    ms = [(p + 128 * dd <= f).astype(np.float32) for dd in range(4)]
    return np.ascontiguousarray(np.stack(ms, axis=1))



def emit_attn_phase(au, es, nc, *, src_hbm, rms_gain_sb, wqk_hbm, bqk_sb, wv_hbm,
                    bv_sb, causal, cos_hbm, sin_hbm, masks_sb, q_hbm, attn_hbm,
                    K_sb, V65, psum, ones128b, ones1b, shift_sb, eps1_sb, tag="", a2=True,
                    feed=None):
    A = au
    q_chunk = es.enter_context(nc.sbuf_tensor(f"qch{tag}", [128, KD, NC], BF16))
    es1 = es.enter_context(ExitStack2())
    nxb = 1 if rms_gain_sb is not None else 2
    x_bf = [es1.enter_context(nc.sbuf_tensor(f"xbf{tag}{i}", [128, KD, NC], BF16)) for i in range(nxb)]
    nwr = 2 if rms_gain_sb is not None else 3
    wring = [es1.enter_context(nc.sbuf_tensor(f"wr{tag}{i}", [128, 2048], BF16)) for i in range(nwr)]
    vring = [es1.enter_context(nc.sbuf_tensor(f"vr{tag}{i}", [128, 1024], BF16)) for i in range(2)]
    qk_ev = [es1.enter_context(nc.sbuf_tensor(f"qk_ev{tag}{i}", [128, NC], BF16)) for i in range(2)]
    partner = [es1.enter_context(nc.sbuf_tensor(f"prt{tag}{i}", [128, NC], BF16)) for i in range(2)]
    cos_sb = es1.enter_context(nc.sbuf_tensor(f"cos{tag}", [128, NC], BF16))
    sin_sb = es1.enter_context(nc.sbuf_tensor(f"sin{tag}", [128, NC], BF16))
    bv_rep = es1.enter_context(nc.sbuf_tensor(f"bvrep{tag}", [128, 1024], BF16))
    if rms_gain_sb is not None:
        x_f32 = es1.enter_context(nc.sbuf_tensor(f"xf32{tag}", [128, KD, NC], F32))
        x2b = [es1.enter_context(nc.sbuf_tensor(f"x2b{tag}{i}", [128, NC], BF16)) for i in range(2)]
        srow = es1.enter_context(nc.sbuf_tensor(f"srow{tag}", [1, NC], F32))
        srow_bf = es1.enter_context(nc.sbuf_tensor(f"srowb{tag}", [1, NC], BF16))

    A.op("vector", lambda e: e.memset(V65[:, :, :, 64:65], 1.0), writes=[("V",)])
    for half in range(2):
        A.op("tensor", lambda e, h=half: e.matmul(out=psum[6][:, 0:NC], lhsT=ones1b[0:1, :],
             rhs=bv_sb[0:1, h * 512:(h + 1) * 512], start=True, stop=True),
             reads=[("bv",), ("ones",)], writes=[("ps", 6)])
        A.op("vector", lambda e, h=half: e.tensor_copy(out=bv_rep[:, h * 512:(h + 1) * 512],
             in_=psum[6][:, 0:NC]), reads=[("ps", 6)], writes=[("bvrep",)])

    for c in range(NCH):
        if feed:
            feed(2)
        t0 = c * NC
        xb, xtok = x_bf[c % len(x_bf)], ("xbf", c % len(x_bf))
        if rms_gain_sb is None:
            for half in range(2):
                A.dma("gpsimd", xb[:, half * 4:(half + 1) * 4, :],
                      src_hbm[:, half * 4:(half + 1) * 4, t0:t0 + NC], writes=[xtok],
                      stream=xtok)
        else:
            for half in range(2):
                A.dma("sync", x_f32[:, half * 4:(half + 1) * 4, :],
                      src_hbm[:, half * 4:(half + 1) * 4, t0:t0 + NC], writes=[("xf32",)],
                      stream=("xf32",))
            for k in range(KD):
                A.op("vector", lambda e, k=k, o=x2b[k % 2]: e.tensor_tensor(
                    out=o[:], in0=x_f32[:, k, :], in1=x_f32[:, k, :], op=OP.mult),
                    reads=[("xf32",)], writes=[("x2b", k % 2)])
                A.op("tensor", lambda e, k=k, i=x2b[k % 2]: e.matmul(
                    out=psum[7][0:1, 0:NC], lhsT=ones128b[:, 0:1], rhs=i[:],
                    start=(k == 0), stop=(k == KD - 1)),
                    reads=[("x2b", k % 2), ("ones",)], writes=[("ps", 7)])
            A.op("vector", lambda e: e.tensor_scalar(out=srow[:, :], in0=psum[7][0:1, 0:NC],
                 scalar1=1.0 / D, scalar2=None, op0=OP.mult),
                 reads=[("ps", 7)], writes=[("srow",)])
            A.op("scalar", lambda e: e.activation(out=srow[:, :], in_=srow[:, :],
                 func=AF.Sqrt, bias=eps1_sb[0:1, 1:2]), reads=[("srow",), ("const",)], writes=[("srow",)])
            A.op("vector", lambda e: e.reciprocal(out=srow[:, :], in_=srow[:, :]),
                 reads=[("srow",)], writes=[("srow",)])
            A.op("vector", lambda e: e.tensor_copy(out=srow_bf[:, :], in_=srow[:, :]),
                 reads=[("srow",)], writes=[("srowb",)])
            A.op("tensor", lambda e: e.matmul(out=psum[6][:, 0:NC], lhsT=ones1b[0:1, :],
                 rhs=srow_bf[:, :], start=True, stop=True),
                 reads=[("srowb",), ("ones",)], writes=[("ps", 6)])
            for k in range(KD):
                A.op("vector", lambda e, k=k, xb=xb: e.scalar_tensor_tensor(
                    out=xb[:, k, :], in0=x_f32[:, k, :], scalar=rms_gain_sb[:, k:k + 1],
                    in1=psum[6][:, 0:NC], op0=OP.mult, op1=OP.mult),
                    reads=[("xf32",), ("ps", 6), ("gain",)], writes=[xtok])

        A.dma("sync", cos_sb[:], cos_hbm[:, t0:t0 + NC], writes=[("cos",)], stream=("cos",))
        A.dma("sync", sin_sb[:], sin_hbm[:, t0:t0 + NC], writes=[("sin",)], stream=("sin",))

        def qk_evac(m, ps, pstok, xb=xb, t0=t0):
            bi = m % 2
            A.op("scalar", lambda e, m=m, bi=bi, ps=ps: e.activation(
                out=qk_ev[bi][:], in_=ps[:, 0:NC], func=AF.Identity,
                bias=bqk_sb[:, m:m + 1], scale=1.0),
                reads=[pstok, ("bqk",)], writes=[("qk_ev", bi)])
            for b0, b1 in ((0, 32), (32, 0), (64, 96), (96, 64)):
                A.op("vector", lambda e, b0=b0, b1=b1, bi=bi: e.tensor_copy(
                    out=partner[bi][b0:b0 + 32, :], in_=qk_ev[bi][b1:b1 + 32, :]),
                    reads=[("qk_ev", bi)], writes=[("prt", bi)])
            A.op("vector", lambda e, bi=bi: e.tensor_tensor(out=qk_ev[bi][:], in0=qk_ev[bi][:],
                 in1=cos_sb[:], op=OP.mult),
                 reads=[("qk_ev", bi), ("cos",)], writes=[("qk_ev", bi)])
            A.op("vector", lambda e, bi=bi: e.tensor_tensor(out=partner[bi][:], in0=partner[bi][:],
                 in1=sin_sb[:], op=OP.mult),
                 reads=[("prt", bi), ("sin",)], writes=[("prt", bi)])
            if m < 8:
                A.op("vector", lambda e, m=m, bi=bi: e.tensor_tensor(out=q_chunk[:, m, :],
                     in0=qk_ev[bi][:], in1=partner[bi][:], op=OP.add),
                     reads=[("qk_ev", bi), ("prt", bi)], writes=[("qch",)])
            else:
                A.op("vector", lambda e, m=m, bi=bi, t0=t0: e.tensor_tensor(
                    out=K_sb[:, m - 8, t0:t0 + NC], in0=qk_ev[bi][:], in1=partner[bi][:], op=OP.add),
                    reads=[("qk_ev", bi), ("prt", bi)], writes=[("K",)])
        emit_proj(A, w_hbm=wqk_hbm, rhs_fn=lambda k, xb=xb, xtok=xtok: (xb[:, k, :], xtok),
                  kd=KD, mt=16, evac=qk_evac, psum=psum, wring=wring, wrtok="wr" + tag)
        A.dma("gpsimd", q_hbm[:, :, t0:t0 + NC], q_chunk[:, :, :],
              reads=[("qch",)], writes=[("qhbm",)], stream=("qhbm",))

        for k0 in range(0, KD, 1):
            vb = vring[k0 % 2]
            vtok = ("vrk", k0 % 2)
            A.dma("sync", vb[:, :], wv_hbm[k0, :, :], writes=[vtok])
            for dk in range(1):
                k = k0 + dk
                for tsub in range(4):
                    for half in range(2):
                        A.op("tensor", lambda e, k=k, dk=dk, tsub=tsub, half=half, vb=vb, xb=xb:
                             e.matmul(out=psum[tsub * 2 + half][:, 0:NC],
                                      lhsT=xb[:, k, tsub * 128:(tsub + 1) * 128],
                                      rhs=vb[:, half * 512:(half + 1) * 512],
                                      start=(k == 0), stop=(k == KD - 1)),
                             reads=[vtok, xtok], writes=[("ps", tsub * 2 + half)])
        for tsub in range(4):
            j = 4 * c + tsub
            for half in range(2):
                b = tsub * 2 + half
                A.op("vector", lambda e, j=j, half=half, b=b: e.tensor_tensor(
                    out=V65[:, j, half * 8:(half + 1) * 8, 0:64],
                    in0=psum[b][:, 0:NC].rearrange("p (h d) -> p h d", h=8),
                    in1=bv_rep[:, half * 512:(half + 1) * 512].rearrange("p (h d) -> p h d", h=8),
                    op=OP.add),
                    reads=[("ps", b), ("bvrep",)], writes=[("V",)])

    A.barrier()
    es1.close()
    if not a2:
        return
    expS = [[es.enter_context(nc.sbuf_tensor(f"expS{tag}{cc}{i}", [128, NC], BF16))
             for i in range(2)] for cc in range(2)]
    recip_row = [es.enter_context(nc.sbuf_tensor(f"recip{tag}{i}", [1, NC], F32)) for i in range(2)]
    recip_bf = [es.enter_context(nc.sbuf_tensor(f"recipb{tag}{i}", [1, NC], BF16)) for i in range(2)]
    rep_sb = [es.enter_context(nc.sbuf_tensor(f"rep{tag}{i}", [64, NC], F32)) for i in range(2)]
    ao_chunk = es.enter_context(nc.sbuf_tensor(f"ao{tag}", [128, KD, 2 * NC], BF16))
    qp_sb = es.enter_context(nc.sbuf_tensor(f"qp{tag}", [128, KD, 2 * NC], BF16))
    for cp in range(NCH // 2):
        t0 = cp * 2 * NC
        A.dma("sync", qp_sb[:, :, :], q_hbm[:, :, t0:t0 + 2 * NC],
              reads=[("qhbm",)], writes=[("qch",)], stream=("qch",))
        jmaxs = (8 * cp + 3, 8 * cp + 7) if causal else (T // 128 - 1, T // 128 - 1)
        jall = jmaxs[1]
        for h in range(NH):
            if feed:
                feed(2)
            hb, mq = 64 * (h % 2), h // 2
            scb = {0: (0, 1), 1: (4, 5)}
            ub = {0: 2, 1: 6}

            def sc(j, cc, hb=hb, mq=mq, cp=cp):
                bank = scb[cc][j % 2]
                A.op("tensor", lambda e, j=j, cc=cc, hb=hb, mq=mq, bank=bank: e.matmul(
                    out=psum[bank][:, 0:NC], lhsT=K_sb[hb:hb + 64, mq, j * 128:(j + 1) * 128],
                    rhs=qp_sb[hb:hb + 64, mq, cc * NC:(cc + 1) * NC], start=True, stop=True,
                    skip_group_check=True),
                    reads=[("K",), ("qch",)], writes=[("ps", bank)])
                A.op("scalar", lambda e, j=j, cc=cc, bank=bank: e.activation(
                    out=expS[cc][j % 2][:], in_=psum[bank][:, 0:NC], func=AF.Exp,
                    scale=0.125, bias=shift_sb[:, 0:1]),
                    reads=[("ps", bank), ("const",)], writes=[("expS", cc, j % 2)])
                if causal and j >= 8 * cp + 4 * cc:
                    dd = j - (8 * cp + 4 * cc)
                    A.op("vector", lambda e, j=j, cc=cc, dd=dd: e.tensor_tensor(
                        out=expS[cc][j % 2][:], in0=expS[cc][j % 2][:], in1=masks_sb[:, dd, :],
                        op=OP.mult),
                        reads=[("expS", cc, j % 2), ("msk",)], writes=[("expS", cc, j % 2)])

            def pv(j, cc, h=h):
                A.op("tensor", lambda e, j=j, cc=cc, h=h, st=(j == 0), sp=(j == jmaxs[cc]):
                     e.matmul(
                     out=psum[ub[cc]][0:65, 0:NC], lhsT=V65[:, j, h, 0:65],
                     rhs=expS[cc][j % 2][:],
                     start=st, stop=sp, skip_group_check=True),
                     reads=[("V",), ("expS", cc, j % 2)], writes=[("ps", ub[cc])])

            def valid(j, cc):
                return j <= jmaxs[cc]
            sc(0, 0)
            if valid(0, 1):
                sc(0, 1)
            for j in range(1, jall + 1):
                for cc in range(2):
                    if valid(j, cc):
                        sc(j, cc)
                for cc in range(2):
                    if valid(j - 1, cc):
                        pv(j - 1, cc)
            for cc in range(2):
                if jmaxs[cc] == jall:
                    pv(jmaxs[cc], cc)
            for cc in range(2):
                rb = 3 + 4 * cc
                A.op("vector", lambda e, cc=cc: e.reciprocal(out=recip_row[cc][:, :],
                     in_=psum[ub[cc]][64:65, 0:NC]),
                     reads=[("ps", ub[cc])], writes=[("recip", cc)])
                A.op("vector", lambda e, cc=cc: e.tensor_copy(out=recip_bf[cc][:, :],
                     in_=recip_row[cc][:, :]),
                     reads=[("recip", cc)], writes=[("recipb", cc)])
                A.op("tensor", lambda e, cc=cc, rb=rb: e.matmul(out=psum[rb][0:64, 0:NC],
                     lhsT=ones1b[0:1, 0:64], rhs=recip_bf[cc][:, :], start=True, stop=True,
                     skip_group_check=True),
                     reads=[("recipb", cc), ("ones",)], writes=[("ps", rb)])
                A.op("vector", lambda e, cc=cc, rb=rb: e.tensor_copy(out=rep_sb[cc][:, :],
                     in_=psum[rb][0:64, 0:NC]),
                     reads=[("ps", rb)], writes=[("repb", cc)])
                A.op("vector", lambda e, hb=hb, mq=mq, cc=cc: e.tensor_tensor(
                    out=ao_chunk[hb:hb + 64, mq, cc * NC:(cc + 1) * NC],
                    in0=psum[ub[cc]][0:64, 0:NC], in1=rep_sb[cc][:, :],
                    op=OP.mult),
                    reads=[("ps", ub[cc]), ("repb", cc)], writes=[("ao",)])
        A.dma("gpsimd", attn_hbm[:, :, t0:t0 + 2 * NC], ao_chunk[:, :, :],
              reads=[("ao",)], writes=[("ahbm",)], stream=("ahbm",))


def emit_outproj_ln(au, es, nc, *, attn_hbm, src_hbm, dst_hbm, w_hbm, b_sb,
                    psum, ones128b, ones1b, eps1_sb, ln=True, tag="o"):
    A = au
    at_ch = es.enter_context(nc.sbuf_tensor(f"atch{tag}", [128, KD, NC], BF16))
    x_ch = es.enter_context(nc.sbuf_tensor(f"xch{tag}", [128, KD, NC], F32))
    r_ch = es.enter_context(nc.sbuf_tensor(f"rch{tag}", [128, KD, NC], F32))
    wring = [es.enter_context(nc.sbuf_tensor(f"wo{tag}{i}", [128, 2048], BF16)) for i in range(4)]
    x2b = [es.enter_context(nc.sbuf_tensor(f"x2o{tag}{i}", [128, NC], BF16)) for i in range(2)]
    mrow = es.enter_context(nc.sbuf_tensor(f"mrow{tag}", [1, NC], F32))
    vrow = es.enter_context(nc.sbuf_tensor(f"vrow{tag}", [1, NC], F32))
    brow = es.enter_context(nc.sbuf_tensor(f"brow{tag}", [1, NC], F32))
    arow_bf = es.enter_context(nc.sbuf_tensor(f"arowb{tag}", [1, NC], BF16))
    brow_bf = es.enter_context(nc.sbuf_tensor(f"browb{tag}", [1, NC], BF16))
    tmp = es.enter_context(nc.sbuf_tensor(f"tmpo{tag}", [128, NC], F32))

    for c in range(NCH):
        t0 = c * NC
        A.dma("sync", at_ch[:, :, :], attn_hbm[:, :, t0:t0 + NC], writes=[("atch",)],
              stream=("atch",))
        for half in range(2):
            A.dma("sync", x_ch[:, half * 4:(half + 1) * 4, :],
                  src_hbm[:, half * 4:(half + 1) * 4, t0:t0 + NC], writes=[("xch",)],
                  stream=("xch",))

        def op_evac(m, ps, pstok):
            A.op("scalar", lambda e, m=m, ps=ps: e.activation(out=ps[:, 0:NC],
                 in_=ps[:, 0:NC], func=AF.Identity, bias=b_sb[:, m:m + 1]),
                 reads=[pstok, ("bout",)], writes=[pstok])
            A.op("vector", lambda e, m=m, ps=ps: e.tensor_tensor(out=r_ch[:, m, :],
                 in0=ps[:, 0:NC], in1=x_ch[:, m, :], op=OP.add),
                 reads=[pstok, ("xch",)], writes=[("rch",)])
        emit_proj(A, w_hbm=w_hbm, rhs_fn=lambda k: (at_ch[:, k, :], ("atch",)),
                  kd=KD, mt=KD, evac=op_evac, psum=psum, wring=wring, wrtok="wo" + tag)
        if not ln:
            for half in range(2):
                A.dma("gpsimd", dst_hbm[:, half * 4:(half + 1) * 4, t0:t0 + NC],
                      r_ch[:, half * 4:(half + 1) * 4, :], reads=[("rch",)],
                      writes=[("dsth",)], stream=("dst_stream",))
            continue
        for k in range(KD):
            A.op("vector", lambda e, k=k, o=x2b[k % 2]: e.tensor_copy(
                out=o[:], in_=r_ch[:, k, :]), reads=[("rch",)], writes=[("x2o", k % 2)])
            A.op("tensor", lambda e, k=k, i=x2b[k % 2]: e.matmul(out=psum[0][0:1, 0:NC],
                 lhsT=ones128b[:, 0:1], rhs=i[:], start=(k == 0), stop=(k == KD - 1)),
                 reads=[("x2o", k % 2), ("ones",)], writes=[("ps", 0)])
        for k in range(KD):
            A.op("scalar", lambda e, k=k, o=x2b[k % 2]: e.activation(
                out=o[:], in_=r_ch[:, k, :], func=AF.Square),
                reads=[("rch",)], writes=[("x2o", k % 2)])
            A.op("tensor", lambda e, k=k, i=x2b[k % 2]: e.matmul(out=psum[1][0:1, 0:NC],
                 lhsT=ones128b[:, 0:1], rhs=i[:], start=(k == 0), stop=(k == KD - 1)),
                 reads=[("x2o", k % 2), ("ones",)], writes=[("ps", 1)])
        A.op("vector", lambda e: e.tensor_scalar(out=mrow[:, :], in0=psum[0][0:1, 0:NC],
             scalar1=1.0 / D, scalar2=None, op0=OP.mult),
             reads=[("ps", 0)], writes=[("mrow",)])
        A.op("vector", lambda e: e.tensor_scalar(out=vrow[:, :], in0=psum[1][0:1, 0:NC],
             scalar1=1.0 / D, scalar2=None, op0=OP.mult),
             reads=[("ps", 1)], writes=[("vrow",)])
        A.op("vector", lambda e: e.tensor_tensor(out=brow[:, :], in0=mrow[:, :],
             in1=mrow[:, :], op=OP.mult), reads=[("mrow",)], writes=[("brow",)])
        A.op("vector", lambda e: e.tensor_tensor(out=vrow[:, :], in0=vrow[:, :],
             in1=brow[:, :], op=OP.subtract), reads=[("vrow",), ("brow",)], writes=[("vrow",)])
        A.op("scalar", lambda e: e.activation(out=vrow[:, :], in_=vrow[:, :],
             func=AF.Sqrt, bias=eps1_sb[0:1, 0:1]), reads=[("vrow",), ("const",)], writes=[("vrow",)])
        A.op("vector", lambda e: e.reciprocal(out=vrow[:, :], in_=vrow[:, :]),
             reads=[("vrow",)], writes=[("vrow",)])
        A.op("vector", lambda e: e.tensor_tensor(out=brow[:, :], in0=mrow[:, :],
             in1=vrow[:, :], op=OP.mult), reads=[("mrow",), ("vrow",)], writes=[("brow",)])
        A.op("vector", lambda e: e.tensor_scalar(out=brow[:, :], in0=brow[:, :],
             scalar1=-1.0, scalar2=None, op0=OP.mult), reads=[("brow",)], writes=[("brow",)])
        A.op("vector", lambda e: e.tensor_copy(out=arow_bf[:, :], in_=vrow[:, :]),
             reads=[("vrow",)], writes=[("arowb",)])
        A.op("vector", lambda e: e.tensor_copy(out=brow_bf[:, :], in_=brow[:, :]),
             reads=[("brow",)], writes=[("browb",)])
        A.op("tensor", lambda e: e.matmul(out=psum[2][:, 0:NC], lhsT=ones1b[0:1, :],
             rhs=arow_bf[:, :], start=True, stop=True), reads=[("arowb",), ("ones",)], writes=[("ps", 2)])
        A.op("tensor", lambda e: e.matmul(out=psum[3][:, 0:NC], lhsT=ones1b[0:1, :],
             rhs=brow_bf[:, :], start=True, stop=True), reads=[("browb",), ("ones",)], writes=[("ps", 3)])
        for k in range(KD):
            A.op("vector", lambda e, k=k: e.tensor_tensor(out=tmp[:], in0=r_ch[:, k, :],
                 in1=psum[2][:, 0:NC], op=OP.mult),
                 reads=[("rch",), ("ps", 2)], writes=[("tmpo",)])
            A.op("vector", lambda e, k=k: e.tensor_tensor(out=r_ch[:, k, :], in0=tmp[:],
                 in1=psum[3][:, 0:NC], op=OP.add),
                 reads=[("tmpo",), ("ps", 3)], writes=[("rch",)])
        for half in range(2):
            A.dma("gpsimd", dst_hbm[:, half * 4:(half + 1) * 4, t0:t0 + NC],
                  r_ch[:, half * 4:(half + 1) * 4, :], reads=[("rch",)], writes=[("dsth",)],
                  stream=("dst_stream",))


def emit_moe_phase(au, es, nc, *, z_hbm, x1_hbm, dst_hbm, rms_gain_sb, gate_hbm,
                   w1_hbm, w2_hbm, w3_hbm, b1_sb, b2_sb, b3_sb, psum,
                   ones128b, ones1b, esel_sb, ident_bf, eps1_sb, tag="m"):
    A = au
    NE, HD = 8, 2048
    z_ch = es.enter_context(nc.sbuf_tensor(f"zch{tag}", [128, KD, NC], F32))
    x1_ch = es.enter_context(nc.sbuf_tensor(f"x1ch{tag}", [128, KD, NC], F32))
    zn_bf = es.enter_context(nc.sbuf_tensor(f"znbf{tag}", [128, KD, NC], BF16))
    x2b = [es.enter_context(nc.sbuf_tensor(f"x2{tag}{i}", [128, NC], BF16)) for i in range(2)]
    srow = es.enter_context(nc.sbuf_tensor(f"srow{tag}", [1, NC], F32))
    srow_bf = es.enter_context(nc.sbuf_tensor(f"srowb{tag}", [1, NC], BF16))
    gate_sb = es.enter_context(nc.sbuf_tensor(f"gate{tag}", [128, KD, 8], BF16))
    glog = es.enter_context(nc.sbuf_tensor(f"glog{tag}", [128, 8], F32))
    gtmp = es.enter_context(nc.sbuf_tensor(f"gtmp{tag}", [128, 8], F32))
    gm1 = es.enter_context(nc.sbuf_tensor(f"gm1{tag}", [128, 1], F32))
    gm2 = es.enter_context(nc.sbuf_tensor(f"gm2{tag}", [128, 1], F32))
    comb_bf = es.enter_context(nc.sbuf_tensor(f"combbf{tag}", [128, 4, 8], BF16))
    combT = es.enter_context(nc.sbuf_tensor(f"combT{tag}", [8, NC], BF16))
    crep = es.enter_context(nc.sbuf_tensor(f"crep{tag}", [128, NC], BF16))
    s1_sb = es.enter_context(nc.sbuf_tensor(f"s1{tag}", [128, 16, NC], BF16))
    h_sb = es.enter_context(nc.sbuf_tensor(f"h{tag}", [128, 16, NC], BF16))
    y_sb = es.enter_context(nc.sbuf_tensor(f"y{tag}", [128, KD, NC], F32))
    wring = [es.enter_context(nc.sbuf_tensor(f"w{tag}{i}", [128, 2048], BF16)) for i in range(4)]
    mrow = es.enter_context(nc.sbuf_tensor(f"mrow{tag}", [1, NC], F32))
    vrow = es.enter_context(nc.sbuf_tensor(f"vrow{tag}", [1, NC], F32))
    brow = es.enter_context(nc.sbuf_tensor(f"brow{tag}", [1, NC], F32))
    arow_bf = es.enter_context(nc.sbuf_tensor(f"arb{tag}", [1, NC], BF16))
    brow_bf = es.enter_context(nc.sbuf_tensor(f"brb{tag}", [1, NC], BF16))
    tmp32 = es.enter_context(nc.sbuf_tensor(f"tmp32{tag}", [128, NC], F32))

    for k in range(KD):
        A.dma("gpsimd", gate_sb[:, k, :], gate_hbm[k, :, :], writes=[("gate",)], stream=("gate",))

    for c in range(NCH):
        t0 = c * NC
        for half in range(2):
            A.dma("sync", z_ch[:, half * 4:(half + 1) * 4, :],
                  z_hbm[:, half * 4:(half + 1) * 4, t0:t0 + NC], writes=[("zch",)])
        for k in range(KD):
            A.op("vector", lambda e, k=k, o=x2b[k % 2]: e.tensor_tensor(
                out=o[:], in0=z_ch[:, k, :], in1=z_ch[:, k, :], op=OP.mult),
                reads=[("zch",)], writes=[("x2b", k % 2)])
            A.op("tensor", lambda e, k=k, i=x2b[k % 2]: e.matmul(
                out=psum[7][0:1, 0:NC], lhsT=ones128b[:, 0:1], rhs=i[:],
                start=(k == 0), stop=(k == KD - 1)),
                reads=[("x2b", k % 2), ("ones",)], writes=[("ps", 7)])
        A.op("vector", lambda e: e.tensor_scalar(out=srow[:, :], in0=psum[7][0:1, 0:NC],
             scalar1=1.0 / D, scalar2=None, op0=OP.mult), reads=[("ps", 7)], writes=[("srow",)])
        A.op("scalar", lambda e: e.activation(out=srow[:, :], in_=srow[:, :],
             func=AF.Sqrt, bias=eps1_sb[0:1, 1:2]), reads=[("srow",), ("const",)], writes=[("srow",)])
        A.op("vector", lambda e: e.reciprocal(out=srow[:, :], in_=srow[:, :]),
             reads=[("srow",)], writes=[("srow",)])
        A.op("vector", lambda e: e.tensor_copy(out=srow_bf[:, :], in_=srow[:, :]),
             reads=[("srow",)], writes=[("srowb",)])
        A.op("tensor", lambda e: e.matmul(out=psum[6][:, 0:NC], lhsT=ones1b[0:1, :],
             rhs=srow_bf[:, :], start=True, stop=True),
             reads=[("srowb",), ("ones",)], writes=[("ps", 6)])
        for k in range(KD):
            A.op("vector", lambda e, k=k: e.scalar_tensor_tensor(
                out=zn_bf[:, k, :], in0=z_ch[:, k, :], scalar=rms_gain_sb[:, k:k + 1],
                in1=psum[6][:, 0:NC], op0=OP.mult, op1=OP.mult),
                reads=[("zch",), ("ps", 6), ("gain2",)], writes=[("znbf",)])

        for tsub in range(4):
            for k in range(KD):
                A.op("tensor", lambda e, k=k, tsub=tsub: e.matmul(
                    out=psum[7][:, 0:8], lhsT=zn_bf[:, k, tsub * 128:(tsub + 1) * 128],
                    rhs=gate_sb[:, k, :], start=(k == 0), stop=(k == KD - 1)),
                    reads=[("znbf",), ("gate",)], writes=[("ps", 7)])
            A.op("vector", lambda e: e.tensor_copy(out=glog[:, :], in_=psum[7][:, 0:8]),
                 reads=[("ps", 7)], writes=[("glog",)])
            A.op("vector", lambda e: e.reduce_max(out=gm1[:, 0:1], in_=glog[:, :], axis=AX.X),
                 reads=[("glog",)], writes=[("gm1",)])
            A.op("vector", lambda e: e.tensor_scalar(out=gtmp[:, :], in0=glog[:, :],
                 scalar1=gm1[:, 0:1], scalar2=None, op0=OP.is_ge),
                 reads=[("glog",), ("gm1",)], writes=[("gtmp",)])
            A.op("vector", lambda e: e.scalar_tensor_tensor(out=gtmp[:, :], in0=gtmp[:, :],
                 scalar=-1e30, in1=glog[:, :], op0=OP.mult, op1=OP.add),
                 reads=[("gtmp",), ("glog",)], writes=[("gtmp",)])
            A.op("vector", lambda e: e.reduce_max(out=gm2[:, 0:1], in_=gtmp[:, :], axis=AX.X),
                 reads=[("gtmp",)], writes=[("gm2",)])
            A.op("vector", lambda e: e.tensor_scalar(out=gm1[:, 0:1], in0=gm1[:, 0:1],
                 scalar1=-1.0, scalar2=None, op0=OP.mult), reads=[("gm1",)], writes=[("gm1",)])
            A.op("scalar", lambda e: e.activation(out=gtmp[:, :], in_=glog[:, :],
                 func=AF.Exp, bias=gm1[:, 0:1]), reads=[("glog",), ("gm1",)], writes=[("gtmp",)])
            A.op("vector", lambda e: e.tensor_scalar(out=glog[:, :], in0=glog[:, :],
                 scalar1=gm2[:, 0:1], scalar2=None, op0=OP.is_ge),
                 reads=[("glog",), ("gm2",)], writes=[("glog",)])
            A.op("vector", lambda e: e.tensor_tensor(out=gtmp[:, :], in0=gtmp[:, :],
                 in1=glog[:, :], op=OP.mult), reads=[("gtmp",), ("glog",)], writes=[("gtmp",)])
            A.op("vector", lambda e: e.tensor_reduce(out=gm2[:, 0:1], in_=gtmp[:, :],
                 axis=AX.X, op=OP.add), reads=[("gtmp",)], writes=[("gm2",)])
            A.op("vector", lambda e: e.reciprocal(out=gm2[:, 0:1], in_=gm2[:, 0:1]),
                 reads=[("gm2",)], writes=[("gm2",)])
            A.op("vector", lambda e, tsub=tsub: e.tensor_scalar(out=comb_bf[:, tsub, :],
                 in0=gtmp[:, :], scalar1=gm2[:, 0:1], scalar2=None, op0=OP.mult),
                 reads=[("gtmp",), ("gm2",)], writes=[("comb",)])
            A.op("tensor", lambda e, tsub=tsub: e.matmul(out=psum[6][0:8, 0:128],
                 lhsT=comb_bf[:, tsub, :], rhs=ident_bf[:], start=True, stop=True),
                 reads=[("comb",), ("ident",)], writes=[("ps", 6)])
            A.op("vector", lambda e, tsub=tsub: e.tensor_copy(
                 out=combT[:, tsub * 128:(tsub + 1) * 128], in_=psum[6][0:8, 0:128]),
                 reads=[("ps", 6)], writes=[("combT",)])

        for ex in range(NE):
            A.op("tensor", lambda e, ex=ex: e.matmul(out=psum[6][:, 0:NC],
                 lhsT=esel_sb[:, ex * 128:(ex + 1) * 128], rhs=combT[:, :],
                 start=True, stop=True), reads=[("esel",), ("combT",)], writes=[("ps", 6)])
            A.op("vector", lambda e: e.tensor_copy(out=crep[:, :], in_=psum[6][:, 0:NC]),
                 reads=[("ps", 6)], writes=[("crep",)])

            def w1_evac(m, ps, pstok, ex=ex):
                A.op("scalar", lambda e, m=m, ps=ps, ex=ex: e.activation(
                    out=s1_sb[:, m, :], in_=ps[:, 0:NC], func=AF.Silu,
                    bias=b1_sb[:, ex, m:m + 1]),
                    reads=[pstok, ("b1",)], writes=[("s1", m)])
                A.op("vector", lambda e, m=m: e.tensor_tensor(out=s1_sb[:, m, :],
                     in0=s1_sb[:, m, :], in1=crep[:, :], op=OP.mult),
                     reads=[("s1", m), ("crep",)], writes=[("s1", m)])
            emit_proj(A, w_hbm=w1_hbm[ex], rhs_fn=lambda k: (zn_bf[:, k, :], ("znbf",)),
                      kd=KD, mt=16, evac=w1_evac, psum=psum, wring=wring, wrtok="we" + tag)

            def w2_evac(m, ps, pstok, ex=ex):
                A.op("scalar", lambda e, m=m, ps=ps, ex=ex: e.activation(
                    out=ps[:, 0:NC], in_=ps[:, 0:NC], func=AF.Identity,
                    bias=b2_sb[:, ex, m:m + 1]),
                    reads=[pstok, ("b2",)], writes=[pstok])
                A.op("vector", lambda e, m=m, ps=ps: e.tensor_tensor(
                    out=h_sb[:, m, :], in0=ps[:, 0:NC], in1=s1_sb[:, m, :],
                    op=OP.mult),
                    reads=[pstok, ("s1", m)], writes=[("h", m)])
            emit_proj(A, w_hbm=w2_hbm[ex], rhs_fn=lambda k: (zn_bf[:, k, :], ("znbf",)),
                      kd=KD, mt=16, evac=w2_evac, psum=psum, wring=wring, wrtok="we" + tag)

            def w3_evac(m, ps, pstok, ex=ex):
                if ex == 0:
                    A.op("vector", lambda e, m=m, ps=ps: e.tensor_copy(out=y_sb[:, m, :],
                         in_=ps[:, 0:NC]), reads=[pstok], writes=[("y", m)])
                else:
                    A.op("vector", lambda e, m=m, ps=ps: e.tensor_tensor(out=y_sb[:, m, :],
                         in0=y_sb[:, m, :], in1=ps[:, 0:NC], op=OP.add),
                         reads=[pstok, ("y", m)], writes=[("y", m)])
            emit_proj(A, w_hbm=w3_hbm[ex], rhs_fn=lambda k: (h_sb[:, k, :], ("h", k)),
                      kd=16, mt=KD, evac=w3_evac, psum=psum, wring=wring, wrtok="we" + tag)

        for m8 in range(8):
            A.op("tensor", lambda e, m8=m8: e.matmul(out=psum[m8][:, 0:NC],
                 lhsT=b3_sb[0:8, m8 * 128:(m8 + 1) * 128], rhs=combT[:, :],
                 start=True, stop=True), reads=[("b3",), ("combT",)], writes=[("ps", m8)])
            A.op("vector", lambda e, m8=m8: e.tensor_tensor(out=y_sb[:, m8, :],
                 in0=y_sb[:, m8, :], in1=psum[m8][:, 0:NC], op=OP.add),
                 reads=[("ps", m8), ("y", m8)], writes=[("y", m8)])

        for half in range(2):
            A.dma("sync", x1_ch[:, half * 4:(half + 1) * 4, :],
                  x1_hbm[:, half * 4:(half + 1) * 4, t0:t0 + NC], writes=[("x1ch",)])
        for k in range(KD):
            A.op("vector", lambda e, k=k: e.tensor_tensor(out=y_sb[:, k, :],
                 in0=y_sb[:, k, :], in1=z_ch[:, k, :], op=OP.add),
                 reads=[("y", k), ("zch",)], writes=[("y", k)])
            A.op("vector", lambda e, k=k: e.tensor_tensor(out=y_sb[:, k, :],
                 in0=y_sb[:, k, :], in1=x1_ch[:, k, :], op=OP.add),
                 reads=[("y", k), ("x1ch",)], writes=[("y", k)])
        rbk = x2b
        for k in range(KD):
            A.op("vector", lambda e, k=k, o=rbk[k % 2]: e.tensor_copy(
                out=o[:], in_=y_sb[:, k, :]), reads=[("y", k)], writes=[("rb", k % 2)])
            A.op("tensor", lambda e, k=k, i=rbk[k % 2]: e.matmul(out=psum[0][0:1, 0:NC],
                 lhsT=ones128b[:, 0:1], rhs=i[:], start=(k == 0), stop=(k == KD - 1)),
                 reads=[("rb", k % 2), ("ones",)], writes=[("ps", 0)])
        for k in range(KD):
            A.op("scalar", lambda e, k=k, o=rbk[k % 2]: e.activation(
                out=o[:], in_=y_sb[:, k, :], func=AF.Square),
                reads=[("y", k)], writes=[("rb", k % 2)])
            A.op("tensor", lambda e, k=k, i=rbk[k % 2]: e.matmul(out=psum[1][0:1, 0:NC],
                 lhsT=ones128b[:, 0:1], rhs=i[:], start=(k == 0), stop=(k == KD - 1)),
                 reads=[("rb", k % 2), ("ones",)], writes=[("ps", 1)])
        A.op("vector", lambda e: e.tensor_scalar(out=mrow[:, :], in0=psum[0][0:1, 0:NC],
             scalar1=1.0 / D, scalar2=None, op0=OP.mult), reads=[("ps", 0)], writes=[("mrow",)])
        A.op("vector", lambda e: e.tensor_scalar(out=vrow[:, :], in0=psum[1][0:1, 0:NC],
             scalar1=1.0 / D, scalar2=None, op0=OP.mult), reads=[("ps", 1)], writes=[("vrow",)])
        A.op("vector", lambda e: e.tensor_tensor(out=brow[:, :], in0=mrow[:, :],
             in1=mrow[:, :], op=OP.mult), reads=[("mrow",)], writes=[("brow",)])
        A.op("vector", lambda e: e.tensor_tensor(out=vrow[:, :], in0=vrow[:, :],
             in1=brow[:, :], op=OP.subtract), reads=[("vrow",), ("brow",)], writes=[("vrow",)])
        A.op("scalar", lambda e: e.activation(out=vrow[:, :], in_=vrow[:, :],
             func=AF.Sqrt, bias=eps1_sb[0:1, 0:1]), reads=[("vrow",), ("const",)], writes=[("vrow",)])
        A.op("vector", lambda e: e.reciprocal(out=vrow[:, :], in_=vrow[:, :]),
             reads=[("vrow",)], writes=[("vrow",)])
        A.op("vector", lambda e: e.tensor_tensor(out=brow[:, :], in0=mrow[:, :],
             in1=vrow[:, :], op=OP.mult), reads=[("mrow",), ("vrow",)], writes=[("brow",)])
        A.op("vector", lambda e: e.tensor_scalar(out=brow[:, :], in0=brow[:, :],
             scalar1=-1.0, scalar2=None, op0=OP.mult), reads=[("brow",)], writes=[("brow",)])
        A.op("vector", lambda e: e.tensor_copy(out=arow_bf[:, :], in_=vrow[:, :]),
             reads=[("vrow",)], writes=[("arb",)])
        A.op("vector", lambda e: e.tensor_copy(out=brow_bf[:, :], in_=brow[:, :]),
             reads=[("brow",)], writes=[("brb",)])
        A.op("tensor", lambda e: e.matmul(out=psum[2][:, 0:NC], lhsT=ones1b[0:1, :],
             rhs=arow_bf[:, :], start=True, stop=True), reads=[("arb",), ("ones",)], writes=[("ps", 2)])
        A.op("tensor", lambda e: e.matmul(out=psum[3][:, 0:NC], lhsT=ones1b[0:1, :],
             rhs=brow_bf[:, :], start=True, stop=True), reads=[("brb",), ("ones",)], writes=[("ps", 3)])
        for k in range(KD):
            A.op("vector", lambda e, k=k: e.tensor_tensor(out=tmp32[:], in0=y_sb[:, k, :],
                 in1=psum[2][:, 0:NC], op=OP.mult),
                 reads=[("y", k), ("ps", 2)], writes=[("tmp32",)])
            A.op("vector", lambda e, k=k: e.tensor_tensor(out=y_sb[:, k, :], in0=tmp32[:],
                 in1=psum[3][:, 0:NC], op=OP.add),
                 reads=[("tmp32",), ("ps", 3)], writes=[("y", k)])
        for half in range(2):
            A.dma("gpsimd", dst_hbm[:, half * 4:(half + 1) * 4, t0:t0 + NC],
                  y_sb[:, half * 4:(half + 1) * 4, :], reads=[("y", half * 4), ("y", half * 4 + 1),
                  ("y", half * 4 + 2), ("y", half * 4 + 3)], writes=[("dsth",)],
                  stream=("dst_stream",))


DI = 2048
KDI = DI // 128
DST = 16
DTR = 64


def emit_mamba_phase(au, es, nc, *, src_hbm, dst_hbm, win_hbm, cw_sb, cb_sb,
                     wx_sb, wdt_sb, bdt_sb, A_sb, D_sb, wout_hbm, esel32_sb,
                     psum, ones128b, ones1b, ones128f32, eps1_sb, tag="c"):
    A = au
    x_bf = es.enter_context(nc.sbuf_tensor(f"xbf{tag}", [128, KD, NC], BF16))
    xi_sb = es.enter_context(nc.sbuf_tensor(f"xi{tag}", [128, KDI, NC + 8], BF16))
    xc_bf = es.enter_context(nc.sbuf_tensor(f"xc{tag}", [128, KDI, NC], BF16))
    zs_bf = es.enter_context(nc.sbuf_tensor(f"zs{tag}", [128, KDI, NC], BF16))
    delta = es.enter_context(nc.sbuf_tensor(f"delta{tag}", [128, KDI, NC], BF16))
    u_bf = es.enter_context(nc.sbuf_tensor(f"u{tag}", [128, 4, NC], BF16))
    g_bf = es.enter_context(nc.sbuf_tensor(f"g{tag}", [128, KDI, NC], BF16))
    y_sb = es.enter_context(nc.sbuf_tensor(f"yy{tag}", [128, 4, NC], F32))
    hcar = es.enter_context(nc.sbuf_tensor(f"hc{tag}", [128, KDI, DST], F32))
    wring = [es.enter_context(nc.sbuf_tensor(f"w{tag}{i}", [128, 2048], BF16)) for i in range(3)]
    xdbl_bc = es.enter_context(nc.sbuf_tensor(f"xbc{tag}", [32, NC], BF16))
    dtT_bf = es.enter_context(nc.sbuf_tensor(f"dtT{tag}", [64, NC], BF16))
    vab = es.enter_context(nc.sbuf_tensor(f"vab{tag}", [128, NC], F32))
    rel = es.enter_context(nc.sbuf_tensor(f"rel{tag}", [128, NC], F32))
    brep = [es.enter_context(nc.sbuf_tensor(f"brep{tag}{i}", [128, NC], BF16)) for i in range(2)]
    crep = [es.enter_context(nc.sbuf_tensor(f"crep{tag}{i}", [128, NC], BF16)) for i in range(2)]
    SEG = NC + 1
    dA_bf = [es.enter_context(nc.sbuf_tensor(f"dA{tag}{i}", [128, 4 * SEG], BF16)) for i in range(2)]
    dBu = [es.enter_context(nc.sbuf_tensor(f"dBu{tag}{i}", [128, 4 * SEG], BF16)) for i in range(2)]
    hn = [es.enter_context(nc.sbuf_tensor(f"hn{tag}{i}", [128, 4 * SEG], BF16)) for i in range(2)]
    tmpb = [es.enter_context(nc.sbuf_tensor(f"tmb{tag}{i}", [128, 4, NC], BF16)) for i in range(2)]
    zcol = es.enter_context(nc.sbuf_tensor(f"zc{tag}", [128, 1], F32))
    x2f = es.enter_context(nc.sbuf_tensor(f"x2f{tag}", [128, KD, NC], F32))
    rbk = [es.enter_context(nc.sbuf_tensor(f"rb{tag}{i}", [128, NC], BF16)) for i in range(2)]
    mrow = es.enter_context(nc.sbuf_tensor(f"mr{tag}", [1, NC], F32))
    vrow = es.enter_context(nc.sbuf_tensor(f"vr{tag}", [1, NC], F32))
    arow_bf = es.enter_context(nc.sbuf_tensor(f"ab{tag}", [1, NC], BF16))
    brow_bf = es.enter_context(nc.sbuf_tensor(f"bb{tag}", [1, NC], BF16))

    A.op("vector", lambda e: e.memset(hcar[:], 0.0), writes=[("hcar",)])
    A.op("vector", lambda e: e.memset(xi_sb[:, :, 0:8], 0.0), writes=[("xi",)])
    A.op("vector", lambda e: e.memset(zcol[:], 0.0), writes=[("zc",)])
    for r in range(2):
        for s in range(4):
            A.op("vector", lambda e, r=r, s=s: e.memset(
                dA_bf[r][:, s * SEG:s * SEG + 1], 0.0), writes=[("dAb", r)])

    for c in range(NCH):
        t0 = c * NC
        A.dma("gpsimd", x_bf[:, :, :], src_hbm[:, :, t0:t0 + NC], writes=[("xbfc",)],
              stream=("xbfc",))
        def xz_evac(m, ps, pstok):
            if m < 16:
                A.op("vector", lambda e, m=m, ps=ps: e.tensor_copy(
                    out=xi_sb[:, m, 8:8 + NC], in_=ps[:, 0:NC]),
                    reads=[pstok], writes=[("xi",)])
            else:
                A.op("scalar", lambda e, m=m, ps=ps: e.activation(
                    out=zs_bf[:, m - 16, :], in_=ps[:, 0:NC], func=AF.Silu),
                    reads=[pstok], writes=[("zs",)])

        emit_proj(A, w_hbm=win_hbm, rhs_fn=lambda k: (x_bf[:, k, :], ("xbfc",)),
                  kd=KD, mt=32, evac=xz_evac, psum=psum, wring=wring, wrtok="wm" + tag)
        for m in range(KDI):
            A.op("vector", lambda e, m=m: e.tensor_scalar(
                out=vab[:], in0=xi_sb[:, m, 5:5 + NC], scalar1=cw_sb[:, m, 0:1],
                scalar2=None, op0=OP.mult),
                reads=[("xi",), ("cw",)], writes=[("vab",)])
            for j in (1, 2, 3):
                A.op("vector", lambda e, m=m, j=j: e.scalar_tensor_tensor(
                    out=vab[:], in0=xi_sb[:, m, 5 + j:5 + j + NC], scalar=cw_sb[:, m, j:j + 1],
                    in1=vab[:], op0=OP.mult, op1=OP.add),
                    reads=[("xi",), ("cw",), ("vab",)], writes=[("vab",)])
            A.op("scalar", lambda e, m=m: e.activation(out=xc_bf[:, m, :], in_=vab[:],
                 func=AF.Silu, bias=cb_sb[:, m:m + 1]),
                 reads=[("vab",), ("cb",)], writes=[("xc",)])
            A.op("vector", lambda e, m=m: e.tensor_copy(out=xi_sb[:, m, 5:8],
                 in_=xi_sb[:, m, 5 + NC:8 + NC]), reads=[("xi",)], writes=[("xi",)])
        for k in range(KDI):
            A.op("tensor", lambda e, k=k: e.matmul(out=psum[7][0:96, 0:NC],
                 lhsT=wx_sb[:, k, :], rhs=xc_bf[:, k, :], start=(k == 0), stop=(k == KDI - 1)),
                 reads=[("wx",), ("xc",)], writes=[("ps", 7)])
        A.op("vector", lambda e: e.tensor_copy(out=dtT_bf[:, :], in_=psum[7][0:64, 0:NC]),
             reads=[("ps", 7)], writes=[("dtT",)])
        A.op("vector", lambda e: e.tensor_copy(out=xdbl_bc[:, :], in_=psum[7][64:96, 0:NC]),
             reads=[("ps", 7)], writes=[("xbc",)])
        for m in range(KDI):
            A.op("tensor", lambda e, m=m: e.matmul(out=psum[6][:, 0:NC],
                 lhsT=wdt_sb[:, m * 128:(m + 1) * 128], rhs=dtT_bf[:, :],
                 start=True, stop=True), reads=[("wdt",), ("dtT",)], writes=[("ps", 6)])
            A.op("scalar", lambda e, m=m: e.activation(out=vab[:], in_=psum[6][:, 0:NC],
                 func=AF.Abs, bias=bdt_sb[:, m:m + 1]),
                 reads=[("ps", 6), ("bdt",)], writes=[("vab",)])
            A.op("scalar", lambda e: e.activation(out=vab[:], in_=vab[:], func=AF.Exp,
                 scale=-1.0), reads=[("vab",)], writes=[("vab",)])
            A.op("scalar", lambda e: e.activation(out=vab[:], in_=vab[:], func=AF.Ln,
                 bias=ones128f32[:, 0:1]), reads=[("vab",), ("onesf",)], writes=[("vab",)])
            A.op("scalar", lambda e, m=m: e.activation(out=rel[:], in_=psum[6][:, 0:NC],
                 func=AF.Relu, bias=bdt_sb[:, m:m + 1]),
                 reads=[("ps", 6), ("bdt",)], writes=[("rel",)])
            A.op("vector", lambda e, m=m: e.tensor_tensor(out=delta[:, m, :], in0=vab[:],
                 in1=rel[:], op=OP.add), reads=[("vab",), ("rel",)], writes=[("delta", m)])
        it = 0
        for mg in range(4):
            ms = range(4 * mg, 4 * mg + 4)
            for mi, m in enumerate(ms):
                A.op("vector", lambda e, m=m, mi=mi: e.tensor_tensor(out=u_bf[:, mi, :],
                     in0=delta[:, m, :], in1=xc_bf[:, m, :], op=OP.mult),
                     reads=[("delta", m), ("xc",)], writes=[("u", mi)])
            for n in range(DST):
                k2 = it % 2
                n2 = n % 2
                A.op("tensor", lambda e, n=n: e.matmul(out=psum[5][:, 0:NC],
                     lhsT=esel32_sb[:, n * 128:(n + 1) * 128], rhs=xdbl_bc[:, :],
                     start=True, stop=True), reads=[("esel32",), ("xbc",)], writes=[("ps", 5)])
                A.op("scalar", lambda e, n=n: e.copy(out=brep[n % 2][:], in_=psum[5][:, 0:NC]),
                     reads=[("ps", 5)], writes=[("brep", n % 2)])
                A.op("tensor", lambda e, n=n: e.matmul(out=psum[5][:, 0:NC],
                     lhsT=esel32_sb[:, (16 + n) * 128:(17 + n) * 128], rhs=xdbl_bc[:, :],
                     start=True, stop=True), reads=[("esel32",), ("xbc",)], writes=[("ps", 5)])
                A.op("scalar", lambda e, n=n: e.copy(out=crep[n % 2][:], in_=psum[5][:, 0:NC]),
                     reads=[("ps", 5)], writes=[("crep", n % 2)])
                for mi, m in enumerate(ms):
                    A.op("scalar", lambda e, m=m, n=n, o=dA_bf[k2], s0=mi * SEG: e.activation(
                         out=o[:, s0 + 1:s0 + SEG], in_=delta[:, m, :],
                         func=AF.Exp, scale=A_sb[:, m, n:n + 1]),
                         reads=[("delta", m), ("A",)], writes=[("dAb", k2)])
                for mi, m in enumerate(ms):
                    eng = "vector" if mi == 0 else "gpsimd"
                    A.op(eng, lambda e, mi=mi, o=dBu[k2], n2=n2, s0=mi * SEG: e.tensor_tensor(
                         out=o[:, s0 + 1:s0 + SEG],
                         in0=u_bf[:, mi, :], in1=brep[n2][:], op=OP.mult),
                         reads=[("u", mi), ("brep", n2)], writes=[("dBu", k2)])
                    A.op("scalar", lambda e, m=m, n=n, o=dBu[k2], s0=mi * SEG: e.copy(
                         out=o[:, s0:s0 + 1], in_=hcar[:, m, n:n + 1]),
                         reads=[("hcar",)], writes=[("dBu", k2)])
                A.op("vector", lambda e, i0=dA_bf[k2], i1=dBu[k2], o=hn[k2]:
                     e.tensor_tensor_scan(
                     out=o[:, :], data0=i0[:, :], data1=i1[:, :], initial=zcol[:, 0:1],
                     op0=OP.mult, op1=OP.add),
                     reads=[("dAb", k2), ("dBu", k2), ("zc",)], writes=[("hn", k2)])
                for mi, m in enumerate(ms):
                    A.op("scalar", lambda e, m=m, n=n, i=hn[k2], s0=mi * SEG: e.copy(
                         out=hcar[:, m, n:n + 1], in_=i[:, s0 + SEG - 1:s0 + SEG]),
                         reads=[("hn", k2)], writes=[("hcar",)])
                if n == 0:
                    for mi in range(4):
                        A.op("vector", lambda e, mi=mi, i=hn[k2], n2=n2, s0=mi * SEG:
                             e.tensor_tensor(
                             out=y_sb[:, mi, :],
                             in0=i[:, s0 + 1:s0 + SEG], in1=crep[n2][:], op=OP.mult),
                             reads=[("hn", k2), ("crep", n2)], writes=[("yc", mi)])
                else:
                    for mi in range(4):
                        A.op("vector", lambda e, mi=mi, i=hn[k2], o=tmpb[k2], n2=n2,
                             s0=mi * SEG: e.tensor_tensor(
                             out=o[:, mi, :],
                             in0=i[:, s0 + 1:s0 + SEG], in1=crep[n2][:], op=OP.mult),
                             reads=[("hn", k2), ("crep", n2)], writes=[("tmb", k2)])
                    A.op("gpsimd", lambda e, i=tmpb[k2]: e.tensor_tensor(
                         out=y_sb[:, 0:4, :].rearrange("p a b -> p (a b)"),
                         in0=y_sb[:, 0:4, :].rearrange("p a b -> p (a b)"),
                         in1=i[:, 0:4, :].rearrange("p a b -> p (a b)"), op=OP.add),
                         reads=[("tmb", k2), ("yc", 0), ("yc", 1), ("yc", 2), ("yc", 3)],
                         writes=[("yc", 0), ("yc", 1), ("yc", 2), ("yc", 3)])
                it += 1
            for mi, m in enumerate(ms):
                A.op("vector", lambda e, m=m, mi=mi: e.scalar_tensor_tensor(out=y_sb[:, mi, :],
                     in0=xc_bf[:, m, :], scalar=D_sb[:, m:m + 1], in1=y_sb[:, mi, :],
                     op0=OP.mult, op1=OP.add),
                     reads=[("xc",), ("D",), ("yc", mi)], writes=[("yc", mi)])
                A.op("vector", lambda e, m=m, mi=mi: e.tensor_tensor(out=g_bf[:, m, :],
                     in0=y_sb[:, mi, :], in1=zs_bf[:, m, :], op=OP.mult),
                     reads=[("yc", mi), ("zs",)], writes=[("u2", m)])
        for half in range(2):
            A.dma("sync", x2f[:, half * 4:(half + 1) * 4, :],
                  src_hbm[:, half * 4:(half + 1) * 4, t0:t0 + NC], writes=[("x2f",)])
        def mo_evac(m, ps, pstok):
            A.op("vector", lambda e, m=m, ps=ps: e.tensor_tensor(out=x2f[:, m, :],
                 in0=x2f[:, m, :], in1=ps[:, 0:NC], op=OP.add),
                 reads=[pstok, ("x2f",)], writes=[("x2f",)])
        emit_proj(A, w_hbm=wout_hbm, rhs_fn=lambda k: (g_bf[:, k, :], ("u2", k)),
                  kd=KDI, mt=KD, evac=mo_evac, psum=psum, wring=wring, wrtok="wm" + tag)
        for k in range(KD):
            A.op("vector", lambda e, k=k, o=rbk[k % 2]: e.tensor_copy(
                out=o[:], in_=x2f[:, k, :]), reads=[("x2f",)], writes=[("rb", k % 2)])
            A.op("tensor", lambda e, k=k, i=rbk[k % 2]: e.matmul(out=psum[0][0:1, 0:NC],
                 lhsT=ones128b[:, 0:1], rhs=i[:], start=(k == 0), stop=(k == KD - 1)),
                 reads=[("rb", k % 2), ("ones",)], writes=[("ps", 0)])
        for k in range(KD):
            A.op("scalar", lambda e, k=k, o=rbk[k % 2]: e.activation(
                out=o[:], in_=x2f[:, k, :], func=AF.Square),
                reads=[("x2f",)], writes=[("rb", k % 2)])
            A.op("tensor", lambda e, k=k, i=rbk[k % 2]: e.matmul(out=psum[1][0:1, 0:NC],
                 lhsT=ones128b[:, 0:1], rhs=i[:], start=(k == 0), stop=(k == KD - 1)),
                 reads=[("rb", k % 2), ("ones",)], writes=[("ps", 1)])
        A.op("vector", lambda e: e.tensor_scalar(out=mrow[:, :], in0=psum[0][0:1, 0:NC],
             scalar1=1.0 / D, scalar2=None, op0=OP.mult), reads=[("ps", 0)], writes=[("mrow",)])
        A.op("vector", lambda e: e.tensor_scalar(out=vrow[:, :], in0=psum[1][0:1, 0:NC],
             scalar1=1.0 / D, scalar2=None, op0=OP.mult), reads=[("ps", 1)], writes=[("vrow",)])
        A.op("vector", lambda e: e.tensor_tensor(out=rel[0:1, :], in0=mrow[:, :],
             in1=mrow[:, :], op=OP.mult), reads=[("mrow",)], writes=[("rel",)])
        A.op("vector", lambda e: e.tensor_tensor(out=vrow[:, :], in0=vrow[:, :],
             in1=rel[0:1, :], op=OP.subtract), reads=[("vrow",), ("rel",)], writes=[("vrow",)])
        A.op("scalar", lambda e: e.activation(out=vrow[:, :], in_=vrow[:, :],
             func=AF.Sqrt, bias=eps1_sb[0:1, 0:1]), reads=[("vrow",), ("const",)], writes=[("vrow",)])
        A.op("vector", lambda e: e.reciprocal(out=vrow[:, :], in_=vrow[:, :]),
             reads=[("vrow",)], writes=[("vrow",)])
        A.op("vector", lambda e: e.tensor_tensor(out=mrow[:, :], in0=mrow[:, :],
             in1=vrow[:, :], op=OP.mult), reads=[("mrow",), ("vrow",)], writes=[("mrow",)])
        A.op("vector", lambda e: e.tensor_scalar(out=mrow[:, :], in0=mrow[:, :],
             scalar1=-1.0, scalar2=None, op0=OP.mult), reads=[("mrow",)], writes=[("mrow",)])
        A.op("vector", lambda e: e.tensor_copy(out=arow_bf[:, :], in_=vrow[:, :]),
             reads=[("vrow",)], writes=[("arb",)])
        A.op("vector", lambda e: e.tensor_copy(out=brow_bf[:, :], in_=mrow[:, :]),
             reads=[("mrow",)], writes=[("brb",)])
        A.op("tensor", lambda e: e.matmul(out=psum[2][:, 0:NC], lhsT=ones1b[0:1, :],
             rhs=arow_bf[:, :], start=True, stop=True), reads=[("arb",), ("ones",)], writes=[("ps", 2)])
        A.op("tensor", lambda e: e.matmul(out=psum[3][:, 0:NC], lhsT=ones1b[0:1, :],
             rhs=brow_bf[:, :], start=True, stop=True), reads=[("brb",), ("ones",)], writes=[("ps", 3)])
        for k in range(KD):
            A.op("vector", lambda e, k=k: e.tensor_tensor(out=rel[:], in0=x2f[:, k, :],
                 in1=psum[2][:, 0:NC], op=OP.mult),
                 reads=[("x2f",), ("ps", 2)], writes=[("rel",)])
            A.op("vector", lambda e, k=k: e.tensor_tensor(out=x2f[:, k, :], in0=rel[:],
                 in1=psum[3][:, 0:NC], op=OP.add),
                 reads=[("rel",), ("ps", 3)], writes=[("x2f",)])
        for half in range(2):
            A.dma("gpsimd", dst_hbm[:, half * 4:(half + 1) * 4, t0:t0 + NC],
                  x2f[:, half * 4:(half + 1) * 4, :], reads=[("x2f",)],
                  writes=[("dsth",)], stream=("dst_stream",))


def bfly_perm_B():
    permB = np.zeros(4096, dtype=np.int64)
    for tB in range(32):
        for pB in range(128):
            b, a = pB // 32, pB % 32
            f = a * 128 + tB * 4 + b
            permB[tB * 128 + pB] = f
    return permB


def bfly_coeffs(wstack):
    coefA = np.zeros((7, 32, 64), np.float32)
    for s in range(7):
        for t in range(32):
            coefA[s, t] = wstack[s, t * 64:(t + 1) * 64]
    coefB = np.zeros((5, 32, 64), np.float32)
    for si, s in enumerate(range(7, 12)):
        sig = 2 ** s
        sigB = 2 ** (s - 7)
        for tB in range(32):
            for kp in range(64):
                a_pB = (kp // sigB) * 2 * sigB + kp % sigB
                b, a = a_pB // 32, a_pB % 32
                i = a * 128 + tB * 4 + b
                k = (i // (2 * sig)) * sig + i % sig
                coefB[si, tB, kp] = wstack[s, k]
    return coefA, coefB


def emit_bfly_build(au, es, nc, *, coef_hbm, which, M_sb, ident_bf, psum, ones1b, tag):
    A = au
    strides = [2 ** s for s in range(7)] if which == "A" else [2 ** i for i in range(5)]
    ns = len(strides)
    work = es.enter_context(nc.sbuf_tensor(f"bw{tag}", [128, 128], F32))
    prod = [es.enter_context(nc.sbuf_tensor(f"bp{tag}{i}", [128, 64], F32)) for i in range(4)]
    cslab = [es.enter_context(nc.sbuf_tensor(f"bc{tag}{i}", [128, ns * 4 * 64], BF16))
             for i in range(2)]
    wtok = ("bwork", tag)
    for t in range(32):
        cs, ctok = cslab[t % 2], ("bcoef", tag, t % 2)
        A.dma("sync", cs[:, :], coef_hbm[t, :, :], writes=[ctok])
        A.op("vector", lambda e: e.tensor_copy(out=work[:], in_=ident_bf[:]),
             reads=[("ident",)], writes=[wtok])
        for si, sig in enumerate(strides):
            base = si * 4 * 64
            for pi, (side, ci) in enumerate(((0, 0), (1, 1), (0, 2), (1, 3))):
                A.op("vector", lambda e, pi=pi, side=side, ci=ci, sig=sig, base=base, cs=cs: e.tensor_tensor(
                    out=prod[pi].rearrange("p (b s) -> p b s", s=sig)[:, :, :],
                    in0=work.rearrange("p (b two s) -> p b two s", two=2, s=sig)[:, :, side, :],
                    in1=cs[:, base + ci * 64: base + (ci + 1) * 64].rearrange(
                        "p (b s) -> p b s", s=sig)[:, :, :],
                    op=OP.mult),
                    reads=[wtok, ctok], writes=[("bprod", tag, pi)])
            A.op("vector", lambda e, sig=sig: e.tensor_tensor(
                out=work.rearrange("p (b two s) -> p b two s", two=2, s=sig)[:, :, 0, :],
                in0=prod[0].rearrange("p (b s) -> p b s", s=sig)[:, :, :],
                in1=prod[1].rearrange("p (b s) -> p b s", s=sig)[:, :, :], op=OP.add),
                reads=[("bprod", tag, 0), ("bprod", tag, 1)], writes=[wtok])
            A.op("vector", lambda e, sig=sig: e.tensor_tensor(
                out=work.rearrange("p (b two s) -> p b two s", two=2, s=sig)[:, :, 1, :],
                in0=prod[2].rearrange("p (b s) -> p b s", s=sig)[:, :, :],
                in1=prod[3].rearrange("p (b s) -> p b s", s=sig)[:, :, :], op=OP.add),
                reads=[("bprod", tag, 2), ("bprod", tag, 3)], writes=[wtok])
        A.op("vector", lambda e, t=t: e.tensor_copy(out=M_sb[:, t, :], in_=work[:]),
             reads=[wtok], writes=[("M", tag)])


def emit_bfly_phase(au, es, nc, *, src_hbm, dst_hbm, wexp_hbm, bexp_sb, MAu, MBu,
                    MAd, MBd, ubias_sb, wcon_hbm, bcon_sb, shuf_hbm, psum,
                    ones128b, ones1b, eps1_sb, tag="d"):
    A = au
    x_bf = [es.enter_context(nc.sbuf_tensor(f"xb{tag}", [128, KD, NC], BF16))]
    h_sbs = [es.enter_context(nc.sbuf_tensor(f"h{tag}{i}", [128, 32, NC], BF16)) for i in range(2)]
    h2_sbs = [es.enter_context(nc.sbuf_tensor(f"h2{tag}{i}", [128, 32, NC], BF16)) for i in range(2)]
    x3f = es.enter_context(nc.sbuf_tensor(f"x3f{tag}", [128, KD, NC], F32))
    wring = [es.enter_context(nc.sbuf_tensor(f"w{tag}{i}", [128, 2048], BF16)) for i in range(3)]
    rbk = [es.enter_context(nc.sbuf_tensor(f"rb{tag}{i}", [128, NC], BF16)) for i in range(2)]
    mrow = es.enter_context(nc.sbuf_tensor(f"mr{tag}", [1, NC], F32))
    vrow = es.enter_context(nc.sbuf_tensor(f"vr{tag}", [1, NC], F32))
    brow = es.enter_context(nc.sbuf_tensor(f"br{tag}", [1, NC], F32))
    arow_bf = es.enter_context(nc.sbuf_tensor(f"ab{tag}", [1, NC], BF16))
    brow_bf = es.enter_context(nc.sbuf_tensor(f"bb{tag}", [1, NC], BF16))

    for c in range(NCH):
        t0 = c * NC
        cp = c % 2
        h_sb, h2_sb = h_sbs[cp], h2_sbs[cp]
        xb = x_bf[0]
        hhtok, h2tok = ("hh", cp), ("h2", cp)
        shuf_hbm_c = shuf_hbm[cp]
        shufA = shuf_hbm_c.rearrange("c b a t -> (c b) a t")
        shufB = shuf_hbm_c.rearrange("c b a t -> (b a) c t")
        A.dma("gpsimd", xb[:, :, :], src_hbm[:, :, t0:t0 + NC], writes=[("xb", 0)],
              stream=("xb", 0))
        def we_evac(m, ps, pstok, h_sb=h_sb, hhtok=hhtok):
            A.op("scalar", lambda e, m=m, ps=ps, h_sb=h_sb, cp=cp: e.activation(
                out=h_sb[:, m, :], in_=ps[:, 0:NC], func=AF.Identity,
                bias=bexp_sb[:, m:m + 1]),
                reads=[pstok, ("bexp",)], writes=[hhtok])
        emit_proj(A, w_hbm=wexp_hbm, rhs_fn=lambda k, xb=xb: (xb[:, k, :], ("xb", 0)),
                  kd=KD, mt=32, evac=we_evac, psum=psum, wring=wring, wrtok="wd" + tag,
                  bank_base=4 * cp, pingpong=False)
        for t in range(32):
            A.op("tensor", lambda e, t=t, h_sb=h_sb, cp=cp: e.matmul(out=psum[4 * cp + t % 4][:, 0:NC],
                 lhsT=MAu[:, t, :], rhs=h_sb[:, t, :], start=True, stop=True),
                 reads=[("M", "Au"), hhtok], writes=[("ps", 4 * cp + t % 4)])
            A.op("vector", lambda e, t=t, h2_sb=h2_sb, cp=cp: e.tensor_copy(out=h2_sb[:, t, :],
                 in_=psum[4 * cp + t % 4][:, 0:NC]), reads=[("ps", 4 * cp + t % 4)], writes=[h2tok])
        for q in range(4):
            A.dma("gpsimd", shufA[:, 8 * q:8 * q + 8, :], h2_sb[:, 8 * q:8 * q + 8, :],
                  reads=[h2tok], writes=[("shufh", cp)], stream=("shufh", cp))
        for q in range(4):
            A.dma("sync", h_sb[:, 8 * q:8 * q + 8, :], shufB[:, 8 * q:8 * q + 8, :],
                  reads=[("shufh", cp)], writes=[hhtok], stream=hhtok)
        for t in range(32):
            A.op("tensor", lambda e, t=t, h_sb=h_sb, cp=cp: e.matmul(out=psum[4 * cp + t % 4][:, 0:NC],
                 lhsT=MBu[:, t, :], rhs=h_sb[:, t, :], start=True, stop=True),
                 reads=[("M", "Bu"), hhtok], writes=[("ps", 4 * cp + t % 4)])
            A.op("scalar", lambda e, t=t, h2_sb=h2_sb, cp=cp: e.activation(out=h2_sb[:, t, :],
                 in_=psum[4 * cp + t % 4][:, 0:NC], func=AF.Gelu, bias=ubias_sb[:, t:t + 1]),
                 reads=[("ps", 4 * cp + t % 4), ("ubias",)], writes=[h2tok])
        for q in range(4):
            A.dma("gpsimd", shufB[:, 8 * q:8 * q + 8, :], h2_sb[:, 8 * q:8 * q + 8, :],
                  reads=[h2tok], writes=[("shufh", cp)], stream=("shufh", cp))
        for q in range(4):
            A.dma("sync", h_sb[:, 8 * q:8 * q + 8, :], shufA[:, 8 * q:8 * q + 8, :],
                  reads=[("shufh", cp)], writes=[hhtok], stream=hhtok)
        for t in range(32):
            A.op("tensor", lambda e, t=t, h_sb=h_sb, cp=cp: e.matmul(out=psum[4 * cp + t % 4][:, 0:NC],
                 lhsT=MAd[:, t, :], rhs=h_sb[:, t, :], start=True, stop=True),
                 reads=[("M", "Ad"), hhtok], writes=[("ps", 4 * cp + t % 4)])
            A.op("vector", lambda e, t=t, h2_sb=h2_sb, cp=cp: e.tensor_copy(out=h2_sb[:, t, :],
                 in_=psum[4 * cp + t % 4][:, 0:NC]), reads=[("ps", 4 * cp + t % 4)], writes=[h2tok])
        for q in range(4):
            A.dma("gpsimd", shufA[:, 8 * q:8 * q + 8, :], h2_sb[:, 8 * q:8 * q + 8, :],
                  reads=[h2tok], writes=[("shufh", cp)], stream=("shufh", cp))
        for q in range(4):
            A.dma("sync", h_sb[:, 8 * q:8 * q + 8, :], shufB[:, 8 * q:8 * q + 8, :],
                  reads=[("shufh", cp)], writes=[hhtok], stream=hhtok)
        for t in range(32):
            A.op("tensor", lambda e, t=t, h_sb=h_sb, cp=cp: e.matmul(out=psum[4 * cp + t % 4][:, 0:NC],
                 lhsT=MBd[:, t, :], rhs=h_sb[:, t, :], start=True, stop=True),
                 reads=[("M", "Bd"), hhtok], writes=[("ps", 4 * cp + t % 4)])
            A.op("vector", lambda e, t=t, h2_sb=h2_sb, cp=cp: e.tensor_copy(out=h2_sb[:, t, :],
                 in_=psum[4 * cp + t % 4][:, 0:NC]), reads=[("ps", 4 * cp + t % 4)], writes=[h2tok])
        for half in range(2):
            A.dma("sync", x3f[:, half * 4:(half + 1) * 4, :],
                  src_hbm[:, half * 4:(half + 1) * 4, t0:t0 + NC], writes=[("x3f",)])
        def wc_evac(m, ps, pstok):
            A.op("scalar", lambda e, m=m, ps=ps: e.activation(out=ps[:, 0:NC],
                 in_=ps[:, 0:NC], func=AF.Identity, bias=bcon_sb[:, m:m + 1]),
                 reads=[pstok, ("bcon",)], writes=[pstok])
            A.op("vector", lambda e, m=m, ps=ps, cp=cp: e.tensor_tensor(out=x3f[:, m, :],
                 in0=x3f[:, m, :], in1=ps[:, 0:NC], op=OP.add),
                 reads=[pstok, ("x3f",)], writes=[("x3f",)])
        emit_proj(A, w_hbm=wcon_hbm, rhs_fn=lambda k, h2_sb=h2_sb, h2tok=h2tok: (h2_sb[:, k, :], h2tok),
                  kd=32, mt=KD, evac=wc_evac, psum=psum, wring=wring, wrtok="wd" + tag,
                  bank_base=4 * cp, pingpong=False)
        for k in range(KD):
            A.op("vector", lambda e, k=k, o=rbk[k % 2], cp=cp: e.tensor_copy(
                out=o[:], in_=x3f[:, k, :]), reads=[("x3f",)], writes=[("rb", k % 2)])
            A.op("tensor", lambda e, k=k, i=rbk[k % 2], cp=cp: e.matmul(out=psum[4 * cp + 0][0:1, 0:NC],
                 lhsT=ones128b[:, 0:1], rhs=i[:], start=(k == 0), stop=(k == KD - 1)),
                 reads=[("rb", k % 2), ("ones",)], writes=[("ps", 4 * cp + 0)])
        for k in range(KD):
            A.op("scalar", lambda e, k=k, o=rbk[k % 2], cp=cp: e.activation(
                out=o[:], in_=x3f[:, k, :], func=AF.Square),
                reads=[("x3f",)], writes=[("rb", k % 2)])
            A.op("tensor", lambda e, k=k, i=rbk[k % 2], cp=cp: e.matmul(out=psum[4 * cp + 1][0:1, 0:NC],
                 lhsT=ones128b[:, 0:1], rhs=i[:], start=(k == 0), stop=(k == KD - 1)),
                 reads=[("rb", k % 2), ("ones",)], writes=[("ps", 4 * cp + 1)])
        A.op("vector", lambda e, cp=cp: e.tensor_scalar(out=mrow[:, :], in0=psum[4 * cp + 0][0:1, 0:NC],
             scalar1=1.0 / D, scalar2=None, op0=OP.mult), reads=[("ps", 4 * cp + 0)], writes=[("mrow",)])
        A.op("vector", lambda e, cp=cp: e.tensor_scalar(out=vrow[:, :], in0=psum[4 * cp + 1][0:1, 0:NC],
             scalar1=1.0 / D, scalar2=None, op0=OP.mult), reads=[("ps", 4 * cp + 1)], writes=[("vrow",)])
        A.op("vector", lambda e: e.tensor_tensor(out=brow[:, :], in0=mrow[:, :],
             in1=mrow[:, :], op=OP.mult), reads=[("mrow",)], writes=[("brow",)])
        A.op("vector", lambda e: e.tensor_tensor(out=vrow[:, :], in0=vrow[:, :],
             in1=brow[:, :], op=OP.subtract), reads=[("vrow",), ("brow",)], writes=[("vrow",)])
        A.op("scalar", lambda e: e.activation(out=vrow[:, :], in_=vrow[:, :],
             func=AF.Sqrt, bias=eps1_sb[0:1, 0:1]), reads=[("vrow",), ("const",)], writes=[("vrow",)])
        A.op("vector", lambda e: e.reciprocal(out=vrow[:, :], in_=vrow[:, :]),
             reads=[("vrow",)], writes=[("vrow",)])
        A.op("vector", lambda e: e.tensor_tensor(out=brow[:, :], in0=mrow[:, :],
             in1=vrow[:, :], op=OP.mult), reads=[("mrow",), ("vrow",)], writes=[("brow",)])
        A.op("vector", lambda e, cp=cp: e.tensor_scalar(out=brow[:, :], in0=brow[:, :],
             scalar1=-1.0, scalar2=None, op0=OP.mult), reads=[("brow",)], writes=[("brow",)])
        A.op("vector", lambda e, cp=cp: e.tensor_copy(out=arow_bf[:, :], in_=vrow[:, :]),
             reads=[("vrow",)], writes=[("arb",)])
        A.op("vector", lambda e, cp=cp: e.tensor_copy(out=brow_bf[:, :], in_=brow[:, :]),
             reads=[("brow",)], writes=[("brb",)])
        A.op("tensor", lambda e, cp=cp: e.matmul(out=psum[4 * cp + 2][:, 0:NC], lhsT=ones1b[0:1, :],
             rhs=arow_bf[:, :], start=True, stop=True), reads=[("arb",), ("ones",)], writes=[("ps", 4 * cp + 2)])
        A.op("tensor", lambda e, cp=cp: e.matmul(out=psum[4 * cp + 3][:, 0:NC], lhsT=ones1b[0:1, :],
             rhs=brow_bf[:, :], start=True, stop=True), reads=[("brb",), ("ones",)], writes=[("ps", 4 * cp + 3)])
        for k in range(KD):
            A.op("vector", lambda e, k=k, cp=cp: e.tensor_tensor(out=x3f[:, k, :], in0=x3f[:, k, :],
                 in1=psum[4 * cp + 2][:, 0:NC], op=OP.mult),
                 reads=[("x3f",), ("ps", 4 * cp + 2)], writes=[("x3f",)])
            A.op("vector", lambda e, k=k, cp=cp: e.tensor_tensor(out=x3f[:, k, :], in0=x3f[:, k, :],
                 in1=psum[4 * cp + 3][:, 0:NC], op=OP.add),
                 reads=[("x3f",), ("ps", 4 * cp + 3)], writes=[("x3f",)])
        for half in range(2):
            A.dma("gpsimd", dst_hbm[:, half * 4:(half + 1) * 4, t0:t0 + NC],
                  x3f[:, half * 4:(half + 1) * 4, :], reads=[("x3f",)],
                  writes=[("dsth",)], stream=("dst_stream",))


def emit_w2bf(au, es, nc, pairs, tag="w2", bufw=16384, defer=False):
    A = au
    bufs = [es.enter_context(nc.sbuf_tensor(f"{tag}{i}", [128, bufw], BF16)) for i in range(2)]
    units = []
    for src, dst in pairs:
        kk, _, F = src.shape
        step = max(1, bufw // F)
        for k0 in range(0, kk, step):
            units.append((src, dst, k0, min(step, kk - k0), F))

    state = {"i": 0}

    def feed(n=1):
        for _ in range(n):
            i = state["i"]
            if i >= len(units):
                return
            src, dst, k0, ks, F = units[i]
            b = bufs[i % 2]
            tok = (tag + "b", i % 2)
            A.dma("gpsimd", b[:, 0:ks * F].rearrange("p (k f) -> p k f", k=ks),
                  src[k0:k0 + ks, :, :].rearrange("k p f -> p k f"), writes=[tok])
            A.dma("sync", dst[k0:k0 + ks, :, :].rearrange("k p f -> p k f"),
                  b[:, 0:ks * F].rearrange("p (k f) -> p k f", k=ks),
                  reads=[tok], writes=[(tag + "st",)], stream=(tag + "st",))
            state["i"] += 1

    if defer:
        return feed
    feed(len(units))


def emit_proj(au, *, w_hbm, rhs_fn, kd, mt, evac, psum, wring, wrtok, kslab=4,
              group=4, eng="sync", bank_base=0, pingpong=True):
    A = au
    nbuf = len(wring)
    i = 0
    for g in range((mt + group - 1) // group):
        gm = min(group, mt - g * group)
        b0 = bank_base + ((g % 2) * group if pingpong else 0)
        for k0 in range(0, kd, kslab):
            ks = min(kslab, kd - k0)
            wb = wring[i % nbuf]
            tok = (wrtok, i % nbuf)
            A.dma(eng, wb[:, 0:ks * gm * 128].rearrange("p (k f) -> p k f", k=ks),
                  w_hbm[k0:k0 + ks, :, g * group * 128:g * group * 128 + gm * 128]
                  .rearrange("k p f -> p k f"), writes=[tok])
            for dk in range(ks):
                k = k0 + dk
                rhs, rtok = rhs_fn(k)
                for m8 in range(gm):
                    A.op("tensor", lambda e, wb=wb, dk=dk, m8=m8, k=k, rhs=rhs, gm=gm, ks=ks, b0=b0:
                         e.matmul(out=psum[b0 + m8][:, 0:NC],
                                  lhsT=wb[:, 0:ks * gm * 128].rearrange(
                                      "p (k f) -> p k f", k=ks)[:, dk, m8 * 128:(m8 + 1) * 128],
                                  rhs=rhs, start=(k == 0), stop=(k == kd - 1)),
                         reads=[tok, rtok], writes=[("ps", b0 + m8)])
            i += 1
        for m8 in range(gm):
            evac(g * group + m8, psum[b0 + m8], ("ps", b0 + m8))



from contextlib import ExitStack
import ml_dtypes
from concourse.bass_utils import run_bass_kernel_spmd

NCORES = 8
_BUILD_CACHE = {}


def _prep_shared(i):
    s = {}
    perm = rope_perm()
    wq = i["L0_attn_w_qkv"][0:D][perm]
    wk = i["L0_attn_w_qkv"][D:2 * D][perm]
    s["a_wqk"] = kpt(np.ascontiguousarray(np.concatenate([wq, wk], 0).T))
    s["a_bqk"] = pvec(np.concatenate([i["L0_attn_b_qkv"][0:D][perm],
                                      i["L0_attn_b_qkv"][D:2 * D][perm]]))
    s["a_wv"] = kpt(np.ascontiguousarray(i["L0_attn_w_qkv"][2 * D:3 * D].T))
    s["a_bv"] = i["L0_attn_b_qkv"][2 * D:3 * D].reshape(1, 1024).copy()
    s["a_wout"] = kpt(np.ascontiguousarray(i["L0_attn_w_out"].T))
    s["a_bout"] = pvec(i["L0_attn_b_out"])
    cos, sin = rope_tables()
    s["cosT"] = cos.astype(ml_dtypes.bfloat16)
    s["sinT"] = sin.astype(ml_dtypes.bfloat16)
    s["masks"] = causal_masks().astype(ml_dtypes.bfloat16)
    qw = i["L0_ffn_qkv_w"]
    s["f_wqk"] = kpt(np.ascontiguousarray(np.concatenate([qw[0:D], qw[D:2 * D]], 0).T))
    s["f_bqk"] = pvec(np.concatenate([i["L0_ffn_qkv_b"][0:D], i["L0_ffn_qkv_b"][D:2 * D]]))
    s["f_wv"] = kpt(np.ascontiguousarray(qw[2 * D:3 * D].T))
    s["f_bv"] = i["L0_ffn_qkv_b"][2 * D:3 * D].reshape(1, 1024).copy()
    s["f_wout"] = kpt(np.ascontiguousarray(i["L0_ffn_out_w"].T))
    s["f_bout"] = pvec(i["L0_ffn_out_b"])
    s["norm1"] = pvec(i["L0_ffn_norm1_w"])
    s["norm2"] = pvec(i["L0_ffn_norm2_w"])
    s["gate"] = kpt(np.ascontiguousarray(i["L0_ffn_gate_w"].T))
    s["w1"] = np.stack([kpt(np.ascontiguousarray(i["L0_ffn_w1"][e].T)) for e in range(8)])
    s["w2"] = np.stack([kpt(np.ascontiguousarray(i["L0_ffn_w2"][e].T)) for e in range(8)])
    s["w3"] = np.stack([kpt(np.ascontiguousarray(i["L0_ffn_w3"][e].T)) for e in range(8)])
    s["b1"] = np.stack([pvec(i["L0_ffn_b1"][e]) for e in range(8)], axis=1)
    s["b2"] = np.stack([pvec(i["L0_ffn_b2"][e]) for e in range(8)], axis=1)
    s["b3"] = i["L0_ffn_b3"].copy()
    esel = np.zeros((8, 1024), np.float32)
    for e in range(8):
        esel[e, e * 128:(e + 1) * 128] = 1.0
    s["esel"] = esel.astype(ml_dtypes.bfloat16)
    s["ident"] = np.eye(128, dtype=np.float32).astype(ml_dtypes.bfloat16)
    s["m_win"] = kpt(np.ascontiguousarray(i["L1_attn_W_in"]))
    s["m_cw"] = np.ascontiguousarray(i["L1_attn_conv_w"].reshape(KDI, 128, 4).transpose(1, 0, 2))
    s["m_cb"] = pvec(i["L1_attn_conv_b"])
    s["m_wx"] = kpt(np.ascontiguousarray(i["L1_attn_W_x"]))
    s["m_wdt"] = i["L1_attn_W_dt"].copy()
    s["m_bdt"] = pvec(i["L1_attn_b_dt"])
    s["m_alog"] = np.ascontiguousarray(i["L1_attn_A_log"].reshape(KDI, 128, DST).transpose(1, 0, 2))
    s["m_d"] = pvec(i["L1_attn_D"])
    s["m_wout"] = kpt(np.ascontiguousarray(i["L1_attn_W_out"]))
    esel32 = np.zeros((32, 32 * 128), np.float32)
    for n in range(16):
        esel32[n, n * 128:(n + 1) * 128] = 1.0
        esel32[16 + n, (16 + n) * 128:(17 + n) * 128] = 1.0
    s["esel32"] = esel32.astype(ml_dtypes.bfloat16)
    permB = bfly_perm_B()

    def coefpack(wstack, which):
        outs = []
        for ci in range(4):
            a_, b_ = divmod(ci, 2)
            cA, cB = bfly_coeffs(wstack[:, :, a_, b_])
            outs.append(cA if which == "A" else cB)
        arr = np.stack(outs, axis=2)
        arr = arr.transpose(1, 0, 2, 3).reshape(32, 1, -1)
        return np.ascontiguousarray(np.broadcast_to(arr, (32, 128, arr.shape[2]))).astype(ml_dtypes.bfloat16)
    s["d_wexp"] = kpt(np.ascontiguousarray(i["L1_ffn_W_exp"]))
    s["d_bexp"] = pvec(i["L1_ffn_b_exp"])
    s["coefAu"] = coefpack(i["L1_ffn_up_weights"], "A")
    s["coefBu"] = coefpack(i["L1_ffn_up_weights"], "B")
    s["coefAd"] = coefpack(i["L1_ffn_down_weights"], "A")
    s["coefBd"] = coefpack(i["L1_ffn_down_weights"], "B")
    s["d_ubias"] = np.ascontiguousarray(i["L1_ffn_up_bias"][permB].reshape(32, 128).T)
    s["d_wcon"] = kpt(np.ascontiguousarray(i["L1_ffn_W_con"][permB, :]))
    s["d_bcon"] = pvec(i["L1_ffn_b_con"])
    s["d_dbias"] = pvec(i["L1_ffn_down_bias"])
    return s


_BIGW = ("a_wqk", "a_wv", "a_wout", "f_wqk", "f_wv", "f_wout", "w1", "w2", "w3",
         "m_win", "m_wout", "d_wexp", "d_wcon")


_EAGER = ("a_wqk", "a_wv", "a_wout")


def _declare(nc, shared, au, es, defer_rest=False):
    ap = {}
    pairs = []
    dpairs = []
    for k, v in shared.items():
        dt = BF16 if v.dtype == ml_dtypes.bfloat16 else F32
        src_ap = nc.dram_tensor(k, list(v.shape), dt, kind="ExternalInput").ap()
        if k in _BIGW:
            mir = nc.dram_tensor(k + "_bf", list(v.shape), BF16, kind="Internal").ap()
            if len(v.shape) == 4:
                pr = (src_ap.rearrange("e k p f -> (e k) p f"),
                      mir.rearrange("e k p f -> (e k) p f"))
            else:
                pr = (src_ap, mir)
            (pairs if (k in _EAGER or not defer_rest) else dpairs).append(pr)
            ap[k] = mir
        else:
            ap[k] = src_ap
    emit_w2bf(au, es, nc, pairs)
    au.barrier()
    return ap, dpairs


def _common_prolog(au, es, nc, w):
    g = {}
    g["psum"] = [es.enter_context(nc.psum_tensor(f"pb{i}", [128, 512], F32)) for i in range(8)]
    g["ones128b"] = es.enter_context(nc.sbuf_tensor("ones128b", [128, 1], BF16))
    g["ones1b"] = es.enter_context(nc.sbuf_tensor("ones1b", [1, 128], BF16))
    g["ones128f32"] = es.enter_context(nc.sbuf_tensor("ones128f32", [128, 1], F32))
    g["shift_sb"] = es.enter_context(nc.sbuf_tensor("shift_sb", [128, 1], F32))
    g["eps1_sb"] = es.enter_context(nc.sbuf_tensor("eps1_sb", [1, 2], F32))
    au.op("vector", lambda e: e.memset(g["ones128b"][:], 1.0), writes=[("ones",)])
    au.op("vector", lambda e: e.memset(g["ones1b"][:], 1.0), writes=[("ones",)])
    au.op("vector", lambda e: e.memset(g["ones128f32"][:], 1.0), writes=[("onesf",)])
    au.op("vector", lambda e: e.memset(g["shift_sb"][:], -SHIFT), writes=[("const",)])
    au.op("vector", lambda e: e.memset(g["eps1_sb"][0:1, 0:1], LN_EPS), writes=[("const",)])
    au.op("vector", lambda e: e.memset(g["eps1_sb"][0:1, 1:2], RMS_EPS), writes=[("const",)])
    return g


def _emit_block0_attn(au, es, nc, w, g, src, dst, q_hbm, attn_hbm, feed=None):
    bqk_sb = es.enter_context(nc.sbuf_tensor("a_bqk_sb", [128, 16], F32))
    bv_sb = es.enter_context(nc.sbuf_tensor("a_bv_sb", [1, 1024], BF16))
    bout_sb = es.enter_context(nc.sbuf_tensor("a_bout_sb", [128, 8], F32))
    masks_sb = es.enter_context(nc.sbuf_tensor("a_masks_sb", [128, 4, NC], BF16))
    au.dma("sync", bqk_sb[:], w["a_bqk"][:, :], writes=[("bqk",)])
    au.dma("gpsimd", bv_sb[:], w["a_bv"][:, :], writes=[("bv",)])
    au.dma("sync", bout_sb[:], w["a_bout"][:, :], writes=[("bout",)])
    au.dma("sync", masks_sb[:], w["masks"][:, :, :], writes=[("msk",)])
    with ExitStack() as es1:
        K_sb = es1.enter_context(nc.sbuf_tensor("a_K", [128, KD, T], BF16))
        V65 = es1.enter_context(nc.sbuf_tensor("a_V65", [128, T // 128, NH, 65], BF16))
        emit_attn_phase(au, es1, nc, src_hbm=src, rms_gain_sb=None, wqk_hbm=w["a_wqk"],
                        bqk_sb=bqk_sb, wv_hbm=w["a_wv"], bv_sb=bv_sb, causal=True,
                        cos_hbm=w["cosT"], sin_hbm=w["sinT"], masks_sb=masks_sb,
                        q_hbm=q_hbm, attn_hbm=attn_hbm, K_sb=K_sb, V65=V65,
                        psum=g["psum"], ones128b=g["ones128b"], ones1b=g["ones1b"],
                        shift_sb=g["shift_sb"], eps1_sb=g["eps1_sb"], tag="a",
                        feed=feed)
        au.barrier()
    with ExitStack() as es2:
        emit_outproj_ln(au, es2, nc, attn_hbm=attn_hbm, src_hbm=src, dst_hbm=dst,
                        w_hbm=w["a_wout"], b_sb=bout_sb, psum=g["psum"],
                        ones128b=g["ones128b"], ones1b=g["ones1b"],
                        eps1_sb=g["eps1_sb"], ln=True, tag="ao")
        au.barrier()


def _emit_block0_ffn(au, es, nc, w, g, src, dst, q_hbm, attn_hbm, z_hbm):
    bqk_sb = es.enter_context(nc.sbuf_tensor("f_bqk_sb", [128, 16], F32))
    bv_sb = es.enter_context(nc.sbuf_tensor("f_bv_sb", [1, 1024], BF16))
    bout_sb = es.enter_context(nc.sbuf_tensor("f_bout_sb", [128, 8], F32))
    n1_sb = es.enter_context(nc.sbuf_tensor("n1_sb", [128, KD], F32))
    n2_sb = es.enter_context(nc.sbuf_tensor("n2_sb", [128, KD], F32))
    b1_sb = es.enter_context(nc.sbuf_tensor("b1_sb", [128, 8, 16], F32))
    b2_sb = es.enter_context(nc.sbuf_tensor("b2_sb", [128, 8, 16], F32))
    b3_sb = es.enter_context(nc.sbuf_tensor("b3_sb", [8, 1024], BF16))
    esel_sb = es.enter_context(nc.sbuf_tensor("esel_sb", [8, 1024], BF16))
    ident_sb = es.enter_context(nc.sbuf_tensor("ident_sb", [128, 128], BF16))
    au.dma("sync", bqk_sb[:], w["f_bqk"][:, :], writes=[("bqk",)])
    au.dma("gpsimd", bv_sb[:], w["f_bv"][:, :], writes=[("bv",)])
    au.dma("sync", bout_sb[:], w["f_bout"][:, :], writes=[("bout",)])
    au.dma("sync", n1_sb[:], w["norm1"][:, :], writes=[("gain",)])
    au.dma("sync", n2_sb[:], w["norm2"][:, :], writes=[("gain2",)])
    au.dma("sync", b1_sb[:], w["b1"][:, :, :], writes=[("b1",)])
    au.dma("sync", b2_sb[:], w["b2"][:, :, :], writes=[("b2",)])
    au.dma("gpsimd", b3_sb[:], w["b3"][:, :], writes=[("b3",)])
    au.dma("sync", esel_sb[:], w["esel"][:, :], writes=[("esel",)])
    au.dma("sync", ident_sb[:], w["ident"][:, :], writes=[("ident",)])
    with ExitStack() as es1:
        K_sb = es1.enter_context(nc.sbuf_tensor("f_K", [128, KD, T], BF16))
        V65 = es1.enter_context(nc.sbuf_tensor("f_V65", [128, T // 128, NH, 65], BF16))
        emit_attn_phase(au, es1, nc, src_hbm=src, rms_gain_sb=n1_sb, wqk_hbm=w["f_wqk"],
                        bqk_sb=bqk_sb, wv_hbm=w["f_wv"], bv_sb=bv_sb, causal=False,
                        cos_hbm=w["cosT"], sin_hbm=w["sinT"], masks_sb=None,
                        q_hbm=q_hbm, attn_hbm=attn_hbm, K_sb=K_sb, V65=V65,
                        psum=g["psum"], ones128b=g["ones128b"], ones1b=g["ones1b"],
                        shift_sb=g["shift_sb"], eps1_sb=g["eps1_sb"], tag="f")
        au.barrier()
    with ExitStack() as es2:
        emit_outproj_ln(au, es2, nc, attn_hbm=attn_hbm, src_hbm=src, dst_hbm=z_hbm,
                        w_hbm=w["f_wout"], b_sb=bout_sb, psum=g["psum"],
                        ones128b=g["ones128b"], ones1b=g["ones1b"],
                        eps1_sb=g["eps1_sb"], ln=False, tag="fo")
        au.barrier()
    emit_moe_phase(au, es, nc, z_hbm=z_hbm, x1_hbm=src, dst_hbm=dst,
                   rms_gain_sb=n2_sb, gate_hbm=w["gate"], w1_hbm=w["w1"],
                   w2_hbm=w["w2"], w3_hbm=w["w3"], b1_sb=b1_sb, b2_sb=b2_sb,
                   b3_sb=b3_sb, psum=g["psum"], ones128b=g["ones128b"],
                   ones1b=g["ones1b"], esel_sb=esel_sb, ident_bf=ident_sb,
                   eps1_sb=g["eps1_sb"], tag="m")
    au.barrier()


def _emit_block1_attn(au, es, nc, w, g, src, dst):
    cw_sb = es.enter_context(nc.sbuf_tensor("cw_sb", [128, KDI, 4], F32))
    cb_sb = es.enter_context(nc.sbuf_tensor("cb_sb", [128, KDI], F32))
    wx_sb = es.enter_context(nc.sbuf_tensor("wx_sb", [128, KDI, 96], BF16))
    wdt_sb = es.enter_context(nc.sbuf_tensor("wdt_sb", [64, 2048], BF16))
    bdt_sb = es.enter_context(nc.sbuf_tensor("bdt_sb", [128, KDI], F32))
    A_sb = es.enter_context(nc.sbuf_tensor("A_sb", [128, KDI, DST], F32))
    D_sb = es.enter_context(nc.sbuf_tensor("D_sb", [128, KDI], F32))
    esel_sb = es.enter_context(nc.sbuf_tensor("esel32_sb", [32, 32 * 128], BF16))
    au.dma("sync", cw_sb[:], w["m_cw"][:, :, :], writes=[("cw",)])
    au.dma("sync", cb_sb[:], w["m_cb"][:, :], writes=[("cb",)])
    for k in range(KDI):
        au.dma("gpsimd", wx_sb[:, k, :], w["m_wx"][k, :, :], writes=[("wx",)], stream=("wx",))
    au.dma("gpsimd", wdt_sb[:, :], w["m_wdt"][:, :], writes=[("wdt",)])
    au.dma("sync", bdt_sb[:], w["m_bdt"][:, :], writes=[("bdt",)])
    au.dma("sync", A_sb[:], w["m_alog"][:, :, :], writes=[("A",)])
    au.dma("sync", D_sb[:], w["m_d"][:, :], writes=[("D",)])
    au.dma("sync", esel_sb[:], w["esel32"][:, :], writes=[("esel32",)])
    au.op("scalar", lambda e: e.activation(out=A_sb[:, :, :], in_=A_sb[:, :, :],
         func=AF.Exp, scale=1.0), reads=[("A",)], writes=[("A",)])
    au.op("vector", lambda e: e.tensor_scalar(out=A_sb[:, :, :], in0=A_sb[:, :, :],
         scalar1=-1.0, scalar2=None, op0=OP.mult), reads=[("A",)], writes=[("A",)])
    emit_mamba_phase(au, es, nc, src_hbm=src, dst_hbm=dst, win_hbm=w["m_win"],
                     cw_sb=cw_sb, cb_sb=cb_sb, wx_sb=wx_sb, wdt_sb=wdt_sb,
                     bdt_sb=bdt_sb, A_sb=A_sb, D_sb=D_sb, wout_hbm=w["m_wout"],
                     esel32_sb=esel_sb, psum=g["psum"], ones128b=g["ones128b"],
                     ones1b=g["ones1b"], ones128f32=g["ones128f32"],
                     eps1_sb=g["eps1_sb"], tag="c")
    au.barrier()


def _emit_block1_ffn(au, es, nc, w, g, src, dst, shuf_hbm):
    ident_sb = es.enter_context(nc.sbuf_tensor("identd_sb", [128, 128], BF16))
    bexp_sb = es.enter_context(nc.sbuf_tensor("bexp_sb", [128, 32], F32))
    ubias_sb = es.enter_context(nc.sbuf_tensor("ubias_sb", [128, 32], F32))
    bcon_sb = es.enter_context(nc.sbuf_tensor("bcon_sb", [128, 8], F32))
    dbias_sb = es.enter_context(nc.sbuf_tensor("dbias_sb", [128, 8], F32))
    MAu = es.enter_context(nc.sbuf_tensor("MAu", [128, 32, 128], BF16))
    MBu = es.enter_context(nc.sbuf_tensor("MBu", [128, 32, 128], BF16))
    MAd = es.enter_context(nc.sbuf_tensor("MAd", [128, 32, 128], BF16))
    MBd = es.enter_context(nc.sbuf_tensor("MBd", [128, 32, 128], BF16))
    au.dma("sync", ident_sb[:], w["ident"][:, :], writes=[("ident",)])
    au.dma("sync", bexp_sb[:], w["d_bexp"][:, :], writes=[("bexp",)])
    au.dma("sync", ubias_sb[:], w["d_ubias"][:, :], writes=[("ubias",)])
    au.dma("sync", bcon_sb[:], w["d_bcon"][:, :], writes=[("bcon",)])
    au.dma("sync", dbias_sb[:], w["d_dbias"][:, :], writes=[("dbias",)])
    au.op("vector", lambda e: e.tensor_tensor(out=bcon_sb[:], in0=bcon_sb[:],
         in1=dbias_sb[:], op=OP.add), reads=[("bcon",), ("dbias",)], writes=[("bcon",)])
    with ExitStack() as esb:
        for nm, dr, M in (("Au", w["coefAu"], MAu), ("Bu", w["coefBu"], MBu),
                          ("Ad", w["coefAd"], MAd), ("Bd", w["coefBd"], MBd)):
            emit_bfly_build(au, esb, nc, coef_hbm=dr, which=nm[0], M_sb=M,
                            ident_bf=ident_sb, psum=g["psum"], ones1b=g["ones1b"], tag=nm)
        au.barrier()
    with ExitStack() as es1:
        emit_bfly_phase(au, es1, nc, src_hbm=src, dst_hbm=dst, wexp_hbm=w["d_wexp"],
                        bexp_sb=bexp_sb, MAu=MAu, MBu=MBu, MAd=MAd, MBd=MBd,
                        ubias_sb=ubias_sb, wcon_hbm=w["d_wcon"], bcon_sb=bcon_sb,
                        shuf_hbm=shuf_hbm, psum=g["psum"], ones128b=g["ones128b"],
                        ones1b=g["ones1b"], eps1_sb=g["eps1_sb"], tag="d")
        au.barrier()


_LAUNCH_W = {
    "b0a": ("a_wqk", "a_wv", "a_wout", "cosT", "sinT", "masks", "a_bqk", "a_bv", "a_bout"),
    "b0f": ("f_wqk", "f_wv", "f_wout", "w1", "w2", "w3", "cosT", "sinT", "norm1", "norm2",
             "gate", "b1", "b2", "b3", "esel", "ident", "f_bqk", "f_bv", "f_bout"),
    "b1a": ("m_win", "m_cw", "m_cb", "m_wx", "m_wdt", "m_bdt", "m_alog", "m_d", "m_wout", "esel32"),
    "b1f": ("d_wexp", "d_bexp", "coefAu", "coefBu", "coefAd", "coefBd", "d_ubias",
             "d_wcon", "d_bcon", "d_dbias", "ident"),
}


def _build_launch(which, shared):
    key = which
    shared = {k: v for k, v in shared.items() if k in _LAUNCH_W[which]}
    if key in _BUILD_CACHE:
        return _BUILD_CACHE[key]
    nc = bass.Bass("TRN2", target_bir_lowering=False)
    xin = nc.dram_tensor("xin", [KD, 128, T], F32, kind="ExternalInput").ap()
    xout = nc.dram_tensor("xout", [KD, 128, T], F32, kind="ExternalOutput").ap()
    src = xin.rearrange("k p t -> p k t")
    dst = xout.rearrange("k p t -> p k t")
    with ExitStack() as es:
        au = Auto(nc, es)
        with ExitStack() as esw:
            w, _ = _declare(nc, shared, au, esw)
        g = _common_prolog(au, es, nc, w)
        if which == "b0a":
            q_hbm = nc.dram_tensor("q_hbm", [128, KD, T], BF16, kind="Internal").ap()
            attn_hbm = nc.dram_tensor("attn_hbm", [128, KD, T], BF16, kind="Internal").ap()
            _emit_block0_attn(au, es, nc, w, g, src, dst, q_hbm, attn_hbm)
        elif which == "b0f":
            q_hbm = nc.dram_tensor("q_hbm", [128, KD, T], BF16, kind="Internal").ap()
            attn_hbm = nc.dram_tensor("attn_hbm", [128, KD, T], BF16, kind="Internal").ap()
            z_hbm = nc.dram_tensor("z_hbm", [128, KD, T], F32, kind="Internal").ap()
            _emit_block0_ffn(au, es, nc, w, g, src, dst, q_hbm, attn_hbm, z_hbm[:, :, :])
        elif which == "b1a":
            _emit_block1_attn(au, es, nc, w, g, src, dst)
        elif which == "b1f":
            shuf = nc.dram_tensor("shuf", [2, 32, 4, 32, NC], BF16, kind="Internal").ap()
            _emit_block1_ffn(au, es, nc, w, g, src, dst, shuf)
        with nc.Block() as block:
            au.finalize(block)
    _BUILD_CACHE[key] = nc
    return nc


def _build_fused(shared):
    key = "fused"
    if key in _BUILD_CACHE:
        return _BUILD_CACHE[key]
    nc = bass.Bass("TRN2", target_bir_lowering=False)
    xin = nc.dram_tensor("xin", [KD, 128, T], F32, kind="ExternalInput").ap()
    xout = nc.dram_tensor("xout", [KD, 128, T], F32, kind="ExternalOutput").ap()
    x1h = nc.dram_tensor("x1h", [128, KD, T], F32, kind="Internal").ap()
    x2h = nc.dram_tensor("x2h", [128, KD, T], F32, kind="Internal").ap()
    x3h = nc.dram_tensor("x3h", [128, KD, T], F32, kind="Internal").ap()
    q_hbm = nc.dram_tensor("q_hbm", [128, KD, T], BF16, kind="Internal").ap()
    attn_hbm = nc.dram_tensor("attn_hbm", [128, KD, T], BF16, kind="Internal").ap()
    z_hbm = nc.dram_tensor("z_hbm", [128, KD, T], F32, kind="Internal").ap()
    shuf = nc.dram_tensor("shuf", [2, 32, 4, 32, NC], BF16, kind="Internal").ap()
    src = xin.rearrange("k p t -> p k t")
    dst = xout.rearrange("k p t -> p k t")
    with ExitStack() as es:
        au = Auto(nc, es)
        with ExitStack() as esw:
            w, dpairs = _declare(nc, shared, au, esw, defer_rest=True)
        g = _common_prolog(au, es, nc, w)
        with ExitStack() as p1:
            feed = emit_w2bf(au, p1, nc, dpairs, tag="cv", bufw=4096, defer=True)
            _emit_block0_attn(au, p1, nc, w, g, src, x1h[:, :, :], q_hbm, attn_hbm,
                              feed=feed)
            feed(10000)
            au.barrier()
        with ExitStack() as p2:
            _emit_block0_ffn(au, p2, nc, w, g, x1h[:, :, :], x2h[:, :, :],
                             q_hbm, attn_hbm, z_hbm[:, :, :])
        with ExitStack() as p3:
            _emit_block1_attn(au, p3, nc, w, g, x2h[:, :, :], x3h[:, :, :])
        with ExitStack() as p4:
            _emit_block1_ffn(au, p4, nc, w, g, x3h[:, :, :], dst, shuf)
            au.barrier()
        with nc.Block() as block:
            au.finalize(block)
    _BUILD_CACHE[key] = nc
    return nc


FUSED = True


def kernel(**inputs):
    x = inputs["L0_attn_x"]
    shared = _prep_shared(inputs)
    xs = [np.ascontiguousarray(x[c].T).reshape(KD, 128, T) for c in range(NCORES)]
    if FUSED:
        nc = _build_fused(shared)
        in_maps = [dict(shared, xin=xs[c]) for c in range(NCORES)]
        res = run_bass_kernel_spmd(nc, in_maps, core_ids=list(range(NCORES)))
        outs = [r["xout"] for r in res.results]
    else:
        cur = xs
        for which in ("b0a", "b0f", "b1a", "b1f"):
            nc = _build_launch(which, shared)
            sh = {k: v for k, v in shared.items() if k in _LAUNCH_W[which]}
            in_maps = [dict(sh, xin=cur[c]) for c in range(NCORES)]
            res = run_bass_kernel_spmd(nc, in_maps, core_ids=list(range(NCORES)))
            cur = [r["xout"] for r in res.results]
        outs = cur
    full = np.stack([o.reshape(D, T).T for o in outs])
    return np.ascontiguousarray(full.astype(np.float32))


# version 21
# speedup vs baseline: 1.0544x; 1.0544x over previous
import os
from contextlib import ExitStack as ExitStack2
import numpy as np
import concourse.bass as bass
import concourse.mybir as mybir
from concourse.alu_op_type import AluOpType as OP

F32 = mybir.dt.float32
BF16 = mybir.dt.bfloat16
AF = mybir.ActivationFunctionType
AX = mybir.AxisListType

T = 4096
D = 1024
KD = D // 128
NC = 512
NCH = T // NC
NH = 16
DH = 64
SHIFT = 12.0
LN_EPS = 1e-5
RMS_EPS = 1e-6

ENGS = ("sync", "tensor", "vector", "scalar", "gpsimd")


class _Op:
    __slots__ = ("eng", "fn", "reads", "writes", "is_async", "weight", "idx",
                 "deps", "marked", "cum", "vc", "marked_fence", "stream")

    def __init__(self, eng, fn, reads, writes, is_async, weight, idx):
        self.eng = eng
        self.fn = fn
        self.reads = reads
        self.writes = writes
        self.is_async = is_async
        self.weight = weight
        self.idx = idx
        self.deps = []
        self.marked = False
        self.cum = 0
        self.vc = None
        self.marked_fence = False
        self.stream = None


class Auto:

    def __init__(self, nc, es):
        self.nc = nc
        self.es = es
        self.ops = []
        self.tok = {}
        self.last_on = {e: None for e in ENGS}

    def op(self, eng, fn, reads=(), writes=(), is_async=False, weight=1, _fence=False):
        o = _Op(eng, fn, tuple(reads), tuple(writes), is_async, weight, len(self.ops))
        deps = set()
        for t in o.reads:
            st = self.tok.get(t)
            if st and st["w"] is not None:
                deps.add(st["w"])
        for t in o.writes:
            st = self.tok.get(t)
            if st:
                if st["w"] is not None:
                    deps.add(st["w"])
                for r in st["r"]:
                    deps.add(r)
        for d in deps:
            if d.eng != eng or d.is_async or is_async or eng != "tensor":
                o.deps.append(d)
        for t in o.reads:
            self.tok.setdefault(t, {"w": None, "r": []})["r"].append(o)
        for t in o.writes:
            self.tok[t] = {"w": o, "r": []}
        self.ops.append(o)
        if not _fence:
            self.last_on[eng] = o
        return o

    def dma(self, eng, out, in_, reads=(), writes=(), stream=None, **kw):
        fn = lambda e, o=out, i=in_, kw=kw: e.dma_start(out=o, in_=i, **kw)
        o = self.op(eng, fn, reads=reads, writes=writes, is_async=True, weight=16)
        o.stream = stream if stream is not None else (o.writes[0] if o.writes else ("anon",))
        return o

    def barrier(self):
        prior = [o for o in (self.last_on[e] for e in ENGS) if o is not None]
        outstanding = [o for o in self.ops if o.is_async and not o.marked_fence]
        for e in ENGS:
            o = self.op(e, lambda eng: eng.nop(), _fence=True)
            o.deps = [d for d in set(prior + outstanding) if d is not o]
        for o in outstanding:
            o.marked_fence = True

    def finalize(self, block):
        def chan(o):
            if o.is_async:
                return ("d", o.eng, o.stream)
            return ("e", o.eng)

        chans = []
        seen = set()
        for o in self.ops:
            c = chan(o)
            if c not in seen:
                seen.add(c)
                chans.append(c)
        for e in ENGS:
            if ("e", e) not in seen:
                seen.add(("e", e))
                chans.append(("e", e))
        sems = {c: self.es.enter_context(self.nc.semaphore(f"pg{i}"))
                for i, c in enumerate(chans)}
        for o in self.ops:
            for d in o.deps:
                d.marked = True
        cum = {c: 0 for c in chans}
        for o in self.ops:
            c = chan(o)
            if o.marked:
                cum[c] += o.weight
            o.cum = cum[c]
        eng_vc = {e: {c: 0 for c in chans} for e in ENGS}
        op_vc = {}
        emit = {e: [] for e in ENGS}
        for o in self.ops:
            need = {}
            for d in o.deps:
                dc = chan(d)
                if d.cum > need.get(dc, 0):
                    need[dc] = d.cum
                dvc = op_vc.get(d.idx)
                if dvc:
                    for c2, v2 in dvc.items():
                        if v2 > need.get(c2, 0):
                            need[c2] = v2
            mycl = eng_vc[o.eng]
            own = ("e", o.eng)
            for c2, v in sorted(need.items(), key=str):
                if c2 == own and o.eng == "tensor" and not o.is_async:
                    continue
                if v > mycl[c2]:
                    emit[o.eng].append(("wait", c2, v))
                    mycl[c2] = v
            emit[o.eng].append(("op", o))
            oc = chan(o)
            if not o.is_async and o.eng == "tensor":
                mycl[oc] = o.cum
            vc = dict(mycl)
            if o.is_async:
                vc[oc] = o.cum
            op_vc[o.idx] = vc

        def run(e, eng_name, seq=emit, sems=sems):
            for item in seq[eng_name]:
                if item[0] == "wait":
                    e.wait_ge(sems[item[1]], item[2])
                else:
                    o = item[1]
                    inst = o.fn(e)
                    if o.marked:
                        s = sems[chan(o)]
                        if inst is None:
                            e.sem_inc(s, o.weight)
                        else:
                            inst.then_inc(s, o.weight)
        for eng_name in ENGS:
            getattr(block, eng_name)(lambda e, en=eng_name: run(e, en))
        self.n_chans = len(chans)
        return emit

    def stats(self):
        n_wait = 0
        return {"ops": len(self.ops), "marked": sum(o.marked for o in self.ops)}



def kpt(w_t):
    fin, fout = w_t.shape
    return np.ascontiguousarray(w_t.reshape(fin // 128, 128, fout))


def pvec(v):
    n = v.shape[0]
    return np.ascontiguousarray(v.reshape(n // 128, 128).T)


def rope_perm():
    perm = np.zeros(D, dtype=np.int64)
    for h in range(NH):
        for k in range(32):
            perm[h * 64 + k] = h * 64 + 2 * k
            perm[h * 64 + 32 + k] = h * 64 + 2 * k + 1
    return perm


def rope_tables():
    inv_freq = 1.0 / 10000.0 ** (np.arange(0, DH, 2, dtype=np.float64) / DH)
    ang = np.outer(inv_freq, np.arange(T, dtype=np.float64))
    cos32, sin32 = np.cos(ang), np.sin(ang)
    cos128 = np.tile(cos32, (4, 1)).astype(np.float32)
    sin128 = np.tile(sin32, (4, 1)).astype(np.float32)
    sgn = np.ones((128, 1), np.float32)
    for b in (0, 2):
        sgn[b * 32:(b + 1) * 32] = -1.0
    return cos128, sin128 * sgn


def causal_masks():
    p = np.arange(128)[:, None]
    f = np.arange(NC)[None, :]
    ms = [(p + 128 * dd <= f).astype(np.float32) for dd in range(4)]
    return np.ascontiguousarray(np.stack(ms, axis=1))



def emit_attn_phase(au, es, nc, *, src_hbm, rms_gain_sb, wqk_hbm, bqk_sb, wv_hbm,
                    bv_sb, causal, cos_hbm, sin_hbm, masks_sb, q_hbm, attn_hbm,
                    K_sb, V65, psum, ones128b, ones1b, shift_sb, eps1_sb, tag="", a2=True,
                    feed=None):
    A = au
    q_chunk = es.enter_context(nc.sbuf_tensor(f"qch{tag}", [128, KD, NC], BF16))
    es1 = es.enter_context(ExitStack2())
    nxb = 1 if rms_gain_sb is not None else 2
    x_bf = [es1.enter_context(nc.sbuf_tensor(f"xbf{tag}{i}", [128, KD, NC], BF16)) for i in range(nxb)]
    nwr = 2 if rms_gain_sb is not None else 3
    wring = [es1.enter_context(nc.sbuf_tensor(f"wr{tag}{i}", [128, 2048], BF16)) for i in range(nwr)]
    vring = [es1.enter_context(nc.sbuf_tensor(f"vr{tag}{i}", [128, 1024], BF16)) for i in range(2)]
    qk_ev = [es1.enter_context(nc.sbuf_tensor(f"qk_ev{tag}{i}", [128, NC], BF16)) for i in range(2)]
    partner = [es1.enter_context(nc.sbuf_tensor(f"prt{tag}{i}", [128, NC], BF16)) for i in range(2)]
    cos_sb = es1.enter_context(nc.sbuf_tensor(f"cos{tag}", [128, NC], BF16))
    sin_sb = es1.enter_context(nc.sbuf_tensor(f"sin{tag}", [128, NC], BF16))
    bv_rep = es1.enter_context(nc.sbuf_tensor(f"bvrep{tag}", [128, 1024], BF16))
    if rms_gain_sb is not None:
        x_f32 = es1.enter_context(nc.sbuf_tensor(f"xf32{tag}", [128, KD, NC], F32))
        x2b = [es1.enter_context(nc.sbuf_tensor(f"x2b{tag}{i}", [128, NC], BF16)) for i in range(2)]
        srow = es1.enter_context(nc.sbuf_tensor(f"srow{tag}", [1, NC], F32))
        srow_bf = es1.enter_context(nc.sbuf_tensor(f"srowb{tag}", [1, NC], BF16))

    A.op("vector", lambda e: e.memset(V65[:, :, :, 64:65], 1.0), writes=[("V",)])
    for half in range(2):
        A.op("tensor", lambda e, h=half: e.matmul(out=psum[6][:, 0:NC], lhsT=ones1b[0:1, :],
             rhs=bv_sb[0:1, h * 512:(h + 1) * 512], start=True, stop=True),
             reads=[("bv",), ("ones",)], writes=[("ps", 6)])
        A.op("vector", lambda e, h=half: e.tensor_copy(out=bv_rep[:, h * 512:(h + 1) * 512],
             in_=psum[6][:, 0:NC]), reads=[("ps", 6)], writes=[("bvrep",)])

    for c in range(NCH):
        if feed:
            feed(2)
        t0 = c * NC
        xb, xtok = x_bf[c % len(x_bf)], ("xbf", c % len(x_bf))
        if rms_gain_sb is None:
            for half in range(2):
                A.dma("gpsimd", xb[:, half * 4:(half + 1) * 4, :],
                      src_hbm[:, half * 4:(half + 1) * 4, t0:t0 + NC], writes=[xtok],
                      stream=xtok)
        else:
            for half in range(2):
                A.dma("sync", x_f32[:, half * 4:(half + 1) * 4, :],
                      src_hbm[:, half * 4:(half + 1) * 4, t0:t0 + NC], writes=[("xf32",)],
                      stream=("xf32",))
            for k in range(KD):
                A.op("vector", lambda e, k=k, o=x2b[k % 2]: e.tensor_tensor(
                    out=o[:], in0=x_f32[:, k, :], in1=x_f32[:, k, :], op=OP.mult),
                    reads=[("xf32",)], writes=[("x2b", k % 2)])
                A.op("tensor", lambda e, k=k, i=x2b[k % 2]: e.matmul(
                    out=psum[7][0:1, 0:NC], lhsT=ones128b[:, 0:1], rhs=i[:],
                    start=(k == 0), stop=(k == KD - 1)),
                    reads=[("x2b", k % 2), ("ones",)], writes=[("ps", 7)])
            A.op("vector", lambda e: e.tensor_scalar(out=srow[:, :], in0=psum[7][0:1, 0:NC],
                 scalar1=1.0 / D, scalar2=None, op0=OP.mult),
                 reads=[("ps", 7)], writes=[("srow",)])
            A.op("scalar", lambda e: e.activation(out=srow[:, :], in_=srow[:, :],
                 func=AF.Sqrt, bias=eps1_sb[0:1, 1:2]), reads=[("srow",), ("const",)], writes=[("srow",)])
            A.op("vector", lambda e: e.reciprocal(out=srow[:, :], in_=srow[:, :]),
                 reads=[("srow",)], writes=[("srow",)])
            A.op("vector", lambda e: e.tensor_copy(out=srow_bf[:, :], in_=srow[:, :]),
                 reads=[("srow",)], writes=[("srowb",)])
            A.op("tensor", lambda e: e.matmul(out=psum[6][:, 0:NC], lhsT=ones1b[0:1, :],
                 rhs=srow_bf[:, :], start=True, stop=True),
                 reads=[("srowb",), ("ones",)], writes=[("ps", 6)])
            for k in range(KD):
                A.op("vector", lambda e, k=k, xb=xb: e.scalar_tensor_tensor(
                    out=xb[:, k, :], in0=x_f32[:, k, :], scalar=rms_gain_sb[:, k:k + 1],
                    in1=psum[6][:, 0:NC], op0=OP.mult, op1=OP.mult),
                    reads=[("xf32",), ("ps", 6), ("gain",)], writes=[xtok])

        A.dma("sync", cos_sb[:], cos_hbm[:, t0:t0 + NC], writes=[("cos",)], stream=("cos",))
        A.dma("sync", sin_sb[:], sin_hbm[:, t0:t0 + NC], writes=[("sin",)], stream=("sin",))

        def qk_evac(m, ps, pstok, xb=xb, t0=t0):
            bi = m % 2
            A.op("scalar", lambda e, m=m, bi=bi, ps=ps: e.activation(
                out=qk_ev[bi][:], in_=ps[:, 0:NC], func=AF.Identity,
                bias=bqk_sb[:, m:m + 1], scale=1.0),
                reads=[pstok, ("bqk",)], writes=[("qk_ev", bi)])
            for b0, b1 in ((0, 32), (32, 0), (64, 96), (96, 64)):
                A.op("vector", lambda e, b0=b0, b1=b1, bi=bi: e.tensor_copy(
                    out=partner[bi][b0:b0 + 32, :], in_=qk_ev[bi][b1:b1 + 32, :]),
                    reads=[("qk_ev", bi)], writes=[("prt", bi)])
            A.op("vector", lambda e, bi=bi: e.tensor_tensor(out=qk_ev[bi][:], in0=qk_ev[bi][:],
                 in1=cos_sb[:], op=OP.mult),
                 reads=[("qk_ev", bi), ("cos",)], writes=[("qk_ev", bi)])
            A.op("vector", lambda e, bi=bi: e.tensor_tensor(out=partner[bi][:], in0=partner[bi][:],
                 in1=sin_sb[:], op=OP.mult),
                 reads=[("prt", bi), ("sin",)], writes=[("prt", bi)])
            if m < 8:
                A.op("vector", lambda e, m=m, bi=bi: e.tensor_tensor(out=q_chunk[:, m, :],
                     in0=qk_ev[bi][:], in1=partner[bi][:], op=OP.add),
                     reads=[("qk_ev", bi), ("prt", bi)], writes=[("qch",)])
            else:
                A.op("vector", lambda e, m=m, bi=bi, t0=t0: e.tensor_tensor(
                    out=K_sb[:, m - 8, t0:t0 + NC], in0=qk_ev[bi][:], in1=partner[bi][:], op=OP.add),
                    reads=[("qk_ev", bi), ("prt", bi)], writes=[("K",)])
        emit_proj(A, w_hbm=wqk_hbm, rhs_fn=lambda k, xb=xb, xtok=xtok: (xb[:, k, :], xtok),
                  kd=KD, mt=16, evac=qk_evac, psum=psum, wring=wring, wrtok="wr" + tag)
        A.dma("gpsimd", q_hbm[:, :, t0:t0 + NC], q_chunk[:, :, :],
              reads=[("qch",)], writes=[("qhbm",)], stream=("qhbm",))

        for k0 in range(0, KD, 1):
            vb = vring[k0 % 2]
            vtok = ("vrk", k0 % 2)
            A.dma("sync", vb[:, :], wv_hbm[k0, :, :], writes=[vtok])
            for dk in range(1):
                k = k0 + dk
                for tsub in range(4):
                    for half in range(2):
                        A.op("tensor", lambda e, k=k, dk=dk, tsub=tsub, half=half, vb=vb, xb=xb:
                             e.matmul(out=psum[tsub * 2 + half][:, 0:NC],
                                      lhsT=xb[:, k, tsub * 128:(tsub + 1) * 128],
                                      rhs=vb[:, half * 512:(half + 1) * 512],
                                      start=(k == 0), stop=(k == KD - 1)),
                             reads=[vtok, xtok], writes=[("ps", tsub * 2 + half)])
        for tsub in range(4):
            j = 4 * c + tsub
            for half in range(2):
                b = tsub * 2 + half
                A.op("vector", lambda e, j=j, half=half, b=b: e.tensor_tensor(
                    out=V65[:, j, half * 8:(half + 1) * 8, 0:64],
                    in0=psum[b][:, 0:NC].rearrange("p (h d) -> p h d", h=8),
                    in1=bv_rep[:, half * 512:(half + 1) * 512].rearrange("p (h d) -> p h d", h=8),
                    op=OP.add),
                    reads=[("ps", b), ("bvrep",)], writes=[("V",)])

    A.barrier()
    es1.close()
    if not a2:
        return
    expS = [[es.enter_context(nc.sbuf_tensor(f"expS{tag}{cc}{i}", [128, NC], BF16))
             for i in range(2)] for cc in range(2)]
    recip_row = [es.enter_context(nc.sbuf_tensor(f"recip{tag}{i}", [1, NC], F32)) for i in range(2)]
    recip_bf = [es.enter_context(nc.sbuf_tensor(f"recipb{tag}{i}", [1, NC], BF16)) for i in range(2)]
    rep_sb = [es.enter_context(nc.sbuf_tensor(f"rep{tag}{i}", [64, NC], F32)) for i in range(2)]
    ao_chunk = es.enter_context(nc.sbuf_tensor(f"ao{tag}", [128, KD, 2 * NC], BF16))
    qp_sb = es.enter_context(nc.sbuf_tensor(f"qp{tag}", [128, KD, 2 * NC], BF16))
    for cp in range(NCH // 2):
        t0 = cp * 2 * NC
        A.dma("sync", qp_sb[:, :, :], q_hbm[:, :, t0:t0 + 2 * NC],
              reads=[("qhbm",)], writes=[("qch",)], stream=("qch",))
        jmaxs = (8 * cp + 3, 8 * cp + 7) if causal else (T // 128 - 1, T // 128 - 1)
        jall = jmaxs[1]
        for h in range(NH):
            if feed:
                feed(2)
            hb, mq = 64 * (h % 2), h // 2
            scb = {0: (0, 1), 1: (4, 5)}
            ub = {0: 2, 1: 6}

            def sc(j, cc, hb=hb, mq=mq, cp=cp):
                bank = scb[cc][j % 2]
                A.op("tensor", lambda e, j=j, cc=cc, hb=hb, mq=mq, bank=bank: e.matmul(
                    out=psum[bank][:, 0:NC], lhsT=K_sb[hb:hb + 64, mq, j * 128:(j + 1) * 128],
                    rhs=qp_sb[hb:hb + 64, mq, cc * NC:(cc + 1) * NC], start=True, stop=True,
                    skip_group_check=True),
                    reads=[("K",), ("qch",)], writes=[("ps", bank)])
                A.op("scalar", lambda e, j=j, cc=cc, bank=bank: e.activation(
                    out=expS[cc][j % 2][:], in_=psum[bank][:, 0:NC], func=AF.Exp,
                    scale=0.125, bias=shift_sb[:, 0:1]),
                    reads=[("ps", bank), ("const",)], writes=[("expS", cc, j % 2)])
                if causal and j >= 8 * cp + 4 * cc:
                    dd = j - (8 * cp + 4 * cc)
                    A.op("vector", lambda e, j=j, cc=cc, dd=dd: e.tensor_tensor(
                        out=expS[cc][j % 2][:], in0=expS[cc][j % 2][:], in1=masks_sb[:, dd, :],
                        op=OP.mult),
                        reads=[("expS", cc, j % 2), ("msk",)], writes=[("expS", cc, j % 2)])

            def pv(j, cc, h=h):
                A.op("tensor", lambda e, j=j, cc=cc, h=h, st=(j == 0), sp=(j == jmaxs[cc]):
                     e.matmul(
                     out=psum[ub[cc]][0:65, 0:NC], lhsT=V65[:, j, h, 0:65],
                     rhs=expS[cc][j % 2][:],
                     start=st, stop=sp, skip_group_check=True),
                     reads=[("V",), ("expS", cc, j % 2)], writes=[("ps", ub[cc])])

            def valid(j, cc):
                return j <= jmaxs[cc]
            sc(0, 0)
            if valid(0, 1):
                sc(0, 1)
            for j in range(1, jall + 1):
                for cc in range(2):
                    if valid(j, cc):
                        sc(j, cc)
                for cc in range(2):
                    if valid(j - 1, cc):
                        pv(j - 1, cc)
            for cc in range(2):
                if jmaxs[cc] == jall:
                    pv(jmaxs[cc], cc)
            for cc in range(2):
                rb = 3 + 4 * cc
                A.op("vector", lambda e, cc=cc: e.reciprocal(out=recip_row[cc][:, :],
                     in_=psum[ub[cc]][64:65, 0:NC]),
                     reads=[("ps", ub[cc])], writes=[("recip", cc)])
                A.op("vector", lambda e, cc=cc: e.tensor_copy(out=recip_bf[cc][:, :],
                     in_=recip_row[cc][:, :]),
                     reads=[("recip", cc)], writes=[("recipb", cc)])
                A.op("tensor", lambda e, cc=cc, rb=rb: e.matmul(out=psum[rb][0:64, 0:NC],
                     lhsT=ones1b[0:1, 0:64], rhs=recip_bf[cc][:, :], start=True, stop=True,
                     skip_group_check=True),
                     reads=[("recipb", cc), ("ones",)], writes=[("ps", rb)])
                A.op("vector", lambda e, cc=cc, rb=rb: e.tensor_copy(out=rep_sb[cc][:, :],
                     in_=psum[rb][0:64, 0:NC]),
                     reads=[("ps", rb)], writes=[("repb", cc)])
                A.op("vector", lambda e, hb=hb, mq=mq, cc=cc: e.tensor_tensor(
                    out=ao_chunk[hb:hb + 64, mq, cc * NC:(cc + 1) * NC],
                    in0=psum[ub[cc]][0:64, 0:NC], in1=rep_sb[cc][:, :],
                    op=OP.mult),
                    reads=[("ps", ub[cc]), ("repb", cc)], writes=[("ao",)])
        A.dma("gpsimd", attn_hbm[:, :, t0:t0 + 2 * NC], ao_chunk[:, :, :],
              reads=[("ao",)], writes=[("ahbm",)], stream=("ahbm",))


def emit_outproj_ln(au, es, nc, *, attn_hbm, src_hbm, dst_hbm, w_hbm, b_sb,
                    psum, ones128b, ones1b, eps1_sb, ln=True, tag="o"):
    A = au
    at_ch = es.enter_context(nc.sbuf_tensor(f"atch{tag}", [128, KD, NC], BF16))
    x_ch = es.enter_context(nc.sbuf_tensor(f"xch{tag}", [128, KD, NC], F32))
    r_ch = es.enter_context(nc.sbuf_tensor(f"rch{tag}", [128, KD, NC], F32))
    wring = [es.enter_context(nc.sbuf_tensor(f"wo{tag}{i}", [128, 2048], BF16)) for i in range(4)]
    x2b = [es.enter_context(nc.sbuf_tensor(f"x2o{tag}{i}", [128, NC], BF16)) for i in range(2)]
    mrow = es.enter_context(nc.sbuf_tensor(f"mrow{tag}", [1, NC], F32))
    vrow = es.enter_context(nc.sbuf_tensor(f"vrow{tag}", [1, NC], F32))
    brow = es.enter_context(nc.sbuf_tensor(f"brow{tag}", [1, NC], F32))
    arow_bf = es.enter_context(nc.sbuf_tensor(f"arowb{tag}", [1, NC], BF16))
    brow_bf = es.enter_context(nc.sbuf_tensor(f"browb{tag}", [1, NC], BF16))
    tmp = es.enter_context(nc.sbuf_tensor(f"tmpo{tag}", [128, NC], F32))

    for c in range(NCH):
        t0 = c * NC
        A.dma("sync", at_ch[:, :, :], attn_hbm[:, :, t0:t0 + NC], writes=[("atch",)],
              stream=("atch",))
        for half in range(2):
            A.dma("sync", x_ch[:, half * 4:(half + 1) * 4, :],
                  src_hbm[:, half * 4:(half + 1) * 4, t0:t0 + NC], writes=[("xch",)],
                  stream=("xch",))

        def op_evac(m, ps, pstok):
            A.op("scalar", lambda e, m=m, ps=ps: e.activation(out=ps[:, 0:NC],
                 in_=ps[:, 0:NC], func=AF.Identity, bias=b_sb[:, m:m + 1]),
                 reads=[pstok, ("bout",)], writes=[pstok])
            A.op("vector", lambda e, m=m, ps=ps: e.tensor_tensor(out=r_ch[:, m, :],
                 in0=ps[:, 0:NC], in1=x_ch[:, m, :], op=OP.add),
                 reads=[pstok, ("xch",)], writes=[("rch",)])
        emit_proj(A, w_hbm=w_hbm, rhs_fn=lambda k: (at_ch[:, k, :], ("atch",)),
                  kd=KD, mt=KD, evac=op_evac, psum=psum, wring=wring, wrtok="wo" + tag)
        if not ln:
            for half in range(2):
                A.dma("gpsimd", dst_hbm[:, half * 4:(half + 1) * 4, t0:t0 + NC],
                      r_ch[:, half * 4:(half + 1) * 4, :], reads=[("rch",)],
                      writes=[("dsth",)], stream=("dst_stream",))
            continue
        for k in range(KD):
            A.op("vector", lambda e, k=k, o=x2b[k % 2]: e.tensor_copy(
                out=o[:], in_=r_ch[:, k, :]), reads=[("rch",)], writes=[("x2o", k % 2)])
            A.op("tensor", lambda e, k=k, i=x2b[k % 2]: e.matmul(out=psum[0][0:1, 0:NC],
                 lhsT=ones128b[:, 0:1], rhs=i[:], start=(k == 0), stop=(k == KD - 1)),
                 reads=[("x2o", k % 2), ("ones",)], writes=[("ps", 0)])
        for k in range(KD):
            A.op("scalar", lambda e, k=k, o=x2b[k % 2]: e.activation(
                out=o[:], in_=r_ch[:, k, :], func=AF.Square),
                reads=[("rch",)], writes=[("x2o", k % 2)])
            A.op("tensor", lambda e, k=k, i=x2b[k % 2]: e.matmul(out=psum[1][0:1, 0:NC],
                 lhsT=ones128b[:, 0:1], rhs=i[:], start=(k == 0), stop=(k == KD - 1)),
                 reads=[("x2o", k % 2), ("ones",)], writes=[("ps", 1)])
        A.op("vector", lambda e: e.tensor_scalar(out=mrow[:, :], in0=psum[0][0:1, 0:NC],
             scalar1=1.0 / D, scalar2=None, op0=OP.mult),
             reads=[("ps", 0)], writes=[("mrow",)])
        A.op("vector", lambda e: e.tensor_scalar(out=vrow[:, :], in0=psum[1][0:1, 0:NC],
             scalar1=1.0 / D, scalar2=None, op0=OP.mult),
             reads=[("ps", 1)], writes=[("vrow",)])
        A.op("vector", lambda e: e.tensor_tensor(out=brow[:, :], in0=mrow[:, :],
             in1=mrow[:, :], op=OP.mult), reads=[("mrow",)], writes=[("brow",)])
        A.op("vector", lambda e: e.tensor_tensor(out=vrow[:, :], in0=vrow[:, :],
             in1=brow[:, :], op=OP.subtract), reads=[("vrow",), ("brow",)], writes=[("vrow",)])
        A.op("scalar", lambda e: e.activation(out=vrow[:, :], in_=vrow[:, :],
             func=AF.Sqrt, bias=eps1_sb[0:1, 0:1]), reads=[("vrow",), ("const",)], writes=[("vrow",)])
        A.op("vector", lambda e: e.reciprocal(out=vrow[:, :], in_=vrow[:, :]),
             reads=[("vrow",)], writes=[("vrow",)])
        A.op("vector", lambda e: e.tensor_tensor(out=brow[:, :], in0=mrow[:, :],
             in1=vrow[:, :], op=OP.mult), reads=[("mrow",), ("vrow",)], writes=[("brow",)])
        A.op("vector", lambda e: e.tensor_scalar(out=brow[:, :], in0=brow[:, :],
             scalar1=-1.0, scalar2=None, op0=OP.mult), reads=[("brow",)], writes=[("brow",)])
        A.op("vector", lambda e: e.tensor_copy(out=arow_bf[:, :], in_=vrow[:, :]),
             reads=[("vrow",)], writes=[("arowb",)])
        A.op("vector", lambda e: e.tensor_copy(out=brow_bf[:, :], in_=brow[:, :]),
             reads=[("brow",)], writes=[("browb",)])
        A.op("tensor", lambda e: e.matmul(out=psum[2][:, 0:NC], lhsT=ones1b[0:1, :],
             rhs=arow_bf[:, :], start=True, stop=True), reads=[("arowb",), ("ones",)], writes=[("ps", 2)])
        A.op("tensor", lambda e: e.matmul(out=psum[3][:, 0:NC], lhsT=ones1b[0:1, :],
             rhs=brow_bf[:, :], start=True, stop=True), reads=[("browb",), ("ones",)], writes=[("ps", 3)])
        for k in range(KD):
            A.op("vector", lambda e, k=k: e.tensor_tensor(out=tmp[:], in0=r_ch[:, k, :],
                 in1=psum[2][:, 0:NC], op=OP.mult),
                 reads=[("rch",), ("ps", 2)], writes=[("tmpo",)])
            A.op("vector", lambda e, k=k: e.tensor_tensor(out=r_ch[:, k, :], in0=tmp[:],
                 in1=psum[3][:, 0:NC], op=OP.add),
                 reads=[("tmpo",), ("ps", 3)], writes=[("rch",)])
        for half in range(2):
            A.dma("gpsimd", dst_hbm[:, half * 4:(half + 1) * 4, t0:t0 + NC],
                  r_ch[:, half * 4:(half + 1) * 4, :], reads=[("rch",)], writes=[("dsth",)],
                  stream=("dst_stream",))


def emit_moe_phase(au, es, nc, *, z_hbm, x1_hbm, dst_hbm, rms_gain_sb, gate_hbm,
                   w1_hbm, w2_hbm, w3_hbm, b1_sb, b2_sb, b3_sb, psum,
                   ones128b, ones1b, esel_sb, ident_bf, eps1_sb, tag="m"):
    A = au
    NE, HD = 8, 2048
    z_ch = es.enter_context(nc.sbuf_tensor(f"zch{tag}", [128, KD, NC], F32))
    x1_ch = es.enter_context(nc.sbuf_tensor(f"x1ch{tag}", [128, KD, NC], F32))
    zn_bf = es.enter_context(nc.sbuf_tensor(f"znbf{tag}", [128, KD, NC], BF16))
    x2b = [es.enter_context(nc.sbuf_tensor(f"x2{tag}{i}", [128, NC], BF16)) for i in range(2)]
    srow = es.enter_context(nc.sbuf_tensor(f"srow{tag}", [1, NC], F32))
    srow_bf = es.enter_context(nc.sbuf_tensor(f"srowb{tag}", [1, NC], BF16))
    gate_sb = es.enter_context(nc.sbuf_tensor(f"gate{tag}", [128, KD, 8], BF16))
    glog = es.enter_context(nc.sbuf_tensor(f"glog{tag}", [128, 8], F32))
    gtmp = es.enter_context(nc.sbuf_tensor(f"gtmp{tag}", [128, 8], F32))
    gm1 = es.enter_context(nc.sbuf_tensor(f"gm1{tag}", [128, 1], F32))
    gm2 = es.enter_context(nc.sbuf_tensor(f"gm2{tag}", [128, 1], F32))
    comb_bf = es.enter_context(nc.sbuf_tensor(f"combbf{tag}", [128, 4, 8], BF16))
    combT = es.enter_context(nc.sbuf_tensor(f"combT{tag}", [8, NC], BF16))
    crep = es.enter_context(nc.sbuf_tensor(f"crep{tag}", [128, NC], BF16))
    s1_sb = es.enter_context(nc.sbuf_tensor(f"s1{tag}", [128, 16, NC], BF16))
    h_sb = es.enter_context(nc.sbuf_tensor(f"h{tag}", [128, 16, NC], BF16))
    y_sb = es.enter_context(nc.sbuf_tensor(f"y{tag}", [128, KD, NC], F32))
    wring = [es.enter_context(nc.sbuf_tensor(f"w{tag}{i}", [128, 2048], BF16)) for i in range(4)]
    mrow = es.enter_context(nc.sbuf_tensor(f"mrow{tag}", [1, NC], F32))
    vrow = es.enter_context(nc.sbuf_tensor(f"vrow{tag}", [1, NC], F32))
    brow = es.enter_context(nc.sbuf_tensor(f"brow{tag}", [1, NC], F32))
    arow_bf = es.enter_context(nc.sbuf_tensor(f"arb{tag}", [1, NC], BF16))
    brow_bf = es.enter_context(nc.sbuf_tensor(f"brb{tag}", [1, NC], BF16))
    tmp32 = es.enter_context(nc.sbuf_tensor(f"tmp32{tag}", [128, NC], F32))

    for k in range(KD):
        A.dma("gpsimd", gate_sb[:, k, :], gate_hbm[k, :, :], writes=[("gate",)], stream=("gate",))

    for c in range(NCH):
        t0 = c * NC
        for half in range(2):
            A.dma("sync", z_ch[:, half * 4:(half + 1) * 4, :],
                  z_hbm[:, half * 4:(half + 1) * 4, t0:t0 + NC], writes=[("zch",)])
        for k in range(KD):
            A.op("vector", lambda e, k=k, o=x2b[k % 2]: e.tensor_tensor(
                out=o[:], in0=z_ch[:, k, :], in1=z_ch[:, k, :], op=OP.mult),
                reads=[("zch",)], writes=[("x2b", k % 2)])
            A.op("tensor", lambda e, k=k, i=x2b[k % 2]: e.matmul(
                out=psum[7][0:1, 0:NC], lhsT=ones128b[:, 0:1], rhs=i[:],
                start=(k == 0), stop=(k == KD - 1)),
                reads=[("x2b", k % 2), ("ones",)], writes=[("ps", 7)])
        A.op("vector", lambda e: e.tensor_scalar(out=srow[:, :], in0=psum[7][0:1, 0:NC],
             scalar1=1.0 / D, scalar2=None, op0=OP.mult), reads=[("ps", 7)], writes=[("srow",)])
        A.op("scalar", lambda e: e.activation(out=srow[:, :], in_=srow[:, :],
             func=AF.Sqrt, bias=eps1_sb[0:1, 1:2]), reads=[("srow",), ("const",)], writes=[("srow",)])
        A.op("vector", lambda e: e.reciprocal(out=srow[:, :], in_=srow[:, :]),
             reads=[("srow",)], writes=[("srow",)])
        A.op("vector", lambda e: e.tensor_copy(out=srow_bf[:, :], in_=srow[:, :]),
             reads=[("srow",)], writes=[("srowb",)])
        A.op("tensor", lambda e: e.matmul(out=psum[6][:, 0:NC], lhsT=ones1b[0:1, :],
             rhs=srow_bf[:, :], start=True, stop=True),
             reads=[("srowb",), ("ones",)], writes=[("ps", 6)])
        for k in range(KD):
            A.op("vector", lambda e, k=k: e.scalar_tensor_tensor(
                out=zn_bf[:, k, :], in0=z_ch[:, k, :], scalar=rms_gain_sb[:, k:k + 1],
                in1=psum[6][:, 0:NC], op0=OP.mult, op1=OP.mult),
                reads=[("zch",), ("ps", 6), ("gain2",)], writes=[("znbf",)])

        for tsub in range(4):
            for k in range(KD):
                A.op("tensor", lambda e, k=k, tsub=tsub: e.matmul(
                    out=psum[7][:, 0:8], lhsT=zn_bf[:, k, tsub * 128:(tsub + 1) * 128],
                    rhs=gate_sb[:, k, :], start=(k == 0), stop=(k == KD - 1)),
                    reads=[("znbf",), ("gate",)], writes=[("ps", 7)])
            A.op("vector", lambda e: e.tensor_copy(out=glog[:, :], in_=psum[7][:, 0:8]),
                 reads=[("ps", 7)], writes=[("glog",)])
            A.op("vector", lambda e: e.reduce_max(out=gm1[:, 0:1], in_=glog[:, :], axis=AX.X),
                 reads=[("glog",)], writes=[("gm1",)])
            A.op("vector", lambda e: e.tensor_scalar(out=gtmp[:, :], in0=glog[:, :],
                 scalar1=gm1[:, 0:1], scalar2=None, op0=OP.is_ge),
                 reads=[("glog",), ("gm1",)], writes=[("gtmp",)])
            A.op("vector", lambda e: e.scalar_tensor_tensor(out=gtmp[:, :], in0=gtmp[:, :],
                 scalar=-1e30, in1=glog[:, :], op0=OP.mult, op1=OP.add),
                 reads=[("gtmp",), ("glog",)], writes=[("gtmp",)])
            A.op("vector", lambda e: e.reduce_max(out=gm2[:, 0:1], in_=gtmp[:, :], axis=AX.X),
                 reads=[("gtmp",)], writes=[("gm2",)])
            A.op("vector", lambda e: e.tensor_scalar(out=gm1[:, 0:1], in0=gm1[:, 0:1],
                 scalar1=-1.0, scalar2=None, op0=OP.mult), reads=[("gm1",)], writes=[("gm1",)])
            A.op("scalar", lambda e: e.activation(out=gtmp[:, :], in_=glog[:, :],
                 func=AF.Exp, bias=gm1[:, 0:1]), reads=[("glog",), ("gm1",)], writes=[("gtmp",)])
            A.op("vector", lambda e: e.tensor_scalar(out=glog[:, :], in0=glog[:, :],
                 scalar1=gm2[:, 0:1], scalar2=None, op0=OP.is_ge),
                 reads=[("glog",), ("gm2",)], writes=[("glog",)])
            A.op("vector", lambda e: e.tensor_tensor(out=gtmp[:, :], in0=gtmp[:, :],
                 in1=glog[:, :], op=OP.mult), reads=[("gtmp",), ("glog",)], writes=[("gtmp",)])
            A.op("vector", lambda e: e.tensor_reduce(out=gm2[:, 0:1], in_=gtmp[:, :],
                 axis=AX.X, op=OP.add), reads=[("gtmp",)], writes=[("gm2",)])
            A.op("vector", lambda e: e.reciprocal(out=gm2[:, 0:1], in_=gm2[:, 0:1]),
                 reads=[("gm2",)], writes=[("gm2",)])
            A.op("vector", lambda e, tsub=tsub: e.tensor_scalar(out=comb_bf[:, tsub, :],
                 in0=gtmp[:, :], scalar1=gm2[:, 0:1], scalar2=None, op0=OP.mult),
                 reads=[("gtmp",), ("gm2",)], writes=[("comb",)])
            A.op("tensor", lambda e, tsub=tsub: e.matmul(out=psum[6][0:8, 0:128],
                 lhsT=comb_bf[:, tsub, :], rhs=ident_bf[:], start=True, stop=True),
                 reads=[("comb",), ("ident",)], writes=[("ps", 6)])
            A.op("vector", lambda e, tsub=tsub: e.tensor_copy(
                 out=combT[:, tsub * 128:(tsub + 1) * 128], in_=psum[6][0:8, 0:128]),
                 reads=[("ps", 6)], writes=[("combT",)])

        for ex in range(NE):
            A.op("tensor", lambda e, ex=ex: e.matmul(out=psum[6][:, 0:NC],
                 lhsT=esel_sb[:, ex * 128:(ex + 1) * 128], rhs=combT[:, :],
                 start=True, stop=True), reads=[("esel",), ("combT",)], writes=[("ps", 6)])
            A.op("vector", lambda e: e.tensor_copy(out=crep[:, :], in_=psum[6][:, 0:NC]),
                 reads=[("ps", 6)], writes=[("crep",)])

            def w1_evac(m, ps, pstok, ex=ex):
                A.op("scalar", lambda e, m=m, ps=ps, ex=ex: e.activation(
                    out=s1_sb[:, m, :], in_=ps[:, 0:NC], func=AF.Silu,
                    bias=b1_sb[:, ex, m:m + 1]),
                    reads=[pstok, ("b1",)], writes=[("s1", m)])
                A.op("vector", lambda e, m=m: e.tensor_tensor(out=s1_sb[:, m, :],
                     in0=s1_sb[:, m, :], in1=crep[:, :], op=OP.mult),
                     reads=[("s1", m), ("crep",)], writes=[("s1", m)])
            emit_proj(A, w_hbm=w1_hbm[ex], rhs_fn=lambda k: (zn_bf[:, k, :], ("znbf",)),
                      kd=KD, mt=16, evac=w1_evac, psum=psum, wring=wring, wrtok="we" + tag)

            def w2_evac(m, ps, pstok, ex=ex):
                A.op("scalar", lambda e, m=m, ps=ps, ex=ex: e.activation(
                    out=ps[:, 0:NC], in_=ps[:, 0:NC], func=AF.Identity,
                    bias=b2_sb[:, ex, m:m + 1]),
                    reads=[pstok, ("b2",)], writes=[pstok])
                A.op("vector", lambda e, m=m, ps=ps: e.tensor_tensor(
                    out=h_sb[:, m, :], in0=ps[:, 0:NC], in1=s1_sb[:, m, :],
                    op=OP.mult),
                    reads=[pstok, ("s1", m)], writes=[("h", m)])
            emit_proj(A, w_hbm=w2_hbm[ex], rhs_fn=lambda k: (zn_bf[:, k, :], ("znbf",)),
                      kd=KD, mt=16, evac=w2_evac, psum=psum, wring=wring, wrtok="we" + tag)

            def w3_evac(m, ps, pstok, ex=ex):
                if ex == 0:
                    A.op("vector", lambda e, m=m, ps=ps: e.tensor_copy(out=y_sb[:, m, :],
                         in_=ps[:, 0:NC]), reads=[pstok], writes=[("y", m)])
                else:
                    A.op("vector", lambda e, m=m, ps=ps: e.tensor_tensor(out=y_sb[:, m, :],
                         in0=y_sb[:, m, :], in1=ps[:, 0:NC], op=OP.add),
                         reads=[pstok, ("y", m)], writes=[("y", m)])
            emit_proj(A, w_hbm=w3_hbm[ex], rhs_fn=lambda k: (h_sb[:, k, :], ("h", k)),
                      kd=16, mt=KD, evac=w3_evac, psum=psum, wring=wring, wrtok="we" + tag)

        for m8 in range(8):
            A.op("tensor", lambda e, m8=m8: e.matmul(out=psum[m8][:, 0:NC],
                 lhsT=b3_sb[0:8, m8 * 128:(m8 + 1) * 128], rhs=combT[:, :],
                 start=True, stop=True), reads=[("b3",), ("combT",)], writes=[("ps", m8)])
            A.op("vector", lambda e, m8=m8: e.tensor_tensor(out=y_sb[:, m8, :],
                 in0=y_sb[:, m8, :], in1=psum[m8][:, 0:NC], op=OP.add),
                 reads=[("ps", m8), ("y", m8)], writes=[("y", m8)])

        for half in range(2):
            A.dma("sync", x1_ch[:, half * 4:(half + 1) * 4, :],
                  x1_hbm[:, half * 4:(half + 1) * 4, t0:t0 + NC], writes=[("x1ch",)])
        for k in range(KD):
            A.op("vector", lambda e, k=k: e.tensor_tensor(out=y_sb[:, k, :],
                 in0=y_sb[:, k, :], in1=z_ch[:, k, :], op=OP.add),
                 reads=[("y", k), ("zch",)], writes=[("y", k)])
            A.op("vector", lambda e, k=k: e.tensor_tensor(out=y_sb[:, k, :],
                 in0=y_sb[:, k, :], in1=x1_ch[:, k, :], op=OP.add),
                 reads=[("y", k), ("x1ch",)], writes=[("y", k)])
        rbk = x2b
        for k in range(KD):
            A.op("vector", lambda e, k=k, o=rbk[k % 2]: e.tensor_copy(
                out=o[:], in_=y_sb[:, k, :]), reads=[("y", k)], writes=[("rb", k % 2)])
            A.op("tensor", lambda e, k=k, i=rbk[k % 2]: e.matmul(out=psum[0][0:1, 0:NC],
                 lhsT=ones128b[:, 0:1], rhs=i[:], start=(k == 0), stop=(k == KD - 1)),
                 reads=[("rb", k % 2), ("ones",)], writes=[("ps", 0)])
        for k in range(KD):
            A.op("scalar", lambda e, k=k, o=rbk[k % 2]: e.activation(
                out=o[:], in_=y_sb[:, k, :], func=AF.Square),
                reads=[("y", k)], writes=[("rb", k % 2)])
            A.op("tensor", lambda e, k=k, i=rbk[k % 2]: e.matmul(out=psum[1][0:1, 0:NC],
                 lhsT=ones128b[:, 0:1], rhs=i[:], start=(k == 0), stop=(k == KD - 1)),
                 reads=[("rb", k % 2), ("ones",)], writes=[("ps", 1)])
        A.op("vector", lambda e: e.tensor_scalar(out=mrow[:, :], in0=psum[0][0:1, 0:NC],
             scalar1=1.0 / D, scalar2=None, op0=OP.mult), reads=[("ps", 0)], writes=[("mrow",)])
        A.op("vector", lambda e: e.tensor_scalar(out=vrow[:, :], in0=psum[1][0:1, 0:NC],
             scalar1=1.0 / D, scalar2=None, op0=OP.mult), reads=[("ps", 1)], writes=[("vrow",)])
        A.op("vector", lambda e: e.tensor_tensor(out=brow[:, :], in0=mrow[:, :],
             in1=mrow[:, :], op=OP.mult), reads=[("mrow",)], writes=[("brow",)])
        A.op("vector", lambda e: e.tensor_tensor(out=vrow[:, :], in0=vrow[:, :],
             in1=brow[:, :], op=OP.subtract), reads=[("vrow",), ("brow",)], writes=[("vrow",)])
        A.op("scalar", lambda e: e.activation(out=vrow[:, :], in_=vrow[:, :],
             func=AF.Sqrt, bias=eps1_sb[0:1, 0:1]), reads=[("vrow",), ("const",)], writes=[("vrow",)])
        A.op("vector", lambda e: e.reciprocal(out=vrow[:, :], in_=vrow[:, :]),
             reads=[("vrow",)], writes=[("vrow",)])
        A.op("vector", lambda e: e.tensor_tensor(out=brow[:, :], in0=mrow[:, :],
             in1=vrow[:, :], op=OP.mult), reads=[("mrow",), ("vrow",)], writes=[("brow",)])
        A.op("vector", lambda e: e.tensor_scalar(out=brow[:, :], in0=brow[:, :],
             scalar1=-1.0, scalar2=None, op0=OP.mult), reads=[("brow",)], writes=[("brow",)])
        A.op("vector", lambda e: e.tensor_copy(out=arow_bf[:, :], in_=vrow[:, :]),
             reads=[("vrow",)], writes=[("arb",)])
        A.op("vector", lambda e: e.tensor_copy(out=brow_bf[:, :], in_=brow[:, :]),
             reads=[("brow",)], writes=[("brb",)])
        A.op("tensor", lambda e: e.matmul(out=psum[2][:, 0:NC], lhsT=ones1b[0:1, :],
             rhs=arow_bf[:, :], start=True, stop=True), reads=[("arb",), ("ones",)], writes=[("ps", 2)])
        A.op("tensor", lambda e: e.matmul(out=psum[3][:, 0:NC], lhsT=ones1b[0:1, :],
             rhs=brow_bf[:, :], start=True, stop=True), reads=[("brb",), ("ones",)], writes=[("ps", 3)])
        for k in range(KD):
            A.op("vector", lambda e, k=k: e.tensor_tensor(out=tmp32[:], in0=y_sb[:, k, :],
                 in1=psum[2][:, 0:NC], op=OP.mult),
                 reads=[("y", k), ("ps", 2)], writes=[("tmp32",)])
            A.op("vector", lambda e, k=k: e.tensor_tensor(out=y_sb[:, k, :], in0=tmp32[:],
                 in1=psum[3][:, 0:NC], op=OP.add),
                 reads=[("tmp32",), ("ps", 3)], writes=[("y", k)])
        for half in range(2):
            A.dma("gpsimd", dst_hbm[:, half * 4:(half + 1) * 4, t0:t0 + NC],
                  y_sb[:, half * 4:(half + 1) * 4, :], reads=[("y", half * 4), ("y", half * 4 + 1),
                  ("y", half * 4 + 2), ("y", half * 4 + 3)], writes=[("dsth",)],
                  stream=("dst_stream",))


DI = 2048
KDI = DI // 128
DST = 16
DTR = 64


def emit_mamba_phase(au, es, nc, *, src_hbm, dst_hbm, win_hbm, cw_sb, cb_sb,
                     wx_sb, wdt_sb, bdt_sb, A_sb, D_sb, wout_hbm, esel32_sb,
                     psum, ones128b, ones1b, ones128f32, eps1_sb, tag="c"):
    A = au
    x_bf = es.enter_context(nc.sbuf_tensor(f"xbf{tag}", [128, KD, NC], BF16))
    xi_sb = es.enter_context(nc.sbuf_tensor(f"xi{tag}", [128, KDI, NC + 8], BF16))
    xc_bf = es.enter_context(nc.sbuf_tensor(f"xc{tag}", [128, KDI, NC], BF16))
    zs_bf = es.enter_context(nc.sbuf_tensor(f"zs{tag}", [128, KDI, NC], BF16))
    delta = es.enter_context(nc.sbuf_tensor(f"delta{tag}", [128, KDI, NC], BF16))
    u_bf = es.enter_context(nc.sbuf_tensor(f"u{tag}", [128, 4, NC], BF16))
    g_bf = es.enter_context(nc.sbuf_tensor(f"g{tag}", [128, KDI, NC], BF16))
    y_sb = es.enter_context(nc.sbuf_tensor(f"yy{tag}", [128, 4, NC], F32))
    hcar = es.enter_context(nc.sbuf_tensor(f"hc{tag}", [128, KDI, DST], F32))
    wring = [es.enter_context(nc.sbuf_tensor(f"w{tag}{i}", [128, 2048], BF16)) for i in range(3)]
    xdbl_bc = es.enter_context(nc.sbuf_tensor(f"xbc{tag}", [32, NC], BF16))
    dtT_bf = es.enter_context(nc.sbuf_tensor(f"dtT{tag}", [64, NC], BF16))
    vab = es.enter_context(nc.sbuf_tensor(f"vab{tag}", [128, NC], F32))
    rel = es.enter_context(nc.sbuf_tensor(f"rel{tag}", [128, NC], F32))
    brep = [es.enter_context(nc.sbuf_tensor(f"brep{tag}{i}", [128, NC], BF16)) for i in range(2)]
    crep = [es.enter_context(nc.sbuf_tensor(f"crep{tag}{i}", [128, NC], BF16)) for i in range(2)]
    SEG = NC + 1
    dA_bf = [es.enter_context(nc.sbuf_tensor(f"dA{tag}{i}", [128, 4 * SEG], BF16)) for i in range(2)]
    dBu = [es.enter_context(nc.sbuf_tensor(f"dBu{tag}{i}", [128, 4 * SEG], BF16)) for i in range(2)]
    hn = [es.enter_context(nc.sbuf_tensor(f"hn{tag}{i}", [128, 4 * SEG], BF16)) for i in range(2)]
    tmpb = [es.enter_context(nc.sbuf_tensor(f"tmb{tag}{i}", [128, 4, NC], BF16)) for i in range(2)]
    zcol = es.enter_context(nc.sbuf_tensor(f"zc{tag}", [128, 1], F32))
    x2f = es.enter_context(nc.sbuf_tensor(f"x2f{tag}", [128, KD, NC], F32))
    rbk = [es.enter_context(nc.sbuf_tensor(f"rb{tag}{i}", [128, NC], BF16)) for i in range(2)]
    mrow = es.enter_context(nc.sbuf_tensor(f"mr{tag}", [1, NC], F32))
    vrow = es.enter_context(nc.sbuf_tensor(f"vr{tag}", [1, NC], F32))
    arow_bf = es.enter_context(nc.sbuf_tensor(f"ab{tag}", [1, NC], BF16))
    brow_bf = es.enter_context(nc.sbuf_tensor(f"bb{tag}", [1, NC], BF16))

    A.op("vector", lambda e: e.memset(hcar[:], 0.0), writes=[("hcar",)])
    A.op("vector", lambda e: e.memset(xi_sb[:, :, 0:8], 0.0), writes=[("xi",)])
    A.op("vector", lambda e: e.memset(zcol[:], 0.0), writes=[("zc",)])
    for r in range(2):
        for s in range(4):
            A.op("vector", lambda e, r=r, s=s: e.memset(
                dA_bf[r][:, s * SEG:s * SEG + 1], 0.0), writes=[("dAb", r)])

    for c in range(NCH):
        t0 = c * NC
        A.dma("gpsimd", x_bf[:, :, :], src_hbm[:, :, t0:t0 + NC], writes=[("xbfc",)],
              stream=("xbfc",))
        def xz_evac(m, ps, pstok):
            if m < 16:
                A.op("vector", lambda e, m=m, ps=ps: e.tensor_copy(
                    out=xi_sb[:, m, 8:8 + NC], in_=ps[:, 0:NC]),
                    reads=[pstok], writes=[("xi",)])
            else:
                A.op("scalar", lambda e, m=m, ps=ps: e.activation(
                    out=zs_bf[:, m - 16, :], in_=ps[:, 0:NC], func=AF.Silu),
                    reads=[pstok], writes=[("zs",)])

        emit_proj(A, w_hbm=win_hbm, rhs_fn=lambda k: (x_bf[:, k, :], ("xbfc",)),
                  kd=KD, mt=32, evac=xz_evac, psum=psum, wring=wring, wrtok="wm" + tag)
        for m in range(KDI):
            A.op("vector", lambda e, m=m: e.tensor_scalar(
                out=vab[:], in0=xi_sb[:, m, 5:5 + NC], scalar1=cw_sb[:, m, 0:1],
                scalar2=None, op0=OP.mult),
                reads=[("xi",), ("cw",)], writes=[("vab",)])
            for j in (1, 2, 3):
                A.op("vector", lambda e, m=m, j=j: e.scalar_tensor_tensor(
                    out=vab[:], in0=xi_sb[:, m, 5 + j:5 + j + NC], scalar=cw_sb[:, m, j:j + 1],
                    in1=vab[:], op0=OP.mult, op1=OP.add),
                    reads=[("xi",), ("cw",), ("vab",)], writes=[("vab",)])
            A.op("scalar", lambda e, m=m: e.activation(out=xc_bf[:, m, :], in_=vab[:],
                 func=AF.Silu, bias=cb_sb[:, m:m + 1]),
                 reads=[("vab",), ("cb",)], writes=[("xc",)])
            A.op("vector", lambda e, m=m: e.tensor_copy(out=xi_sb[:, m, 5:8],
                 in_=xi_sb[:, m, 5 + NC:8 + NC]), reads=[("xi",)], writes=[("xi",)])
        for k in range(KDI):
            A.op("tensor", lambda e, k=k: e.matmul(out=psum[7][0:96, 0:NC],
                 lhsT=wx_sb[:, k, :], rhs=xc_bf[:, k, :], start=(k == 0), stop=(k == KDI - 1)),
                 reads=[("wx",), ("xc",)], writes=[("ps", 7)])
        A.op("vector", lambda e: e.tensor_copy(out=dtT_bf[:, :], in_=psum[7][0:64, 0:NC]),
             reads=[("ps", 7)], writes=[("dtT",)])
        A.op("vector", lambda e: e.tensor_copy(out=xdbl_bc[:, :], in_=psum[7][64:96, 0:NC]),
             reads=[("ps", 7)], writes=[("xbc",)])
        for m in range(KDI):
            A.op("tensor", lambda e, m=m: e.matmul(out=psum[6][:, 0:NC],
                 lhsT=wdt_sb[:, m * 128:(m + 1) * 128], rhs=dtT_bf[:, :],
                 start=True, stop=True), reads=[("wdt",), ("dtT",)], writes=[("ps", 6)])
            A.op("scalar", lambda e, m=m: e.activation(out=vab[:], in_=psum[6][:, 0:NC],
                 func=AF.Abs, bias=bdt_sb[:, m:m + 1]),
                 reads=[("ps", 6), ("bdt",)], writes=[("vab",)])
            A.op("scalar", lambda e: e.activation(out=vab[:], in_=vab[:], func=AF.Exp,
                 scale=-1.0), reads=[("vab",)], writes=[("vab",)])
            A.op("scalar", lambda e: e.activation(out=vab[:], in_=vab[:], func=AF.Ln,
                 bias=ones128f32[:, 0:1]), reads=[("vab",), ("onesf",)], writes=[("vab",)])
            A.op("scalar", lambda e, m=m: e.activation(out=rel[:], in_=psum[6][:, 0:NC],
                 func=AF.Relu, bias=bdt_sb[:, m:m + 1]),
                 reads=[("ps", 6), ("bdt",)], writes=[("rel",)])
            A.op("vector", lambda e, m=m: e.tensor_tensor(out=delta[:, m, :], in0=vab[:],
                 in1=rel[:], op=OP.add), reads=[("vab",), ("rel",)], writes=[("delta", m)])
        it = 0
        for mg in range(4):
            ms = range(4 * mg, 4 * mg + 4)
            for mi, m in enumerate(ms):
                A.op("vector", lambda e, m=m, mi=mi: e.tensor_tensor(out=u_bf[:, mi, :],
                     in0=delta[:, m, :], in1=xc_bf[:, m, :], op=OP.mult),
                     reads=[("delta", m), ("xc",)], writes=[("u", mi)])
            for n in range(DST):
                k2 = it % 2
                n2 = n % 2
                A.op("tensor", lambda e, n=n: e.matmul(out=psum[5][:, 0:NC],
                     lhsT=esel32_sb[:, n * 128:(n + 1) * 128], rhs=xdbl_bc[:, :],
                     start=True, stop=True), reads=[("esel32",), ("xbc",)], writes=[("ps", 5)])
                A.op("scalar", lambda e, n=n: e.copy(out=brep[n % 2][:], in_=psum[5][:, 0:NC]),
                     reads=[("ps", 5)], writes=[("brep", n % 2)])
                A.op("tensor", lambda e, n=n: e.matmul(out=psum[5][:, 0:NC],
                     lhsT=esel32_sb[:, (16 + n) * 128:(17 + n) * 128], rhs=xdbl_bc[:, :],
                     start=True, stop=True), reads=[("esel32",), ("xbc",)], writes=[("ps", 5)])
                A.op("scalar", lambda e, n=n: e.copy(out=crep[n % 2][:], in_=psum[5][:, 0:NC]),
                     reads=[("ps", 5)], writes=[("crep", n % 2)])
                for mi, m in enumerate(ms):
                    A.op("scalar", lambda e, m=m, n=n, o=dA_bf[k2], s0=mi * SEG: e.activation(
                         out=o[:, s0 + 1:s0 + SEG], in_=delta[:, m, :],
                         func=AF.Exp, scale=A_sb[:, m, n:n + 1]),
                         reads=[("delta", m), ("A",)], writes=[("dAb", k2)])
                for mi, m in enumerate(ms):
                    eng = "vector" if mi == 0 else "gpsimd"
                    A.op(eng, lambda e, mi=mi, o=dBu[k2], n2=n2, s0=mi * SEG: e.tensor_tensor(
                         out=o[:, s0 + 1:s0 + SEG],
                         in0=u_bf[:, mi, :], in1=brep[n2][:], op=OP.mult),
                         reads=[("u", mi), ("brep", n2)], writes=[("dBu", k2)])
                A.op("scalar", lambda e, n=n, mg=mg, o=dBu[k2]: e.copy(
                     out=o[:, 0:4 * SEG].rearrange("p (a b) -> p a b", a=4)[:, :, 0:1],
                     in_=hcar[:, 4 * mg:4 * mg + 4, n:n + 1]),
                     reads=[("hcar",)], writes=[("dBs", k2)])
                A.op("vector", lambda e, i0=dA_bf[k2], i1=dBu[k2], o=hn[k2]:
                     e.tensor_tensor_scan(
                     out=o[:, :], data0=i0[:, :], data1=i1[:, :], initial=zcol[:, 0:1],
                     op0=OP.mult, op1=OP.add),
                     reads=[("dAb", k2), ("dBu", k2), ("dBs", k2), ("zc",)], writes=[("hn", k2)])
                A.op("scalar", lambda e, n=n, mg=mg, i=hn[k2]: e.copy(
                     out=hcar[:, 4 * mg:4 * mg + 4, n:n + 1],
                     in_=i[:, 0:4 * SEG].rearrange("p (a b) -> p a b", a=4)[:, :, SEG - 1:SEG]),
                     reads=[("hn", k2)], writes=[("hcar",)])
                if n == 0:
                    for mi in range(4):
                        A.op("vector", lambda e, mi=mi, i=hn[k2], n2=n2, s0=mi * SEG:
                             e.tensor_tensor(
                             out=y_sb[:, mi, :],
                             in0=i[:, s0 + 1:s0 + SEG], in1=crep[n2][:], op=OP.mult),
                             reads=[("hn", k2), ("crep", n2)], writes=[("yc", mi)])
                else:
                    for mi in range(4):
                        A.op("vector", lambda e, mi=mi, i=hn[k2], o=tmpb[k2], n2=n2,
                             s0=mi * SEG: e.tensor_tensor(
                             out=o[:, mi, :],
                             in0=i[:, s0 + 1:s0 + SEG], in1=crep[n2][:], op=OP.mult),
                             reads=[("hn", k2), ("crep", n2)], writes=[("tmb", k2)])
                    A.op("gpsimd", lambda e, i=tmpb[k2]: e.tensor_tensor(
                         out=y_sb[:, 0:4, :].rearrange("p a b -> p (a b)"),
                         in0=y_sb[:, 0:4, :].rearrange("p a b -> p (a b)"),
                         in1=i[:, 0:4, :].rearrange("p a b -> p (a b)"), op=OP.add),
                         reads=[("tmb", k2), ("yc", 0), ("yc", 1), ("yc", 2), ("yc", 3)],
                         writes=[("yc", 0), ("yc", 1), ("yc", 2), ("yc", 3)])
                it += 1
            for mi, m in enumerate(ms):
                A.op("vector", lambda e, m=m, mi=mi: e.scalar_tensor_tensor(out=y_sb[:, mi, :],
                     in0=xc_bf[:, m, :], scalar=D_sb[:, m:m + 1], in1=y_sb[:, mi, :],
                     op0=OP.mult, op1=OP.add),
                     reads=[("xc",), ("D",), ("yc", mi)], writes=[("yc", mi)])
                A.op("vector", lambda e, m=m, mi=mi: e.tensor_tensor(out=g_bf[:, m, :],
                     in0=y_sb[:, mi, :], in1=zs_bf[:, m, :], op=OP.mult),
                     reads=[("yc", mi), ("zs",)], writes=[("u2", m)])
        for half in range(2):
            A.dma("sync", x2f[:, half * 4:(half + 1) * 4, :],
                  src_hbm[:, half * 4:(half + 1) * 4, t0:t0 + NC], writes=[("x2f",)])
        def mo_evac(m, ps, pstok):
            A.op("vector", lambda e, m=m, ps=ps: e.tensor_tensor(out=x2f[:, m, :],
                 in0=x2f[:, m, :], in1=ps[:, 0:NC], op=OP.add),
                 reads=[pstok, ("x2f",)], writes=[("x2f",)])
        emit_proj(A, w_hbm=wout_hbm, rhs_fn=lambda k: (g_bf[:, k, :], ("u2", k)),
                  kd=KDI, mt=KD, evac=mo_evac, psum=psum, wring=wring, wrtok="wm" + tag)
        for k in range(KD):
            A.op("vector", lambda e, k=k, o=rbk[k % 2]: e.tensor_copy(
                out=o[:], in_=x2f[:, k, :]), reads=[("x2f",)], writes=[("rb", k % 2)])
            A.op("tensor", lambda e, k=k, i=rbk[k % 2]: e.matmul(out=psum[0][0:1, 0:NC],
                 lhsT=ones128b[:, 0:1], rhs=i[:], start=(k == 0), stop=(k == KD - 1)),
                 reads=[("rb", k % 2), ("ones",)], writes=[("ps", 0)])
        for k in range(KD):
            A.op("scalar", lambda e, k=k, o=rbk[k % 2]: e.activation(
                out=o[:], in_=x2f[:, k, :], func=AF.Square),
                reads=[("x2f",)], writes=[("rb", k % 2)])
            A.op("tensor", lambda e, k=k, i=rbk[k % 2]: e.matmul(out=psum[1][0:1, 0:NC],
                 lhsT=ones128b[:, 0:1], rhs=i[:], start=(k == 0), stop=(k == KD - 1)),
                 reads=[("rb", k % 2), ("ones",)], writes=[("ps", 1)])
        A.op("vector", lambda e: e.tensor_scalar(out=mrow[:, :], in0=psum[0][0:1, 0:NC],
             scalar1=1.0 / D, scalar2=None, op0=OP.mult), reads=[("ps", 0)], writes=[("mrow",)])
        A.op("vector", lambda e: e.tensor_scalar(out=vrow[:, :], in0=psum[1][0:1, 0:NC],
             scalar1=1.0 / D, scalar2=None, op0=OP.mult), reads=[("ps", 1)], writes=[("vrow",)])
        A.op("vector", lambda e: e.tensor_tensor(out=rel[0:1, :], in0=mrow[:, :],
             in1=mrow[:, :], op=OP.mult), reads=[("mrow",)], writes=[("rel",)])
        A.op("vector", lambda e: e.tensor_tensor(out=vrow[:, :], in0=vrow[:, :],
             in1=rel[0:1, :], op=OP.subtract), reads=[("vrow",), ("rel",)], writes=[("vrow",)])
        A.op("scalar", lambda e: e.activation(out=vrow[:, :], in_=vrow[:, :],
             func=AF.Sqrt, bias=eps1_sb[0:1, 0:1]), reads=[("vrow",), ("const",)], writes=[("vrow",)])
        A.op("vector", lambda e: e.reciprocal(out=vrow[:, :], in_=vrow[:, :]),
             reads=[("vrow",)], writes=[("vrow",)])
        A.op("vector", lambda e: e.tensor_tensor(out=mrow[:, :], in0=mrow[:, :],
             in1=vrow[:, :], op=OP.mult), reads=[("mrow",), ("vrow",)], writes=[("mrow",)])
        A.op("vector", lambda e: e.tensor_scalar(out=mrow[:, :], in0=mrow[:, :],
             scalar1=-1.0, scalar2=None, op0=OP.mult), reads=[("mrow",)], writes=[("mrow",)])
        A.op("vector", lambda e: e.tensor_copy(out=arow_bf[:, :], in_=vrow[:, :]),
             reads=[("vrow",)], writes=[("arb",)])
        A.op("vector", lambda e: e.tensor_copy(out=brow_bf[:, :], in_=mrow[:, :]),
             reads=[("mrow",)], writes=[("brb",)])
        A.op("tensor", lambda e: e.matmul(out=psum[2][:, 0:NC], lhsT=ones1b[0:1, :],
             rhs=arow_bf[:, :], start=True, stop=True), reads=[("arb",), ("ones",)], writes=[("ps", 2)])
        A.op("tensor", lambda e: e.matmul(out=psum[3][:, 0:NC], lhsT=ones1b[0:1, :],
             rhs=brow_bf[:, :], start=True, stop=True), reads=[("brb",), ("ones",)], writes=[("ps", 3)])
        for k in range(KD):
            A.op("vector", lambda e, k=k: e.tensor_tensor(out=rel[:], in0=x2f[:, k, :],
                 in1=psum[2][:, 0:NC], op=OP.mult),
                 reads=[("x2f",), ("ps", 2)], writes=[("rel",)])
            A.op("vector", lambda e, k=k: e.tensor_tensor(out=x2f[:, k, :], in0=rel[:],
                 in1=psum[3][:, 0:NC], op=OP.add),
                 reads=[("rel",), ("ps", 3)], writes=[("x2f",)])
        for half in range(2):
            A.dma("gpsimd", dst_hbm[:, half * 4:(half + 1) * 4, t0:t0 + NC],
                  x2f[:, half * 4:(half + 1) * 4, :], reads=[("x2f",)],
                  writes=[("dsth",)], stream=("dst_stream",))


def bfly_perm_B():
    permB = np.zeros(4096, dtype=np.int64)
    for tB in range(32):
        for pB in range(128):
            b, a = pB // 32, pB % 32
            f = a * 128 + tB * 4 + b
            permB[tB * 128 + pB] = f
    return permB


def bfly_coeffs(wstack):
    coefA = np.zeros((7, 32, 64), np.float32)
    for s in range(7):
        for t in range(32):
            coefA[s, t] = wstack[s, t * 64:(t + 1) * 64]
    coefB = np.zeros((5, 32, 64), np.float32)
    for si, s in enumerate(range(7, 12)):
        sig = 2 ** s
        sigB = 2 ** (s - 7)
        for tB in range(32):
            for kp in range(64):
                a_pB = (kp // sigB) * 2 * sigB + kp % sigB
                b, a = a_pB // 32, a_pB % 32
                i = a * 128 + tB * 4 + b
                k = (i // (2 * sig)) * sig + i % sig
                coefB[si, tB, kp] = wstack[s, k]
    return coefA, coefB


def emit_bfly_build(au, es, nc, *, coef_hbm, which, M_sb, ident_bf, psum, ones1b, tag):
    A = au
    strides = [2 ** s for s in range(7)] if which == "A" else [2 ** i for i in range(5)]
    ns = len(strides)
    work = es.enter_context(nc.sbuf_tensor(f"bw{tag}", [128, 128], F32))
    prod = [es.enter_context(nc.sbuf_tensor(f"bp{tag}{i}", [128, 64], F32)) for i in range(4)]
    cslab = [es.enter_context(nc.sbuf_tensor(f"bc{tag}{i}", [128, ns * 4 * 64], BF16))
             for i in range(2)]
    wtok = ("bwork", tag)
    for t in range(32):
        cs, ctok = cslab[t % 2], ("bcoef", tag, t % 2)
        A.dma("sync", cs[:, :], coef_hbm[t, :, :], writes=[ctok])
        A.op("vector", lambda e: e.tensor_copy(out=work[:], in_=ident_bf[:]),
             reads=[("ident",)], writes=[wtok])
        for si, sig in enumerate(strides):
            base = si * 4 * 64
            for pi, (side, ci) in enumerate(((0, 0), (1, 1), (0, 2), (1, 3))):
                A.op("vector", lambda e, pi=pi, side=side, ci=ci, sig=sig, base=base, cs=cs: e.tensor_tensor(
                    out=prod[pi].rearrange("p (b s) -> p b s", s=sig)[:, :, :],
                    in0=work.rearrange("p (b two s) -> p b two s", two=2, s=sig)[:, :, side, :],
                    in1=cs[:, base + ci * 64: base + (ci + 1) * 64].rearrange(
                        "p (b s) -> p b s", s=sig)[:, :, :],
                    op=OP.mult),
                    reads=[wtok, ctok], writes=[("bprod", tag, pi)])
            A.op("vector", lambda e, sig=sig: e.tensor_tensor(
                out=work.rearrange("p (b two s) -> p b two s", two=2, s=sig)[:, :, 0, :],
                in0=prod[0].rearrange("p (b s) -> p b s", s=sig)[:, :, :],
                in1=prod[1].rearrange("p (b s) -> p b s", s=sig)[:, :, :], op=OP.add),
                reads=[("bprod", tag, 0), ("bprod", tag, 1)], writes=[wtok])
            A.op("vector", lambda e, sig=sig: e.tensor_tensor(
                out=work.rearrange("p (b two s) -> p b two s", two=2, s=sig)[:, :, 1, :],
                in0=prod[2].rearrange("p (b s) -> p b s", s=sig)[:, :, :],
                in1=prod[3].rearrange("p (b s) -> p b s", s=sig)[:, :, :], op=OP.add),
                reads=[("bprod", tag, 2), ("bprod", tag, 3)], writes=[wtok])
        A.op("vector", lambda e, t=t: e.tensor_copy(out=M_sb[:, t, :], in_=work[:]),
             reads=[wtok], writes=[("M", tag)])


def emit_bfly_phase(au, es, nc, *, src_hbm, dst_hbm, wexp_hbm, bexp_sb, MAu, MBu,
                    MAd, MBd, ubias_sb, wcon_hbm, bcon_sb, shuf_hbm, psum,
                    ones128b, ones1b, eps1_sb, tag="d"):
    A = au
    x_bf = [es.enter_context(nc.sbuf_tensor(f"xb{tag}", [128, KD, NC], BF16))]
    h_sbs = [es.enter_context(nc.sbuf_tensor(f"h{tag}{i}", [128, 32, NC], BF16)) for i in range(2)]
    h2_sbs = [es.enter_context(nc.sbuf_tensor(f"h2{tag}{i}", [128, 32, NC], BF16)) for i in range(2)]
    x3f = es.enter_context(nc.sbuf_tensor(f"x3f{tag}", [128, KD, NC], F32))
    wring = [es.enter_context(nc.sbuf_tensor(f"w{tag}{i}", [128, 2048], BF16)) for i in range(3)]
    rbk = [es.enter_context(nc.sbuf_tensor(f"rb{tag}{i}", [128, NC], BF16)) for i in range(2)]
    mrow = es.enter_context(nc.sbuf_tensor(f"mr{tag}", [1, NC], F32))
    vrow = es.enter_context(nc.sbuf_tensor(f"vr{tag}", [1, NC], F32))
    brow = es.enter_context(nc.sbuf_tensor(f"br{tag}", [1, NC], F32))
    arow_bf = es.enter_context(nc.sbuf_tensor(f"ab{tag}", [1, NC], BF16))
    brow_bf = es.enter_context(nc.sbuf_tensor(f"bb{tag}", [1, NC], BF16))

    for c in range(NCH):
        t0 = c * NC
        cp = c % 2
        h_sb, h2_sb = h_sbs[cp], h2_sbs[cp]
        xb = x_bf[0]
        hhtok, h2tok = ("hh", cp), ("h2", cp)
        shuf_hbm_c = shuf_hbm[cp]
        shufA = shuf_hbm_c.rearrange("c b a t -> (c b) a t")
        shufB = shuf_hbm_c.rearrange("c b a t -> (b a) c t")
        A.dma("gpsimd", xb[:, :, :], src_hbm[:, :, t0:t0 + NC], writes=[("xb", 0)],
              stream=("xb", 0))
        def we_evac(m, ps, pstok, h_sb=h_sb, hhtok=hhtok):
            A.op("scalar", lambda e, m=m, ps=ps, h_sb=h_sb, cp=cp: e.activation(
                out=h_sb[:, m, :], in_=ps[:, 0:NC], func=AF.Identity,
                bias=bexp_sb[:, m:m + 1]),
                reads=[pstok, ("bexp",)], writes=[hhtok])
        emit_proj(A, w_hbm=wexp_hbm, rhs_fn=lambda k, xb=xb: (xb[:, k, :], ("xb", 0)),
                  kd=KD, mt=32, evac=we_evac, psum=psum, wring=wring, wrtok="wd" + tag,
                  bank_base=4 * cp, pingpong=False)
        for t in range(32):
            A.op("tensor", lambda e, t=t, h_sb=h_sb, cp=cp: e.matmul(out=psum[4 * cp + t % 4][:, 0:NC],
                 lhsT=MAu[:, t, :], rhs=h_sb[:, t, :], start=True, stop=True),
                 reads=[("M", "Au"), hhtok], writes=[("ps", 4 * cp + t % 4)])
            A.op("vector", lambda e, t=t, h2_sb=h2_sb, cp=cp: e.tensor_copy(out=h2_sb[:, t, :],
                 in_=psum[4 * cp + t % 4][:, 0:NC]), reads=[("ps", 4 * cp + t % 4)], writes=[h2tok])
        for q in range(4):
            A.dma("gpsimd", shufA[:, 8 * q:8 * q + 8, :], h2_sb[:, 8 * q:8 * q + 8, :],
                  reads=[h2tok], writes=[("shufh", cp)], stream=("shufh", cp))
        for q in range(4):
            A.dma("sync", h_sb[:, 8 * q:8 * q + 8, :], shufB[:, 8 * q:8 * q + 8, :],
                  reads=[("shufh", cp)], writes=[hhtok], stream=hhtok)
        for t in range(32):
            A.op("tensor", lambda e, t=t, h_sb=h_sb, cp=cp: e.matmul(out=psum[4 * cp + t % 4][:, 0:NC],
                 lhsT=MBu[:, t, :], rhs=h_sb[:, t, :], start=True, stop=True),
                 reads=[("M", "Bu"), hhtok], writes=[("ps", 4 * cp + t % 4)])
            A.op("scalar", lambda e, t=t, h2_sb=h2_sb, cp=cp: e.activation(out=h2_sb[:, t, :],
                 in_=psum[4 * cp + t % 4][:, 0:NC], func=AF.Gelu, bias=ubias_sb[:, t:t + 1]),
                 reads=[("ps", 4 * cp + t % 4), ("ubias",)], writes=[h2tok])
        for q in range(4):
            A.dma("gpsimd", shufB[:, 8 * q:8 * q + 8, :], h2_sb[:, 8 * q:8 * q + 8, :],
                  reads=[h2tok], writes=[("shufh", cp)], stream=("shufh", cp))
        for q in range(4):
            A.dma("sync", h_sb[:, 8 * q:8 * q + 8, :], shufA[:, 8 * q:8 * q + 8, :],
                  reads=[("shufh", cp)], writes=[hhtok], stream=hhtok)
        for t in range(32):
            A.op("tensor", lambda e, t=t, h_sb=h_sb, cp=cp: e.matmul(out=psum[4 * cp + t % 4][:, 0:NC],
                 lhsT=MAd[:, t, :], rhs=h_sb[:, t, :], start=True, stop=True),
                 reads=[("M", "Ad"), hhtok], writes=[("ps", 4 * cp + t % 4)])
            A.op("vector", lambda e, t=t, h2_sb=h2_sb, cp=cp: e.tensor_copy(out=h2_sb[:, t, :],
                 in_=psum[4 * cp + t % 4][:, 0:NC]), reads=[("ps", 4 * cp + t % 4)], writes=[h2tok])
        for q in range(4):
            A.dma("gpsimd", shufA[:, 8 * q:8 * q + 8, :], h2_sb[:, 8 * q:8 * q + 8, :],
                  reads=[h2tok], writes=[("shufh", cp)], stream=("shufh", cp))
        for q in range(4):
            A.dma("sync", h_sb[:, 8 * q:8 * q + 8, :], shufB[:, 8 * q:8 * q + 8, :],
                  reads=[("shufh", cp)], writes=[hhtok], stream=hhtok)
        for t in range(32):
            A.op("tensor", lambda e, t=t, h_sb=h_sb, cp=cp: e.matmul(out=psum[4 * cp + t % 4][:, 0:NC],
                 lhsT=MBd[:, t, :], rhs=h_sb[:, t, :], start=True, stop=True),
                 reads=[("M", "Bd"), hhtok], writes=[("ps", 4 * cp + t % 4)])
            A.op("vector", lambda e, t=t, h2_sb=h2_sb, cp=cp: e.tensor_copy(out=h2_sb[:, t, :],
                 in_=psum[4 * cp + t % 4][:, 0:NC]), reads=[("ps", 4 * cp + t % 4)], writes=[h2tok])
        for half in range(2):
            A.dma("sync", x3f[:, half * 4:(half + 1) * 4, :],
                  src_hbm[:, half * 4:(half + 1) * 4, t0:t0 + NC], writes=[("x3f",)])
        def wc_evac(m, ps, pstok):
            A.op("scalar", lambda e, m=m, ps=ps: e.activation(out=ps[:, 0:NC],
                 in_=ps[:, 0:NC], func=AF.Identity, bias=bcon_sb[:, m:m + 1]),
                 reads=[pstok, ("bcon",)], writes=[pstok])
            A.op("vector", lambda e, m=m, ps=ps, cp=cp: e.tensor_tensor(out=x3f[:, m, :],
                 in0=x3f[:, m, :], in1=ps[:, 0:NC], op=OP.add),
                 reads=[pstok, ("x3f",)], writes=[("x3f",)])
        emit_proj(A, w_hbm=wcon_hbm, rhs_fn=lambda k, h2_sb=h2_sb, h2tok=h2tok: (h2_sb[:, k, :], h2tok),
                  kd=32, mt=KD, evac=wc_evac, psum=psum, wring=wring, wrtok="wd" + tag,
                  bank_base=4 * cp, pingpong=False)
        for k in range(KD):
            A.op("vector", lambda e, k=k, o=rbk[k % 2], cp=cp: e.tensor_copy(
                out=o[:], in_=x3f[:, k, :]), reads=[("x3f",)], writes=[("rb", k % 2)])
            A.op("tensor", lambda e, k=k, i=rbk[k % 2], cp=cp: e.matmul(out=psum[4 * cp + 0][0:1, 0:NC],
                 lhsT=ones128b[:, 0:1], rhs=i[:], start=(k == 0), stop=(k == KD - 1)),
                 reads=[("rb", k % 2), ("ones",)], writes=[("ps", 4 * cp + 0)])
        for k in range(KD):
            A.op("scalar", lambda e, k=k, o=rbk[k % 2], cp=cp: e.activation(
                out=o[:], in_=x3f[:, k, :], func=AF.Square),
                reads=[("x3f",)], writes=[("rb", k % 2)])
            A.op("tensor", lambda e, k=k, i=rbk[k % 2], cp=cp: e.matmul(out=psum[4 * cp + 1][0:1, 0:NC],
                 lhsT=ones128b[:, 0:1], rhs=i[:], start=(k == 0), stop=(k == KD - 1)),
                 reads=[("rb", k % 2), ("ones",)], writes=[("ps", 4 * cp + 1)])
        A.op("vector", lambda e, cp=cp: e.tensor_scalar(out=mrow[:, :], in0=psum[4 * cp + 0][0:1, 0:NC],
             scalar1=1.0 / D, scalar2=None, op0=OP.mult), reads=[("ps", 4 * cp + 0)], writes=[("mrow",)])
        A.op("vector", lambda e, cp=cp: e.tensor_scalar(out=vrow[:, :], in0=psum[4 * cp + 1][0:1, 0:NC],
             scalar1=1.0 / D, scalar2=None, op0=OP.mult), reads=[("ps", 4 * cp + 1)], writes=[("vrow",)])
        A.op("vector", lambda e: e.tensor_tensor(out=brow[:, :], in0=mrow[:, :],
             in1=mrow[:, :], op=OP.mult), reads=[("mrow",)], writes=[("brow",)])
        A.op("vector", lambda e: e.tensor_tensor(out=vrow[:, :], in0=vrow[:, :],
             in1=brow[:, :], op=OP.subtract), reads=[("vrow",), ("brow",)], writes=[("vrow",)])
        A.op("scalar", lambda e: e.activation(out=vrow[:, :], in_=vrow[:, :],
             func=AF.Sqrt, bias=eps1_sb[0:1, 0:1]), reads=[("vrow",), ("const",)], writes=[("vrow",)])
        A.op("vector", lambda e: e.reciprocal(out=vrow[:, :], in_=vrow[:, :]),
             reads=[("vrow",)], writes=[("vrow",)])
        A.op("vector", lambda e: e.tensor_tensor(out=brow[:, :], in0=mrow[:, :],
             in1=vrow[:, :], op=OP.mult), reads=[("mrow",), ("vrow",)], writes=[("brow",)])
        A.op("vector", lambda e, cp=cp: e.tensor_scalar(out=brow[:, :], in0=brow[:, :],
             scalar1=-1.0, scalar2=None, op0=OP.mult), reads=[("brow",)], writes=[("brow",)])
        A.op("vector", lambda e, cp=cp: e.tensor_copy(out=arow_bf[:, :], in_=vrow[:, :]),
             reads=[("vrow",)], writes=[("arb",)])
        A.op("vector", lambda e, cp=cp: e.tensor_copy(out=brow_bf[:, :], in_=brow[:, :]),
             reads=[("brow",)], writes=[("brb",)])
        A.op("tensor", lambda e, cp=cp: e.matmul(out=psum[4 * cp + 2][:, 0:NC], lhsT=ones1b[0:1, :],
             rhs=arow_bf[:, :], start=True, stop=True), reads=[("arb",), ("ones",)], writes=[("ps", 4 * cp + 2)])
        A.op("tensor", lambda e, cp=cp: e.matmul(out=psum[4 * cp + 3][:, 0:NC], lhsT=ones1b[0:1, :],
             rhs=brow_bf[:, :], start=True, stop=True), reads=[("brb",), ("ones",)], writes=[("ps", 4 * cp + 3)])
        for k in range(KD):
            A.op("vector", lambda e, k=k, cp=cp: e.tensor_tensor(out=x3f[:, k, :], in0=x3f[:, k, :],
                 in1=psum[4 * cp + 2][:, 0:NC], op=OP.mult),
                 reads=[("x3f",), ("ps", 4 * cp + 2)], writes=[("x3f",)])
            A.op("vector", lambda e, k=k, cp=cp: e.tensor_tensor(out=x3f[:, k, :], in0=x3f[:, k, :],
                 in1=psum[4 * cp + 3][:, 0:NC], op=OP.add),
                 reads=[("x3f",), ("ps", 4 * cp + 3)], writes=[("x3f",)])
        for half in range(2):
            A.dma("gpsimd", dst_hbm[:, half * 4:(half + 1) * 4, t0:t0 + NC],
                  x3f[:, half * 4:(half + 1) * 4, :], reads=[("x3f",)],
                  writes=[("dsth",)], stream=("dst_stream",))


def emit_w2bf(au, es, nc, pairs, tag="w2", bufw=16384, defer=False):
    A = au
    bufs = [es.enter_context(nc.sbuf_tensor(f"{tag}{i}", [128, bufw], BF16)) for i in range(2)]
    units = []
    for src, dst in pairs:
        kk, _, F = src.shape
        step = max(1, bufw // F)
        for k0 in range(0, kk, step):
            units.append((src, dst, k0, min(step, kk - k0), F))

    state = {"i": 0}

    def feed(n=1):
        for _ in range(n):
            i = state["i"]
            if i >= len(units):
                return
            src, dst, k0, ks, F = units[i]
            b = bufs[i % 2]
            tok = (tag + "b", i % 2)
            A.dma("gpsimd", b[:, 0:ks * F].rearrange("p (k f) -> p k f", k=ks),
                  src[k0:k0 + ks, :, :].rearrange("k p f -> p k f"), writes=[tok])
            A.dma("sync", dst[k0:k0 + ks, :, :].rearrange("k p f -> p k f"),
                  b[:, 0:ks * F].rearrange("p (k f) -> p k f", k=ks),
                  reads=[tok], writes=[(tag + "st",)], stream=(tag + "st",))
            state["i"] += 1

    if defer:
        return feed
    feed(len(units))


def emit_proj(au, *, w_hbm, rhs_fn, kd, mt, evac, psum, wring, wrtok, kslab=4,
              group=4, eng="sync", bank_base=0, pingpong=True):
    A = au
    nbuf = len(wring)
    i = 0
    for g in range((mt + group - 1) // group):
        gm = min(group, mt - g * group)
        b0 = bank_base + ((g % 2) * group if pingpong else 0)
        for k0 in range(0, kd, kslab):
            ks = min(kslab, kd - k0)
            wb = wring[i % nbuf]
            tok = (wrtok, i % nbuf)
            A.dma(eng, wb[:, 0:ks * gm * 128].rearrange("p (k f) -> p k f", k=ks),
                  w_hbm[k0:k0 + ks, :, g * group * 128:g * group * 128 + gm * 128]
                  .rearrange("k p f -> p k f"), writes=[tok])
            for dk in range(ks):
                k = k0 + dk
                rhs, rtok = rhs_fn(k)
                for m8 in range(gm):
                    A.op("tensor", lambda e, wb=wb, dk=dk, m8=m8, k=k, rhs=rhs, gm=gm, ks=ks, b0=b0:
                         e.matmul(out=psum[b0 + m8][:, 0:NC],
                                  lhsT=wb[:, 0:ks * gm * 128].rearrange(
                                      "p (k f) -> p k f", k=ks)[:, dk, m8 * 128:(m8 + 1) * 128],
                                  rhs=rhs, start=(k == 0), stop=(k == kd - 1)),
                         reads=[tok, rtok], writes=[("ps", b0 + m8)])
            i += 1
        for m8 in range(gm):
            evac(g * group + m8, psum[b0 + m8], ("ps", b0 + m8))



from contextlib import ExitStack
import ml_dtypes
from concourse.bass_utils import run_bass_kernel_spmd

NCORES = 8
_BUILD_CACHE = {}


def _prep_shared(i):
    s = {}
    perm = rope_perm()
    wq = i["L0_attn_w_qkv"][0:D][perm]
    wk = i["L0_attn_w_qkv"][D:2 * D][perm]
    s["a_wqk"] = kpt(np.ascontiguousarray(np.concatenate([wq, wk], 0).T))
    s["a_bqk"] = pvec(np.concatenate([i["L0_attn_b_qkv"][0:D][perm],
                                      i["L0_attn_b_qkv"][D:2 * D][perm]]))
    s["a_wv"] = kpt(np.ascontiguousarray(i["L0_attn_w_qkv"][2 * D:3 * D].T))
    s["a_bv"] = i["L0_attn_b_qkv"][2 * D:3 * D].reshape(1, 1024).copy()
    s["a_wout"] = kpt(np.ascontiguousarray(i["L0_attn_w_out"].T))
    s["a_bout"] = pvec(i["L0_attn_b_out"])
    cos, sin = rope_tables()
    s["cosT"] = cos.astype(ml_dtypes.bfloat16)
    s["sinT"] = sin.astype(ml_dtypes.bfloat16)
    s["masks"] = causal_masks().astype(ml_dtypes.bfloat16)
    qw = i["L0_ffn_qkv_w"]
    s["f_wqk"] = kpt(np.ascontiguousarray(np.concatenate([qw[0:D], qw[D:2 * D]], 0).T))
    s["f_bqk"] = pvec(np.concatenate([i["L0_ffn_qkv_b"][0:D], i["L0_ffn_qkv_b"][D:2 * D]]))
    s["f_wv"] = kpt(np.ascontiguousarray(qw[2 * D:3 * D].T))
    s["f_bv"] = i["L0_ffn_qkv_b"][2 * D:3 * D].reshape(1, 1024).copy()
    s["f_wout"] = kpt(np.ascontiguousarray(i["L0_ffn_out_w"].T))
    s["f_bout"] = pvec(i["L0_ffn_out_b"])
    s["norm1"] = pvec(i["L0_ffn_norm1_w"])
    s["norm2"] = pvec(i["L0_ffn_norm2_w"])
    s["gate"] = kpt(np.ascontiguousarray(i["L0_ffn_gate_w"].T))
    s["w1"] = np.stack([kpt(np.ascontiguousarray(i["L0_ffn_w1"][e].T)) for e in range(8)])
    s["w2"] = np.stack([kpt(np.ascontiguousarray(i["L0_ffn_w2"][e].T)) for e in range(8)])
    s["w3"] = np.stack([kpt(np.ascontiguousarray(i["L0_ffn_w3"][e].T)) for e in range(8)])
    s["b1"] = np.stack([pvec(i["L0_ffn_b1"][e]) for e in range(8)], axis=1)
    s["b2"] = np.stack([pvec(i["L0_ffn_b2"][e]) for e in range(8)], axis=1)
    s["b3"] = i["L0_ffn_b3"].copy()
    esel = np.zeros((8, 1024), np.float32)
    for e in range(8):
        esel[e, e * 128:(e + 1) * 128] = 1.0
    s["esel"] = esel.astype(ml_dtypes.bfloat16)
    s["ident"] = np.eye(128, dtype=np.float32).astype(ml_dtypes.bfloat16)
    s["m_win"] = kpt(np.ascontiguousarray(i["L1_attn_W_in"]))
    s["m_cw"] = np.ascontiguousarray(i["L1_attn_conv_w"].reshape(KDI, 128, 4).transpose(1, 0, 2))
    s["m_cb"] = pvec(i["L1_attn_conv_b"])
    s["m_wx"] = kpt(np.ascontiguousarray(i["L1_attn_W_x"]))
    s["m_wdt"] = i["L1_attn_W_dt"].copy()
    s["m_bdt"] = pvec(i["L1_attn_b_dt"])
    s["m_alog"] = np.ascontiguousarray(i["L1_attn_A_log"].reshape(KDI, 128, DST).transpose(1, 0, 2))
    s["m_d"] = pvec(i["L1_attn_D"])
    s["m_wout"] = kpt(np.ascontiguousarray(i["L1_attn_W_out"]))
    esel32 = np.zeros((32, 32 * 128), np.float32)
    for n in range(16):
        esel32[n, n * 128:(n + 1) * 128] = 1.0
        esel32[16 + n, (16 + n) * 128:(17 + n) * 128] = 1.0
    s["esel32"] = esel32.astype(ml_dtypes.bfloat16)
    permB = bfly_perm_B()

    def coefpack(wstack, which):
        outs = []
        for ci in range(4):
            a_, b_ = divmod(ci, 2)
            cA, cB = bfly_coeffs(wstack[:, :, a_, b_])
            outs.append(cA if which == "A" else cB)
        arr = np.stack(outs, axis=2)
        arr = arr.transpose(1, 0, 2, 3).reshape(32, 1, -1)
        return np.ascontiguousarray(np.broadcast_to(arr, (32, 128, arr.shape[2]))).astype(ml_dtypes.bfloat16)
    s["d_wexp"] = kpt(np.ascontiguousarray(i["L1_ffn_W_exp"]))
    s["d_bexp"] = pvec(i["L1_ffn_b_exp"])
    s["coefAu"] = coefpack(i["L1_ffn_up_weights"], "A")
    s["coefBu"] = coefpack(i["L1_ffn_up_weights"], "B")
    s["coefAd"] = coefpack(i["L1_ffn_down_weights"], "A")
    s["coefBd"] = coefpack(i["L1_ffn_down_weights"], "B")
    s["d_ubias"] = np.ascontiguousarray(i["L1_ffn_up_bias"][permB].reshape(32, 128).T)
    s["d_wcon"] = kpt(np.ascontiguousarray(i["L1_ffn_W_con"][permB, :]))
    s["d_bcon"] = pvec(i["L1_ffn_b_con"])
    s["d_dbias"] = pvec(i["L1_ffn_down_bias"])
    return s


_BIGW = ("a_wqk", "a_wv", "a_wout", "f_wqk", "f_wv", "f_wout", "w1", "w2", "w3",
         "m_win", "m_wout", "d_wexp", "d_wcon")


_EAGER = ("a_wqk", "a_wv", "a_wout")


def _declare(nc, shared, au, es, defer_rest=False):
    ap = {}
    pairs = []
    dpairs = []
    for k, v in shared.items():
        dt = BF16 if v.dtype == ml_dtypes.bfloat16 else F32
        src_ap = nc.dram_tensor(k, list(v.shape), dt, kind="ExternalInput").ap()
        if k in _BIGW:
            mir = nc.dram_tensor(k + "_bf", list(v.shape), BF16, kind="Internal").ap()
            if len(v.shape) == 4:
                pr = (src_ap.rearrange("e k p f -> (e k) p f"),
                      mir.rearrange("e k p f -> (e k) p f"))
            else:
                pr = (src_ap, mir)
            (pairs if (k in _EAGER or not defer_rest) else dpairs).append(pr)
            ap[k] = mir
        else:
            ap[k] = src_ap
    emit_w2bf(au, es, nc, pairs)
    au.barrier()
    return ap, dpairs


def _common_prolog(au, es, nc, w):
    g = {}
    g["psum"] = [es.enter_context(nc.psum_tensor(f"pb{i}", [128, 512], F32)) for i in range(8)]
    g["ones128b"] = es.enter_context(nc.sbuf_tensor("ones128b", [128, 1], BF16))
    g["ones1b"] = es.enter_context(nc.sbuf_tensor("ones1b", [1, 128], BF16))
    g["ones128f32"] = es.enter_context(nc.sbuf_tensor("ones128f32", [128, 1], F32))
    g["shift_sb"] = es.enter_context(nc.sbuf_tensor("shift_sb", [128, 1], F32))
    g["eps1_sb"] = es.enter_context(nc.sbuf_tensor("eps1_sb", [1, 2], F32))
    au.op("vector", lambda e: e.memset(g["ones128b"][:], 1.0), writes=[("ones",)])
    au.op("vector", lambda e: e.memset(g["ones1b"][:], 1.0), writes=[("ones",)])
    au.op("vector", lambda e: e.memset(g["ones128f32"][:], 1.0), writes=[("onesf",)])
    au.op("vector", lambda e: e.memset(g["shift_sb"][:], -SHIFT), writes=[("const",)])
    au.op("vector", lambda e: e.memset(g["eps1_sb"][0:1, 0:1], LN_EPS), writes=[("const",)])
    au.op("vector", lambda e: e.memset(g["eps1_sb"][0:1, 1:2], RMS_EPS), writes=[("const",)])
    return g


def _emit_block0_attn(au, es, nc, w, g, src, dst, q_hbm, attn_hbm, feed=None):
    bqk_sb = es.enter_context(nc.sbuf_tensor("a_bqk_sb", [128, 16], F32))
    bv_sb = es.enter_context(nc.sbuf_tensor("a_bv_sb", [1, 1024], BF16))
    bout_sb = es.enter_context(nc.sbuf_tensor("a_bout_sb", [128, 8], F32))
    masks_sb = es.enter_context(nc.sbuf_tensor("a_masks_sb", [128, 4, NC], BF16))
    au.dma("sync", bqk_sb[:], w["a_bqk"][:, :], writes=[("bqk",)])
    au.dma("gpsimd", bv_sb[:], w["a_bv"][:, :], writes=[("bv",)])
    au.dma("sync", bout_sb[:], w["a_bout"][:, :], writes=[("bout",)])
    au.dma("sync", masks_sb[:], w["masks"][:, :, :], writes=[("msk",)])
    with ExitStack() as es1:
        K_sb = es1.enter_context(nc.sbuf_tensor("a_K", [128, KD, T], BF16))
        V65 = es1.enter_context(nc.sbuf_tensor("a_V65", [128, T // 128, NH, 65], BF16))
        emit_attn_phase(au, es1, nc, src_hbm=src, rms_gain_sb=None, wqk_hbm=w["a_wqk"],
                        bqk_sb=bqk_sb, wv_hbm=w["a_wv"], bv_sb=bv_sb, causal=True,
                        cos_hbm=w["cosT"], sin_hbm=w["sinT"], masks_sb=masks_sb,
                        q_hbm=q_hbm, attn_hbm=attn_hbm, K_sb=K_sb, V65=V65,
                        psum=g["psum"], ones128b=g["ones128b"], ones1b=g["ones1b"],
                        shift_sb=g["shift_sb"], eps1_sb=g["eps1_sb"], tag="a",
                        feed=feed)
        au.barrier()
    with ExitStack() as es2:
        emit_outproj_ln(au, es2, nc, attn_hbm=attn_hbm, src_hbm=src, dst_hbm=dst,
                        w_hbm=w["a_wout"], b_sb=bout_sb, psum=g["psum"],
                        ones128b=g["ones128b"], ones1b=g["ones1b"],
                        eps1_sb=g["eps1_sb"], ln=True, tag="ao")
        au.barrier()


def _emit_block0_ffn(au, es, nc, w, g, src, dst, q_hbm, attn_hbm, z_hbm):
    bqk_sb = es.enter_context(nc.sbuf_tensor("f_bqk_sb", [128, 16], F32))
    bv_sb = es.enter_context(nc.sbuf_tensor("f_bv_sb", [1, 1024], BF16))
    bout_sb = es.enter_context(nc.sbuf_tensor("f_bout_sb", [128, 8], F32))
    n1_sb = es.enter_context(nc.sbuf_tensor("n1_sb", [128, KD], F32))
    n2_sb = es.enter_context(nc.sbuf_tensor("n2_sb", [128, KD], F32))
    b1_sb = es.enter_context(nc.sbuf_tensor("b1_sb", [128, 8, 16], F32))
    b2_sb = es.enter_context(nc.sbuf_tensor("b2_sb", [128, 8, 16], F32))
    b3_sb = es.enter_context(nc.sbuf_tensor("b3_sb", [8, 1024], BF16))
    esel_sb = es.enter_context(nc.sbuf_tensor("esel_sb", [8, 1024], BF16))
    ident_sb = es.enter_context(nc.sbuf_tensor("ident_sb", [128, 128], BF16))
    au.dma("sync", bqk_sb[:], w["f_bqk"][:, :], writes=[("bqk",)])
    au.dma("gpsimd", bv_sb[:], w["f_bv"][:, :], writes=[("bv",)])
    au.dma("sync", bout_sb[:], w["f_bout"][:, :], writes=[("bout",)])
    au.dma("sync", n1_sb[:], w["norm1"][:, :], writes=[("gain",)])
    au.dma("sync", n2_sb[:], w["norm2"][:, :], writes=[("gain2",)])
    au.dma("sync", b1_sb[:], w["b1"][:, :, :], writes=[("b1",)])
    au.dma("sync", b2_sb[:], w["b2"][:, :, :], writes=[("b2",)])
    au.dma("gpsimd", b3_sb[:], w["b3"][:, :], writes=[("b3",)])
    au.dma("sync", esel_sb[:], w["esel"][:, :], writes=[("esel",)])
    au.dma("sync", ident_sb[:], w["ident"][:, :], writes=[("ident",)])
    with ExitStack() as es1:
        K_sb = es1.enter_context(nc.sbuf_tensor("f_K", [128, KD, T], BF16))
        V65 = es1.enter_context(nc.sbuf_tensor("f_V65", [128, T // 128, NH, 65], BF16))
        emit_attn_phase(au, es1, nc, src_hbm=src, rms_gain_sb=n1_sb, wqk_hbm=w["f_wqk"],
                        bqk_sb=bqk_sb, wv_hbm=w["f_wv"], bv_sb=bv_sb, causal=False,
                        cos_hbm=w["cosT"], sin_hbm=w["sinT"], masks_sb=None,
                        q_hbm=q_hbm, attn_hbm=attn_hbm, K_sb=K_sb, V65=V65,
                        psum=g["psum"], ones128b=g["ones128b"], ones1b=g["ones1b"],
                        shift_sb=g["shift_sb"], eps1_sb=g["eps1_sb"], tag="f")
        au.barrier()
    with ExitStack() as es2:
        emit_outproj_ln(au, es2, nc, attn_hbm=attn_hbm, src_hbm=src, dst_hbm=z_hbm,
                        w_hbm=w["f_wout"], b_sb=bout_sb, psum=g["psum"],
                        ones128b=g["ones128b"], ones1b=g["ones1b"],
                        eps1_sb=g["eps1_sb"], ln=False, tag="fo")
        au.barrier()
    emit_moe_phase(au, es, nc, z_hbm=z_hbm, x1_hbm=src, dst_hbm=dst,
                   rms_gain_sb=n2_sb, gate_hbm=w["gate"], w1_hbm=w["w1"],
                   w2_hbm=w["w2"], w3_hbm=w["w3"], b1_sb=b1_sb, b2_sb=b2_sb,
                   b3_sb=b3_sb, psum=g["psum"], ones128b=g["ones128b"],
                   ones1b=g["ones1b"], esel_sb=esel_sb, ident_bf=ident_sb,
                   eps1_sb=g["eps1_sb"], tag="m")
    au.barrier()


def _emit_block1_attn(au, es, nc, w, g, src, dst):
    cw_sb = es.enter_context(nc.sbuf_tensor("cw_sb", [128, KDI, 4], F32))
    cb_sb = es.enter_context(nc.sbuf_tensor("cb_sb", [128, KDI], F32))
    wx_sb = es.enter_context(nc.sbuf_tensor("wx_sb", [128, KDI, 96], BF16))
    wdt_sb = es.enter_context(nc.sbuf_tensor("wdt_sb", [64, 2048], BF16))
    bdt_sb = es.enter_context(nc.sbuf_tensor("bdt_sb", [128, KDI], F32))
    A_sb = es.enter_context(nc.sbuf_tensor("A_sb", [128, KDI, DST], F32))
    D_sb = es.enter_context(nc.sbuf_tensor("D_sb", [128, KDI], F32))
    esel_sb = es.enter_context(nc.sbuf_tensor("esel32_sb", [32, 32 * 128], BF16))
    au.dma("sync", cw_sb[:], w["m_cw"][:, :, :], writes=[("cw",)])
    au.dma("sync", cb_sb[:], w["m_cb"][:, :], writes=[("cb",)])
    for k in range(KDI):
        au.dma("gpsimd", wx_sb[:, k, :], w["m_wx"][k, :, :], writes=[("wx",)], stream=("wx",))
    au.dma("gpsimd", wdt_sb[:, :], w["m_wdt"][:, :], writes=[("wdt",)])
    au.dma("sync", bdt_sb[:], w["m_bdt"][:, :], writes=[("bdt",)])
    au.dma("sync", A_sb[:], w["m_alog"][:, :, :], writes=[("A",)])
    au.dma("sync", D_sb[:], w["m_d"][:, :], writes=[("D",)])
    au.dma("sync", esel_sb[:], w["esel32"][:, :], writes=[("esel32",)])
    au.op("scalar", lambda e: e.activation(out=A_sb[:, :, :], in_=A_sb[:, :, :],
         func=AF.Exp, scale=1.0), reads=[("A",)], writes=[("A",)])
    au.op("vector", lambda e: e.tensor_scalar(out=A_sb[:, :, :], in0=A_sb[:, :, :],
         scalar1=-1.0, scalar2=None, op0=OP.mult), reads=[("A",)], writes=[("A",)])
    emit_mamba_phase(au, es, nc, src_hbm=src, dst_hbm=dst, win_hbm=w["m_win"],
                     cw_sb=cw_sb, cb_sb=cb_sb, wx_sb=wx_sb, wdt_sb=wdt_sb,
                     bdt_sb=bdt_sb, A_sb=A_sb, D_sb=D_sb, wout_hbm=w["m_wout"],
                     esel32_sb=esel_sb, psum=g["psum"], ones128b=g["ones128b"],
                     ones1b=g["ones1b"], ones128f32=g["ones128f32"],
                     eps1_sb=g["eps1_sb"], tag="c")
    au.barrier()


def _emit_block1_ffn(au, es, nc, w, g, src, dst, shuf_hbm):
    ident_sb = es.enter_context(nc.sbuf_tensor("identd_sb", [128, 128], BF16))
    bexp_sb = es.enter_context(nc.sbuf_tensor("bexp_sb", [128, 32], F32))
    ubias_sb = es.enter_context(nc.sbuf_tensor("ubias_sb", [128, 32], F32))
    bcon_sb = es.enter_context(nc.sbuf_tensor("bcon_sb", [128, 8], F32))
    dbias_sb = es.enter_context(nc.sbuf_tensor("dbias_sb", [128, 8], F32))
    MAu = es.enter_context(nc.sbuf_tensor("MAu", [128, 32, 128], BF16))
    MBu = es.enter_context(nc.sbuf_tensor("MBu", [128, 32, 128], BF16))
    MAd = es.enter_context(nc.sbuf_tensor("MAd", [128, 32, 128], BF16))
    MBd = es.enter_context(nc.sbuf_tensor("MBd", [128, 32, 128], BF16))
    au.dma("sync", ident_sb[:], w["ident"][:, :], writes=[("ident",)])
    au.dma("sync", bexp_sb[:], w["d_bexp"][:, :], writes=[("bexp",)])
    au.dma("sync", ubias_sb[:], w["d_ubias"][:, :], writes=[("ubias",)])
    au.dma("sync", bcon_sb[:], w["d_bcon"][:, :], writes=[("bcon",)])
    au.dma("sync", dbias_sb[:], w["d_dbias"][:, :], writes=[("dbias",)])
    au.op("vector", lambda e: e.tensor_tensor(out=bcon_sb[:], in0=bcon_sb[:],
         in1=dbias_sb[:], op=OP.add), reads=[("bcon",), ("dbias",)], writes=[("bcon",)])
    with ExitStack() as esb:
        for nm, dr, M in (("Au", w["coefAu"], MAu), ("Bu", w["coefBu"], MBu),
                          ("Ad", w["coefAd"], MAd), ("Bd", w["coefBd"], MBd)):
            emit_bfly_build(au, esb, nc, coef_hbm=dr, which=nm[0], M_sb=M,
                            ident_bf=ident_sb, psum=g["psum"], ones1b=g["ones1b"], tag=nm)
        au.barrier()
    with ExitStack() as es1:
        emit_bfly_phase(au, es1, nc, src_hbm=src, dst_hbm=dst, wexp_hbm=w["d_wexp"],
                        bexp_sb=bexp_sb, MAu=MAu, MBu=MBu, MAd=MAd, MBd=MBd,
                        ubias_sb=ubias_sb, wcon_hbm=w["d_wcon"], bcon_sb=bcon_sb,
                        shuf_hbm=shuf_hbm, psum=g["psum"], ones128b=g["ones128b"],
                        ones1b=g["ones1b"], eps1_sb=g["eps1_sb"], tag="d")
        au.barrier()


_LAUNCH_W = {
    "b0a": ("a_wqk", "a_wv", "a_wout", "cosT", "sinT", "masks", "a_bqk", "a_bv", "a_bout"),
    "b0f": ("f_wqk", "f_wv", "f_wout", "w1", "w2", "w3", "cosT", "sinT", "norm1", "norm2",
             "gate", "b1", "b2", "b3", "esel", "ident", "f_bqk", "f_bv", "f_bout"),
    "b1a": ("m_win", "m_cw", "m_cb", "m_wx", "m_wdt", "m_bdt", "m_alog", "m_d", "m_wout", "esel32"),
    "b1f": ("d_wexp", "d_bexp", "coefAu", "coefBu", "coefAd", "coefBd", "d_ubias",
             "d_wcon", "d_bcon", "d_dbias", "ident"),
}


def _build_launch(which, shared):
    key = which
    shared = {k: v for k, v in shared.items() if k in _LAUNCH_W[which]}
    if key in _BUILD_CACHE:
        return _BUILD_CACHE[key]
    nc = bass.Bass("TRN2", target_bir_lowering=False)
    xin = nc.dram_tensor("xin", [KD, 128, T], F32, kind="ExternalInput").ap()
    xout = nc.dram_tensor("xout", [KD, 128, T], F32, kind="ExternalOutput").ap()
    src = xin.rearrange("k p t -> p k t")
    dst = xout.rearrange("k p t -> p k t")
    with ExitStack() as es:
        au = Auto(nc, es)
        with ExitStack() as esw:
            w, _ = _declare(nc, shared, au, esw)
        g = _common_prolog(au, es, nc, w)
        if which == "b0a":
            q_hbm = nc.dram_tensor("q_hbm", [128, KD, T], BF16, kind="Internal").ap()
            attn_hbm = nc.dram_tensor("attn_hbm", [128, KD, T], BF16, kind="Internal").ap()
            _emit_block0_attn(au, es, nc, w, g, src, dst, q_hbm, attn_hbm)
        elif which == "b0f":
            q_hbm = nc.dram_tensor("q_hbm", [128, KD, T], BF16, kind="Internal").ap()
            attn_hbm = nc.dram_tensor("attn_hbm", [128, KD, T], BF16, kind="Internal").ap()
            z_hbm = nc.dram_tensor("z_hbm", [128, KD, T], F32, kind="Internal").ap()
            _emit_block0_ffn(au, es, nc, w, g, src, dst, q_hbm, attn_hbm, z_hbm[:, :, :])
        elif which == "b1a":
            _emit_block1_attn(au, es, nc, w, g, src, dst)
        elif which == "b1f":
            shuf = nc.dram_tensor("shuf", [2, 32, 4, 32, NC], BF16, kind="Internal").ap()
            _emit_block1_ffn(au, es, nc, w, g, src, dst, shuf)
        with nc.Block() as block:
            au.finalize(block)
    _BUILD_CACHE[key] = nc
    return nc


def _build_fused(shared):
    key = "fused"
    if key in _BUILD_CACHE:
        return _BUILD_CACHE[key]
    nc = bass.Bass("TRN2", target_bir_lowering=False)
    xin = nc.dram_tensor("xin", [KD, 128, T], F32, kind="ExternalInput").ap()
    xout = nc.dram_tensor("xout", [KD, 128, T], F32, kind="ExternalOutput").ap()
    x1h = nc.dram_tensor("x1h", [128, KD, T], F32, kind="Internal").ap()
    x2h = nc.dram_tensor("x2h", [128, KD, T], F32, kind="Internal").ap()
    x3h = nc.dram_tensor("x3h", [128, KD, T], F32, kind="Internal").ap()
    q_hbm = nc.dram_tensor("q_hbm", [128, KD, T], BF16, kind="Internal").ap()
    attn_hbm = nc.dram_tensor("attn_hbm", [128, KD, T], BF16, kind="Internal").ap()
    z_hbm = nc.dram_tensor("z_hbm", [128, KD, T], F32, kind="Internal").ap()
    shuf = nc.dram_tensor("shuf", [2, 32, 4, 32, NC], BF16, kind="Internal").ap()
    src = xin.rearrange("k p t -> p k t")
    dst = xout.rearrange("k p t -> p k t")
    with ExitStack() as es:
        au = Auto(nc, es)
        with ExitStack() as esw:
            w, dpairs = _declare(nc, shared, au, esw, defer_rest=True)
        g = _common_prolog(au, es, nc, w)
        with ExitStack() as p1:
            feed = emit_w2bf(au, p1, nc, dpairs, tag="cv", bufw=4096, defer=True)
            _emit_block0_attn(au, p1, nc, w, g, src, x1h[:, :, :], q_hbm, attn_hbm,
                              feed=feed)
            feed(10000)
            au.barrier()
        with ExitStack() as p2:
            _emit_block0_ffn(au, p2, nc, w, g, x1h[:, :, :], x2h[:, :, :],
                             q_hbm, attn_hbm, z_hbm[:, :, :])
        with ExitStack() as p3:
            _emit_block1_attn(au, p3, nc, w, g, x2h[:, :, :], x3h[:, :, :])
        with ExitStack() as p4:
            _emit_block1_ffn(au, p4, nc, w, g, x3h[:, :, :], dst, shuf)
            au.barrier()
        with nc.Block() as block:
            au.finalize(block)
    _BUILD_CACHE[key] = nc
    return nc


FUSED = True


def kernel(**inputs):
    x = inputs["L0_attn_x"]
    shared = _prep_shared(inputs)
    xs = [np.ascontiguousarray(x[c].T).reshape(KD, 128, T) for c in range(NCORES)]
    if FUSED:
        nc = _build_fused(shared)
        in_maps = [dict(shared, xin=xs[c]) for c in range(NCORES)]
        res = run_bass_kernel_spmd(nc, in_maps, core_ids=list(range(NCORES)))
        outs = [r["xout"] for r in res.results]
    else:
        cur = xs
        for which in ("b0a", "b0f", "b1a", "b1f"):
            nc = _build_launch(which, shared)
            sh = {k: v for k, v in shared.items() if k in _LAUNCH_W[which]}
            in_maps = [dict(sh, xin=cur[c]) for c in range(NCORES)]
            res = run_bass_kernel_spmd(nc, in_maps, core_ids=list(range(NCORES)))
            cur = [r["xout"] for r in res.results]
        outs = cur
    full = np.stack([o.reshape(D, T).T for o in outs])
    return np.ascontiguousarray(full.astype(np.float32))


# version 22
# speedup vs baseline: 1.0665x; 1.0114x over previous
import os
from contextlib import ExitStack as ExitStack2
import numpy as np
import concourse.bass as bass
import concourse.mybir as mybir
from concourse.alu_op_type import AluOpType as OP

F32 = mybir.dt.float32
BF16 = mybir.dt.bfloat16
AF = mybir.ActivationFunctionType
AX = mybir.AxisListType

T = 4096
D = 1024
KD = D // 128
NC = 512
NCH = T // NC
NH = 16
DH = 64
SHIFT = 12.0
LN_EPS = 1e-5
RMS_EPS = 1e-6

ENGS = ("sync", "tensor", "vector", "scalar", "gpsimd")


class _Op:
    __slots__ = ("eng", "fn", "reads", "writes", "is_async", "weight", "idx",
                 "deps", "marked", "cum", "vc", "marked_fence", "stream")

    def __init__(self, eng, fn, reads, writes, is_async, weight, idx):
        self.eng = eng
        self.fn = fn
        self.reads = reads
        self.writes = writes
        self.is_async = is_async
        self.weight = weight
        self.idx = idx
        self.deps = []
        self.marked = False
        self.cum = 0
        self.vc = None
        self.marked_fence = False
        self.stream = None


class Auto:

    def __init__(self, nc, es):
        self.nc = nc
        self.es = es
        self.ops = []
        self.tok = {}
        self.last_on = {e: None for e in ENGS}

    def op(self, eng, fn, reads=(), writes=(), is_async=False, weight=1, _fence=False):
        o = _Op(eng, fn, tuple(reads), tuple(writes), is_async, weight, len(self.ops))
        deps = set()
        for t in o.reads:
            st = self.tok.get(t)
            if st and st["w"] is not None:
                deps.add(st["w"])
        for t in o.writes:
            st = self.tok.get(t)
            if st:
                if st["w"] is not None:
                    deps.add(st["w"])
                for r in st["r"]:
                    deps.add(r)
        for d in deps:
            if d.eng != eng or d.is_async or is_async or eng != "tensor":
                o.deps.append(d)
        for t in o.reads:
            self.tok.setdefault(t, {"w": None, "r": []})["r"].append(o)
        for t in o.writes:
            self.tok[t] = {"w": o, "r": []}
        self.ops.append(o)
        if not _fence:
            self.last_on[eng] = o
        return o

    def dma(self, eng, out, in_, reads=(), writes=(), stream=None, **kw):
        fn = lambda e, o=out, i=in_, kw=kw: e.dma_start(out=o, in_=i, **kw)
        o = self.op(eng, fn, reads=reads, writes=writes, is_async=True, weight=16)
        o.stream = stream if stream is not None else (o.writes[0] if o.writes else ("anon",))
        return o

    def barrier(self):
        prior = [o for o in (self.last_on[e] for e in ENGS) if o is not None]
        outstanding = [o for o in self.ops if o.is_async and not o.marked_fence]
        for e in ENGS:
            o = self.op(e, lambda eng: eng.nop(), _fence=True)
            o.deps = [d for d in set(prior + outstanding) if d is not o]
        for o in outstanding:
            o.marked_fence = True

    def finalize(self, block):
        def chan(o):
            if o.is_async:
                return ("d", o.eng, o.stream)
            return ("e", o.eng)

        chans = []
        seen = set()
        for o in self.ops:
            c = chan(o)
            if c not in seen:
                seen.add(c)
                chans.append(c)
        for e in ENGS:
            if ("e", e) not in seen:
                seen.add(("e", e))
                chans.append(("e", e))
        sems = {c: self.es.enter_context(self.nc.semaphore(f"pg{i}"))
                for i, c in enumerate(chans)}
        for o in self.ops:
            for d in o.deps:
                d.marked = True
        cum = {c: 0 for c in chans}
        for o in self.ops:
            c = chan(o)
            if o.marked:
                cum[c] += o.weight
            o.cum = cum[c]
        eng_vc = {e: {c: 0 for c in chans} for e in ENGS}
        op_vc = {}
        emit = {e: [] for e in ENGS}
        for o in self.ops:
            need = {}
            for d in o.deps:
                dc = chan(d)
                if d.cum > need.get(dc, 0):
                    need[dc] = d.cum
                dvc = op_vc.get(d.idx)
                if dvc:
                    for c2, v2 in dvc.items():
                        if v2 > need.get(c2, 0):
                            need[c2] = v2
            mycl = eng_vc[o.eng]
            own = ("e", o.eng)
            for c2, v in sorted(need.items(), key=str):
                if c2 == own and o.eng == "tensor" and not o.is_async:
                    continue
                if v > mycl[c2]:
                    emit[o.eng].append(("wait", c2, v))
                    mycl[c2] = v
            emit[o.eng].append(("op", o))
            oc = chan(o)
            if not o.is_async and o.eng == "tensor":
                mycl[oc] = o.cum
            vc = dict(mycl)
            if o.is_async:
                vc[oc] = o.cum
            op_vc[o.idx] = vc

        def run(e, eng_name, seq=emit, sems=sems):
            for item in seq[eng_name]:
                if item[0] == "wait":
                    e.wait_ge(sems[item[1]], item[2])
                else:
                    o = item[1]
                    inst = o.fn(e)
                    if o.marked:
                        s = sems[chan(o)]
                        if inst is None:
                            e.sem_inc(s, o.weight)
                        else:
                            inst.then_inc(s, o.weight)
        for eng_name in ENGS:
            getattr(block, eng_name)(lambda e, en=eng_name: run(e, en))
        self.n_chans = len(chans)
        return emit

    def stats(self):
        n_wait = 0
        return {"ops": len(self.ops), "marked": sum(o.marked for o in self.ops)}



def kpt(w_t):
    fin, fout = w_t.shape
    return np.ascontiguousarray(w_t.reshape(fin // 128, 128, fout))


def pvec(v):
    n = v.shape[0]
    return np.ascontiguousarray(v.reshape(n // 128, 128).T)


def rope_perm():
    perm = np.zeros(D, dtype=np.int64)
    for h in range(NH):
        for k in range(32):
            perm[h * 64 + k] = h * 64 + 2 * k
            perm[h * 64 + 32 + k] = h * 64 + 2 * k + 1
    return perm


def rope_tables():
    inv_freq = 1.0 / 10000.0 ** (np.arange(0, DH, 2, dtype=np.float64) / DH)
    ang = np.outer(inv_freq, np.arange(T, dtype=np.float64))
    cos32, sin32 = np.cos(ang), np.sin(ang)
    cos128 = np.tile(cos32, (4, 1)).astype(np.float32)
    sin128 = np.tile(sin32, (4, 1)).astype(np.float32)
    sgn = np.ones((128, 1), np.float32)
    for b in (0, 2):
        sgn[b * 32:(b + 1) * 32] = -1.0
    return cos128, sin128 * sgn


def causal_masks():
    p = np.arange(128)[:, None]
    f = np.arange(NC)[None, :]
    ms = [(p + 128 * dd <= f).astype(np.float32) for dd in range(4)]
    return np.ascontiguousarray(np.stack(ms, axis=1))



def emit_attn_phase(au, es, nc, *, src_hbm, rms_gain_sb, wqk_hbm, bqk_sb, wv_hbm,
                    bv_sb, causal, cos_hbm, sin_hbm, masks_sb, q_hbm, attn_hbm,
                    K_sb, V65, psum, ones128b, ones1b, shift_sb, eps1_sb, tag="", a2=True,
                    feed=None):
    A = au
    q_chunk = es.enter_context(nc.sbuf_tensor(f"qch{tag}", [128, KD, NC], BF16))
    es1 = es.enter_context(ExitStack2())
    nxb = 1 if rms_gain_sb is not None else 2
    x_bf = [es1.enter_context(nc.sbuf_tensor(f"xbf{tag}{i}", [128, KD, NC], BF16)) for i in range(nxb)]
    nwr = 2 if rms_gain_sb is not None else 3
    wring = [es1.enter_context(nc.sbuf_tensor(f"wr{tag}{i}", [128, 2048], BF16)) for i in range(nwr)]
    vring = [es1.enter_context(nc.sbuf_tensor(f"vr{tag}{i}", [128, 1024], BF16)) for i in range(2)]
    qk_ev = [es1.enter_context(nc.sbuf_tensor(f"qk_ev{tag}{i}", [128, NC], BF16)) for i in range(2)]
    partner = [es1.enter_context(nc.sbuf_tensor(f"prt{tag}{i}", [128, NC], BF16)) for i in range(2)]
    cos_sb = es1.enter_context(nc.sbuf_tensor(f"cos{tag}", [128, NC], BF16))
    sin_sb = es1.enter_context(nc.sbuf_tensor(f"sin{tag}", [128, NC], BF16))
    bv_rep = es1.enter_context(nc.sbuf_tensor(f"bvrep{tag}", [128, 1024], BF16))
    if rms_gain_sb is not None:
        x_f32 = es1.enter_context(nc.sbuf_tensor(f"xf32{tag}", [128, KD, NC], F32))
        x2b = [es1.enter_context(nc.sbuf_tensor(f"x2b{tag}{i}", [128, NC], BF16)) for i in range(2)]
        srow = es1.enter_context(nc.sbuf_tensor(f"srow{tag}", [1, NC], F32))
        srow_bf = es1.enter_context(nc.sbuf_tensor(f"srowb{tag}", [1, NC], BF16))

    A.op("vector", lambda e: e.memset(V65[:, :, :, 64:65], 1.0), writes=[("V",)])
    for half in range(2):
        A.op("tensor", lambda e, h=half: e.matmul(out=psum[6][:, 0:NC], lhsT=ones1b[0:1, :],
             rhs=bv_sb[0:1, h * 512:(h + 1) * 512], start=True, stop=True),
             reads=[("bv",), ("ones",)], writes=[("ps", 6)])
        A.op("vector", lambda e, h=half: e.tensor_copy(out=bv_rep[:, h * 512:(h + 1) * 512],
             in_=psum[6][:, 0:NC]), reads=[("ps", 6)], writes=[("bvrep",)])

    for c in range(NCH):
        if feed:
            feed(2)
        t0 = c * NC
        xb, xtok = x_bf[c % len(x_bf)], ("xbf", c % len(x_bf))
        if rms_gain_sb is None:
            for half in range(2):
                A.dma("gpsimd", xb[:, half * 4:(half + 1) * 4, :],
                      src_hbm[:, half * 4:(half + 1) * 4, t0:t0 + NC], writes=[xtok],
                      stream=xtok)
        else:
            for half in range(2):
                A.dma("sync", x_f32[:, half * 4:(half + 1) * 4, :],
                      src_hbm[:, half * 4:(half + 1) * 4, t0:t0 + NC], writes=[("xf32",)],
                      stream=("xf32",))
            for k in range(KD):
                A.op("vector", lambda e, k=k, o=x2b[k % 2]: e.tensor_tensor(
                    out=o[:], in0=x_f32[:, k, :], in1=x_f32[:, k, :], op=OP.mult),
                    reads=[("xf32",)], writes=[("x2b", k % 2)])
                A.op("tensor", lambda e, k=k, i=x2b[k % 2]: e.matmul(
                    out=psum[7][0:1, 0:NC], lhsT=ones128b[:, 0:1], rhs=i[:],
                    start=(k == 0), stop=(k == KD - 1)),
                    reads=[("x2b", k % 2), ("ones",)], writes=[("ps", 7)])
            A.op("vector", lambda e: e.tensor_scalar(out=srow[:, :], in0=psum[7][0:1, 0:NC],
                 scalar1=1.0 / D, scalar2=None, op0=OP.mult),
                 reads=[("ps", 7)], writes=[("srow",)])
            A.op("scalar", lambda e: e.activation(out=srow[:, :], in_=srow[:, :],
                 func=AF.Sqrt, bias=eps1_sb[0:1, 1:2]), reads=[("srow",), ("const",)], writes=[("srow",)])
            A.op("vector", lambda e: e.reciprocal(out=srow[:, :], in_=srow[:, :]),
                 reads=[("srow",)], writes=[("srow",)])
            A.op("vector", lambda e: e.tensor_copy(out=srow_bf[:, :], in_=srow[:, :]),
                 reads=[("srow",)], writes=[("srowb",)])
            A.op("tensor", lambda e: e.matmul(out=psum[6][:, 0:NC], lhsT=ones1b[0:1, :],
                 rhs=srow_bf[:, :], start=True, stop=True),
                 reads=[("srowb",), ("ones",)], writes=[("ps", 6)])
            for k in range(KD):
                A.op("vector", lambda e, k=k, xb=xb: e.scalar_tensor_tensor(
                    out=xb[:, k, :], in0=x_f32[:, k, :], scalar=rms_gain_sb[:, k:k + 1],
                    in1=psum[6][:, 0:NC], op0=OP.mult, op1=OP.mult),
                    reads=[("xf32",), ("ps", 6), ("gain",)], writes=[xtok])

        A.dma("sync", cos_sb[:], cos_hbm[:, t0:t0 + NC], writes=[("cos",)], stream=("cos",))
        A.dma("sync", sin_sb[:], sin_hbm[:, t0:t0 + NC], writes=[("sin",)], stream=("sin",))

        def qk_evac(m, ps, pstok, xb=xb, t0=t0):
            bi = m % 2
            A.op("scalar", lambda e, m=m, bi=bi, ps=ps: e.activation(
                out=qk_ev[bi][:], in_=ps[:, 0:NC], func=AF.Identity,
                bias=bqk_sb[:, m:m + 1], scale=1.0),
                reads=[pstok, ("bqk",)], writes=[("qk_ev", bi)])
            for b0, b1 in ((0, 32), (32, 0), (64, 96), (96, 64)):
                A.op("vector", lambda e, b0=b0, b1=b1, bi=bi: e.tensor_copy(
                    out=partner[bi][b0:b0 + 32, :], in_=qk_ev[bi][b1:b1 + 32, :]),
                    reads=[("qk_ev", bi)], writes=[("prt", bi)])
            A.op("vector", lambda e, bi=bi: e.tensor_tensor(out=qk_ev[bi][:], in0=qk_ev[bi][:],
                 in1=cos_sb[:], op=OP.mult),
                 reads=[("qk_ev", bi), ("cos",)], writes=[("qk_ev", bi)])
            A.op("vector", lambda e, bi=bi: e.tensor_tensor(out=partner[bi][:], in0=partner[bi][:],
                 in1=sin_sb[:], op=OP.mult),
                 reads=[("prt", bi), ("sin",)], writes=[("prt", bi)])
            if m < 8:
                A.op("vector", lambda e, m=m, bi=bi: e.tensor_tensor(out=q_chunk[:, m, :],
                     in0=qk_ev[bi][:], in1=partner[bi][:], op=OP.add),
                     reads=[("qk_ev", bi), ("prt", bi)], writes=[("qch",)])
            else:
                A.op("vector", lambda e, m=m, bi=bi, t0=t0: e.tensor_tensor(
                    out=K_sb[:, m - 8, t0:t0 + NC], in0=qk_ev[bi][:], in1=partner[bi][:], op=OP.add),
                    reads=[("qk_ev", bi), ("prt", bi)], writes=[("K",)])
        emit_proj(A, w_hbm=wqk_hbm, rhs_fn=lambda k, xb=xb, xtok=xtok: (xb[:, k, :], xtok),
                  kd=KD, mt=16, evac=qk_evac, psum=psum, wring=wring, wrtok="wr" + tag)
        A.dma("gpsimd", q_hbm[:, :, t0:t0 + NC], q_chunk[:, :, :],
              reads=[("qch",)], writes=[("qhbm",)], stream=("qhbm",))

        for k0 in range(0, KD, 1):
            vb = vring[k0 % 2]
            vtok = ("vrk", k0 % 2)
            A.dma("sync", vb[:, :], wv_hbm[k0, :, :], writes=[vtok])
            for dk in range(1):
                k = k0 + dk
                for tsub in range(4):
                    for half in range(2):
                        A.op("tensor", lambda e, k=k, dk=dk, tsub=tsub, half=half, vb=vb, xb=xb:
                             e.matmul(out=psum[tsub * 2 + half][:, 0:NC],
                                      lhsT=xb[:, k, tsub * 128:(tsub + 1) * 128],
                                      rhs=vb[:, half * 512:(half + 1) * 512],
                                      start=(k == 0), stop=(k == KD - 1)),
                             reads=[vtok, xtok], writes=[("ps", tsub * 2 + half)])
        for tsub in range(4):
            j = 4 * c + tsub
            for half in range(2):
                b = tsub * 2 + half
                A.op("vector", lambda e, j=j, half=half, b=b: e.tensor_tensor(
                    out=V65[:, j, half * 8:(half + 1) * 8, 0:64],
                    in0=psum[b][:, 0:NC].rearrange("p (h d) -> p h d", h=8),
                    in1=bv_rep[:, half * 512:(half + 1) * 512].rearrange("p (h d) -> p h d", h=8),
                    op=OP.add),
                    reads=[("ps", b), ("bvrep",)], writes=[("V",)])

    A.barrier()
    es1.close()
    if not a2:
        return
    expS = [[es.enter_context(nc.sbuf_tensor(f"expS{tag}{cc}{i}", [128, NC], BF16))
             for i in range(2)] for cc in range(2)]
    recip_row = [es.enter_context(nc.sbuf_tensor(f"recip{tag}{i}", [1, NC], F32)) for i in range(2)]
    recip_bf = [es.enter_context(nc.sbuf_tensor(f"recipb{tag}{i}", [1, NC], BF16)) for i in range(2)]
    rep_sb = [es.enter_context(nc.sbuf_tensor(f"rep{tag}{i}", [64, NC], F32)) for i in range(2)]
    ao_chunk = es.enter_context(nc.sbuf_tensor(f"ao{tag}", [128, KD, 2 * NC], BF16))
    qp_sb = es.enter_context(nc.sbuf_tensor(f"qp{tag}", [128, KD, 2 * NC], BF16))
    for cp in range(NCH // 2):
        t0 = cp * 2 * NC
        A.dma("sync", qp_sb[:, :, :], q_hbm[:, :, t0:t0 + 2 * NC],
              reads=[("qhbm",)], writes=[("qch",)], stream=("qch",))
        jmaxs = (8 * cp + 3, 8 * cp + 7) if causal else (T // 128 - 1, T // 128 - 1)
        jall = jmaxs[1]
        for h in range(NH):
            if feed:
                feed(2)
            hb, mq = 64 * (h % 2), h // 2
            scb = {0: (0, 1), 1: (4, 5)}
            ub = {0: 2, 1: 6}

            def sc(j, cc, hb=hb, mq=mq, cp=cp):
                bank = scb[cc][j % 2]
                A.op("tensor", lambda e, j=j, cc=cc, hb=hb, mq=mq, bank=bank: e.matmul(
                    out=psum[bank][:, 0:NC], lhsT=K_sb[hb:hb + 64, mq, j * 128:(j + 1) * 128],
                    rhs=qp_sb[hb:hb + 64, mq, cc * NC:(cc + 1) * NC], start=True, stop=True,
                    skip_group_check=True),
                    reads=[("K",), ("qch",)], writes=[("ps", bank)])
                A.op("scalar", lambda e, j=j, cc=cc, bank=bank: e.activation(
                    out=expS[cc][j % 2][:], in_=psum[bank][:, 0:NC], func=AF.Exp,
                    scale=0.125, bias=shift_sb[:, 0:1]),
                    reads=[("ps", bank), ("const",)], writes=[("expS", cc, j % 2)])
                if causal and j >= 8 * cp + 4 * cc:
                    dd = j - (8 * cp + 4 * cc)
                    A.op("vector", lambda e, j=j, cc=cc, dd=dd: e.tensor_tensor(
                        out=expS[cc][j % 2][:], in0=expS[cc][j % 2][:], in1=masks_sb[:, dd, :],
                        op=OP.mult),
                        reads=[("expS", cc, j % 2), ("msk",)], writes=[("expS", cc, j % 2)])

            def pv(j, cc, h=h):
                A.op("tensor", lambda e, j=j, cc=cc, h=h, st=(j == 0), sp=(j == jmaxs[cc]):
                     e.matmul(
                     out=psum[ub[cc]][0:65, 0:NC], lhsT=V65[:, j, h, 0:65],
                     rhs=expS[cc][j % 2][:],
                     start=st, stop=sp, skip_group_check=True),
                     reads=[("V",), ("expS", cc, j % 2)], writes=[("ps", ub[cc])])

            def valid(j, cc):
                return j <= jmaxs[cc]
            sc(0, 0)
            if valid(0, 1):
                sc(0, 1)
            for j in range(1, jall + 1):
                for cc in range(2):
                    if valid(j, cc):
                        sc(j, cc)
                for cc in range(2):
                    if valid(j - 1, cc):
                        pv(j - 1, cc)
            for cc in range(2):
                if jmaxs[cc] == jall:
                    pv(jmaxs[cc], cc)
            for cc in range(2):
                rb = 3 + 4 * cc
                A.op("vector", lambda e, cc=cc: e.reciprocal(out=recip_row[cc][:, :],
                     in_=psum[ub[cc]][64:65, 0:NC]),
                     reads=[("ps", ub[cc])], writes=[("recip", cc)])
                A.op("vector", lambda e, cc=cc: e.tensor_copy(out=recip_bf[cc][:, :],
                     in_=recip_row[cc][:, :]),
                     reads=[("recip", cc)], writes=[("recipb", cc)])
                A.op("tensor", lambda e, cc=cc, rb=rb: e.matmul(out=psum[rb][0:64, 0:NC],
                     lhsT=ones1b[0:1, 0:64], rhs=recip_bf[cc][:, :], start=True, stop=True,
                     skip_group_check=True),
                     reads=[("recipb", cc), ("ones",)], writes=[("ps", rb)])
                A.op("vector", lambda e, cc=cc, rb=rb: e.tensor_copy(out=rep_sb[cc][:, :],
                     in_=psum[rb][0:64, 0:NC]),
                     reads=[("ps", rb)], writes=[("repb", cc)])
                A.op("vector", lambda e, hb=hb, mq=mq, cc=cc: e.tensor_tensor(
                    out=ao_chunk[hb:hb + 64, mq, cc * NC:(cc + 1) * NC],
                    in0=psum[ub[cc]][0:64, 0:NC], in1=rep_sb[cc][:, :],
                    op=OP.mult),
                    reads=[("ps", ub[cc]), ("repb", cc)], writes=[("ao",)])
        A.dma("gpsimd", attn_hbm[:, :, t0:t0 + 2 * NC], ao_chunk[:, :, :],
              reads=[("ao",)], writes=[("ahbm",)], stream=("ahbm",))


def emit_outproj_ln(au, es, nc, *, attn_hbm, src_hbm, dst_hbm, w_hbm, b_sb,
                    psum, ones128b, ones1b, eps1_sb, ln=True, tag="o"):
    A = au
    at_ch = es.enter_context(nc.sbuf_tensor(f"atch{tag}", [128, KD, NC], BF16))
    x_ch = es.enter_context(nc.sbuf_tensor(f"xch{tag}", [128, KD, NC], F32))
    r_ch = es.enter_context(nc.sbuf_tensor(f"rch{tag}", [128, KD, NC], F32))
    wring = [es.enter_context(nc.sbuf_tensor(f"wo{tag}{i}", [128, 2048], BF16)) for i in range(4)]
    x2b = [es.enter_context(nc.sbuf_tensor(f"x2o{tag}{i}", [128, NC], BF16)) for i in range(2)]
    mrow = es.enter_context(nc.sbuf_tensor(f"mrow{tag}", [1, NC], F32))
    vrow = es.enter_context(nc.sbuf_tensor(f"vrow{tag}", [1, NC], F32))
    brow = es.enter_context(nc.sbuf_tensor(f"brow{tag}", [1, NC], F32))
    arow_bf = es.enter_context(nc.sbuf_tensor(f"arowb{tag}", [1, NC], BF16))
    brow_bf = es.enter_context(nc.sbuf_tensor(f"browb{tag}", [1, NC], BF16))
    tmp = es.enter_context(nc.sbuf_tensor(f"tmpo{tag}", [128, NC], F32))

    for c in range(NCH):
        t0 = c * NC
        A.dma("sync", at_ch[:, :, :], attn_hbm[:, :, t0:t0 + NC], writes=[("atch",)],
              stream=("atch",))
        for half in range(2):
            A.dma("sync", x_ch[:, half * 4:(half + 1) * 4, :],
                  src_hbm[:, half * 4:(half + 1) * 4, t0:t0 + NC], writes=[("xch",)],
                  stream=("xch",))

        def op_evac(m, ps, pstok):
            A.op("scalar", lambda e, m=m, ps=ps: e.activation(out=ps[:, 0:NC],
                 in_=ps[:, 0:NC], func=AF.Identity, bias=b_sb[:, m:m + 1]),
                 reads=[pstok, ("bout",)], writes=[pstok])
            A.op("vector", lambda e, m=m, ps=ps: e.tensor_tensor(out=r_ch[:, m, :],
                 in0=ps[:, 0:NC], in1=x_ch[:, m, :], op=OP.add),
                 reads=[pstok, ("xch",)], writes=[("rch",)])
        emit_proj(A, w_hbm=w_hbm, rhs_fn=lambda k: (at_ch[:, k, :], ("atch",)),
                  kd=KD, mt=KD, evac=op_evac, psum=psum, wring=wring, wrtok="wo" + tag)
        if not ln:
            for half in range(2):
                A.dma("gpsimd", dst_hbm[:, half * 4:(half + 1) * 4, t0:t0 + NC],
                      r_ch[:, half * 4:(half + 1) * 4, :], reads=[("rch",)],
                      writes=[("dsth",)], stream=("dst_stream",))
            continue
        for k in range(KD):
            A.op("vector", lambda e, k=k, o=x2b[k % 2]: e.tensor_copy(
                out=o[:], in_=r_ch[:, k, :]), reads=[("rch",)], writes=[("x2o", k % 2)])
            A.op("tensor", lambda e, k=k, i=x2b[k % 2]: e.matmul(out=psum[0][0:1, 0:NC],
                 lhsT=ones128b[:, 0:1], rhs=i[:], start=(k == 0), stop=(k == KD - 1)),
                 reads=[("x2o", k % 2), ("ones",)], writes=[("ps", 0)])
        for k in range(KD):
            A.op("scalar", lambda e, k=k, o=x2b[k % 2]: e.activation(
                out=o[:], in_=r_ch[:, k, :], func=AF.Square),
                reads=[("rch",)], writes=[("x2o", k % 2)])
            A.op("tensor", lambda e, k=k, i=x2b[k % 2]: e.matmul(out=psum[1][0:1, 0:NC],
                 lhsT=ones128b[:, 0:1], rhs=i[:], start=(k == 0), stop=(k == KD - 1)),
                 reads=[("x2o", k % 2), ("ones",)], writes=[("ps", 1)])
        A.op("vector", lambda e: e.tensor_scalar(out=mrow[:, :], in0=psum[0][0:1, 0:NC],
             scalar1=1.0 / D, scalar2=None, op0=OP.mult),
             reads=[("ps", 0)], writes=[("mrow",)])
        A.op("vector", lambda e: e.tensor_scalar(out=vrow[:, :], in0=psum[1][0:1, 0:NC],
             scalar1=1.0 / D, scalar2=None, op0=OP.mult),
             reads=[("ps", 1)], writes=[("vrow",)])
        A.op("vector", lambda e: e.tensor_tensor(out=brow[:, :], in0=mrow[:, :],
             in1=mrow[:, :], op=OP.mult), reads=[("mrow",)], writes=[("brow",)])
        A.op("vector", lambda e: e.tensor_tensor(out=vrow[:, :], in0=vrow[:, :],
             in1=brow[:, :], op=OP.subtract), reads=[("vrow",), ("brow",)], writes=[("vrow",)])
        A.op("scalar", lambda e: e.activation(out=vrow[:, :], in_=vrow[:, :],
             func=AF.Sqrt, bias=eps1_sb[0:1, 0:1]), reads=[("vrow",), ("const",)], writes=[("vrow",)])
        A.op("vector", lambda e: e.reciprocal(out=vrow[:, :], in_=vrow[:, :]),
             reads=[("vrow",)], writes=[("vrow",)])
        A.op("vector", lambda e: e.tensor_tensor(out=brow[:, :], in0=mrow[:, :],
             in1=vrow[:, :], op=OP.mult), reads=[("mrow",), ("vrow",)], writes=[("brow",)])
        A.op("vector", lambda e: e.tensor_scalar(out=brow[:, :], in0=brow[:, :],
             scalar1=-1.0, scalar2=None, op0=OP.mult), reads=[("brow",)], writes=[("brow",)])
        A.op("vector", lambda e: e.tensor_copy(out=arow_bf[:, :], in_=vrow[:, :]),
             reads=[("vrow",)], writes=[("arowb",)])
        A.op("vector", lambda e: e.tensor_copy(out=brow_bf[:, :], in_=brow[:, :]),
             reads=[("brow",)], writes=[("browb",)])
        A.op("tensor", lambda e: e.matmul(out=psum[2][:, 0:NC], lhsT=ones1b[0:1, :],
             rhs=arow_bf[:, :], start=True, stop=True), reads=[("arowb",), ("ones",)], writes=[("ps", 2)])
        A.op("tensor", lambda e: e.matmul(out=psum[3][:, 0:NC], lhsT=ones1b[0:1, :],
             rhs=brow_bf[:, :], start=True, stop=True), reads=[("browb",), ("ones",)], writes=[("ps", 3)])
        for k in range(KD):
            A.op("vector", lambda e, k=k: e.tensor_tensor(out=tmp[:], in0=r_ch[:, k, :],
                 in1=psum[2][:, 0:NC], op=OP.mult),
                 reads=[("rch",), ("ps", 2)], writes=[("tmpo",)])
            A.op("vector", lambda e, k=k: e.tensor_tensor(out=r_ch[:, k, :], in0=tmp[:],
                 in1=psum[3][:, 0:NC], op=OP.add),
                 reads=[("tmpo",), ("ps", 3)], writes=[("rch",)])
        for half in range(2):
            A.dma("gpsimd", dst_hbm[:, half * 4:(half + 1) * 4, t0:t0 + NC],
                  r_ch[:, half * 4:(half + 1) * 4, :], reads=[("rch",)], writes=[("dsth",)],
                  stream=("dst_stream",))


def emit_moe_phase(au, es, nc, *, z_hbm, x1_hbm, dst_hbm, rms_gain_sb, gate_hbm,
                   w1_hbm, w2_hbm, w3_hbm, b1_sb, b2_sb, b3_sb, psum,
                   ones128b, ones1b, esel_sb, ident_bf, eps1_sb, tag="m"):
    A = au
    NE, HD = 8, 2048
    z_ch = es.enter_context(nc.sbuf_tensor(f"zch{tag}", [128, KD, NC], F32))
    x1_ch = es.enter_context(nc.sbuf_tensor(f"x1ch{tag}", [128, KD, NC], F32))
    zn_bf = es.enter_context(nc.sbuf_tensor(f"znbf{tag}", [128, KD, NC], BF16))
    x2b = [es.enter_context(nc.sbuf_tensor(f"x2{tag}{i}", [128, NC], BF16)) for i in range(2)]
    srow = es.enter_context(nc.sbuf_tensor(f"srow{tag}", [1, NC], F32))
    srow_bf = es.enter_context(nc.sbuf_tensor(f"srowb{tag}", [1, NC], BF16))
    gate_sb = es.enter_context(nc.sbuf_tensor(f"gate{tag}", [128, KD, 8], BF16))
    glog = es.enter_context(nc.sbuf_tensor(f"glog{tag}", [128, 8], F32))
    gtmp = es.enter_context(nc.sbuf_tensor(f"gtmp{tag}", [128, 8], F32))
    gm1 = es.enter_context(nc.sbuf_tensor(f"gm1{tag}", [128, 1], F32))
    gm2 = es.enter_context(nc.sbuf_tensor(f"gm2{tag}", [128, 1], F32))
    comb_bf = es.enter_context(nc.sbuf_tensor(f"combbf{tag}", [128, 4, 8], BF16))
    combT = es.enter_context(nc.sbuf_tensor(f"combT{tag}", [8, NC], BF16))
    crep = es.enter_context(nc.sbuf_tensor(f"crep{tag}", [128, NC], BF16))
    s1_sb = es.enter_context(nc.sbuf_tensor(f"s1{tag}", [128, 16, NC], BF16))
    h_sb = es.enter_context(nc.sbuf_tensor(f"h{tag}", [128, 16, NC], BF16))
    y_sb = es.enter_context(nc.sbuf_tensor(f"y{tag}", [128, KD, NC], F32))
    wring = [es.enter_context(nc.sbuf_tensor(f"w{tag}{i}", [128, 2048], BF16)) for i in range(4)]
    mrow = es.enter_context(nc.sbuf_tensor(f"mrow{tag}", [1, NC], F32))
    vrow = es.enter_context(nc.sbuf_tensor(f"vrow{tag}", [1, NC], F32))
    brow = es.enter_context(nc.sbuf_tensor(f"brow{tag}", [1, NC], F32))
    arow_bf = es.enter_context(nc.sbuf_tensor(f"arb{tag}", [1, NC], BF16))
    brow_bf = es.enter_context(nc.sbuf_tensor(f"brb{tag}", [1, NC], BF16))
    tmp32 = es.enter_context(nc.sbuf_tensor(f"tmp32{tag}", [128, NC], F32))

    for k in range(KD):
        A.dma("gpsimd", gate_sb[:, k, :], gate_hbm[k, :, :], writes=[("gate",)], stream=("gate",))

    for c in range(NCH):
        t0 = c * NC
        for half in range(2):
            A.dma("sync", z_ch[:, half * 4:(half + 1) * 4, :],
                  z_hbm[:, half * 4:(half + 1) * 4, t0:t0 + NC], writes=[("zch",)])
        for k in range(KD):
            A.op("vector", lambda e, k=k, o=x2b[k % 2]: e.tensor_tensor(
                out=o[:], in0=z_ch[:, k, :], in1=z_ch[:, k, :], op=OP.mult),
                reads=[("zch",)], writes=[("x2b", k % 2)])
            A.op("tensor", lambda e, k=k, i=x2b[k % 2]: e.matmul(
                out=psum[7][0:1, 0:NC], lhsT=ones128b[:, 0:1], rhs=i[:],
                start=(k == 0), stop=(k == KD - 1)),
                reads=[("x2b", k % 2), ("ones",)], writes=[("ps", 7)])
        A.op("vector", lambda e: e.tensor_scalar(out=srow[:, :], in0=psum[7][0:1, 0:NC],
             scalar1=1.0 / D, scalar2=None, op0=OP.mult), reads=[("ps", 7)], writes=[("srow",)])
        A.op("scalar", lambda e: e.activation(out=srow[:, :], in_=srow[:, :],
             func=AF.Sqrt, bias=eps1_sb[0:1, 1:2]), reads=[("srow",), ("const",)], writes=[("srow",)])
        A.op("vector", lambda e: e.reciprocal(out=srow[:, :], in_=srow[:, :]),
             reads=[("srow",)], writes=[("srow",)])
        A.op("vector", lambda e: e.tensor_copy(out=srow_bf[:, :], in_=srow[:, :]),
             reads=[("srow",)], writes=[("srowb",)])
        A.op("tensor", lambda e: e.matmul(out=psum[6][:, 0:NC], lhsT=ones1b[0:1, :],
             rhs=srow_bf[:, :], start=True, stop=True),
             reads=[("srowb",), ("ones",)], writes=[("ps", 6)])
        for k in range(KD):
            A.op("vector", lambda e, k=k: e.scalar_tensor_tensor(
                out=zn_bf[:, k, :], in0=z_ch[:, k, :], scalar=rms_gain_sb[:, k:k + 1],
                in1=psum[6][:, 0:NC], op0=OP.mult, op1=OP.mult),
                reads=[("zch",), ("ps", 6), ("gain2",)], writes=[("znbf",)])

        for tsub in range(4):
            for k in range(KD):
                A.op("tensor", lambda e, k=k, tsub=tsub: e.matmul(
                    out=psum[7][:, 0:8], lhsT=zn_bf[:, k, tsub * 128:(tsub + 1) * 128],
                    rhs=gate_sb[:, k, :], start=(k == 0), stop=(k == KD - 1)),
                    reads=[("znbf",), ("gate",)], writes=[("ps", 7)])
            A.op("vector", lambda e: e.tensor_copy(out=glog[:, :], in_=psum[7][:, 0:8]),
                 reads=[("ps", 7)], writes=[("glog",)])
            A.op("vector", lambda e: e.reduce_max(out=gm1[:, 0:1], in_=glog[:, :], axis=AX.X),
                 reads=[("glog",)], writes=[("gm1",)])
            A.op("vector", lambda e: e.tensor_scalar(out=gtmp[:, :], in0=glog[:, :],
                 scalar1=gm1[:, 0:1], scalar2=None, op0=OP.is_ge),
                 reads=[("glog",), ("gm1",)], writes=[("gtmp",)])
            A.op("vector", lambda e: e.scalar_tensor_tensor(out=gtmp[:, :], in0=gtmp[:, :],
                 scalar=-1e30, in1=glog[:, :], op0=OP.mult, op1=OP.add),
                 reads=[("gtmp",), ("glog",)], writes=[("gtmp",)])
            A.op("vector", lambda e: e.reduce_max(out=gm2[:, 0:1], in_=gtmp[:, :], axis=AX.X),
                 reads=[("gtmp",)], writes=[("gm2",)])
            A.op("vector", lambda e: e.tensor_scalar(out=gm1[:, 0:1], in0=gm1[:, 0:1],
                 scalar1=-1.0, scalar2=None, op0=OP.mult), reads=[("gm1",)], writes=[("gm1",)])
            A.op("scalar", lambda e: e.activation(out=gtmp[:, :], in_=glog[:, :],
                 func=AF.Exp, bias=gm1[:, 0:1]), reads=[("glog",), ("gm1",)], writes=[("gtmp",)])
            A.op("vector", lambda e: e.tensor_scalar(out=glog[:, :], in0=glog[:, :],
                 scalar1=gm2[:, 0:1], scalar2=None, op0=OP.is_ge),
                 reads=[("glog",), ("gm2",)], writes=[("glog",)])
            A.op("vector", lambda e: e.tensor_tensor(out=gtmp[:, :], in0=gtmp[:, :],
                 in1=glog[:, :], op=OP.mult), reads=[("gtmp",), ("glog",)], writes=[("gtmp",)])
            A.op("vector", lambda e: e.tensor_reduce(out=gm2[:, 0:1], in_=gtmp[:, :],
                 axis=AX.X, op=OP.add), reads=[("gtmp",)], writes=[("gm2",)])
            A.op("vector", lambda e: e.reciprocal(out=gm2[:, 0:1], in_=gm2[:, 0:1]),
                 reads=[("gm2",)], writes=[("gm2",)])
            A.op("vector", lambda e, tsub=tsub: e.tensor_scalar(out=comb_bf[:, tsub, :],
                 in0=gtmp[:, :], scalar1=gm2[:, 0:1], scalar2=None, op0=OP.mult),
                 reads=[("gtmp",), ("gm2",)], writes=[("comb",)])
            A.op("tensor", lambda e, tsub=tsub: e.matmul(out=psum[6][0:8, 0:128],
                 lhsT=comb_bf[:, tsub, :], rhs=ident_bf[:], start=True, stop=True),
                 reads=[("comb",), ("ident",)], writes=[("ps", 6)])
            A.op("vector", lambda e, tsub=tsub: e.tensor_copy(
                 out=combT[:, tsub * 128:(tsub + 1) * 128], in_=psum[6][0:8, 0:128]),
                 reads=[("ps", 6)], writes=[("combT",)])

        for ex in range(NE):
            A.op("tensor", lambda e, ex=ex: e.matmul(out=psum[6][:, 0:NC],
                 lhsT=esel_sb[:, ex * 128:(ex + 1) * 128], rhs=combT[:, :],
                 start=True, stop=True), reads=[("esel",), ("combT",)], writes=[("ps", 6)])
            A.op("vector", lambda e: e.tensor_copy(out=crep[:, :], in_=psum[6][:, 0:NC]),
                 reads=[("ps", 6)], writes=[("crep",)])

            def w1_evac(m, ps, pstok, ex=ex):
                A.op("scalar", lambda e, m=m, ps=ps, ex=ex: e.activation(
                    out=s1_sb[:, m, :], in_=ps[:, 0:NC], func=AF.Silu,
                    bias=b1_sb[:, ex, m:m + 1]),
                    reads=[pstok, ("b1",)], writes=[("s1", m)])
                A.op("vector", lambda e, m=m: e.tensor_tensor(out=s1_sb[:, m, :],
                     in0=s1_sb[:, m, :], in1=crep[:, :], op=OP.mult),
                     reads=[("s1", m), ("crep",)], writes=[("s1", m)])
            emit_proj(A, w_hbm=w1_hbm[ex], rhs_fn=lambda k: (zn_bf[:, k, :], ("znbf",)),
                      kd=KD, mt=16, evac=w1_evac, psum=psum, wring=wring, wrtok="we" + tag)

            def w2_evac(m, ps, pstok, ex=ex):
                A.op("scalar", lambda e, m=m, ps=ps, ex=ex: e.activation(
                    out=ps[:, 0:NC], in_=ps[:, 0:NC], func=AF.Identity,
                    bias=b2_sb[:, ex, m:m + 1]),
                    reads=[pstok, ("b2",)], writes=[pstok])
                A.op("vector", lambda e, m=m, ps=ps: e.tensor_tensor(
                    out=h_sb[:, m, :], in0=ps[:, 0:NC], in1=s1_sb[:, m, :],
                    op=OP.mult),
                    reads=[pstok, ("s1", m)], writes=[("h", m)])
            emit_proj(A, w_hbm=w2_hbm[ex], rhs_fn=lambda k: (zn_bf[:, k, :], ("znbf",)),
                      kd=KD, mt=16, evac=w2_evac, psum=psum, wring=wring, wrtok="we" + tag)

            def w3_evac(m, ps, pstok, ex=ex):
                if ex == 0:
                    A.op("vector", lambda e, m=m, ps=ps: e.tensor_copy(out=y_sb[:, m, :],
                         in_=ps[:, 0:NC]), reads=[pstok], writes=[("y", m)])
                else:
                    A.op("vector", lambda e, m=m, ps=ps: e.tensor_tensor(out=y_sb[:, m, :],
                         in0=y_sb[:, m, :], in1=ps[:, 0:NC], op=OP.add),
                         reads=[pstok, ("y", m)], writes=[("y", m)])
            emit_proj(A, w_hbm=w3_hbm[ex], rhs_fn=lambda k: (h_sb[:, k, :], ("h", k)),
                      kd=16, mt=KD, evac=w3_evac, psum=psum, wring=wring, wrtok="we" + tag)

        for m8 in range(8):
            A.op("tensor", lambda e, m8=m8: e.matmul(out=psum[m8][:, 0:NC],
                 lhsT=b3_sb[0:8, m8 * 128:(m8 + 1) * 128], rhs=combT[:, :],
                 start=True, stop=True), reads=[("b3",), ("combT",)], writes=[("ps", m8)])
            A.op("vector", lambda e, m8=m8: e.tensor_tensor(out=y_sb[:, m8, :],
                 in0=y_sb[:, m8, :], in1=psum[m8][:, 0:NC], op=OP.add),
                 reads=[("ps", m8), ("y", m8)], writes=[("y", m8)])

        for half in range(2):
            A.dma("sync", x1_ch[:, half * 4:(half + 1) * 4, :],
                  x1_hbm[:, half * 4:(half + 1) * 4, t0:t0 + NC], writes=[("x1ch",)])
        for k in range(KD):
            A.op("vector", lambda e, k=k: e.tensor_tensor(out=y_sb[:, k, :],
                 in0=y_sb[:, k, :], in1=z_ch[:, k, :], op=OP.add),
                 reads=[("y", k), ("zch",)], writes=[("y", k)])
            A.op("vector", lambda e, k=k: e.tensor_tensor(out=y_sb[:, k, :],
                 in0=y_sb[:, k, :], in1=x1_ch[:, k, :], op=OP.add),
                 reads=[("y", k), ("x1ch",)], writes=[("y", k)])
        rbk = x2b
        for k in range(KD):
            A.op("vector", lambda e, k=k, o=rbk[k % 2]: e.tensor_copy(
                out=o[:], in_=y_sb[:, k, :]), reads=[("y", k)], writes=[("rb", k % 2)])
            A.op("tensor", lambda e, k=k, i=rbk[k % 2]: e.matmul(out=psum[0][0:1, 0:NC],
                 lhsT=ones128b[:, 0:1], rhs=i[:], start=(k == 0), stop=(k == KD - 1)),
                 reads=[("rb", k % 2), ("ones",)], writes=[("ps", 0)])
        for k in range(KD):
            A.op("scalar", lambda e, k=k, o=rbk[k % 2]: e.activation(
                out=o[:], in_=y_sb[:, k, :], func=AF.Square),
                reads=[("y", k)], writes=[("rb", k % 2)])
            A.op("tensor", lambda e, k=k, i=rbk[k % 2]: e.matmul(out=psum[1][0:1, 0:NC],
                 lhsT=ones128b[:, 0:1], rhs=i[:], start=(k == 0), stop=(k == KD - 1)),
                 reads=[("rb", k % 2), ("ones",)], writes=[("ps", 1)])
        A.op("vector", lambda e: e.tensor_scalar(out=mrow[:, :], in0=psum[0][0:1, 0:NC],
             scalar1=1.0 / D, scalar2=None, op0=OP.mult), reads=[("ps", 0)], writes=[("mrow",)])
        A.op("vector", lambda e: e.tensor_scalar(out=vrow[:, :], in0=psum[1][0:1, 0:NC],
             scalar1=1.0 / D, scalar2=None, op0=OP.mult), reads=[("ps", 1)], writes=[("vrow",)])
        A.op("vector", lambda e: e.tensor_tensor(out=brow[:, :], in0=mrow[:, :],
             in1=mrow[:, :], op=OP.mult), reads=[("mrow",)], writes=[("brow",)])
        A.op("vector", lambda e: e.tensor_tensor(out=vrow[:, :], in0=vrow[:, :],
             in1=brow[:, :], op=OP.subtract), reads=[("vrow",), ("brow",)], writes=[("vrow",)])
        A.op("scalar", lambda e: e.activation(out=vrow[:, :], in_=vrow[:, :],
             func=AF.Sqrt, bias=eps1_sb[0:1, 0:1]), reads=[("vrow",), ("const",)], writes=[("vrow",)])
        A.op("vector", lambda e: e.reciprocal(out=vrow[:, :], in_=vrow[:, :]),
             reads=[("vrow",)], writes=[("vrow",)])
        A.op("vector", lambda e: e.tensor_tensor(out=brow[:, :], in0=mrow[:, :],
             in1=vrow[:, :], op=OP.mult), reads=[("mrow",), ("vrow",)], writes=[("brow",)])
        A.op("vector", lambda e: e.tensor_scalar(out=brow[:, :], in0=brow[:, :],
             scalar1=-1.0, scalar2=None, op0=OP.mult), reads=[("brow",)], writes=[("brow",)])
        A.op("vector", lambda e: e.tensor_copy(out=arow_bf[:, :], in_=vrow[:, :]),
             reads=[("vrow",)], writes=[("arb",)])
        A.op("vector", lambda e: e.tensor_copy(out=brow_bf[:, :], in_=brow[:, :]),
             reads=[("brow",)], writes=[("brb",)])
        A.op("tensor", lambda e: e.matmul(out=psum[2][:, 0:NC], lhsT=ones1b[0:1, :],
             rhs=arow_bf[:, :], start=True, stop=True), reads=[("arb",), ("ones",)], writes=[("ps", 2)])
        A.op("tensor", lambda e: e.matmul(out=psum[3][:, 0:NC], lhsT=ones1b[0:1, :],
             rhs=brow_bf[:, :], start=True, stop=True), reads=[("brb",), ("ones",)], writes=[("ps", 3)])
        for k in range(KD):
            A.op("vector", lambda e, k=k: e.tensor_tensor(out=tmp32[:], in0=y_sb[:, k, :],
                 in1=psum[2][:, 0:NC], op=OP.mult),
                 reads=[("y", k), ("ps", 2)], writes=[("tmp32",)])
            A.op("vector", lambda e, k=k: e.tensor_tensor(out=y_sb[:, k, :], in0=tmp32[:],
                 in1=psum[3][:, 0:NC], op=OP.add),
                 reads=[("tmp32",), ("ps", 3)], writes=[("y", k)])
        for half in range(2):
            A.dma("gpsimd", dst_hbm[:, half * 4:(half + 1) * 4, t0:t0 + NC],
                  y_sb[:, half * 4:(half + 1) * 4, :], reads=[("y", half * 4), ("y", half * 4 + 1),
                  ("y", half * 4 + 2), ("y", half * 4 + 3)], writes=[("dsth",)],
                  stream=("dst_stream",))


DI = 2048
KDI = DI // 128
DST = 16
DTR = 64


def emit_mamba_phase(au, es, nc, *, src_hbm, dst_hbm, win_hbm, cw_sb, cb_sb,
                     wx_sb, wdt_sb, bdt_sb, A_sb, D_sb, wout_hbm, esel32_sb,
                     psum, ones128b, ones1b, ones128f32, eps1_sb, tag="c"):
    A = au
    x_bf = es.enter_context(nc.sbuf_tensor(f"xbf{tag}", [128, KD, NC], BF16))
    xi_sb = es.enter_context(nc.sbuf_tensor(f"xi{tag}", [128, KDI, NC + 8], BF16))
    xc_bf = es.enter_context(nc.sbuf_tensor(f"xc{tag}", [128, KDI, NC], BF16))
    zs_bf = es.enter_context(nc.sbuf_tensor(f"zs{tag}", [128, KDI, NC], BF16))
    delta = es.enter_context(nc.sbuf_tensor(f"delta{tag}", [128, KDI, NC], BF16))
    u_bf = es.enter_context(nc.sbuf_tensor(f"u{tag}", [128, 4, NC], BF16))
    g_bf = es.enter_context(nc.sbuf_tensor(f"g{tag}", [128, KDI, NC], BF16))
    y_sb = es.enter_context(nc.sbuf_tensor(f"yy{tag}", [128, 4, NC], F32))
    hcar = es.enter_context(nc.sbuf_tensor(f"hc{tag}", [128, KDI, DST], F32))
    wring = [es.enter_context(nc.sbuf_tensor(f"w{tag}{i}", [128, 2048], BF16)) for i in range(3)]
    xdbl_bc = es.enter_context(nc.sbuf_tensor(f"xbc{tag}", [32, NC], BF16))
    dtT_bf = es.enter_context(nc.sbuf_tensor(f"dtT{tag}", [64, NC], BF16))
    vab = es.enter_context(nc.sbuf_tensor(f"vab{tag}", [128, NC], F32))
    rel = es.enter_context(nc.sbuf_tensor(f"rel{tag}", [128, NC], F32))
    brep = [es.enter_context(nc.sbuf_tensor(f"brep{tag}{i}", [128, NC], BF16)) for i in range(2)]
    crep = [es.enter_context(nc.sbuf_tensor(f"crep{tag}{i}", [128, NC], BF16)) for i in range(2)]
    SEG = NC + 1
    dA_bf = [es.enter_context(nc.sbuf_tensor(f"dA{tag}{i}", [128, 4 * SEG], BF16)) for i in range(2)]
    dBu = [es.enter_context(nc.sbuf_tensor(f"dBu{tag}{i}", [128, 4 * SEG], BF16)) for i in range(2)]
    hn = [es.enter_context(nc.sbuf_tensor(f"hn{tag}{i}", [128, 4 * SEG], BF16)) for i in range(2)]
    tmpb = [es.enter_context(nc.sbuf_tensor(f"tmb{tag}{i}", [128, 4, NC], BF16)) for i in range(3)]
    zcol = es.enter_context(nc.sbuf_tensor(f"zc{tag}", [128, 1], F32))
    x2f = es.enter_context(nc.sbuf_tensor(f"x2f{tag}", [128, KD, NC], F32))
    rbk = [es.enter_context(nc.sbuf_tensor(f"rb{tag}{i}", [128, NC], BF16)) for i in range(2)]
    mrow = es.enter_context(nc.sbuf_tensor(f"mr{tag}", [1, NC], F32))
    vrow = es.enter_context(nc.sbuf_tensor(f"vr{tag}", [1, NC], F32))
    arow_bf = es.enter_context(nc.sbuf_tensor(f"ab{tag}", [1, NC], BF16))
    brow_bf = es.enter_context(nc.sbuf_tensor(f"bb{tag}", [1, NC], BF16))

    A.op("vector", lambda e: e.memset(hcar[:], 0.0), writes=[("hcar",)])
    A.op("vector", lambda e: e.memset(xi_sb[:, :, 0:8], 0.0), writes=[("xi",)])
    A.op("vector", lambda e: e.memset(zcol[:], 0.0), writes=[("zc",)])
    for r in range(2):
        for s in range(4):
            A.op("vector", lambda e, r=r, s=s: e.memset(
                dA_bf[r][:, s * SEG:s * SEG + 1], 0.0), writes=[("dAb", r)])

    for c in range(NCH):
        t0 = c * NC
        A.dma("gpsimd", x_bf[:, :, :], src_hbm[:, :, t0:t0 + NC], writes=[("xbfc",)],
              stream=("xbfc",))
        def xz_evac(m, ps, pstok):
            if m < 16:
                A.op("vector", lambda e, m=m, ps=ps: e.tensor_copy(
                    out=xi_sb[:, m, 8:8 + NC], in_=ps[:, 0:NC]),
                    reads=[pstok], writes=[("xi",)])
            else:
                A.op("scalar", lambda e, m=m, ps=ps: e.activation(
                    out=zs_bf[:, m - 16, :], in_=ps[:, 0:NC], func=AF.Silu),
                    reads=[pstok], writes=[("zs",)])

        emit_proj(A, w_hbm=win_hbm, rhs_fn=lambda k: (x_bf[:, k, :], ("xbfc",)),
                  kd=KD, mt=32, evac=xz_evac, psum=psum, wring=wring, wrtok="wm" + tag)
        for m in range(KDI):
            A.op("vector", lambda e, m=m: e.tensor_scalar(
                out=vab[:], in0=xi_sb[:, m, 5:5 + NC], scalar1=cw_sb[:, m, 0:1],
                scalar2=None, op0=OP.mult),
                reads=[("xi",), ("cw",)], writes=[("vab",)])
            for j in (1, 2, 3):
                A.op("vector", lambda e, m=m, j=j: e.scalar_tensor_tensor(
                    out=vab[:], in0=xi_sb[:, m, 5 + j:5 + j + NC], scalar=cw_sb[:, m, j:j + 1],
                    in1=vab[:], op0=OP.mult, op1=OP.add),
                    reads=[("xi",), ("cw",), ("vab",)], writes=[("vab",)])
            A.op("scalar", lambda e, m=m: e.activation(out=xc_bf[:, m, :], in_=vab[:],
                 func=AF.Silu, bias=cb_sb[:, m:m + 1]),
                 reads=[("vab",), ("cb",)], writes=[("xc",)])
            A.op("vector", lambda e, m=m: e.tensor_copy(out=xi_sb[:, m, 5:8],
                 in_=xi_sb[:, m, 5 + NC:8 + NC]), reads=[("xi",)], writes=[("xi",)])
        for k in range(KDI):
            A.op("tensor", lambda e, k=k: e.matmul(out=psum[7][0:96, 0:NC],
                 lhsT=wx_sb[:, k, :], rhs=xc_bf[:, k, :], start=(k == 0), stop=(k == KDI - 1)),
                 reads=[("wx",), ("xc",)], writes=[("ps", 7)])
        A.op("vector", lambda e: e.tensor_copy(out=dtT_bf[:, :], in_=psum[7][0:64, 0:NC]),
             reads=[("ps", 7)], writes=[("dtT",)])
        A.op("vector", lambda e: e.tensor_copy(out=xdbl_bc[:, :], in_=psum[7][64:96, 0:NC]),
             reads=[("ps", 7)], writes=[("xbc",)])
        for m in range(KDI):
            A.op("tensor", lambda e, m=m: e.matmul(out=psum[6][:, 0:NC],
                 lhsT=wdt_sb[:, m * 128:(m + 1) * 128], rhs=dtT_bf[:, :],
                 start=True, stop=True), reads=[("wdt",), ("dtT",)], writes=[("ps", 6)])
            A.op("scalar", lambda e, m=m: e.activation(out=vab[:], in_=psum[6][:, 0:NC],
                 func=AF.Abs, bias=bdt_sb[:, m:m + 1]),
                 reads=[("ps", 6), ("bdt",)], writes=[("vab",)])
            A.op("scalar", lambda e: e.activation(out=vab[:], in_=vab[:], func=AF.Exp,
                 scale=-1.0), reads=[("vab",)], writes=[("vab",)])
            A.op("scalar", lambda e: e.activation(out=vab[:], in_=vab[:], func=AF.Ln,
                 bias=ones128f32[:, 0:1]), reads=[("vab",), ("onesf",)], writes=[("vab",)])
            A.op("scalar", lambda e, m=m: e.activation(out=rel[:], in_=psum[6][:, 0:NC],
                 func=AF.Relu, bias=bdt_sb[:, m:m + 1]),
                 reads=[("ps", 6), ("bdt",)], writes=[("rel",)])
            A.op("vector", lambda e, m=m: e.tensor_tensor(out=delta[:, m, :], in0=vab[:],
                 in1=rel[:], op=OP.add), reads=[("vab",), ("rel",)], writes=[("delta", m)])
        it = 0
        for mg in range(4):
            ms = range(4 * mg, 4 * mg + 4)
            for mi, m in enumerate(ms):
                A.op("vector", lambda e, m=m, mi=mi: e.tensor_tensor(out=u_bf[:, mi, :],
                     in0=delta[:, m, :], in1=xc_bf[:, m, :], op=OP.mult),
                     reads=[("delta", m), ("xc",)], writes=[("u", mi)])
            pend = []

            def yflush():
                pk = pend.pop(0)
                A.op("vector", lambda e, i=tmpb[pk]: e.tensor_tensor(
                     out=y_sb[:, 0:4, :].rearrange("p a b -> p (a b)"),
                     in0=y_sb[:, 0:4, :].rearrange("p a b -> p (a b)"),
                     in1=i[:, 0:4, :].rearrange("p a b -> p (a b)"), op=OP.add),
                     reads=[("tmb", pk), ("yc", 0), ("yc", 1), ("yc", 2), ("yc", 3)],
                     writes=[("yc", 0), ("yc", 1), ("yc", 2), ("yc", 3)])

            for n in range(DST):
                k2 = it % 2
                n2 = n % 2
                A.op("tensor", lambda e, n=n: e.matmul(out=psum[5][:, 0:NC],
                     lhsT=esel32_sb[:, n * 128:(n + 1) * 128], rhs=xdbl_bc[:, :],
                     start=True, stop=True), reads=[("esel32",), ("xbc",)], writes=[("ps", 5)])
                A.op("scalar", lambda e, n=n: e.copy(out=brep[n % 2][:], in_=psum[5][:, 0:NC]),
                     reads=[("ps", 5)], writes=[("brep", n % 2)])
                A.op("tensor", lambda e, n=n: e.matmul(out=psum[5][:, 0:NC],
                     lhsT=esel32_sb[:, (16 + n) * 128:(17 + n) * 128], rhs=xdbl_bc[:, :],
                     start=True, stop=True), reads=[("esel32",), ("xbc",)], writes=[("ps", 5)])
                A.op("scalar", lambda e, n=n: e.copy(out=crep[n % 2][:], in_=psum[5][:, 0:NC]),
                     reads=[("ps", 5)], writes=[("crep", n % 2)])
                for mi, m in enumerate(ms):
                    A.op("scalar", lambda e, m=m, n=n, o=dA_bf[k2], s0=mi * SEG: e.activation(
                         out=o[:, s0 + 1:s0 + SEG], in_=delta[:, m, :],
                         func=AF.Exp, scale=A_sb[:, m, n:n + 1]),
                         reads=[("delta", m), ("A",)], writes=[("dAb", k2)])
                for mi, m in enumerate(ms):
                    A.op("gpsimd", lambda e, mi=mi, o=dBu[k2], n2=n2, s0=mi * SEG: e.tensor_tensor(
                         out=o[:, s0 + 1:s0 + SEG],
                         in0=u_bf[:, mi, :], in1=brep[n2][:], op=OP.mult),
                         reads=[("u", mi), ("brep", n2)], writes=[("dBu", k2)])
                A.op("scalar", lambda e, n=n, mg=mg, o=dBu[k2]: e.copy(
                     out=o[:, 0:4 * SEG].rearrange("p (a b) -> p a b", a=4)[:, :, 0:1],
                     in_=hcar[:, 4 * mg:4 * mg + 4, n:n + 1]),
                     reads=[("hcar",)], writes=[("dBs", k2)])
                A.op("vector", lambda e, i0=dA_bf[k2], i1=dBu[k2], o=hn[k2]:
                     e.tensor_tensor_scan(
                     out=o[:, :], data0=i0[:, :], data1=i1[:, :], initial=zcol[:, 0:1],
                     op0=OP.mult, op1=OP.add),
                     reads=[("dAb", k2), ("dBu", k2), ("dBs", k2), ("zc",)], writes=[("hn", k2)])
                A.op("scalar", lambda e, n=n, mg=mg, i=hn[k2]: e.copy(
                     out=hcar[:, 4 * mg:4 * mg + 4, n:n + 1],
                     in_=i[:, 0:4 * SEG].rearrange("p (a b) -> p a b", a=4)[:, :, SEG - 1:SEG]),
                     reads=[("hn", k2)], writes=[("hcar",)])
                if n == 0:
                    for mi in range(4):
                        A.op("vector", lambda e, mi=mi, i=hn[k2], n2=n2, s0=mi * SEG:
                             e.tensor_tensor(
                             out=y_sb[:, mi, :],
                             in0=i[:, s0 + 1:s0 + SEG], in1=crep[n2][:], op=OP.mult),
                             reads=[("hn", k2), ("crep", n2)], writes=[("yc", mi)])
                else:
                    k3 = n % 3
                    for mi in range(4):
                        A.op("vector", lambda e, mi=mi, i=hn[k2], o=tmpb[k3], n2=n2,
                             s0=mi * SEG: e.tensor_tensor(
                             out=o[:, mi, :],
                             in0=i[:, s0 + 1:s0 + SEG], in1=crep[n2][:], op=OP.mult),
                             reads=[("hn", k2), ("crep", n2)], writes=[("tmb", k3)])
                    pend.append(k3)
                    if len(pend) >= 3:
                        yflush()
                it += 1
            while pend:
                yflush()
            for mi, m in enumerate(ms):
                A.op("vector", lambda e, m=m, mi=mi: e.scalar_tensor_tensor(out=y_sb[:, mi, :],
                     in0=xc_bf[:, m, :], scalar=D_sb[:, m:m + 1], in1=y_sb[:, mi, :],
                     op0=OP.mult, op1=OP.add),
                     reads=[("xc",), ("D",), ("yc", mi)], writes=[("yc", mi)])
                A.op("vector", lambda e, m=m, mi=mi: e.tensor_tensor(out=g_bf[:, m, :],
                     in0=y_sb[:, mi, :], in1=zs_bf[:, m, :], op=OP.mult),
                     reads=[("yc", mi), ("zs",)], writes=[("u2", m)])
        for half in range(2):
            A.dma("sync", x2f[:, half * 4:(half + 1) * 4, :],
                  src_hbm[:, half * 4:(half + 1) * 4, t0:t0 + NC], writes=[("x2f",)])
        def mo_evac(m, ps, pstok):
            A.op("vector", lambda e, m=m, ps=ps: e.tensor_tensor(out=x2f[:, m, :],
                 in0=x2f[:, m, :], in1=ps[:, 0:NC], op=OP.add),
                 reads=[pstok, ("x2f",)], writes=[("x2f",)])
        emit_proj(A, w_hbm=wout_hbm, rhs_fn=lambda k: (g_bf[:, k, :], ("u2", k)),
                  kd=KDI, mt=KD, evac=mo_evac, psum=psum, wring=wring, wrtok="wm" + tag)
        for k in range(KD):
            A.op("vector", lambda e, k=k, o=rbk[k % 2]: e.tensor_copy(
                out=o[:], in_=x2f[:, k, :]), reads=[("x2f",)], writes=[("rb", k % 2)])
            A.op("tensor", lambda e, k=k, i=rbk[k % 2]: e.matmul(out=psum[0][0:1, 0:NC],
                 lhsT=ones128b[:, 0:1], rhs=i[:], start=(k == 0), stop=(k == KD - 1)),
                 reads=[("rb", k % 2), ("ones",)], writes=[("ps", 0)])
        for k in range(KD):
            A.op("scalar", lambda e, k=k, o=rbk[k % 2]: e.activation(
                out=o[:], in_=x2f[:, k, :], func=AF.Square),
                reads=[("x2f",)], writes=[("rb", k % 2)])
            A.op("tensor", lambda e, k=k, i=rbk[k % 2]: e.matmul(out=psum[1][0:1, 0:NC],
                 lhsT=ones128b[:, 0:1], rhs=i[:], start=(k == 0), stop=(k == KD - 1)),
                 reads=[("rb", k % 2), ("ones",)], writes=[("ps", 1)])
        A.op("vector", lambda e: e.tensor_scalar(out=mrow[:, :], in0=psum[0][0:1, 0:NC],
             scalar1=1.0 / D, scalar2=None, op0=OP.mult), reads=[("ps", 0)], writes=[("mrow",)])
        A.op("vector", lambda e: e.tensor_scalar(out=vrow[:, :], in0=psum[1][0:1, 0:NC],
             scalar1=1.0 / D, scalar2=None, op0=OP.mult), reads=[("ps", 1)], writes=[("vrow",)])
        A.op("vector", lambda e: e.tensor_tensor(out=rel[0:1, :], in0=mrow[:, :],
             in1=mrow[:, :], op=OP.mult), reads=[("mrow",)], writes=[("rel",)])
        A.op("vector", lambda e: e.tensor_tensor(out=vrow[:, :], in0=vrow[:, :],
             in1=rel[0:1, :], op=OP.subtract), reads=[("vrow",), ("rel",)], writes=[("vrow",)])
        A.op("scalar", lambda e: e.activation(out=vrow[:, :], in_=vrow[:, :],
             func=AF.Sqrt, bias=eps1_sb[0:1, 0:1]), reads=[("vrow",), ("const",)], writes=[("vrow",)])
        A.op("vector", lambda e: e.reciprocal(out=vrow[:, :], in_=vrow[:, :]),
             reads=[("vrow",)], writes=[("vrow",)])
        A.op("vector", lambda e: e.tensor_tensor(out=mrow[:, :], in0=mrow[:, :],
             in1=vrow[:, :], op=OP.mult), reads=[("mrow",), ("vrow",)], writes=[("mrow",)])
        A.op("vector", lambda e: e.tensor_scalar(out=mrow[:, :], in0=mrow[:, :],
             scalar1=-1.0, scalar2=None, op0=OP.mult), reads=[("mrow",)], writes=[("mrow",)])
        A.op("vector", lambda e: e.tensor_copy(out=arow_bf[:, :], in_=vrow[:, :]),
             reads=[("vrow",)], writes=[("arb",)])
        A.op("vector", lambda e: e.tensor_copy(out=brow_bf[:, :], in_=mrow[:, :]),
             reads=[("mrow",)], writes=[("brb",)])
        A.op("tensor", lambda e: e.matmul(out=psum[2][:, 0:NC], lhsT=ones1b[0:1, :],
             rhs=arow_bf[:, :], start=True, stop=True), reads=[("arb",), ("ones",)], writes=[("ps", 2)])
        A.op("tensor", lambda e: e.matmul(out=psum[3][:, 0:NC], lhsT=ones1b[0:1, :],
             rhs=brow_bf[:, :], start=True, stop=True), reads=[("brb",), ("ones",)], writes=[("ps", 3)])
        for k in range(KD):
            A.op("vector", lambda e, k=k: e.tensor_tensor(out=rel[:], in0=x2f[:, k, :],
                 in1=psum[2][:, 0:NC], op=OP.mult),
                 reads=[("x2f",), ("ps", 2)], writes=[("rel",)])
            A.op("vector", lambda e, k=k: e.tensor_tensor(out=x2f[:, k, :], in0=rel[:],
                 in1=psum[3][:, 0:NC], op=OP.add),
                 reads=[("rel",), ("ps", 3)], writes=[("x2f",)])
        for half in range(2):
            A.dma("gpsimd", dst_hbm[:, half * 4:(half + 1) * 4, t0:t0 + NC],
                  x2f[:, half * 4:(half + 1) * 4, :], reads=[("x2f",)],
                  writes=[("dsth",)], stream=("dst_stream",))


def bfly_perm_B():
    permB = np.zeros(4096, dtype=np.int64)
    for tB in range(32):
        for pB in range(128):
            b, a = pB // 32, pB % 32
            f = a * 128 + tB * 4 + b
            permB[tB * 128 + pB] = f
    return permB


def bfly_coeffs(wstack):
    coefA = np.zeros((7, 32, 64), np.float32)
    for s in range(7):
        for t in range(32):
            coefA[s, t] = wstack[s, t * 64:(t + 1) * 64]
    coefB = np.zeros((5, 32, 64), np.float32)
    for si, s in enumerate(range(7, 12)):
        sig = 2 ** s
        sigB = 2 ** (s - 7)
        for tB in range(32):
            for kp in range(64):
                a_pB = (kp // sigB) * 2 * sigB + kp % sigB
                b, a = a_pB // 32, a_pB % 32
                i = a * 128 + tB * 4 + b
                k = (i // (2 * sig)) * sig + i % sig
                coefB[si, tB, kp] = wstack[s, k]
    return coefA, coefB


def emit_bfly_build(au, es, nc, *, coef_hbm, which, M_sb, ident_bf, psum, ones1b, tag):
    A = au
    strides = [2 ** s for s in range(7)] if which == "A" else [2 ** i for i in range(5)]
    ns = len(strides)
    work = es.enter_context(nc.sbuf_tensor(f"bw{tag}", [128, 128], F32))
    prod = [es.enter_context(nc.sbuf_tensor(f"bp{tag}{i}", [128, 64], F32)) for i in range(4)]
    cslab = [es.enter_context(nc.sbuf_tensor(f"bc{tag}{i}", [128, ns * 4 * 64], BF16))
             for i in range(2)]
    wtok = ("bwork", tag)
    for t in range(32):
        cs, ctok = cslab[t % 2], ("bcoef", tag, t % 2)
        A.dma("sync", cs[:, :], coef_hbm[t, :, :], writes=[ctok])
        A.op("vector", lambda e: e.tensor_copy(out=work[:], in_=ident_bf[:]),
             reads=[("ident",)], writes=[wtok])
        for si, sig in enumerate(strides):
            base = si * 4 * 64
            for pi, (side, ci) in enumerate(((0, 0), (1, 1), (0, 2), (1, 3))):
                A.op("vector", lambda e, pi=pi, side=side, ci=ci, sig=sig, base=base, cs=cs: e.tensor_tensor(
                    out=prod[pi].rearrange("p (b s) -> p b s", s=sig)[:, :, :],
                    in0=work.rearrange("p (b two s) -> p b two s", two=2, s=sig)[:, :, side, :],
                    in1=cs[:, base + ci * 64: base + (ci + 1) * 64].rearrange(
                        "p (b s) -> p b s", s=sig)[:, :, :],
                    op=OP.mult),
                    reads=[wtok, ctok], writes=[("bprod", tag, pi)])
            A.op("vector", lambda e, sig=sig: e.tensor_tensor(
                out=work.rearrange("p (b two s) -> p b two s", two=2, s=sig)[:, :, 0, :],
                in0=prod[0].rearrange("p (b s) -> p b s", s=sig)[:, :, :],
                in1=prod[1].rearrange("p (b s) -> p b s", s=sig)[:, :, :], op=OP.add),
                reads=[("bprod", tag, 0), ("bprod", tag, 1)], writes=[wtok])
            A.op("vector", lambda e, sig=sig: e.tensor_tensor(
                out=work.rearrange("p (b two s) -> p b two s", two=2, s=sig)[:, :, 1, :],
                in0=prod[2].rearrange("p (b s) -> p b s", s=sig)[:, :, :],
                in1=prod[3].rearrange("p (b s) -> p b s", s=sig)[:, :, :], op=OP.add),
                reads=[("bprod", tag, 2), ("bprod", tag, 3)], writes=[wtok])
        A.op("vector", lambda e, t=t: e.tensor_copy(out=M_sb[:, t, :], in_=work[:]),
             reads=[wtok], writes=[("M", tag)])


def emit_bfly_phase(au, es, nc, *, src_hbm, dst_hbm, wexp_hbm, bexp_sb, MAu, MBu,
                    MAd, MBd, ubias_sb, wcon_hbm, bcon_sb, shuf_hbm, psum,
                    ones128b, ones1b, eps1_sb, tag="d"):
    A = au
    x_bf = [es.enter_context(nc.sbuf_tensor(f"xb{tag}", [128, KD, NC], BF16))]
    h_sbs = [es.enter_context(nc.sbuf_tensor(f"h{tag}{i}", [128, 32, NC], BF16)) for i in range(2)]
    h2_sbs = [es.enter_context(nc.sbuf_tensor(f"h2{tag}{i}", [128, 32, NC], BF16)) for i in range(2)]
    x3f = es.enter_context(nc.sbuf_tensor(f"x3f{tag}", [128, KD, NC], F32))
    wring = [es.enter_context(nc.sbuf_tensor(f"w{tag}{i}", [128, 2048], BF16)) for i in range(3)]
    rbk = [es.enter_context(nc.sbuf_tensor(f"rb{tag}{i}", [128, NC], BF16)) for i in range(2)]
    mrow = es.enter_context(nc.sbuf_tensor(f"mr{tag}", [1, NC], F32))
    vrow = es.enter_context(nc.sbuf_tensor(f"vr{tag}", [1, NC], F32))
    brow = es.enter_context(nc.sbuf_tensor(f"br{tag}", [1, NC], F32))
    arow_bf = es.enter_context(nc.sbuf_tensor(f"ab{tag}", [1, NC], BF16))
    brow_bf = es.enter_context(nc.sbuf_tensor(f"bb{tag}", [1, NC], BF16))

    for c in range(NCH):
        t0 = c * NC
        cp = c % 2
        h_sb, h2_sb = h_sbs[cp], h2_sbs[cp]
        xb = x_bf[0]
        hhtok, h2tok = ("hh", cp), ("h2", cp)
        shuf_hbm_c = shuf_hbm[cp]
        shufA = shuf_hbm_c.rearrange("c b a t -> (c b) a t")
        shufB = shuf_hbm_c.rearrange("c b a t -> (b a) c t")
        A.dma("gpsimd", xb[:, :, :], src_hbm[:, :, t0:t0 + NC], writes=[("xb", 0)],
              stream=("xb", 0))
        def we_evac(m, ps, pstok, h_sb=h_sb, hhtok=hhtok):
            A.op("scalar", lambda e, m=m, ps=ps, h_sb=h_sb, cp=cp: e.activation(
                out=h_sb[:, m, :], in_=ps[:, 0:NC], func=AF.Identity,
                bias=bexp_sb[:, m:m + 1]),
                reads=[pstok, ("bexp",)], writes=[hhtok])
        emit_proj(A, w_hbm=wexp_hbm, rhs_fn=lambda k, xb=xb: (xb[:, k, :], ("xb", 0)),
                  kd=KD, mt=32, evac=we_evac, psum=psum, wring=wring, wrtok="wd" + tag,
                  bank_base=4 * cp, pingpong=False)
        for t in range(32):
            A.op("tensor", lambda e, t=t, h_sb=h_sb, cp=cp: e.matmul(out=psum[4 * cp + t % 4][:, 0:NC],
                 lhsT=MAu[:, t, :], rhs=h_sb[:, t, :], start=True, stop=True),
                 reads=[("M", "Au"), hhtok], writes=[("ps", 4 * cp + t % 4)])
            A.op("vector", lambda e, t=t, h2_sb=h2_sb, cp=cp: e.tensor_copy(out=h2_sb[:, t, :],
                 in_=psum[4 * cp + t % 4][:, 0:NC]), reads=[("ps", 4 * cp + t % 4)], writes=[h2tok])
        for q in range(4):
            A.dma("gpsimd", shufA[:, 8 * q:8 * q + 8, :], h2_sb[:, 8 * q:8 * q + 8, :],
                  reads=[h2tok], writes=[("shufh", cp)], stream=("shufh", cp))
        for q in range(4):
            A.dma("sync", h_sb[:, 8 * q:8 * q + 8, :], shufB[:, 8 * q:8 * q + 8, :],
                  reads=[("shufh", cp)], writes=[hhtok], stream=hhtok)
        for t in range(32):
            A.op("tensor", lambda e, t=t, h_sb=h_sb, cp=cp: e.matmul(out=psum[4 * cp + t % 4][:, 0:NC],
                 lhsT=MBu[:, t, :], rhs=h_sb[:, t, :], start=True, stop=True),
                 reads=[("M", "Bu"), hhtok], writes=[("ps", 4 * cp + t % 4)])
            A.op("scalar", lambda e, t=t, h2_sb=h2_sb, cp=cp: e.activation(out=h2_sb[:, t, :],
                 in_=psum[4 * cp + t % 4][:, 0:NC], func=AF.Gelu, bias=ubias_sb[:, t:t + 1]),
                 reads=[("ps", 4 * cp + t % 4), ("ubias",)], writes=[h2tok])
        for q in range(4):
            A.dma("gpsimd", shufB[:, 8 * q:8 * q + 8, :], h2_sb[:, 8 * q:8 * q + 8, :],
                  reads=[h2tok], writes=[("shufh", cp)], stream=("shufh", cp))
        for q in range(4):
            A.dma("sync", h_sb[:, 8 * q:8 * q + 8, :], shufA[:, 8 * q:8 * q + 8, :],
                  reads=[("shufh", cp)], writes=[hhtok], stream=hhtok)
        for t in range(32):
            A.op("tensor", lambda e, t=t, h_sb=h_sb, cp=cp: e.matmul(out=psum[4 * cp + t % 4][:, 0:NC],
                 lhsT=MAd[:, t, :], rhs=h_sb[:, t, :], start=True, stop=True),
                 reads=[("M", "Ad"), hhtok], writes=[("ps", 4 * cp + t % 4)])
            A.op("vector", lambda e, t=t, h2_sb=h2_sb, cp=cp: e.tensor_copy(out=h2_sb[:, t, :],
                 in_=psum[4 * cp + t % 4][:, 0:NC]), reads=[("ps", 4 * cp + t % 4)], writes=[h2tok])
        for q in range(4):
            A.dma("gpsimd", shufA[:, 8 * q:8 * q + 8, :], h2_sb[:, 8 * q:8 * q + 8, :],
                  reads=[h2tok], writes=[("shufh", cp)], stream=("shufh", cp))
        for q in range(4):
            A.dma("sync", h_sb[:, 8 * q:8 * q + 8, :], shufB[:, 8 * q:8 * q + 8, :],
                  reads=[("shufh", cp)], writes=[hhtok], stream=hhtok)
        for t in range(32):
            A.op("tensor", lambda e, t=t, h_sb=h_sb, cp=cp: e.matmul(out=psum[4 * cp + t % 4][:, 0:NC],
                 lhsT=MBd[:, t, :], rhs=h_sb[:, t, :], start=True, stop=True),
                 reads=[("M", "Bd"), hhtok], writes=[("ps", 4 * cp + t % 4)])
            A.op("vector", lambda e, t=t, h2_sb=h2_sb, cp=cp: e.tensor_copy(out=h2_sb[:, t, :],
                 in_=psum[4 * cp + t % 4][:, 0:NC]), reads=[("ps", 4 * cp + t % 4)], writes=[h2tok])
        for half in range(2):
            A.dma("sync", x3f[:, half * 4:(half + 1) * 4, :],
                  src_hbm[:, half * 4:(half + 1) * 4, t0:t0 + NC], writes=[("x3f",)])
        def wc_evac(m, ps, pstok):
            A.op("scalar", lambda e, m=m, ps=ps: e.activation(out=ps[:, 0:NC],
                 in_=ps[:, 0:NC], func=AF.Identity, bias=bcon_sb[:, m:m + 1]),
                 reads=[pstok, ("bcon",)], writes=[pstok])
            A.op("vector", lambda e, m=m, ps=ps, cp=cp: e.tensor_tensor(out=x3f[:, m, :],
                 in0=x3f[:, m, :], in1=ps[:, 0:NC], op=OP.add),
                 reads=[pstok, ("x3f",)], writes=[("x3f",)])
        emit_proj(A, w_hbm=wcon_hbm, rhs_fn=lambda k, h2_sb=h2_sb, h2tok=h2tok: (h2_sb[:, k, :], h2tok),
                  kd=32, mt=KD, evac=wc_evac, psum=psum, wring=wring, wrtok="wd" + tag,
                  bank_base=4 * cp, pingpong=False)
        for k in range(KD):
            A.op("vector", lambda e, k=k, o=rbk[k % 2], cp=cp: e.tensor_copy(
                out=o[:], in_=x3f[:, k, :]), reads=[("x3f",)], writes=[("rb", k % 2)])
            A.op("tensor", lambda e, k=k, i=rbk[k % 2], cp=cp: e.matmul(out=psum[4 * cp + 0][0:1, 0:NC],
                 lhsT=ones128b[:, 0:1], rhs=i[:], start=(k == 0), stop=(k == KD - 1)),
                 reads=[("rb", k % 2), ("ones",)], writes=[("ps", 4 * cp + 0)])
        for k in range(KD):
            A.op("scalar", lambda e, k=k, o=rbk[k % 2], cp=cp: e.activation(
                out=o[:], in_=x3f[:, k, :], func=AF.Square),
                reads=[("x3f",)], writes=[("rb", k % 2)])
            A.op("tensor", lambda e, k=k, i=rbk[k % 2], cp=cp: e.matmul(out=psum[4 * cp + 1][0:1, 0:NC],
                 lhsT=ones128b[:, 0:1], rhs=i[:], start=(k == 0), stop=(k == KD - 1)),
                 reads=[("rb", k % 2), ("ones",)], writes=[("ps", 4 * cp + 1)])
        A.op("vector", lambda e, cp=cp: e.tensor_scalar(out=mrow[:, :], in0=psum[4 * cp + 0][0:1, 0:NC],
             scalar1=1.0 / D, scalar2=None, op0=OP.mult), reads=[("ps", 4 * cp + 0)], writes=[("mrow",)])
        A.op("vector", lambda e, cp=cp: e.tensor_scalar(out=vrow[:, :], in0=psum[4 * cp + 1][0:1, 0:NC],
             scalar1=1.0 / D, scalar2=None, op0=OP.mult), reads=[("ps", 4 * cp + 1)], writes=[("vrow",)])
        A.op("vector", lambda e: e.tensor_tensor(out=brow[:, :], in0=mrow[:, :],
             in1=mrow[:, :], op=OP.mult), reads=[("mrow",)], writes=[("brow",)])
        A.op("vector", lambda e: e.tensor_tensor(out=vrow[:, :], in0=vrow[:, :],
             in1=brow[:, :], op=OP.subtract), reads=[("vrow",), ("brow",)], writes=[("vrow",)])
        A.op("scalar", lambda e: e.activation(out=vrow[:, :], in_=vrow[:, :],
             func=AF.Sqrt, bias=eps1_sb[0:1, 0:1]), reads=[("vrow",), ("const",)], writes=[("vrow",)])
        A.op("vector", lambda e: e.reciprocal(out=vrow[:, :], in_=vrow[:, :]),
             reads=[("vrow",)], writes=[("vrow",)])
        A.op("vector", lambda e: e.tensor_tensor(out=brow[:, :], in0=mrow[:, :],
             in1=vrow[:, :], op=OP.mult), reads=[("mrow",), ("vrow",)], writes=[("brow",)])
        A.op("vector", lambda e, cp=cp: e.tensor_scalar(out=brow[:, :], in0=brow[:, :],
             scalar1=-1.0, scalar2=None, op0=OP.mult), reads=[("brow",)], writes=[("brow",)])
        A.op("vector", lambda e, cp=cp: e.tensor_copy(out=arow_bf[:, :], in_=vrow[:, :]),
             reads=[("vrow",)], writes=[("arb",)])
        A.op("vector", lambda e, cp=cp: e.tensor_copy(out=brow_bf[:, :], in_=brow[:, :]),
             reads=[("brow",)], writes=[("brb",)])
        A.op("tensor", lambda e, cp=cp: e.matmul(out=psum[4 * cp + 2][:, 0:NC], lhsT=ones1b[0:1, :],
             rhs=arow_bf[:, :], start=True, stop=True), reads=[("arb",), ("ones",)], writes=[("ps", 4 * cp + 2)])
        A.op("tensor", lambda e, cp=cp: e.matmul(out=psum[4 * cp + 3][:, 0:NC], lhsT=ones1b[0:1, :],
             rhs=brow_bf[:, :], start=True, stop=True), reads=[("brb",), ("ones",)], writes=[("ps", 4 * cp + 3)])
        for k in range(KD):
            A.op("vector", lambda e, k=k, cp=cp: e.tensor_tensor(out=x3f[:, k, :], in0=x3f[:, k, :],
                 in1=psum[4 * cp + 2][:, 0:NC], op=OP.mult),
                 reads=[("x3f",), ("ps", 4 * cp + 2)], writes=[("x3f",)])
            A.op("vector", lambda e, k=k, cp=cp: e.tensor_tensor(out=x3f[:, k, :], in0=x3f[:, k, :],
                 in1=psum[4 * cp + 3][:, 0:NC], op=OP.add),
                 reads=[("x3f",), ("ps", 4 * cp + 3)], writes=[("x3f",)])
        for half in range(2):
            A.dma("gpsimd", dst_hbm[:, half * 4:(half + 1) * 4, t0:t0 + NC],
                  x3f[:, half * 4:(half + 1) * 4, :], reads=[("x3f",)],
                  writes=[("dsth",)], stream=("dst_stream",))


def emit_w2bf(au, es, nc, pairs, tag="w2", bufw=16384, defer=False):
    A = au
    bufs = [es.enter_context(nc.sbuf_tensor(f"{tag}{i}", [128, bufw], BF16)) for i in range(2)]
    units = []
    for src, dst in pairs:
        kk, _, F = src.shape
        step = max(1, bufw // F)
        for k0 in range(0, kk, step):
            units.append((src, dst, k0, min(step, kk - k0), F))

    state = {"i": 0}

    def feed(n=1):
        for _ in range(n):
            i = state["i"]
            if i >= len(units):
                return
            src, dst, k0, ks, F = units[i]
            b = bufs[i % 2]
            tok = (tag + "b", i % 2)
            A.dma("gpsimd", b[:, 0:ks * F].rearrange("p (k f) -> p k f", k=ks),
                  src[k0:k0 + ks, :, :].rearrange("k p f -> p k f"), writes=[tok])
            A.dma("sync", dst[k0:k0 + ks, :, :].rearrange("k p f -> p k f"),
                  b[:, 0:ks * F].rearrange("p (k f) -> p k f", k=ks),
                  reads=[tok], writes=[(tag + "st",)], stream=(tag + "st",))
            state["i"] += 1

    if defer:
        return feed
    feed(len(units))


def emit_proj(au, *, w_hbm, rhs_fn, kd, mt, evac, psum, wring, wrtok, kslab=4,
              group=4, eng="sync", bank_base=0, pingpong=True):
    A = au
    nbuf = len(wring)
    i = 0
    for g in range((mt + group - 1) // group):
        gm = min(group, mt - g * group)
        b0 = bank_base + ((g % 2) * group if pingpong else 0)
        for k0 in range(0, kd, kslab):
            ks = min(kslab, kd - k0)
            wb = wring[i % nbuf]
            tok = (wrtok, i % nbuf)
            A.dma(eng, wb[:, 0:ks * gm * 128].rearrange("p (k f) -> p k f", k=ks),
                  w_hbm[k0:k0 + ks, :, g * group * 128:g * group * 128 + gm * 128]
                  .rearrange("k p f -> p k f"), writes=[tok])
            for dk in range(ks):
                k = k0 + dk
                rhs, rtok = rhs_fn(k)
                for m8 in range(gm):
                    A.op("tensor", lambda e, wb=wb, dk=dk, m8=m8, k=k, rhs=rhs, gm=gm, ks=ks, b0=b0:
                         e.matmul(out=psum[b0 + m8][:, 0:NC],
                                  lhsT=wb[:, 0:ks * gm * 128].rearrange(
                                      "p (k f) -> p k f", k=ks)[:, dk, m8 * 128:(m8 + 1) * 128],
                                  rhs=rhs, start=(k == 0), stop=(k == kd - 1)),
                         reads=[tok, rtok], writes=[("ps", b0 + m8)])
            i += 1
        for m8 in range(gm):
            evac(g * group + m8, psum[b0 + m8], ("ps", b0 + m8))



from contextlib import ExitStack
import ml_dtypes
from concourse.bass_utils import run_bass_kernel_spmd

NCORES = 8
_BUILD_CACHE = {}


def _prep_shared(i):
    s = {}
    perm = rope_perm()
    wq = i["L0_attn_w_qkv"][0:D][perm]
    wk = i["L0_attn_w_qkv"][D:2 * D][perm]
    s["a_wqk"] = kpt(np.ascontiguousarray(np.concatenate([wq, wk], 0).T))
    s["a_bqk"] = pvec(np.concatenate([i["L0_attn_b_qkv"][0:D][perm],
                                      i["L0_attn_b_qkv"][D:2 * D][perm]]))
    s["a_wv"] = kpt(np.ascontiguousarray(i["L0_attn_w_qkv"][2 * D:3 * D].T))
    s["a_bv"] = i["L0_attn_b_qkv"][2 * D:3 * D].reshape(1, 1024).copy()
    s["a_wout"] = kpt(np.ascontiguousarray(i["L0_attn_w_out"].T))
    s["a_bout"] = pvec(i["L0_attn_b_out"])
    cos, sin = rope_tables()
    s["cosT"] = cos.astype(ml_dtypes.bfloat16)
    s["sinT"] = sin.astype(ml_dtypes.bfloat16)
    s["masks"] = causal_masks().astype(ml_dtypes.bfloat16)
    qw = i["L0_ffn_qkv_w"]
    s["f_wqk"] = kpt(np.ascontiguousarray(np.concatenate([qw[0:D], qw[D:2 * D]], 0).T))
    s["f_bqk"] = pvec(np.concatenate([i["L0_ffn_qkv_b"][0:D], i["L0_ffn_qkv_b"][D:2 * D]]))
    s["f_wv"] = kpt(np.ascontiguousarray(qw[2 * D:3 * D].T))
    s["f_bv"] = i["L0_ffn_qkv_b"][2 * D:3 * D].reshape(1, 1024).copy()
    s["f_wout"] = kpt(np.ascontiguousarray(i["L0_ffn_out_w"].T))
    s["f_bout"] = pvec(i["L0_ffn_out_b"])
    s["norm1"] = pvec(i["L0_ffn_norm1_w"])
    s["norm2"] = pvec(i["L0_ffn_norm2_w"])
    s["gate"] = kpt(np.ascontiguousarray(i["L0_ffn_gate_w"].T))
    s["w1"] = np.stack([kpt(np.ascontiguousarray(i["L0_ffn_w1"][e].T)) for e in range(8)])
    s["w2"] = np.stack([kpt(np.ascontiguousarray(i["L0_ffn_w2"][e].T)) for e in range(8)])
    s["w3"] = np.stack([kpt(np.ascontiguousarray(i["L0_ffn_w3"][e].T)) for e in range(8)])
    s["b1"] = np.stack([pvec(i["L0_ffn_b1"][e]) for e in range(8)], axis=1)
    s["b2"] = np.stack([pvec(i["L0_ffn_b2"][e]) for e in range(8)], axis=1)
    s["b3"] = i["L0_ffn_b3"].copy()
    esel = np.zeros((8, 1024), np.float32)
    for e in range(8):
        esel[e, e * 128:(e + 1) * 128] = 1.0
    s["esel"] = esel.astype(ml_dtypes.bfloat16)
    s["ident"] = np.eye(128, dtype=np.float32).astype(ml_dtypes.bfloat16)
    s["m_win"] = kpt(np.ascontiguousarray(i["L1_attn_W_in"]))
    s["m_cw"] = np.ascontiguousarray(i["L1_attn_conv_w"].reshape(KDI, 128, 4).transpose(1, 0, 2))
    s["m_cb"] = pvec(i["L1_attn_conv_b"])
    s["m_wx"] = kpt(np.ascontiguousarray(i["L1_attn_W_x"]))
    s["m_wdt"] = i["L1_attn_W_dt"].copy()
    s["m_bdt"] = pvec(i["L1_attn_b_dt"])
    s["m_alog"] = np.ascontiguousarray(i["L1_attn_A_log"].reshape(KDI, 128, DST).transpose(1, 0, 2))
    s["m_d"] = pvec(i["L1_attn_D"])
    s["m_wout"] = kpt(np.ascontiguousarray(i["L1_attn_W_out"]))
    esel32 = np.zeros((32, 32 * 128), np.float32)
    for n in range(16):
        esel32[n, n * 128:(n + 1) * 128] = 1.0
        esel32[16 + n, (16 + n) * 128:(17 + n) * 128] = 1.0
    s["esel32"] = esel32.astype(ml_dtypes.bfloat16)
    permB = bfly_perm_B()

    def coefpack(wstack, which):
        outs = []
        for ci in range(4):
            a_, b_ = divmod(ci, 2)
            cA, cB = bfly_coeffs(wstack[:, :, a_, b_])
            outs.append(cA if which == "A" else cB)
        arr = np.stack(outs, axis=2)
        arr = arr.transpose(1, 0, 2, 3).reshape(32, 1, -1)
        return np.ascontiguousarray(np.broadcast_to(arr, (32, 128, arr.shape[2]))).astype(ml_dtypes.bfloat16)
    s["d_wexp"] = kpt(np.ascontiguousarray(i["L1_ffn_W_exp"]))
    s["d_bexp"] = pvec(i["L1_ffn_b_exp"])
    s["coefAu"] = coefpack(i["L1_ffn_up_weights"], "A")
    s["coefBu"] = coefpack(i["L1_ffn_up_weights"], "B")
    s["coefAd"] = coefpack(i["L1_ffn_down_weights"], "A")
    s["coefBd"] = coefpack(i["L1_ffn_down_weights"], "B")
    s["d_ubias"] = np.ascontiguousarray(i["L1_ffn_up_bias"][permB].reshape(32, 128).T)
    s["d_wcon"] = kpt(np.ascontiguousarray(i["L1_ffn_W_con"][permB, :]))
    s["d_bcon"] = pvec(i["L1_ffn_b_con"])
    s["d_dbias"] = pvec(i["L1_ffn_down_bias"])
    return s


_BIGW = ("a_wqk", "a_wv", "a_wout", "f_wqk", "f_wv", "f_wout", "w1", "w2", "w3",
         "m_win", "m_wout", "d_wexp", "d_wcon")


_EAGER = ("a_wqk", "a_wv", "a_wout")


def _declare(nc, shared, au, es, defer_rest=False):
    ap = {}
    pairs = []
    dpairs = []
    for k, v in shared.items():
        dt = BF16 if v.dtype == ml_dtypes.bfloat16 else F32
        src_ap = nc.dram_tensor(k, list(v.shape), dt, kind="ExternalInput").ap()
        if k in _BIGW:
            mir = nc.dram_tensor(k + "_bf", list(v.shape), BF16, kind="Internal").ap()
            if len(v.shape) == 4:
                pr = (src_ap.rearrange("e k p f -> (e k) p f"),
                      mir.rearrange("e k p f -> (e k) p f"))
            else:
                pr = (src_ap, mir)
            (pairs if (k in _EAGER or not defer_rest) else dpairs).append(pr)
            ap[k] = mir
        else:
            ap[k] = src_ap
    emit_w2bf(au, es, nc, pairs)
    au.barrier()
    return ap, dpairs


def _common_prolog(au, es, nc, w):
    g = {}
    g["psum"] = [es.enter_context(nc.psum_tensor(f"pb{i}", [128, 512], F32)) for i in range(8)]
    g["ones128b"] = es.enter_context(nc.sbuf_tensor("ones128b", [128, 1], BF16))
    g["ones1b"] = es.enter_context(nc.sbuf_tensor("ones1b", [1, 128], BF16))
    g["ones128f32"] = es.enter_context(nc.sbuf_tensor("ones128f32", [128, 1], F32))
    g["shift_sb"] = es.enter_context(nc.sbuf_tensor("shift_sb", [128, 1], F32))
    g["eps1_sb"] = es.enter_context(nc.sbuf_tensor("eps1_sb", [1, 2], F32))
    au.op("vector", lambda e: e.memset(g["ones128b"][:], 1.0), writes=[("ones",)])
    au.op("vector", lambda e: e.memset(g["ones1b"][:], 1.0), writes=[("ones",)])
    au.op("vector", lambda e: e.memset(g["ones128f32"][:], 1.0), writes=[("onesf",)])
    au.op("vector", lambda e: e.memset(g["shift_sb"][:], -SHIFT), writes=[("const",)])
    au.op("vector", lambda e: e.memset(g["eps1_sb"][0:1, 0:1], LN_EPS), writes=[("const",)])
    au.op("vector", lambda e: e.memset(g["eps1_sb"][0:1, 1:2], RMS_EPS), writes=[("const",)])
    return g


def _emit_block0_attn(au, es, nc, w, g, src, dst, q_hbm, attn_hbm, feed=None):
    bqk_sb = es.enter_context(nc.sbuf_tensor("a_bqk_sb", [128, 16], F32))
    bv_sb = es.enter_context(nc.sbuf_tensor("a_bv_sb", [1, 1024], BF16))
    bout_sb = es.enter_context(nc.sbuf_tensor("a_bout_sb", [128, 8], F32))
    masks_sb = es.enter_context(nc.sbuf_tensor("a_masks_sb", [128, 4, NC], BF16))
    au.dma("sync", bqk_sb[:], w["a_bqk"][:, :], writes=[("bqk",)])
    au.dma("gpsimd", bv_sb[:], w["a_bv"][:, :], writes=[("bv",)])
    au.dma("sync", bout_sb[:], w["a_bout"][:, :], writes=[("bout",)])
    au.dma("sync", masks_sb[:], w["masks"][:, :, :], writes=[("msk",)])
    with ExitStack() as es1:
        K_sb = es1.enter_context(nc.sbuf_tensor("a_K", [128, KD, T], BF16))
        V65 = es1.enter_context(nc.sbuf_tensor("a_V65", [128, T // 128, NH, 65], BF16))
        emit_attn_phase(au, es1, nc, src_hbm=src, rms_gain_sb=None, wqk_hbm=w["a_wqk"],
                        bqk_sb=bqk_sb, wv_hbm=w["a_wv"], bv_sb=bv_sb, causal=True,
                        cos_hbm=w["cosT"], sin_hbm=w["sinT"], masks_sb=masks_sb,
                        q_hbm=q_hbm, attn_hbm=attn_hbm, K_sb=K_sb, V65=V65,
                        psum=g["psum"], ones128b=g["ones128b"], ones1b=g["ones1b"],
                        shift_sb=g["shift_sb"], eps1_sb=g["eps1_sb"], tag="a",
                        feed=feed)
        au.barrier()
    with ExitStack() as es2:
        emit_outproj_ln(au, es2, nc, attn_hbm=attn_hbm, src_hbm=src, dst_hbm=dst,
                        w_hbm=w["a_wout"], b_sb=bout_sb, psum=g["psum"],
                        ones128b=g["ones128b"], ones1b=g["ones1b"],
                        eps1_sb=g["eps1_sb"], ln=True, tag="ao")
        au.barrier()


def _emit_block0_ffn(au, es, nc, w, g, src, dst, q_hbm, attn_hbm, z_hbm):
    bqk_sb = es.enter_context(nc.sbuf_tensor("f_bqk_sb", [128, 16], F32))
    bv_sb = es.enter_context(nc.sbuf_tensor("f_bv_sb", [1, 1024], BF16))
    bout_sb = es.enter_context(nc.sbuf_tensor("f_bout_sb", [128, 8], F32))
    n1_sb = es.enter_context(nc.sbuf_tensor("n1_sb", [128, KD], F32))
    n2_sb = es.enter_context(nc.sbuf_tensor("n2_sb", [128, KD], F32))
    b1_sb = es.enter_context(nc.sbuf_tensor("b1_sb", [128, 8, 16], F32))
    b2_sb = es.enter_context(nc.sbuf_tensor("b2_sb", [128, 8, 16], F32))
    b3_sb = es.enter_context(nc.sbuf_tensor("b3_sb", [8, 1024], BF16))
    esel_sb = es.enter_context(nc.sbuf_tensor("esel_sb", [8, 1024], BF16))
    ident_sb = es.enter_context(nc.sbuf_tensor("ident_sb", [128, 128], BF16))
    au.dma("sync", bqk_sb[:], w["f_bqk"][:, :], writes=[("bqk",)])
    au.dma("gpsimd", bv_sb[:], w["f_bv"][:, :], writes=[("bv",)])
    au.dma("sync", bout_sb[:], w["f_bout"][:, :], writes=[("bout",)])
    au.dma("sync", n1_sb[:], w["norm1"][:, :], writes=[("gain",)])
    au.dma("sync", n2_sb[:], w["norm2"][:, :], writes=[("gain2",)])
    au.dma("sync", b1_sb[:], w["b1"][:, :, :], writes=[("b1",)])
    au.dma("sync", b2_sb[:], w["b2"][:, :, :], writes=[("b2",)])
    au.dma("gpsimd", b3_sb[:], w["b3"][:, :], writes=[("b3",)])
    au.dma("sync", esel_sb[:], w["esel"][:, :], writes=[("esel",)])
    au.dma("sync", ident_sb[:], w["ident"][:, :], writes=[("ident",)])
    with ExitStack() as es1:
        K_sb = es1.enter_context(nc.sbuf_tensor("f_K", [128, KD, T], BF16))
        V65 = es1.enter_context(nc.sbuf_tensor("f_V65", [128, T // 128, NH, 65], BF16))
        emit_attn_phase(au, es1, nc, src_hbm=src, rms_gain_sb=n1_sb, wqk_hbm=w["f_wqk"],
                        bqk_sb=bqk_sb, wv_hbm=w["f_wv"], bv_sb=bv_sb, causal=False,
                        cos_hbm=w["cosT"], sin_hbm=w["sinT"], masks_sb=None,
                        q_hbm=q_hbm, attn_hbm=attn_hbm, K_sb=K_sb, V65=V65,
                        psum=g["psum"], ones128b=g["ones128b"], ones1b=g["ones1b"],
                        shift_sb=g["shift_sb"], eps1_sb=g["eps1_sb"], tag="f")
        au.barrier()
    with ExitStack() as es2:
        emit_outproj_ln(au, es2, nc, attn_hbm=attn_hbm, src_hbm=src, dst_hbm=z_hbm,
                        w_hbm=w["f_wout"], b_sb=bout_sb, psum=g["psum"],
                        ones128b=g["ones128b"], ones1b=g["ones1b"],
                        eps1_sb=g["eps1_sb"], ln=False, tag="fo")
        au.barrier()
    emit_moe_phase(au, es, nc, z_hbm=z_hbm, x1_hbm=src, dst_hbm=dst,
                   rms_gain_sb=n2_sb, gate_hbm=w["gate"], w1_hbm=w["w1"],
                   w2_hbm=w["w2"], w3_hbm=w["w3"], b1_sb=b1_sb, b2_sb=b2_sb,
                   b3_sb=b3_sb, psum=g["psum"], ones128b=g["ones128b"],
                   ones1b=g["ones1b"], esel_sb=esel_sb, ident_bf=ident_sb,
                   eps1_sb=g["eps1_sb"], tag="m")
    au.barrier()


def _emit_block1_attn(au, es, nc, w, g, src, dst):
    cw_sb = es.enter_context(nc.sbuf_tensor("cw_sb", [128, KDI, 4], F32))
    cb_sb = es.enter_context(nc.sbuf_tensor("cb_sb", [128, KDI], F32))
    wx_sb = es.enter_context(nc.sbuf_tensor("wx_sb", [128, KDI, 96], BF16))
    wdt_sb = es.enter_context(nc.sbuf_tensor("wdt_sb", [64, 2048], BF16))
    bdt_sb = es.enter_context(nc.sbuf_tensor("bdt_sb", [128, KDI], F32))
    A_sb = es.enter_context(nc.sbuf_tensor("A_sb", [128, KDI, DST], F32))
    D_sb = es.enter_context(nc.sbuf_tensor("D_sb", [128, KDI], F32))
    esel_sb = es.enter_context(nc.sbuf_tensor("esel32_sb", [32, 32 * 128], BF16))
    au.dma("sync", cw_sb[:], w["m_cw"][:, :, :], writes=[("cw",)])
    au.dma("sync", cb_sb[:], w["m_cb"][:, :], writes=[("cb",)])
    for k in range(KDI):
        au.dma("gpsimd", wx_sb[:, k, :], w["m_wx"][k, :, :], writes=[("wx",)], stream=("wx",))
    au.dma("gpsimd", wdt_sb[:, :], w["m_wdt"][:, :], writes=[("wdt",)])
    au.dma("sync", bdt_sb[:], w["m_bdt"][:, :], writes=[("bdt",)])
    au.dma("sync", A_sb[:], w["m_alog"][:, :, :], writes=[("A",)])
    au.dma("sync", D_sb[:], w["m_d"][:, :], writes=[("D",)])
    au.dma("sync", esel_sb[:], w["esel32"][:, :], writes=[("esel32",)])
    au.op("scalar", lambda e: e.activation(out=A_sb[:, :, :], in_=A_sb[:, :, :],
         func=AF.Exp, scale=1.0), reads=[("A",)], writes=[("A",)])
    au.op("vector", lambda e: e.tensor_scalar(out=A_sb[:, :, :], in0=A_sb[:, :, :],
         scalar1=-1.0, scalar2=None, op0=OP.mult), reads=[("A",)], writes=[("A",)])
    emit_mamba_phase(au, es, nc, src_hbm=src, dst_hbm=dst, win_hbm=w["m_win"],
                     cw_sb=cw_sb, cb_sb=cb_sb, wx_sb=wx_sb, wdt_sb=wdt_sb,
                     bdt_sb=bdt_sb, A_sb=A_sb, D_sb=D_sb, wout_hbm=w["m_wout"],
                     esel32_sb=esel_sb, psum=g["psum"], ones128b=g["ones128b"],
                     ones1b=g["ones1b"], ones128f32=g["ones128f32"],
                     eps1_sb=g["eps1_sb"], tag="c")
    au.barrier()


def _emit_block1_ffn(au, es, nc, w, g, src, dst, shuf_hbm):
    ident_sb = es.enter_context(nc.sbuf_tensor("identd_sb", [128, 128], BF16))
    bexp_sb = es.enter_context(nc.sbuf_tensor("bexp_sb", [128, 32], F32))
    ubias_sb = es.enter_context(nc.sbuf_tensor("ubias_sb", [128, 32], F32))
    bcon_sb = es.enter_context(nc.sbuf_tensor("bcon_sb", [128, 8], F32))
    dbias_sb = es.enter_context(nc.sbuf_tensor("dbias_sb", [128, 8], F32))
    MAu = es.enter_context(nc.sbuf_tensor("MAu", [128, 32, 128], BF16))
    MBu = es.enter_context(nc.sbuf_tensor("MBu", [128, 32, 128], BF16))
    MAd = es.enter_context(nc.sbuf_tensor("MAd", [128, 32, 128], BF16))
    MBd = es.enter_context(nc.sbuf_tensor("MBd", [128, 32, 128], BF16))
    au.dma("sync", ident_sb[:], w["ident"][:, :], writes=[("ident",)])
    au.dma("sync", bexp_sb[:], w["d_bexp"][:, :], writes=[("bexp",)])
    au.dma("sync", ubias_sb[:], w["d_ubias"][:, :], writes=[("ubias",)])
    au.dma("sync", bcon_sb[:], w["d_bcon"][:, :], writes=[("bcon",)])
    au.dma("sync", dbias_sb[:], w["d_dbias"][:, :], writes=[("dbias",)])
    au.op("vector", lambda e: e.tensor_tensor(out=bcon_sb[:], in0=bcon_sb[:],
         in1=dbias_sb[:], op=OP.add), reads=[("bcon",), ("dbias",)], writes=[("bcon",)])
    with ExitStack() as esb:
        for nm, dr, M in (("Au", w["coefAu"], MAu), ("Bu", w["coefBu"], MBu),
                          ("Ad", w["coefAd"], MAd), ("Bd", w["coefBd"], MBd)):
            emit_bfly_build(au, esb, nc, coef_hbm=dr, which=nm[0], M_sb=M,
                            ident_bf=ident_sb, psum=g["psum"], ones1b=g["ones1b"], tag=nm)
        au.barrier()
    with ExitStack() as es1:
        emit_bfly_phase(au, es1, nc, src_hbm=src, dst_hbm=dst, wexp_hbm=w["d_wexp"],
                        bexp_sb=bexp_sb, MAu=MAu, MBu=MBu, MAd=MAd, MBd=MBd,
                        ubias_sb=ubias_sb, wcon_hbm=w["d_wcon"], bcon_sb=bcon_sb,
                        shuf_hbm=shuf_hbm, psum=g["psum"], ones128b=g["ones128b"],
                        ones1b=g["ones1b"], eps1_sb=g["eps1_sb"], tag="d")
        au.barrier()


_LAUNCH_W = {
    "b0a": ("a_wqk", "a_wv", "a_wout", "cosT", "sinT", "masks", "a_bqk", "a_bv", "a_bout"),
    "b0f": ("f_wqk", "f_wv", "f_wout", "w1", "w2", "w3", "cosT", "sinT", "norm1", "norm2",
             "gate", "b1", "b2", "b3", "esel", "ident", "f_bqk", "f_bv", "f_bout"),
    "b1a": ("m_win", "m_cw", "m_cb", "m_wx", "m_wdt", "m_bdt", "m_alog", "m_d", "m_wout", "esel32"),
    "b1f": ("d_wexp", "d_bexp", "coefAu", "coefBu", "coefAd", "coefBd", "d_ubias",
             "d_wcon", "d_bcon", "d_dbias", "ident"),
}


def _build_launch(which, shared):
    key = which
    shared = {k: v for k, v in shared.items() if k in _LAUNCH_W[which]}
    if key in _BUILD_CACHE:
        return _BUILD_CACHE[key]
    nc = bass.Bass("TRN2", target_bir_lowering=False)
    xin = nc.dram_tensor("xin", [KD, 128, T], F32, kind="ExternalInput").ap()
    xout = nc.dram_tensor("xout", [KD, 128, T], F32, kind="ExternalOutput").ap()
    src = xin.rearrange("k p t -> p k t")
    dst = xout.rearrange("k p t -> p k t")
    with ExitStack() as es:
        au = Auto(nc, es)
        with ExitStack() as esw:
            w, _ = _declare(nc, shared, au, esw)
        g = _common_prolog(au, es, nc, w)
        if which == "b0a":
            q_hbm = nc.dram_tensor("q_hbm", [128, KD, T], BF16, kind="Internal").ap()
            attn_hbm = nc.dram_tensor("attn_hbm", [128, KD, T], BF16, kind="Internal").ap()
            _emit_block0_attn(au, es, nc, w, g, src, dst, q_hbm, attn_hbm)
        elif which == "b0f":
            q_hbm = nc.dram_tensor("q_hbm", [128, KD, T], BF16, kind="Internal").ap()
            attn_hbm = nc.dram_tensor("attn_hbm", [128, KD, T], BF16, kind="Internal").ap()
            z_hbm = nc.dram_tensor("z_hbm", [128, KD, T], F32, kind="Internal").ap()
            _emit_block0_ffn(au, es, nc, w, g, src, dst, q_hbm, attn_hbm, z_hbm[:, :, :])
        elif which == "b1a":
            _emit_block1_attn(au, es, nc, w, g, src, dst)
        elif which == "b1f":
            shuf = nc.dram_tensor("shuf", [2, 32, 4, 32, NC], BF16, kind="Internal").ap()
            _emit_block1_ffn(au, es, nc, w, g, src, dst, shuf)
        with nc.Block() as block:
            au.finalize(block)
    _BUILD_CACHE[key] = nc
    return nc


def _build_fused(shared):
    key = "fused"
    if key in _BUILD_CACHE:
        return _BUILD_CACHE[key]
    nc = bass.Bass("TRN2", target_bir_lowering=False)
    xin = nc.dram_tensor("xin", [KD, 128, T], F32, kind="ExternalInput").ap()
    xout = nc.dram_tensor("xout", [KD, 128, T], F32, kind="ExternalOutput").ap()
    x1h = nc.dram_tensor("x1h", [128, KD, T], F32, kind="Internal").ap()
    x2h = nc.dram_tensor("x2h", [128, KD, T], F32, kind="Internal").ap()
    x3h = nc.dram_tensor("x3h", [128, KD, T], F32, kind="Internal").ap()
    q_hbm = nc.dram_tensor("q_hbm", [128, KD, T], BF16, kind="Internal").ap()
    attn_hbm = nc.dram_tensor("attn_hbm", [128, KD, T], BF16, kind="Internal").ap()
    z_hbm = nc.dram_tensor("z_hbm", [128, KD, T], F32, kind="Internal").ap()
    shuf = nc.dram_tensor("shuf", [2, 32, 4, 32, NC], BF16, kind="Internal").ap()
    src = xin.rearrange("k p t -> p k t")
    dst = xout.rearrange("k p t -> p k t")
    with ExitStack() as es:
        au = Auto(nc, es)
        with ExitStack() as esw:
            w, dpairs = _declare(nc, shared, au, esw, defer_rest=True)
        g = _common_prolog(au, es, nc, w)
        with ExitStack() as p1:
            feed = emit_w2bf(au, p1, nc, dpairs, tag="cv", bufw=4096, defer=True)
            _emit_block0_attn(au, p1, nc, w, g, src, x1h[:, :, :], q_hbm, attn_hbm,
                              feed=feed)
            feed(10000)
            au.barrier()
        with ExitStack() as p2:
            _emit_block0_ffn(au, p2, nc, w, g, x1h[:, :, :], x2h[:, :, :],
                             q_hbm, attn_hbm, z_hbm[:, :, :])
        with ExitStack() as p3:
            _emit_block1_attn(au, p3, nc, w, g, x2h[:, :, :], x3h[:, :, :])
        with ExitStack() as p4:
            _emit_block1_ffn(au, p4, nc, w, g, x3h[:, :, :], dst, shuf)
            au.barrier()
        with nc.Block() as block:
            au.finalize(block)
    _BUILD_CACHE[key] = nc
    return nc


FUSED = True


def kernel(**inputs):
    x = inputs["L0_attn_x"]
    shared = _prep_shared(inputs)
    xs = [np.ascontiguousarray(x[c].T).reshape(KD, 128, T) for c in range(NCORES)]
    if FUSED:
        nc = _build_fused(shared)
        in_maps = [dict(shared, xin=xs[c]) for c in range(NCORES)]
        res = run_bass_kernel_spmd(nc, in_maps, core_ids=list(range(NCORES)))
        outs = [r["xout"] for r in res.results]
    else:
        cur = xs
        for which in ("b0a", "b0f", "b1a", "b1f"):
            nc = _build_launch(which, shared)
            sh = {k: v for k, v in shared.items() if k in _LAUNCH_W[which]}
            in_maps = [dict(sh, xin=cur[c]) for c in range(NCORES)]
            res = run_bass_kernel_spmd(nc, in_maps, core_ids=list(range(NCORES)))
            cur = [r["xout"] for r in res.results]
        outs = cur
    full = np.stack([o.reshape(D, T).T for o in outs])
    return np.ascontiguousarray(full.astype(np.float32))


# version 23 (final)
# speedup vs baseline: 1.1125x; 1.0432x over previous
import os
from contextlib import ExitStack as ExitStack2
import numpy as np
import concourse.bass as bass
import concourse.mybir as mybir
from concourse.alu_op_type import AluOpType as OP

F32 = mybir.dt.float32
BF16 = mybir.dt.bfloat16
AF = mybir.ActivationFunctionType
AX = mybir.AxisListType

T = 4096
D = 1024
KD = D // 128
NC = 512
NCH = T // NC
NH = 16
DH = 64
SHIFT = 12.0
LN_EPS = 1e-5
RMS_EPS = 1e-6

ENGS = ("sync", "tensor", "vector", "scalar", "gpsimd")


class _Op:
    __slots__ = ("eng", "fn", "reads", "writes", "is_async", "weight", "idx",
                 "deps", "marked", "cum", "vc", "marked_fence", "stream")

    def __init__(self, eng, fn, reads, writes, is_async, weight, idx):
        self.eng = eng
        self.fn = fn
        self.reads = reads
        self.writes = writes
        self.is_async = is_async
        self.weight = weight
        self.idx = idx
        self.deps = []
        self.marked = False
        self.cum = 0
        self.vc = None
        self.marked_fence = False
        self.stream = None


class Auto:

    def __init__(self, nc, es):
        self.nc = nc
        self.es = es
        self.ops = []
        self.tok = {}
        self.last_on = {e: None for e in ENGS}

    def op(self, eng, fn, reads=(), writes=(), is_async=False, weight=1, _fence=False):
        o = _Op(eng, fn, tuple(reads), tuple(writes), is_async, weight, len(self.ops))
        deps = set()
        for t in o.reads:
            st = self.tok.get(t)
            if st and st["w"] is not None:
                deps.add(st["w"])
        for t in o.writes:
            st = self.tok.get(t)
            if st:
                if st["w"] is not None:
                    deps.add(st["w"])
                for r in st["r"]:
                    deps.add(r)
        for d in deps:
            if d.eng != eng or d.is_async or is_async or eng != "tensor":
                o.deps.append(d)
        for t in o.reads:
            self.tok.setdefault(t, {"w": None, "r": []})["r"].append(o)
        for t in o.writes:
            self.tok[t] = {"w": o, "r": []}
        self.ops.append(o)
        if not _fence:
            self.last_on[eng] = o
        return o

    def dma(self, eng, out, in_, reads=(), writes=(), stream=None, **kw):
        fn = lambda e, o=out, i=in_, kw=kw: e.dma_start(out=o, in_=i, **kw)
        o = self.op(eng, fn, reads=reads, writes=writes, is_async=True, weight=16)
        o.stream = stream if stream is not None else (o.writes[0] if o.writes else ("anon",))
        return o

    def barrier(self):
        prior = [o for o in (self.last_on[e] for e in ENGS) if o is not None]
        outstanding = [o for o in self.ops if o.is_async and not o.marked_fence]
        for e in ENGS:
            o = self.op(e, lambda eng: eng.nop(), _fence=True)
            o.deps = [d for d in set(prior + outstanding) if d is not o]
        for o in outstanding:
            o.marked_fence = True

    def finalize(self, block):
        def chan(o):
            if o.is_async:
                return ("d", o.eng, o.stream)
            return ("e", o.eng)

        chans = []
        seen = set()
        for o in self.ops:
            c = chan(o)
            if c not in seen:
                seen.add(c)
                chans.append(c)
        for e in ENGS:
            if ("e", e) not in seen:
                seen.add(("e", e))
                chans.append(("e", e))
        sems = {c: self.es.enter_context(self.nc.semaphore(f"pg{i}"))
                for i, c in enumerate(chans)}
        for o in self.ops:
            for d in o.deps:
                d.marked = True
        cum = {c: 0 for c in chans}
        for o in self.ops:
            c = chan(o)
            if o.marked:
                cum[c] += o.weight
            o.cum = cum[c]
        eng_vc = {e: {c: 0 for c in chans} for e in ENGS}
        op_vc = {}
        emit = {e: [] for e in ENGS}
        for o in self.ops:
            need = {}
            for d in o.deps:
                dc = chan(d)
                if d.cum > need.get(dc, 0):
                    need[dc] = d.cum
                dvc = op_vc.get(d.idx)
                if dvc:
                    for c2, v2 in dvc.items():
                        if v2 > need.get(c2, 0):
                            need[c2] = v2
            mycl = eng_vc[o.eng]
            own = ("e", o.eng)
            for c2, v in sorted(need.items(), key=str):
                if c2 == own and o.eng == "tensor" and not o.is_async:
                    continue
                if v > mycl[c2]:
                    emit[o.eng].append(("wait", c2, v))
                    mycl[c2] = v
            emit[o.eng].append(("op", o))
            oc = chan(o)
            if not o.is_async and o.eng == "tensor":
                mycl[oc] = o.cum
            vc = dict(mycl)
            if o.is_async:
                vc[oc] = o.cum
            op_vc[o.idx] = vc

        def run(e, eng_name, seq=emit, sems=sems):
            for item in seq[eng_name]:
                if item[0] == "wait":
                    e.wait_ge(sems[item[1]], item[2])
                else:
                    o = item[1]
                    inst = o.fn(e)
                    if o.marked:
                        s = sems[chan(o)]
                        if inst is None:
                            e.sem_inc(s, o.weight)
                        else:
                            inst.then_inc(s, o.weight)
        for eng_name in ENGS:
            getattr(block, eng_name)(lambda e, en=eng_name: run(e, en))
        self.n_chans = len(chans)
        return emit

    def stats(self):
        n_wait = 0
        return {"ops": len(self.ops), "marked": sum(o.marked for o in self.ops)}



def kpt(w_t):
    fin, fout = w_t.shape
    return np.ascontiguousarray(w_t.reshape(fin // 128, 128, fout))


def pvec(v):
    n = v.shape[0]
    return np.ascontiguousarray(v.reshape(n // 128, 128).T)


def rope_perm():
    perm = np.zeros(D, dtype=np.int64)
    for h in range(NH):
        for k in range(32):
            perm[h * 64 + k] = h * 64 + 2 * k
            perm[h * 64 + 32 + k] = h * 64 + 2 * k + 1
    return perm


def rope_tables():
    inv_freq = 1.0 / 10000.0 ** (np.arange(0, DH, 2, dtype=np.float64) / DH)
    ang = np.outer(inv_freq, np.arange(T, dtype=np.float64))
    cos32, sin32 = np.cos(ang), np.sin(ang)
    cos128 = np.tile(cos32, (4, 1)).astype(np.float32)
    sin128 = np.tile(sin32, (4, 1)).astype(np.float32)
    sgn = np.ones((128, 1), np.float32)
    for b in (0, 2):
        sgn[b * 32:(b + 1) * 32] = -1.0
    return cos128, sin128 * sgn


def causal_masks():
    p = np.arange(128)[:, None]
    f = np.arange(NC)[None, :]
    ms = [(p + 128 * dd <= f).astype(np.float32) for dd in range(4)]
    return np.ascontiguousarray(np.stack(ms, axis=1))



def emit_attn_phase(au, es, nc, *, src_hbm, rms_gain_sb, wqk_hbm, bqk_sb, wv_hbm,
                    bv_sb, causal, cos_hbm, sin_hbm, masks_sb, q_hbm, attn_hbm,
                    K_sb, V65, psum, ones128b, ones1b, shift_sb, eps1_sb, tag="", a2=True,
                    feed=None):
    A = au
    q_chunk = es.enter_context(nc.sbuf_tensor(f"qch{tag}", [128, KD, NC], BF16))
    es1 = es.enter_context(ExitStack2())
    nxb = 1 if rms_gain_sb is not None else 2
    x_bf = [es1.enter_context(nc.sbuf_tensor(f"xbf{tag}{i}", [128, KD, NC], BF16)) for i in range(nxb)]
    nwr = 2 if rms_gain_sb is not None else 3
    wring = [es1.enter_context(nc.sbuf_tensor(f"wr{tag}{i}", [128, 2048], BF16)) for i in range(nwr)]
    vring = [es1.enter_context(nc.sbuf_tensor(f"vr{tag}{i}", [128, 1024], BF16)) for i in range(2)]
    qk_ev = [es1.enter_context(nc.sbuf_tensor(f"qk_ev{tag}{i}", [128, NC], BF16)) for i in range(2)]
    partner = [es1.enter_context(nc.sbuf_tensor(f"prt{tag}{i}", [128, NC], BF16)) for i in range(2)]
    cos_sb = es1.enter_context(nc.sbuf_tensor(f"cos{tag}", [128, NC], BF16))
    sin_sb = es1.enter_context(nc.sbuf_tensor(f"sin{tag}", [128, NC], BF16))
    bv_rep = es1.enter_context(nc.sbuf_tensor(f"bvrep{tag}", [128, 1024], BF16))
    if rms_gain_sb is not None:
        x_f32 = es1.enter_context(nc.sbuf_tensor(f"xf32{tag}", [128, KD, NC], F32))
        x2b = [es1.enter_context(nc.sbuf_tensor(f"x2b{tag}{i}", [128, NC], BF16)) for i in range(2)]
        srow = es1.enter_context(nc.sbuf_tensor(f"srow{tag}", [1, NC], F32))
        srow_bf = es1.enter_context(nc.sbuf_tensor(f"srowb{tag}", [1, NC], BF16))

    A.op("vector", lambda e: e.memset(V65[:, :, :, 64:65], 1.0), writes=[("V",)])
    for half in range(2):
        A.op("tensor", lambda e, h=half: e.matmul(out=psum[6][:, 0:NC], lhsT=ones1b[0:1, :],
             rhs=bv_sb[0:1, h * 512:(h + 1) * 512], start=True, stop=True),
             reads=[("bv",), ("ones",)], writes=[("ps", 6)])
        A.op("vector", lambda e, h=half: e.tensor_copy(out=bv_rep[:, h * 512:(h + 1) * 512],
             in_=psum[6][:, 0:NC]), reads=[("ps", 6)], writes=[("bvrep",)])

    for c in range(NCH):
        if feed:
            feed(2)
        t0 = c * NC
        xb, xtok = x_bf[c % len(x_bf)], ("xbf", c % len(x_bf))
        if rms_gain_sb is None:
            for half in range(2):
                A.dma("gpsimd", xb[:, half * 4:(half + 1) * 4, :],
                      src_hbm[:, half * 4:(half + 1) * 4, t0:t0 + NC], writes=[xtok],
                      stream=xtok)
        else:
            for half in range(2):
                A.dma("sync", x_f32[:, half * 4:(half + 1) * 4, :],
                      src_hbm[:, half * 4:(half + 1) * 4, t0:t0 + NC], writes=[("xf32",)],
                      stream=("xf32",))
            for k in range(KD):
                A.op("vector", lambda e, k=k, o=x2b[k % 2]: e.tensor_tensor(
                    out=o[:], in0=x_f32[:, k, :], in1=x_f32[:, k, :], op=OP.mult),
                    reads=[("xf32",)], writes=[("x2b", k % 2)])
                A.op("tensor", lambda e, k=k, i=x2b[k % 2]: e.matmul(
                    out=psum[7][0:1, 0:NC], lhsT=ones128b[:, 0:1], rhs=i[:],
                    start=(k == 0), stop=(k == KD - 1)),
                    reads=[("x2b", k % 2), ("ones",)], writes=[("ps", 7)])
            A.op("vector", lambda e: e.tensor_scalar(out=srow[:, :], in0=psum[7][0:1, 0:NC],
                 scalar1=1.0 / D, scalar2=None, op0=OP.mult),
                 reads=[("ps", 7)], writes=[("srow",)])
            A.op("scalar", lambda e: e.activation(out=srow[:, :], in_=srow[:, :],
                 func=AF.Sqrt, bias=eps1_sb[0:1, 1:2]), reads=[("srow",), ("const",)], writes=[("srow",)])
            A.op("vector", lambda e: e.reciprocal(out=srow[:, :], in_=srow[:, :]),
                 reads=[("srow",)], writes=[("srow",)])
            A.op("vector", lambda e: e.tensor_copy(out=srow_bf[:, :], in_=srow[:, :]),
                 reads=[("srow",)], writes=[("srowb",)])
            A.op("tensor", lambda e: e.matmul(out=psum[6][:, 0:NC], lhsT=ones1b[0:1, :],
                 rhs=srow_bf[:, :], start=True, stop=True),
                 reads=[("srowb",), ("ones",)], writes=[("ps", 6)])
            for k in range(KD):
                A.op("vector", lambda e, k=k, xb=xb: e.scalar_tensor_tensor(
                    out=xb[:, k, :], in0=x_f32[:, k, :], scalar=rms_gain_sb[:, k:k + 1],
                    in1=psum[6][:, 0:NC], op0=OP.mult, op1=OP.mult),
                    reads=[("xf32",), ("ps", 6), ("gain",)], writes=[xtok])

        A.dma("sync", cos_sb[:], cos_hbm[:, t0:t0 + NC], writes=[("cos",)], stream=("cos",))
        A.dma("sync", sin_sb[:], sin_hbm[:, t0:t0 + NC], writes=[("sin",)], stream=("sin",))

        def qk_evac(m, ps, pstok, xb=xb, t0=t0):
            bi = m % 2
            A.op("scalar", lambda e, m=m, bi=bi, ps=ps: e.activation(
                out=qk_ev[bi][:], in_=ps[:, 0:NC], func=AF.Identity,
                bias=bqk_sb[:, m:m + 1], scale=1.0),
                reads=[pstok, ("bqk",)], writes=[("qk_ev", bi)])
            for b0, b1 in ((0, 32), (32, 0), (64, 96), (96, 64)):
                A.op("vector", lambda e, b0=b0, b1=b1, bi=bi: e.tensor_copy(
                    out=partner[bi][b0:b0 + 32, :], in_=qk_ev[bi][b1:b1 + 32, :]),
                    reads=[("qk_ev", bi)], writes=[("prt", bi)])
            A.op("vector", lambda e, bi=bi: e.tensor_tensor(out=qk_ev[bi][:], in0=qk_ev[bi][:],
                 in1=cos_sb[:], op=OP.mult),
                 reads=[("qk_ev", bi), ("cos",)], writes=[("qk_ev", bi)])
            A.op("vector", lambda e, bi=bi: e.tensor_tensor(out=partner[bi][:], in0=partner[bi][:],
                 in1=sin_sb[:], op=OP.mult),
                 reads=[("prt", bi), ("sin",)], writes=[("prt", bi)])
            if m < 8:
                A.op("vector", lambda e, m=m, bi=bi: e.tensor_tensor(out=q_chunk[:, m, :],
                     in0=qk_ev[bi][:], in1=partner[bi][:], op=OP.add),
                     reads=[("qk_ev", bi), ("prt", bi)], writes=[("qch",)])
            else:
                A.op("vector", lambda e, m=m, bi=bi, t0=t0: e.tensor_tensor(
                    out=K_sb[:, m - 8, t0:t0 + NC], in0=qk_ev[bi][:], in1=partner[bi][:], op=OP.add),
                    reads=[("qk_ev", bi), ("prt", bi)], writes=[("K",)])
        emit_proj(A, w_hbm=wqk_hbm, rhs_fn=lambda k, xb=xb, xtok=xtok: (xb[:, k, :], xtok),
                  kd=KD, mt=16, evac=qk_evac, psum=psum, wring=wring, wrtok="wr" + tag)
        A.dma("gpsimd", q_hbm[:, :, t0:t0 + NC], q_chunk[:, :, :],
              reads=[("qch",)], writes=[("qhbm",)], stream=("qhbm",))

        for k0 in range(0, KD, 1):
            vb = vring[k0 % 2]
            vtok = ("vrk", k0 % 2)
            A.dma("sync", vb[:, :], wv_hbm[k0, :, :], writes=[vtok])
            for dk in range(1):
                k = k0 + dk
                for tsub in range(4):
                    for half in range(2):
                        A.op("tensor", lambda e, k=k, dk=dk, tsub=tsub, half=half, vb=vb, xb=xb:
                             e.matmul(out=psum[tsub * 2 + half][:, 0:NC],
                                      lhsT=xb[:, k, tsub * 128:(tsub + 1) * 128],
                                      rhs=vb[:, half * 512:(half + 1) * 512],
                                      start=(k == 0), stop=(k == KD - 1)),
                             reads=[vtok, xtok], writes=[("ps", tsub * 2 + half)])
        for tsub in range(4):
            j = 4 * c + tsub
            for half in range(2):
                b = tsub * 2 + half
                A.op("vector", lambda e, j=j, half=half, b=b: e.tensor_tensor(
                    out=V65[:, j, half * 8:(half + 1) * 8, 0:64],
                    in0=psum[b][:, 0:NC].rearrange("p (h d) -> p h d", h=8),
                    in1=bv_rep[:, half * 512:(half + 1) * 512].rearrange("p (h d) -> p h d", h=8),
                    op=OP.add),
                    reads=[("ps", b), ("bvrep",)], writes=[("V",)])

    A.barrier()
    es1.close()
    if not a2:
        return
    expS = [[es.enter_context(nc.sbuf_tensor(f"expS{tag}{cc}{i}", [128, NC], BF16))
             for i in range(2)] for cc in range(2)]
    recip_row = [es.enter_context(nc.sbuf_tensor(f"recip{tag}{i}", [1, NC], F32)) for i in range(2)]
    recip_bf = [es.enter_context(nc.sbuf_tensor(f"recipb{tag}{i}", [1, NC], BF16)) for i in range(2)]
    rep_sb = [es.enter_context(nc.sbuf_tensor(f"rep{tag}{i}", [64, NC], F32)) for i in range(2)]
    ao_chunk = es.enter_context(nc.sbuf_tensor(f"ao{tag}", [128, KD, 2 * NC], BF16))
    qp_sb = es.enter_context(nc.sbuf_tensor(f"qp{tag}", [128, KD, 2 * NC], BF16))
    for cp in range(NCH // 2):
        t0 = cp * 2 * NC
        A.dma("sync", qp_sb[:, :, :], q_hbm[:, :, t0:t0 + 2 * NC],
              reads=[("qhbm",)], writes=[("qch",)], stream=("qch",))
        jmaxs = (8 * cp + 3, 8 * cp + 7) if causal else (T // 128 - 1, T // 128 - 1)
        jall = jmaxs[1]
        for h in range(NH):
            if feed:
                feed(2)
            hb, mq = 64 * (h % 2), h // 2
            scb = {0: (0, 1), 1: (4, 5)}
            ub = {0: 2, 1: 6}

            def sc(j, cc, hb=hb, mq=mq, cp=cp):
                bank = scb[cc][j % 2]
                A.op("tensor", lambda e, j=j, cc=cc, hb=hb, mq=mq, bank=bank: e.matmul(
                    out=psum[bank][:, 0:NC], lhsT=K_sb[hb:hb + 64, mq, j * 128:(j + 1) * 128],
                    rhs=qp_sb[hb:hb + 64, mq, cc * NC:(cc + 1) * NC], start=True, stop=True,
                    skip_group_check=True),
                    reads=[("K",), ("qch",)], writes=[("ps", bank)])
                A.op("scalar", lambda e, j=j, cc=cc, bank=bank: e.activation(
                    out=expS[cc][j % 2][:], in_=psum[bank][:, 0:NC], func=AF.Exp,
                    scale=0.125, bias=shift_sb[:, 0:1]),
                    reads=[("ps", bank), ("const",)], writes=[("expS", cc, j % 2)])
                if causal and j >= 8 * cp + 4 * cc:
                    dd = j - (8 * cp + 4 * cc)
                    A.op("vector", lambda e, j=j, cc=cc, dd=dd: e.tensor_tensor(
                        out=expS[cc][j % 2][:], in0=expS[cc][j % 2][:], in1=masks_sb[:, dd, :],
                        op=OP.mult),
                        reads=[("expS", cc, j % 2), ("msk",)], writes=[("expS", cc, j % 2)])

            def pv(j, cc, h=h):
                A.op("tensor", lambda e, j=j, cc=cc, h=h, st=(j == 0), sp=(j == jmaxs[cc]):
                     e.matmul(
                     out=psum[ub[cc]][0:65, 0:NC], lhsT=V65[:, j, h, 0:65],
                     rhs=expS[cc][j % 2][:],
                     start=st, stop=sp, skip_group_check=True),
                     reads=[("V",), ("expS", cc, j % 2)], writes=[("ps", ub[cc])])

            def valid(j, cc):
                return j <= jmaxs[cc]
            sc(0, 0)
            if valid(0, 1):
                sc(0, 1)
            for j in range(1, jall + 1):
                for cc in range(2):
                    if valid(j, cc):
                        sc(j, cc)
                for cc in range(2):
                    if valid(j - 1, cc):
                        pv(j - 1, cc)
            for cc in range(2):
                if jmaxs[cc] == jall:
                    pv(jmaxs[cc], cc)
            for cc in range(2):
                rb = 3 + 4 * cc
                A.op("vector", lambda e, cc=cc: e.reciprocal(out=recip_row[cc][:, :],
                     in_=psum[ub[cc]][64:65, 0:NC]),
                     reads=[("ps", ub[cc])], writes=[("recip", cc)])
                A.op("vector", lambda e, cc=cc: e.tensor_copy(out=recip_bf[cc][:, :],
                     in_=recip_row[cc][:, :]),
                     reads=[("recip", cc)], writes=[("recipb", cc)])
                A.op("tensor", lambda e, cc=cc, rb=rb: e.matmul(out=psum[rb][0:64, 0:NC],
                     lhsT=ones1b[0:1, 0:64], rhs=recip_bf[cc][:, :], start=True, stop=True,
                     skip_group_check=True),
                     reads=[("recipb", cc), ("ones",)], writes=[("ps", rb)])
                A.op("vector", lambda e, cc=cc, rb=rb: e.tensor_copy(out=rep_sb[cc][:, :],
                     in_=psum[rb][0:64, 0:NC]),
                     reads=[("ps", rb)], writes=[("repb", cc)])
                A.op("vector", lambda e, hb=hb, mq=mq, cc=cc: e.tensor_tensor(
                    out=ao_chunk[hb:hb + 64, mq, cc * NC:(cc + 1) * NC],
                    in0=psum[ub[cc]][0:64, 0:NC], in1=rep_sb[cc][:, :],
                    op=OP.mult),
                    reads=[("ps", ub[cc]), ("repb", cc)], writes=[("ao",)])
        A.dma("gpsimd", attn_hbm[:, :, t0:t0 + 2 * NC], ao_chunk[:, :, :],
              reads=[("ao",)], writes=[("ahbm",)], stream=("ahbm",))


def emit_outproj_ln(au, es, nc, *, attn_hbm, src_hbm, dst_hbm, w_hbm, b_sb,
                    psum, ones128b, ones1b, eps1_sb, ln=True, tag="o"):
    A = au
    at_ch = es.enter_context(nc.sbuf_tensor(f"atch{tag}", [128, KD, NC], BF16))
    x_ch = es.enter_context(nc.sbuf_tensor(f"xch{tag}", [128, KD, NC], F32))
    r_ch = es.enter_context(nc.sbuf_tensor(f"rch{tag}", [128, KD, NC], F32))
    wring = [es.enter_context(nc.sbuf_tensor(f"wo{tag}{i}", [128, 2048], BF16)) for i in range(4)]
    x2b = [es.enter_context(nc.sbuf_tensor(f"x2o{tag}{i}", [128, NC], BF16)) for i in range(2)]
    mrow = es.enter_context(nc.sbuf_tensor(f"mrow{tag}", [1, NC], F32))
    vrow = es.enter_context(nc.sbuf_tensor(f"vrow{tag}", [1, NC], F32))
    brow = es.enter_context(nc.sbuf_tensor(f"brow{tag}", [1, NC], F32))
    arow_bf = es.enter_context(nc.sbuf_tensor(f"arowb{tag}", [1, NC], BF16))
    brow_bf = es.enter_context(nc.sbuf_tensor(f"browb{tag}", [1, NC], BF16))
    tmp = es.enter_context(nc.sbuf_tensor(f"tmpo{tag}", [128, NC], F32))

    for c in range(NCH):
        t0 = c * NC
        A.dma("sync", at_ch[:, :, :], attn_hbm[:, :, t0:t0 + NC], writes=[("atch",)],
              stream=("atch",))
        for half in range(2):
            A.dma("sync", x_ch[:, half * 4:(half + 1) * 4, :],
                  src_hbm[:, half * 4:(half + 1) * 4, t0:t0 + NC], writes=[("xch",)],
                  stream=("xch",))

        def op_evac(m, ps, pstok):
            A.op("scalar", lambda e, m=m, ps=ps: e.activation(out=ps[:, 0:NC],
                 in_=ps[:, 0:NC], func=AF.Identity, bias=b_sb[:, m:m + 1]),
                 reads=[pstok, ("bout",)], writes=[pstok])
            A.op("vector", lambda e, m=m, ps=ps: e.tensor_tensor(out=r_ch[:, m, :],
                 in0=ps[:, 0:NC], in1=x_ch[:, m, :], op=OP.add),
                 reads=[pstok, ("xch",)], writes=[("rch",)])
        emit_proj(A, w_hbm=w_hbm, rhs_fn=lambda k: (at_ch[:, k, :], ("atch",)),
                  kd=KD, mt=KD, evac=op_evac, psum=psum, wring=wring, wrtok="wo" + tag)
        if not ln:
            for half in range(2):
                A.dma("gpsimd", dst_hbm[:, half * 4:(half + 1) * 4, t0:t0 + NC],
                      r_ch[:, half * 4:(half + 1) * 4, :], reads=[("rch",)],
                      writes=[("dsth",)], stream=("dst_stream",))
            continue
        for k in range(KD):
            A.op("vector", lambda e, k=k, o=x2b[k % 2]: e.tensor_copy(
                out=o[:], in_=r_ch[:, k, :]), reads=[("rch",)], writes=[("x2o", k % 2)])
            A.op("tensor", lambda e, k=k, i=x2b[k % 2]: e.matmul(out=psum[0][0:1, 0:NC],
                 lhsT=ones128b[:, 0:1], rhs=i[:], start=(k == 0), stop=(k == KD - 1)),
                 reads=[("x2o", k % 2), ("ones",)], writes=[("ps", 0)])
        for k in range(KD):
            A.op("scalar", lambda e, k=k, o=x2b[k % 2]: e.activation(
                out=o[:], in_=r_ch[:, k, :], func=AF.Square),
                reads=[("rch",)], writes=[("x2o", k % 2)])
            A.op("tensor", lambda e, k=k, i=x2b[k % 2]: e.matmul(out=psum[1][0:1, 0:NC],
                 lhsT=ones128b[:, 0:1], rhs=i[:], start=(k == 0), stop=(k == KD - 1)),
                 reads=[("x2o", k % 2), ("ones",)], writes=[("ps", 1)])
        A.op("vector", lambda e: e.tensor_scalar(out=mrow[:, :], in0=psum[0][0:1, 0:NC],
             scalar1=1.0 / D, scalar2=None, op0=OP.mult),
             reads=[("ps", 0)], writes=[("mrow",)])
        A.op("vector", lambda e: e.tensor_scalar(out=vrow[:, :], in0=psum[1][0:1, 0:NC],
             scalar1=1.0 / D, scalar2=None, op0=OP.mult),
             reads=[("ps", 1)], writes=[("vrow",)])
        A.op("vector", lambda e: e.tensor_tensor(out=brow[:, :], in0=mrow[:, :],
             in1=mrow[:, :], op=OP.mult), reads=[("mrow",)], writes=[("brow",)])
        A.op("vector", lambda e: e.tensor_tensor(out=vrow[:, :], in0=vrow[:, :],
             in1=brow[:, :], op=OP.subtract), reads=[("vrow",), ("brow",)], writes=[("vrow",)])
        A.op("scalar", lambda e: e.activation(out=vrow[:, :], in_=vrow[:, :],
             func=AF.Sqrt, bias=eps1_sb[0:1, 0:1]), reads=[("vrow",), ("const",)], writes=[("vrow",)])
        A.op("vector", lambda e: e.reciprocal(out=vrow[:, :], in_=vrow[:, :]),
             reads=[("vrow",)], writes=[("vrow",)])
        A.op("vector", lambda e: e.tensor_tensor(out=brow[:, :], in0=mrow[:, :],
             in1=vrow[:, :], op=OP.mult), reads=[("mrow",), ("vrow",)], writes=[("brow",)])
        A.op("vector", lambda e: e.tensor_scalar(out=brow[:, :], in0=brow[:, :],
             scalar1=-1.0, scalar2=None, op0=OP.mult), reads=[("brow",)], writes=[("brow",)])
        A.op("vector", lambda e: e.tensor_copy(out=arow_bf[:, :], in_=vrow[:, :]),
             reads=[("vrow",)], writes=[("arowb",)])
        A.op("vector", lambda e: e.tensor_copy(out=brow_bf[:, :], in_=brow[:, :]),
             reads=[("brow",)], writes=[("browb",)])
        A.op("tensor", lambda e: e.matmul(out=psum[2][:, 0:NC], lhsT=ones1b[0:1, :],
             rhs=arow_bf[:, :], start=True, stop=True), reads=[("arowb",), ("ones",)], writes=[("ps", 2)])
        A.op("tensor", lambda e: e.matmul(out=psum[3][:, 0:NC], lhsT=ones1b[0:1, :],
             rhs=brow_bf[:, :], start=True, stop=True), reads=[("browb",), ("ones",)], writes=[("ps", 3)])
        for k in range(KD):
            A.op("vector", lambda e, k=k: e.tensor_tensor(out=tmp[:], in0=r_ch[:, k, :],
                 in1=psum[2][:, 0:NC], op=OP.mult),
                 reads=[("rch",), ("ps", 2)], writes=[("tmpo",)])
            A.op("vector", lambda e, k=k: e.tensor_tensor(out=r_ch[:, k, :], in0=tmp[:],
                 in1=psum[3][:, 0:NC], op=OP.add),
                 reads=[("tmpo",), ("ps", 3)], writes=[("rch",)])
        for half in range(2):
            A.dma("gpsimd", dst_hbm[:, half * 4:(half + 1) * 4, t0:t0 + NC],
                  r_ch[:, half * 4:(half + 1) * 4, :], reads=[("rch",)], writes=[("dsth",)],
                  stream=("dst_stream",))


def emit_moe_phase(au, es, nc, *, z_hbm, x1_hbm, dst_hbm, rms_gain_sb, gate_hbm,
                   w1_hbm, w2_hbm, w3_hbm, b1_sb, b2_sb, b3_sb, psum,
                   ones128b, ones1b, esel_sb, ident_bf, eps1_sb, tag="m"):
    A = au
    NE, HD = 8, 2048
    z_ch = es.enter_context(nc.sbuf_tensor(f"zch{tag}", [128, KD, NC], F32))
    x1_ch = es.enter_context(nc.sbuf_tensor(f"x1ch{tag}", [128, KD, NC], F32))
    zn_bf = es.enter_context(nc.sbuf_tensor(f"znbf{tag}", [128, KD, NC], BF16))
    x2b = [es.enter_context(nc.sbuf_tensor(f"x2{tag}{i}", [128, NC], BF16)) for i in range(2)]
    srow = es.enter_context(nc.sbuf_tensor(f"srow{tag}", [1, NC], F32))
    srow_bf = es.enter_context(nc.sbuf_tensor(f"srowb{tag}", [1, NC], BF16))
    gate_sb = es.enter_context(nc.sbuf_tensor(f"gate{tag}", [128, KD, 8], BF16))
    glog = es.enter_context(nc.sbuf_tensor(f"glog{tag}", [128, 8], F32))
    gtmp = es.enter_context(nc.sbuf_tensor(f"gtmp{tag}", [128, 8], F32))
    gm1 = es.enter_context(nc.sbuf_tensor(f"gm1{tag}", [128, 1], F32))
    gm2 = es.enter_context(nc.sbuf_tensor(f"gm2{tag}", [128, 1], F32))
    comb_bf = es.enter_context(nc.sbuf_tensor(f"combbf{tag}", [128, 4, 8], BF16))
    combT = es.enter_context(nc.sbuf_tensor(f"combT{tag}", [8, NC], BF16))
    crep = es.enter_context(nc.sbuf_tensor(f"crep{tag}", [128, NC], BF16))
    s1_sb = es.enter_context(nc.sbuf_tensor(f"s1{tag}", [128, 16, NC], BF16))
    h_sb = es.enter_context(nc.sbuf_tensor(f"h{tag}", [128, 16, NC], BF16))
    y_sb = es.enter_context(nc.sbuf_tensor(f"y{tag}", [128, KD, NC], F32))
    wring = [es.enter_context(nc.sbuf_tensor(f"w{tag}{i}", [128, 2048], BF16)) for i in range(4)]
    mrow = es.enter_context(nc.sbuf_tensor(f"mrow{tag}", [1, NC], F32))
    vrow = es.enter_context(nc.sbuf_tensor(f"vrow{tag}", [1, NC], F32))
    brow = es.enter_context(nc.sbuf_tensor(f"brow{tag}", [1, NC], F32))
    arow_bf = es.enter_context(nc.sbuf_tensor(f"arb{tag}", [1, NC], BF16))
    brow_bf = es.enter_context(nc.sbuf_tensor(f"brb{tag}", [1, NC], BF16))
    tmp32 = es.enter_context(nc.sbuf_tensor(f"tmp32{tag}", [128, NC], F32))

    for k in range(KD):
        A.dma("gpsimd", gate_sb[:, k, :], gate_hbm[k, :, :], writes=[("gate",)], stream=("gate",))

    for c in range(NCH):
        t0 = c * NC
        for half in range(2):
            A.dma("sync", z_ch[:, half * 4:(half + 1) * 4, :],
                  z_hbm[:, half * 4:(half + 1) * 4, t0:t0 + NC], writes=[("zch",)])
        for k in range(KD):
            A.op("vector", lambda e, k=k, o=x2b[k % 2]: e.tensor_tensor(
                out=o[:], in0=z_ch[:, k, :], in1=z_ch[:, k, :], op=OP.mult),
                reads=[("zch",)], writes=[("x2b", k % 2)])
            A.op("tensor", lambda e, k=k, i=x2b[k % 2]: e.matmul(
                out=psum[7][0:1, 0:NC], lhsT=ones128b[:, 0:1], rhs=i[:],
                start=(k == 0), stop=(k == KD - 1)),
                reads=[("x2b", k % 2), ("ones",)], writes=[("ps", 7)])
        A.op("vector", lambda e: e.tensor_scalar(out=srow[:, :], in0=psum[7][0:1, 0:NC],
             scalar1=1.0 / D, scalar2=None, op0=OP.mult), reads=[("ps", 7)], writes=[("srow",)])
        A.op("scalar", lambda e: e.activation(out=srow[:, :], in_=srow[:, :],
             func=AF.Sqrt, bias=eps1_sb[0:1, 1:2]), reads=[("srow",), ("const",)], writes=[("srow",)])
        A.op("vector", lambda e: e.reciprocal(out=srow[:, :], in_=srow[:, :]),
             reads=[("srow",)], writes=[("srow",)])
        A.op("vector", lambda e: e.tensor_copy(out=srow_bf[:, :], in_=srow[:, :]),
             reads=[("srow",)], writes=[("srowb",)])
        A.op("tensor", lambda e: e.matmul(out=psum[6][:, 0:NC], lhsT=ones1b[0:1, :],
             rhs=srow_bf[:, :], start=True, stop=True),
             reads=[("srowb",), ("ones",)], writes=[("ps", 6)])
        for k in range(KD):
            A.op("vector", lambda e, k=k: e.scalar_tensor_tensor(
                out=zn_bf[:, k, :], in0=z_ch[:, k, :], scalar=rms_gain_sb[:, k:k + 1],
                in1=psum[6][:, 0:NC], op0=OP.mult, op1=OP.mult),
                reads=[("zch",), ("ps", 6), ("gain2",)], writes=[("znbf",)])

        for tsub in range(4):
            for k in range(KD):
                A.op("tensor", lambda e, k=k, tsub=tsub: e.matmul(
                    out=psum[7][:, 0:8], lhsT=zn_bf[:, k, tsub * 128:(tsub + 1) * 128],
                    rhs=gate_sb[:, k, :], start=(k == 0), stop=(k == KD - 1)),
                    reads=[("znbf",), ("gate",)], writes=[("ps", 7)])
            A.op("vector", lambda e: e.tensor_copy(out=glog[:, :], in_=psum[7][:, 0:8]),
                 reads=[("ps", 7)], writes=[("glog",)])
            A.op("vector", lambda e: e.reduce_max(out=gm1[:, 0:1], in_=glog[:, :], axis=AX.X),
                 reads=[("glog",)], writes=[("gm1",)])
            A.op("vector", lambda e: e.tensor_scalar(out=gtmp[:, :], in0=glog[:, :],
                 scalar1=gm1[:, 0:1], scalar2=None, op0=OP.is_ge),
                 reads=[("glog",), ("gm1",)], writes=[("gtmp",)])
            A.op("vector", lambda e: e.scalar_tensor_tensor(out=gtmp[:, :], in0=gtmp[:, :],
                 scalar=-1e30, in1=glog[:, :], op0=OP.mult, op1=OP.add),
                 reads=[("gtmp",), ("glog",)], writes=[("gtmp",)])
            A.op("vector", lambda e: e.reduce_max(out=gm2[:, 0:1], in_=gtmp[:, :], axis=AX.X),
                 reads=[("gtmp",)], writes=[("gm2",)])
            A.op("vector", lambda e: e.tensor_scalar(out=gm1[:, 0:1], in0=gm1[:, 0:1],
                 scalar1=-1.0, scalar2=None, op0=OP.mult), reads=[("gm1",)], writes=[("gm1",)])
            A.op("scalar", lambda e: e.activation(out=gtmp[:, :], in_=glog[:, :],
                 func=AF.Exp, bias=gm1[:, 0:1]), reads=[("glog",), ("gm1",)], writes=[("gtmp",)])
            A.op("vector", lambda e: e.tensor_scalar(out=glog[:, :], in0=glog[:, :],
                 scalar1=gm2[:, 0:1], scalar2=None, op0=OP.is_ge),
                 reads=[("glog",), ("gm2",)], writes=[("glog",)])
            A.op("vector", lambda e: e.tensor_tensor(out=gtmp[:, :], in0=gtmp[:, :],
                 in1=glog[:, :], op=OP.mult), reads=[("gtmp",), ("glog",)], writes=[("gtmp",)])
            A.op("vector", lambda e: e.tensor_reduce(out=gm2[:, 0:1], in_=gtmp[:, :],
                 axis=AX.X, op=OP.add), reads=[("gtmp",)], writes=[("gm2",)])
            A.op("vector", lambda e: e.reciprocal(out=gm2[:, 0:1], in_=gm2[:, 0:1]),
                 reads=[("gm2",)], writes=[("gm2",)])
            A.op("vector", lambda e, tsub=tsub: e.tensor_scalar(out=comb_bf[:, tsub, :],
                 in0=gtmp[:, :], scalar1=gm2[:, 0:1], scalar2=None, op0=OP.mult),
                 reads=[("gtmp",), ("gm2",)], writes=[("comb",)])
            A.op("tensor", lambda e, tsub=tsub: e.matmul(out=psum[6][0:8, 0:128],
                 lhsT=comb_bf[:, tsub, :], rhs=ident_bf[:], start=True, stop=True),
                 reads=[("comb",), ("ident",)], writes=[("ps", 6)])
            A.op("vector", lambda e, tsub=tsub: e.tensor_copy(
                 out=combT[:, tsub * 128:(tsub + 1) * 128], in_=psum[6][0:8, 0:128]),
                 reads=[("ps", 6)], writes=[("combT",)])

        for ex in range(NE):
            A.op("tensor", lambda e, ex=ex: e.matmul(out=psum[6][:, 0:NC],
                 lhsT=esel_sb[:, ex * 128:(ex + 1) * 128], rhs=combT[:, :],
                 start=True, stop=True), reads=[("esel",), ("combT",)], writes=[("ps", 6)])
            A.op("vector", lambda e: e.tensor_copy(out=crep[:, :], in_=psum[6][:, 0:NC]),
                 reads=[("ps", 6)], writes=[("crep",)])

            def w1_evac(m, ps, pstok, ex=ex):
                A.op("scalar", lambda e, m=m, ps=ps, ex=ex: e.activation(
                    out=s1_sb[:, m, :], in_=ps[:, 0:NC], func=AF.Silu,
                    bias=b1_sb[:, ex, m:m + 1]),
                    reads=[pstok, ("b1",)], writes=[("s1", m)])
                A.op("vector", lambda e, m=m: e.tensor_tensor(out=s1_sb[:, m, :],
                     in0=s1_sb[:, m, :], in1=crep[:, :], op=OP.mult),
                     reads=[("s1", m), ("crep",)], writes=[("s1", m)])
            emit_proj(A, w_hbm=w1_hbm[ex], rhs_fn=lambda k: (zn_bf[:, k, :], ("znbf",)),
                      kd=KD, mt=16, evac=w1_evac, psum=psum, wring=wring, wrtok="we" + tag)

            def w2_evac(m, ps, pstok, ex=ex):
                A.op("scalar", lambda e, m=m, ps=ps, ex=ex: e.activation(
                    out=ps[:, 0:NC], in_=ps[:, 0:NC], func=AF.Identity,
                    bias=b2_sb[:, ex, m:m + 1]),
                    reads=[pstok, ("b2",)], writes=[pstok])
                A.op("vector", lambda e, m=m, ps=ps: e.tensor_tensor(
                    out=h_sb[:, m, :], in0=ps[:, 0:NC], in1=s1_sb[:, m, :],
                    op=OP.mult),
                    reads=[pstok, ("s1", m)], writes=[("h", m)])
            emit_proj(A, w_hbm=w2_hbm[ex], rhs_fn=lambda k: (zn_bf[:, k, :], ("znbf",)),
                      kd=KD, mt=16, evac=w2_evac, psum=psum, wring=wring, wrtok="we" + tag)

            def w3_evac(m, ps, pstok, ex=ex):
                if ex == 0:
                    A.op("vector", lambda e, m=m, ps=ps: e.tensor_copy(out=y_sb[:, m, :],
                         in_=ps[:, 0:NC]), reads=[pstok], writes=[("y", m)])
                else:
                    A.op("vector", lambda e, m=m, ps=ps: e.tensor_tensor(out=y_sb[:, m, :],
                         in0=y_sb[:, m, :], in1=ps[:, 0:NC], op=OP.add),
                         reads=[pstok, ("y", m)], writes=[("y", m)])
            emit_proj(A, w_hbm=w3_hbm[ex], rhs_fn=lambda k: (h_sb[:, k, :], ("h", k)),
                      kd=16, mt=KD, evac=w3_evac, psum=psum, wring=wring, wrtok="we" + tag)

        for m8 in range(8):
            A.op("tensor", lambda e, m8=m8: e.matmul(out=psum[m8][:, 0:NC],
                 lhsT=b3_sb[0:8, m8 * 128:(m8 + 1) * 128], rhs=combT[:, :],
                 start=True, stop=True), reads=[("b3",), ("combT",)], writes=[("ps", m8)])
            A.op("vector", lambda e, m8=m8: e.tensor_tensor(out=y_sb[:, m8, :],
                 in0=y_sb[:, m8, :], in1=psum[m8][:, 0:NC], op=OP.add),
                 reads=[("ps", m8), ("y", m8)], writes=[("y", m8)])

        for half in range(2):
            A.dma("sync", x1_ch[:, half * 4:(half + 1) * 4, :],
                  x1_hbm[:, half * 4:(half + 1) * 4, t0:t0 + NC], writes=[("x1ch",)])
        for k in range(KD):
            A.op("vector", lambda e, k=k: e.tensor_tensor(out=y_sb[:, k, :],
                 in0=y_sb[:, k, :], in1=z_ch[:, k, :], op=OP.add),
                 reads=[("y", k), ("zch",)], writes=[("y", k)])
            A.op("vector", lambda e, k=k: e.tensor_tensor(out=y_sb[:, k, :],
                 in0=y_sb[:, k, :], in1=x1_ch[:, k, :], op=OP.add),
                 reads=[("y", k), ("x1ch",)], writes=[("y", k)])
        rbk = x2b
        for k in range(KD):
            A.op("vector", lambda e, k=k, o=rbk[k % 2]: e.tensor_copy(
                out=o[:], in_=y_sb[:, k, :]), reads=[("y", k)], writes=[("rb", k % 2)])
            A.op("tensor", lambda e, k=k, i=rbk[k % 2]: e.matmul(out=psum[0][0:1, 0:NC],
                 lhsT=ones128b[:, 0:1], rhs=i[:], start=(k == 0), stop=(k == KD - 1)),
                 reads=[("rb", k % 2), ("ones",)], writes=[("ps", 0)])
        for k in range(KD):
            A.op("scalar", lambda e, k=k, o=rbk[k % 2]: e.activation(
                out=o[:], in_=y_sb[:, k, :], func=AF.Square),
                reads=[("y", k)], writes=[("rb", k % 2)])
            A.op("tensor", lambda e, k=k, i=rbk[k % 2]: e.matmul(out=psum[1][0:1, 0:NC],
                 lhsT=ones128b[:, 0:1], rhs=i[:], start=(k == 0), stop=(k == KD - 1)),
                 reads=[("rb", k % 2), ("ones",)], writes=[("ps", 1)])
        A.op("vector", lambda e: e.tensor_scalar(out=mrow[:, :], in0=psum[0][0:1, 0:NC],
             scalar1=1.0 / D, scalar2=None, op0=OP.mult), reads=[("ps", 0)], writes=[("mrow",)])
        A.op("vector", lambda e: e.tensor_scalar(out=vrow[:, :], in0=psum[1][0:1, 0:NC],
             scalar1=1.0 / D, scalar2=None, op0=OP.mult), reads=[("ps", 1)], writes=[("vrow",)])
        A.op("vector", lambda e: e.tensor_tensor(out=brow[:, :], in0=mrow[:, :],
             in1=mrow[:, :], op=OP.mult), reads=[("mrow",)], writes=[("brow",)])
        A.op("vector", lambda e: e.tensor_tensor(out=vrow[:, :], in0=vrow[:, :],
             in1=brow[:, :], op=OP.subtract), reads=[("vrow",), ("brow",)], writes=[("vrow",)])
        A.op("scalar", lambda e: e.activation(out=vrow[:, :], in_=vrow[:, :],
             func=AF.Sqrt, bias=eps1_sb[0:1, 0:1]), reads=[("vrow",), ("const",)], writes=[("vrow",)])
        A.op("vector", lambda e: e.reciprocal(out=vrow[:, :], in_=vrow[:, :]),
             reads=[("vrow",)], writes=[("vrow",)])
        A.op("vector", lambda e: e.tensor_tensor(out=brow[:, :], in0=mrow[:, :],
             in1=vrow[:, :], op=OP.mult), reads=[("mrow",), ("vrow",)], writes=[("brow",)])
        A.op("vector", lambda e: e.tensor_scalar(out=brow[:, :], in0=brow[:, :],
             scalar1=-1.0, scalar2=None, op0=OP.mult), reads=[("brow",)], writes=[("brow",)])
        A.op("vector", lambda e: e.tensor_copy(out=arow_bf[:, :], in_=vrow[:, :]),
             reads=[("vrow",)], writes=[("arb",)])
        A.op("vector", lambda e: e.tensor_copy(out=brow_bf[:, :], in_=brow[:, :]),
             reads=[("brow",)], writes=[("brb",)])
        A.op("tensor", lambda e: e.matmul(out=psum[2][:, 0:NC], lhsT=ones1b[0:1, :],
             rhs=arow_bf[:, :], start=True, stop=True), reads=[("arb",), ("ones",)], writes=[("ps", 2)])
        A.op("tensor", lambda e: e.matmul(out=psum[3][:, 0:NC], lhsT=ones1b[0:1, :],
             rhs=brow_bf[:, :], start=True, stop=True), reads=[("brb",), ("ones",)], writes=[("ps", 3)])
        for k in range(KD):
            A.op("vector", lambda e, k=k: e.tensor_tensor(out=tmp32[:], in0=y_sb[:, k, :],
                 in1=psum[2][:, 0:NC], op=OP.mult),
                 reads=[("y", k), ("ps", 2)], writes=[("tmp32",)])
            A.op("vector", lambda e, k=k: e.tensor_tensor(out=y_sb[:, k, :], in0=tmp32[:],
                 in1=psum[3][:, 0:NC], op=OP.add),
                 reads=[("tmp32",), ("ps", 3)], writes=[("y", k)])
        for half in range(2):
            A.dma("gpsimd", dst_hbm[:, half * 4:(half + 1) * 4, t0:t0 + NC],
                  y_sb[:, half * 4:(half + 1) * 4, :], reads=[("y", half * 4), ("y", half * 4 + 1),
                  ("y", half * 4 + 2), ("y", half * 4 + 3)], writes=[("dsth",)],
                  stream=("dst_stream",))


DI = 2048
KDI = DI // 128
DST = 16
DTR = 64


def emit_mamba_phase(au, es, nc, *, src_hbm, dst_hbm, win_hbm, cw_sb, cb_sb,
                     wx_sb, wdt_sb, bdt_sb, A_sb, D_sb, wout_hbm, esel32_sb,
                     psum, ones128b, ones1b, ones128f32, eps1_sb, tag="c"):
    A = au
    x_bf = es.enter_context(nc.sbuf_tensor(f"xbf{tag}", [128, KD, NC], BF16))
    xi_sb = es.enter_context(nc.sbuf_tensor(f"xi{tag}", [128, KDI, NC + 8], BF16))
    xc_bf = es.enter_context(nc.sbuf_tensor(f"xc{tag}", [128, KDI, NC], BF16))
    zs_bf = es.enter_context(nc.sbuf_tensor(f"zs{tag}", [128, KDI, NC], BF16))
    delta = es.enter_context(nc.sbuf_tensor(f"delta{tag}", [128, KDI, NC], BF16))
    u_bf = es.enter_context(nc.sbuf_tensor(f"u{tag}", [128, 4, NC], BF16))
    g_bf = es.enter_context(nc.sbuf_tensor(f"g{tag}", [128, KDI, NC], BF16))
    y_sb = es.enter_context(nc.sbuf_tensor(f"yy{tag}", [128, 4, NC], F32))
    hcar = es.enter_context(nc.sbuf_tensor(f"hc{tag}", [128, KDI, DST], F32))
    wring = [es.enter_context(nc.sbuf_tensor(f"w{tag}{i}", [128, 2048], BF16)) for i in range(3)]
    xdbl_bc = es.enter_context(nc.sbuf_tensor(f"xbc{tag}", [32, NC], BF16))
    dtT_bf = es.enter_context(nc.sbuf_tensor(f"dtT{tag}", [64, NC], BF16))
    vab = es.enter_context(nc.sbuf_tensor(f"vab{tag}", [128, NC], F32))
    rel = es.enter_context(nc.sbuf_tensor(f"rel{tag}", [128, NC], F32))
    brep = [es.enter_context(nc.sbuf_tensor(f"brep{tag}{i}", [128, NC], BF16)) for i in range(2)]
    crep = [es.enter_context(nc.sbuf_tensor(f"crep{tag}{i}", [128, NC], BF16)) for i in range(2)]
    SEG = NC + 1
    dA_bf = [es.enter_context(nc.sbuf_tensor(f"dA{tag}{i}", [128, 4 * SEG], BF16)) for i in range(2)]
    dBu = [es.enter_context(nc.sbuf_tensor(f"dBu{tag}{i}", [128, 4 * SEG], BF16)) for i in range(2)]
    hn = [es.enter_context(nc.sbuf_tensor(f"hn{tag}{i}", [128, 4 * SEG], BF16)) for i in range(3)]
    tmpb = [es.enter_context(nc.sbuf_tensor(f"tmb{tag}{i}", [128, 4, NC], BF16)) for i in range(3)]
    zcol = es.enter_context(nc.sbuf_tensor(f"zc{tag}", [128, 1], F32))
    x2f = es.enter_context(nc.sbuf_tensor(f"x2f{tag}", [128, KD, NC], F32))
    rbk = [es.enter_context(nc.sbuf_tensor(f"rb{tag}{i}", [128, NC], BF16)) for i in range(2)]
    mrow = es.enter_context(nc.sbuf_tensor(f"mr{tag}", [1, NC], F32))
    vrow = es.enter_context(nc.sbuf_tensor(f"vr{tag}", [1, NC], F32))
    arow_bf = es.enter_context(nc.sbuf_tensor(f"ab{tag}", [1, NC], BF16))
    brow_bf = es.enter_context(nc.sbuf_tensor(f"bb{tag}", [1, NC], BF16))

    A.op("vector", lambda e: e.memset(hcar[:], 0.0), writes=[("hcar",)])
    A.op("vector", lambda e: e.memset(xi_sb[:, :, 0:8], 0.0), writes=[("xi",)])
    A.op("vector", lambda e: e.memset(zcol[:], 0.0), writes=[("zc",)])
    for r in range(2):
        for s in range(4):
            A.op("vector", lambda e, r=r, s=s: e.memset(
                dA_bf[r][:, s * SEG:s * SEG + 1], 0.0), writes=[("dAb", r)])

    for c in range(NCH):
        t0 = c * NC
        A.dma("gpsimd", x_bf[:, :, :], src_hbm[:, :, t0:t0 + NC], writes=[("xbfc",)],
              stream=("xbfc",))
        def xz_evac(m, ps, pstok):
            if m < 16:
                A.op("vector", lambda e, m=m, ps=ps: e.tensor_copy(
                    out=xi_sb[:, m, 8:8 + NC], in_=ps[:, 0:NC]),
                    reads=[pstok], writes=[("xi",)])
            else:
                A.op("scalar", lambda e, m=m, ps=ps: e.activation(
                    out=zs_bf[:, m - 16, :], in_=ps[:, 0:NC], func=AF.Silu),
                    reads=[pstok], writes=[("zs",)])

        emit_proj(A, w_hbm=win_hbm, rhs_fn=lambda k: (x_bf[:, k, :], ("xbfc",)),
                  kd=KD, mt=32, evac=xz_evac, psum=psum, wring=wring, wrtok="wm" + tag)
        for m in range(KDI):
            A.op("vector", lambda e, m=m: e.tensor_scalar(
                out=vab[:], in0=xi_sb[:, m, 5:5 + NC], scalar1=cw_sb[:, m, 0:1],
                scalar2=None, op0=OP.mult),
                reads=[("xi",), ("cw",)], writes=[("vab",)])
            for j in (1, 2, 3):
                A.op("vector", lambda e, m=m, j=j: e.scalar_tensor_tensor(
                    out=vab[:], in0=xi_sb[:, m, 5 + j:5 + j + NC], scalar=cw_sb[:, m, j:j + 1],
                    in1=vab[:], op0=OP.mult, op1=OP.add),
                    reads=[("xi",), ("cw",), ("vab",)], writes=[("vab",)])
            A.op("scalar", lambda e, m=m: e.activation(out=xc_bf[:, m, :], in_=vab[:],
                 func=AF.Silu, bias=cb_sb[:, m:m + 1]),
                 reads=[("vab",), ("cb",)], writes=[("xc",)])
            A.op("vector", lambda e, m=m: e.tensor_copy(out=xi_sb[:, m, 5:8],
                 in_=xi_sb[:, m, 5 + NC:8 + NC]), reads=[("xi",)], writes=[("xi",)])
        for k in range(KDI):
            A.op("tensor", lambda e, k=k: e.matmul(out=psum[7][0:96, 0:NC],
                 lhsT=wx_sb[:, k, :], rhs=xc_bf[:, k, :], start=(k == 0), stop=(k == KDI - 1)),
                 reads=[("wx",), ("xc",)], writes=[("ps", 7)])
        A.op("vector", lambda e: e.tensor_copy(out=dtT_bf[:, :], in_=psum[7][0:64, 0:NC]),
             reads=[("ps", 7)], writes=[("dtT",)])
        A.op("vector", lambda e: e.tensor_copy(out=xdbl_bc[:, :], in_=psum[7][64:96, 0:NC]),
             reads=[("ps", 7)], writes=[("xbc",)])
        for m in range(KDI):
            A.op("tensor", lambda e, m=m: e.matmul(out=psum[6][:, 0:NC],
                 lhsT=wdt_sb[:, m * 128:(m + 1) * 128], rhs=dtT_bf[:, :],
                 start=True, stop=True), reads=[("wdt",), ("dtT",)], writes=[("ps", 6)])
            A.op("scalar", lambda e, m=m: e.activation(out=vab[:], in_=psum[6][:, 0:NC],
                 func=AF.Abs, bias=bdt_sb[:, m:m + 1]),
                 reads=[("ps", 6), ("bdt",)], writes=[("vab",)])
            A.op("scalar", lambda e: e.activation(out=vab[:], in_=vab[:], func=AF.Exp,
                 scale=-1.0), reads=[("vab",)], writes=[("vab",)])
            A.op("scalar", lambda e: e.activation(out=vab[:], in_=vab[:], func=AF.Ln,
                 bias=ones128f32[:, 0:1]), reads=[("vab",), ("onesf",)], writes=[("vab",)])
            A.op("scalar", lambda e, m=m: e.activation(out=rel[:], in_=psum[6][:, 0:NC],
                 func=AF.Relu, bias=bdt_sb[:, m:m + 1]),
                 reads=[("ps", 6), ("bdt",)], writes=[("rel",)])
            A.op("vector", lambda e, m=m: e.tensor_tensor(out=delta[:, m, :], in0=vab[:],
                 in1=rel[:], op=OP.add), reads=[("vab",), ("rel",)], writes=[("delta", m)])
        it = 0
        for mg in range(4):
            ms = range(4 * mg, 4 * mg + 4)
            for mi, m in enumerate(ms):
                A.op("vector", lambda e, m=m, mi=mi: e.tensor_tensor(out=u_bf[:, mi, :],
                     in0=delta[:, m, :], in1=xc_bf[:, m, :], op=OP.mult),
                     reads=[("delta", m), ("xc",)], writes=[("u", mi)])
            pend = []
            pupd = []

            def yflush():
                pk = pend.pop(0)
                A.op("vector", lambda e, i=tmpb[pk]: e.tensor_tensor(
                     out=y_sb[:, 0:4, :].rearrange("p a b -> p (a b)"),
                     in0=y_sb[:, 0:4, :].rearrange("p a b -> p (a b)"),
                     in1=i[:, 0:4, :].rearrange("p a b -> p (a b)"), op=OP.add),
                     reads=[("tmb", pk), ("yc", 0), ("yc", 1), ("yc", 2), ("yc", 3)],
                     writes=[("yc", 0), ("yc", 1), ("yc", 2), ("yc", 3)])

            for n in range(DST):
                kA = it % 2
                k2 = it % 3
                n2 = n % 2
                A.op("tensor", lambda e, n=n: e.matmul(out=psum[5][:, 0:NC],
                     lhsT=esel32_sb[:, n * 128:(n + 1) * 128], rhs=xdbl_bc[:, :],
                     start=True, stop=True), reads=[("esel32",), ("xbc",)], writes=[("ps", 5)])
                A.op("scalar", lambda e, n=n: e.copy(out=brep[n % 2][:], in_=psum[5][:, 0:NC]),
                     reads=[("ps", 5)], writes=[("brep", n % 2)])
                A.op("tensor", lambda e, n=n: e.matmul(out=psum[5][:, 0:NC],
                     lhsT=esel32_sb[:, (16 + n) * 128:(17 + n) * 128], rhs=xdbl_bc[:, :],
                     start=True, stop=True), reads=[("esel32",), ("xbc",)], writes=[("ps", 5)])
                A.op("scalar", lambda e, n=n: e.copy(out=crep[n % 2][:], in_=psum[5][:, 0:NC]),
                     reads=[("ps", 5)], writes=[("crep", n % 2)])
                for mi, m in enumerate(ms):
                    A.op("scalar", lambda e, m=m, n=n, o=dA_bf[kA], s0=mi * SEG: e.activation(
                         out=o[:, s0 + 1:s0 + SEG], in_=delta[:, m, :],
                         func=AF.Exp, scale=A_sb[:, m, n:n + 1]),
                         reads=[("delta", m), ("A",)], writes=[("dAb", kA)])
                for mi, m in enumerate(ms):
                    A.op("gpsimd", lambda e, mi=mi, o=dBu[kA], n2=n2, s0=mi * SEG: e.tensor_tensor(
                         out=o[:, s0 + 1:s0 + SEG],
                         in0=u_bf[:, mi, :], in1=brep[n2][:], op=OP.mult),
                         reads=[("u", mi), ("brep", n2)], writes=[("dBu", kA)])
                A.op("scalar", lambda e, n=n, mg=mg, o=dBu[kA]: e.copy(
                     out=o[:, 0:4 * SEG].rearrange("p (a b) -> p a b", a=4)[:, :, 0:1],
                     in_=hcar[:, 4 * mg:4 * mg + 4, n:n + 1]),
                     reads=[("hcar",)], writes=[("dBs", kA)])
                A.op("vector", lambda e, i0=dA_bf[kA], i1=dBu[kA], o=hn[k2]:
                     e.tensor_tensor_scan(
                     out=o[:, :], data0=i0[:, :], data1=i1[:, :], initial=zcol[:, 0:1],
                     op0=OP.mult, op1=OP.add),
                     reads=[("dAb", kA), ("dBu", kA), ("dBs", kA), ("zc",)], writes=[("hn", k2)])
                pupd.append(lambda k2=k2, n=n, mg=mg: A.op(
                     "scalar", lambda e, n=n, mg=mg, i=hn[k2]: e.copy(
                     out=hcar[:, 4 * mg:4 * mg + 4, n:n + 1],
                     in_=i[:, 0:4 * SEG].rearrange("p (a b) -> p a b", a=4)[:, :, SEG - 1:SEG]),
                     reads=[("hn", k2)], writes=[("hcar",)]))
                if len(pupd) >= 3:
                    pupd.pop(0)()
                if n == 0:
                    for mi in range(4):
                        A.op("vector", lambda e, mi=mi, i=hn[k2], n2=n2, s0=mi * SEG:
                             e.tensor_tensor(
                             out=y_sb[:, mi, :],
                             in0=i[:, s0 + 1:s0 + SEG], in1=crep[n2][:], op=OP.mult),
                             reads=[("hn", k2), ("crep", n2)], writes=[("yc", mi)])
                else:
                    k3 = n % 3
                    for mi in range(4):
                        A.op("vector", lambda e, mi=mi, i=hn[k2], o=tmpb[k3], n2=n2,
                             s0=mi * SEG: e.tensor_tensor(
                             out=o[:, mi, :],
                             in0=i[:, s0 + 1:s0 + SEG], in1=crep[n2][:], op=OP.mult),
                             reads=[("hn", k2), ("crep", n2)], writes=[("tmb", k3)])
                    pend.append(k3)
                    if len(pend) >= 3:
                        yflush()
                it += 1
            while pupd:
                pupd.pop(0)()
            while pend:
                yflush()
            for mi, m in enumerate(ms):
                A.op("vector", lambda e, m=m, mi=mi: e.scalar_tensor_tensor(out=y_sb[:, mi, :],
                     in0=xc_bf[:, m, :], scalar=D_sb[:, m:m + 1], in1=y_sb[:, mi, :],
                     op0=OP.mult, op1=OP.add),
                     reads=[("xc",), ("D",), ("yc", mi)], writes=[("yc", mi)])
                A.op("vector", lambda e, m=m, mi=mi: e.tensor_tensor(out=g_bf[:, m, :],
                     in0=y_sb[:, mi, :], in1=zs_bf[:, m, :], op=OP.mult),
                     reads=[("yc", mi), ("zs",)], writes=[("u2", m)])
        for half in range(2):
            A.dma("sync", x2f[:, half * 4:(half + 1) * 4, :],
                  src_hbm[:, half * 4:(half + 1) * 4, t0:t0 + NC], writes=[("x2f",)])
        def mo_evac(m, ps, pstok):
            A.op("vector", lambda e, m=m, ps=ps: e.tensor_tensor(out=x2f[:, m, :],
                 in0=x2f[:, m, :], in1=ps[:, 0:NC], op=OP.add),
                 reads=[pstok, ("x2f",)], writes=[("x2f",)])
        emit_proj(A, w_hbm=wout_hbm, rhs_fn=lambda k: (g_bf[:, k, :], ("u2", k)),
                  kd=KDI, mt=KD, evac=mo_evac, psum=psum, wring=wring, wrtok="wm" + tag)
        for k in range(KD):
            A.op("vector", lambda e, k=k, o=rbk[k % 2]: e.tensor_copy(
                out=o[:], in_=x2f[:, k, :]), reads=[("x2f",)], writes=[("rb", k % 2)])
            A.op("tensor", lambda e, k=k, i=rbk[k % 2]: e.matmul(out=psum[0][0:1, 0:NC],
                 lhsT=ones128b[:, 0:1], rhs=i[:], start=(k == 0), stop=(k == KD - 1)),
                 reads=[("rb", k % 2), ("ones",)], writes=[("ps", 0)])
        for k in range(KD):
            A.op("scalar", lambda e, k=k, o=rbk[k % 2]: e.activation(
                out=o[:], in_=x2f[:, k, :], func=AF.Square),
                reads=[("x2f",)], writes=[("rb", k % 2)])
            A.op("tensor", lambda e, k=k, i=rbk[k % 2]: e.matmul(out=psum[1][0:1, 0:NC],
                 lhsT=ones128b[:, 0:1], rhs=i[:], start=(k == 0), stop=(k == KD - 1)),
                 reads=[("rb", k % 2), ("ones",)], writes=[("ps", 1)])
        A.op("vector", lambda e: e.tensor_scalar(out=mrow[:, :], in0=psum[0][0:1, 0:NC],
             scalar1=1.0 / D, scalar2=None, op0=OP.mult), reads=[("ps", 0)], writes=[("mrow",)])
        A.op("vector", lambda e: e.tensor_scalar(out=vrow[:, :], in0=psum[1][0:1, 0:NC],
             scalar1=1.0 / D, scalar2=None, op0=OP.mult), reads=[("ps", 1)], writes=[("vrow",)])
        A.op("vector", lambda e: e.tensor_tensor(out=rel[0:1, :], in0=mrow[:, :],
             in1=mrow[:, :], op=OP.mult), reads=[("mrow",)], writes=[("rel",)])
        A.op("vector", lambda e: e.tensor_tensor(out=vrow[:, :], in0=vrow[:, :],
             in1=rel[0:1, :], op=OP.subtract), reads=[("vrow",), ("rel",)], writes=[("vrow",)])
        A.op("scalar", lambda e: e.activation(out=vrow[:, :], in_=vrow[:, :],
             func=AF.Sqrt, bias=eps1_sb[0:1, 0:1]), reads=[("vrow",), ("const",)], writes=[("vrow",)])
        A.op("vector", lambda e: e.reciprocal(out=vrow[:, :], in_=vrow[:, :]),
             reads=[("vrow",)], writes=[("vrow",)])
        A.op("vector", lambda e: e.tensor_tensor(out=mrow[:, :], in0=mrow[:, :],
             in1=vrow[:, :], op=OP.mult), reads=[("mrow",), ("vrow",)], writes=[("mrow",)])
        A.op("vector", lambda e: e.tensor_scalar(out=mrow[:, :], in0=mrow[:, :],
             scalar1=-1.0, scalar2=None, op0=OP.mult), reads=[("mrow",)], writes=[("mrow",)])
        A.op("vector", lambda e: e.tensor_copy(out=arow_bf[:, :], in_=vrow[:, :]),
             reads=[("vrow",)], writes=[("arb",)])
        A.op("vector", lambda e: e.tensor_copy(out=brow_bf[:, :], in_=mrow[:, :]),
             reads=[("mrow",)], writes=[("brb",)])
        A.op("tensor", lambda e: e.matmul(out=psum[2][:, 0:NC], lhsT=ones1b[0:1, :],
             rhs=arow_bf[:, :], start=True, stop=True), reads=[("arb",), ("ones",)], writes=[("ps", 2)])
        A.op("tensor", lambda e: e.matmul(out=psum[3][:, 0:NC], lhsT=ones1b[0:1, :],
             rhs=brow_bf[:, :], start=True, stop=True), reads=[("brb",), ("ones",)], writes=[("ps", 3)])
        for k in range(KD):
            A.op("vector", lambda e, k=k: e.tensor_tensor(out=rel[:], in0=x2f[:, k, :],
                 in1=psum[2][:, 0:NC], op=OP.mult),
                 reads=[("x2f",), ("ps", 2)], writes=[("rel",)])
            A.op("vector", lambda e, k=k: e.tensor_tensor(out=x2f[:, k, :], in0=rel[:],
                 in1=psum[3][:, 0:NC], op=OP.add),
                 reads=[("rel",), ("ps", 3)], writes=[("x2f",)])
        for half in range(2):
            A.dma("gpsimd", dst_hbm[:, half * 4:(half + 1) * 4, t0:t0 + NC],
                  x2f[:, half * 4:(half + 1) * 4, :], reads=[("x2f",)],
                  writes=[("dsth",)], stream=("dst_stream",))


def bfly_perm_B():
    permB = np.zeros(4096, dtype=np.int64)
    for tB in range(32):
        for pB in range(128):
            b, a = pB // 32, pB % 32
            f = a * 128 + tB * 4 + b
            permB[tB * 128 + pB] = f
    return permB


def bfly_coeffs(wstack):
    coefA = np.zeros((7, 32, 64), np.float32)
    for s in range(7):
        for t in range(32):
            coefA[s, t] = wstack[s, t * 64:(t + 1) * 64]
    coefB = np.zeros((5, 32, 64), np.float32)
    for si, s in enumerate(range(7, 12)):
        sig = 2 ** s
        sigB = 2 ** (s - 7)
        for tB in range(32):
            for kp in range(64):
                a_pB = (kp // sigB) * 2 * sigB + kp % sigB
                b, a = a_pB // 32, a_pB % 32
                i = a * 128 + tB * 4 + b
                k = (i // (2 * sig)) * sig + i % sig
                coefB[si, tB, kp] = wstack[s, k]
    return coefA, coefB


def emit_bfly_build(au, es, nc, *, coef_hbm, which, M_sb, ident_bf, psum, ones1b, tag):
    A = au
    strides = [2 ** s for s in range(7)] if which == "A" else [2 ** i for i in range(5)]
    ns = len(strides)
    work = es.enter_context(nc.sbuf_tensor(f"bw{tag}", [128, 128], F32))
    prod = [es.enter_context(nc.sbuf_tensor(f"bp{tag}{i}", [128, 64], F32)) for i in range(4)]
    cslab = [es.enter_context(nc.sbuf_tensor(f"bc{tag}{i}", [128, ns * 4 * 64], BF16))
             for i in range(2)]
    wtok = ("bwork", tag)
    for t in range(32):
        cs, ctok = cslab[t % 2], ("bcoef", tag, t % 2)
        A.dma("sync", cs[:, :], coef_hbm[t, :, :], writes=[ctok])
        A.op("vector", lambda e: e.tensor_copy(out=work[:], in_=ident_bf[:]),
             reads=[("ident",)], writes=[wtok])
        for si, sig in enumerate(strides):
            base = si * 4 * 64
            for pi, (side, ci) in enumerate(((0, 0), (1, 1), (0, 2), (1, 3))):
                A.op("vector", lambda e, pi=pi, side=side, ci=ci, sig=sig, base=base, cs=cs: e.tensor_tensor(
                    out=prod[pi].rearrange("p (b s) -> p b s", s=sig)[:, :, :],
                    in0=work.rearrange("p (b two s) -> p b two s", two=2, s=sig)[:, :, side, :],
                    in1=cs[:, base + ci * 64: base + (ci + 1) * 64].rearrange(
                        "p (b s) -> p b s", s=sig)[:, :, :],
                    op=OP.mult),
                    reads=[wtok, ctok], writes=[("bprod", tag, pi)])
            A.op("vector", lambda e, sig=sig: e.tensor_tensor(
                out=work.rearrange("p (b two s) -> p b two s", two=2, s=sig)[:, :, 0, :],
                in0=prod[0].rearrange("p (b s) -> p b s", s=sig)[:, :, :],
                in1=prod[1].rearrange("p (b s) -> p b s", s=sig)[:, :, :], op=OP.add),
                reads=[("bprod", tag, 0), ("bprod", tag, 1)], writes=[wtok])
            A.op("vector", lambda e, sig=sig: e.tensor_tensor(
                out=work.rearrange("p (b two s) -> p b two s", two=2, s=sig)[:, :, 1, :],
                in0=prod[2].rearrange("p (b s) -> p b s", s=sig)[:, :, :],
                in1=prod[3].rearrange("p (b s) -> p b s", s=sig)[:, :, :], op=OP.add),
                reads=[("bprod", tag, 2), ("bprod", tag, 3)], writes=[wtok])
        A.op("vector", lambda e, t=t: e.tensor_copy(out=M_sb[:, t, :], in_=work[:]),
             reads=[wtok], writes=[("M", tag)])


def emit_bfly_phase(au, es, nc, *, src_hbm, dst_hbm, wexp_hbm, bexp_sb, MAu, MBu,
                    MAd, MBd, ubias_sb, wcon_hbm, bcon_sb, shuf_hbm, psum,
                    ones128b, ones1b, eps1_sb, tag="d"):
    A = au
    x_bf = [es.enter_context(nc.sbuf_tensor(f"xb{tag}", [128, KD, NC], BF16))]
    h_sbs = [es.enter_context(nc.sbuf_tensor(f"h{tag}{i}", [128, 32, NC], BF16)) for i in range(2)]
    h2_sbs = [es.enter_context(nc.sbuf_tensor(f"h2{tag}{i}", [128, 32, NC], BF16)) for i in range(2)]
    x3f = es.enter_context(nc.sbuf_tensor(f"x3f{tag}", [128, KD, NC], F32))
    wring = [es.enter_context(nc.sbuf_tensor(f"w{tag}{i}", [128, 2048], BF16)) for i in range(3)]
    rbk = [es.enter_context(nc.sbuf_tensor(f"rb{tag}{i}", [128, NC], BF16)) for i in range(2)]
    mrow = es.enter_context(nc.sbuf_tensor(f"mr{tag}", [1, NC], F32))
    vrow = es.enter_context(nc.sbuf_tensor(f"vr{tag}", [1, NC], F32))
    brow = es.enter_context(nc.sbuf_tensor(f"br{tag}", [1, NC], F32))
    arow_bf = es.enter_context(nc.sbuf_tensor(f"ab{tag}", [1, NC], BF16))
    brow_bf = es.enter_context(nc.sbuf_tensor(f"bb{tag}", [1, NC], BF16))

    for c in range(NCH):
        t0 = c * NC
        cp = c % 2
        h_sb, h2_sb = h_sbs[cp], h2_sbs[cp]
        xb = x_bf[0]
        hhtok, h2tok = ("hh", cp), ("h2", cp)
        shuf_hbm_c = shuf_hbm[cp]
        shufA = shuf_hbm_c.rearrange("c b a t -> (c b) a t")
        shufB = shuf_hbm_c.rearrange("c b a t -> (b a) c t")
        A.dma("gpsimd", xb[:, :, :], src_hbm[:, :, t0:t0 + NC], writes=[("xb", 0)],
              stream=("xb", 0))
        def we_evac(m, ps, pstok, h_sb=h_sb, hhtok=hhtok):
            A.op("scalar", lambda e, m=m, ps=ps, h_sb=h_sb, cp=cp: e.activation(
                out=h_sb[:, m, :], in_=ps[:, 0:NC], func=AF.Identity,
                bias=bexp_sb[:, m:m + 1]),
                reads=[pstok, ("bexp",)], writes=[hhtok])
        emit_proj(A, w_hbm=wexp_hbm, rhs_fn=lambda k, xb=xb: (xb[:, k, :], ("xb", 0)),
                  kd=KD, mt=32, evac=we_evac, psum=psum, wring=wring, wrtok="wd" + tag,
                  bank_base=4 * cp, pingpong=False)
        for t in range(32):
            A.op("tensor", lambda e, t=t, h_sb=h_sb, cp=cp: e.matmul(out=psum[4 * cp + t % 4][:, 0:NC],
                 lhsT=MAu[:, t, :], rhs=h_sb[:, t, :], start=True, stop=True),
                 reads=[("M", "Au"), hhtok], writes=[("ps", 4 * cp + t % 4)])
            A.op("vector", lambda e, t=t, h2_sb=h2_sb, cp=cp: e.tensor_copy(out=h2_sb[:, t, :],
                 in_=psum[4 * cp + t % 4][:, 0:NC]), reads=[("ps", 4 * cp + t % 4)], writes=[h2tok])
        for q in range(4):
            A.dma("gpsimd", shufA[:, 8 * q:8 * q + 8, :], h2_sb[:, 8 * q:8 * q + 8, :],
                  reads=[h2tok], writes=[("shufh", cp)], stream=("shufh", cp))
        for q in range(4):
            A.dma("sync", h_sb[:, 8 * q:8 * q + 8, :], shufB[:, 8 * q:8 * q + 8, :],
                  reads=[("shufh", cp)], writes=[hhtok], stream=hhtok)
        for t in range(32):
            A.op("tensor", lambda e, t=t, h_sb=h_sb, cp=cp: e.matmul(out=psum[4 * cp + t % 4][:, 0:NC],
                 lhsT=MBu[:, t, :], rhs=h_sb[:, t, :], start=True, stop=True),
                 reads=[("M", "Bu"), hhtok], writes=[("ps", 4 * cp + t % 4)])
            A.op("scalar", lambda e, t=t, h2_sb=h2_sb, cp=cp: e.activation(out=h2_sb[:, t, :],
                 in_=psum[4 * cp + t % 4][:, 0:NC], func=AF.Gelu, bias=ubias_sb[:, t:t + 1]),
                 reads=[("ps", 4 * cp + t % 4), ("ubias",)], writes=[h2tok])
        for q in range(4):
            A.dma("gpsimd", shufB[:, 8 * q:8 * q + 8, :], h2_sb[:, 8 * q:8 * q + 8, :],
                  reads=[h2tok], writes=[("shufh", cp)], stream=("shufh", cp))
        for q in range(4):
            A.dma("sync", h_sb[:, 8 * q:8 * q + 8, :], shufA[:, 8 * q:8 * q + 8, :],
                  reads=[("shufh", cp)], writes=[hhtok], stream=hhtok)
        for t in range(32):
            A.op("tensor", lambda e, t=t, h_sb=h_sb, cp=cp: e.matmul(out=psum[4 * cp + t % 4][:, 0:NC],
                 lhsT=MAd[:, t, :], rhs=h_sb[:, t, :], start=True, stop=True),
                 reads=[("M", "Ad"), hhtok], writes=[("ps", 4 * cp + t % 4)])
            A.op("vector", lambda e, t=t, h2_sb=h2_sb, cp=cp: e.tensor_copy(out=h2_sb[:, t, :],
                 in_=psum[4 * cp + t % 4][:, 0:NC]), reads=[("ps", 4 * cp + t % 4)], writes=[h2tok])
        for q in range(4):
            A.dma("gpsimd", shufA[:, 8 * q:8 * q + 8, :], h2_sb[:, 8 * q:8 * q + 8, :],
                  reads=[h2tok], writes=[("shufh", cp)], stream=("shufh", cp))
        for q in range(4):
            A.dma("sync", h_sb[:, 8 * q:8 * q + 8, :], shufB[:, 8 * q:8 * q + 8, :],
                  reads=[("shufh", cp)], writes=[hhtok], stream=hhtok)
        for t in range(32):
            A.op("tensor", lambda e, t=t, h_sb=h_sb, cp=cp: e.matmul(out=psum[4 * cp + t % 4][:, 0:NC],
                 lhsT=MBd[:, t, :], rhs=h_sb[:, t, :], start=True, stop=True),
                 reads=[("M", "Bd"), hhtok], writes=[("ps", 4 * cp + t % 4)])
            A.op("vector", lambda e, t=t, h2_sb=h2_sb, cp=cp: e.tensor_copy(out=h2_sb[:, t, :],
                 in_=psum[4 * cp + t % 4][:, 0:NC]), reads=[("ps", 4 * cp + t % 4)], writes=[h2tok])
        for half in range(2):
            A.dma("sync", x3f[:, half * 4:(half + 1) * 4, :],
                  src_hbm[:, half * 4:(half + 1) * 4, t0:t0 + NC], writes=[("x3f",)])
        def wc_evac(m, ps, pstok):
            A.op("scalar", lambda e, m=m, ps=ps: e.activation(out=ps[:, 0:NC],
                 in_=ps[:, 0:NC], func=AF.Identity, bias=bcon_sb[:, m:m + 1]),
                 reads=[pstok, ("bcon",)], writes=[pstok])
            A.op("vector", lambda e, m=m, ps=ps, cp=cp: e.tensor_tensor(out=x3f[:, m, :],
                 in0=x3f[:, m, :], in1=ps[:, 0:NC], op=OP.add),
                 reads=[pstok, ("x3f",)], writes=[("x3f",)])
        emit_proj(A, w_hbm=wcon_hbm, rhs_fn=lambda k, h2_sb=h2_sb, h2tok=h2tok: (h2_sb[:, k, :], h2tok),
                  kd=32, mt=KD, evac=wc_evac, psum=psum, wring=wring, wrtok="wd" + tag,
                  bank_base=4 * cp, pingpong=False)
        for k in range(KD):
            A.op("vector", lambda e, k=k, o=rbk[k % 2], cp=cp: e.tensor_copy(
                out=o[:], in_=x3f[:, k, :]), reads=[("x3f",)], writes=[("rb", k % 2)])
            A.op("tensor", lambda e, k=k, i=rbk[k % 2], cp=cp: e.matmul(out=psum[4 * cp + 0][0:1, 0:NC],
                 lhsT=ones128b[:, 0:1], rhs=i[:], start=(k == 0), stop=(k == KD - 1)),
                 reads=[("rb", k % 2), ("ones",)], writes=[("ps", 4 * cp + 0)])
        for k in range(KD):
            A.op("scalar", lambda e, k=k, o=rbk[k % 2], cp=cp: e.activation(
                out=o[:], in_=x3f[:, k, :], func=AF.Square),
                reads=[("x3f",)], writes=[("rb", k % 2)])
            A.op("tensor", lambda e, k=k, i=rbk[k % 2], cp=cp: e.matmul(out=psum[4 * cp + 1][0:1, 0:NC],
                 lhsT=ones128b[:, 0:1], rhs=i[:], start=(k == 0), stop=(k == KD - 1)),
                 reads=[("rb", k % 2), ("ones",)], writes=[("ps", 4 * cp + 1)])
        A.op("vector", lambda e, cp=cp: e.tensor_scalar(out=mrow[:, :], in0=psum[4 * cp + 0][0:1, 0:NC],
             scalar1=1.0 / D, scalar2=None, op0=OP.mult), reads=[("ps", 4 * cp + 0)], writes=[("mrow",)])
        A.op("vector", lambda e, cp=cp: e.tensor_scalar(out=vrow[:, :], in0=psum[4 * cp + 1][0:1, 0:NC],
             scalar1=1.0 / D, scalar2=None, op0=OP.mult), reads=[("ps", 4 * cp + 1)], writes=[("vrow",)])
        A.op("vector", lambda e: e.tensor_tensor(out=brow[:, :], in0=mrow[:, :],
             in1=mrow[:, :], op=OP.mult), reads=[("mrow",)], writes=[("brow",)])
        A.op("vector", lambda e: e.tensor_tensor(out=vrow[:, :], in0=vrow[:, :],
             in1=brow[:, :], op=OP.subtract), reads=[("vrow",), ("brow",)], writes=[("vrow",)])
        A.op("scalar", lambda e: e.activation(out=vrow[:, :], in_=vrow[:, :],
             func=AF.Sqrt, bias=eps1_sb[0:1, 0:1]), reads=[("vrow",), ("const",)], writes=[("vrow",)])
        A.op("vector", lambda e: e.reciprocal(out=vrow[:, :], in_=vrow[:, :]),
             reads=[("vrow",)], writes=[("vrow",)])
        A.op("vector", lambda e: e.tensor_tensor(out=brow[:, :], in0=mrow[:, :],
             in1=vrow[:, :], op=OP.mult), reads=[("mrow",), ("vrow",)], writes=[("brow",)])
        A.op("vector", lambda e, cp=cp: e.tensor_scalar(out=brow[:, :], in0=brow[:, :],
             scalar1=-1.0, scalar2=None, op0=OP.mult), reads=[("brow",)], writes=[("brow",)])
        A.op("vector", lambda e, cp=cp: e.tensor_copy(out=arow_bf[:, :], in_=vrow[:, :]),
             reads=[("vrow",)], writes=[("arb",)])
        A.op("vector", lambda e, cp=cp: e.tensor_copy(out=brow_bf[:, :], in_=brow[:, :]),
             reads=[("brow",)], writes=[("brb",)])
        A.op("tensor", lambda e, cp=cp: e.matmul(out=psum[4 * cp + 2][:, 0:NC], lhsT=ones1b[0:1, :],
             rhs=arow_bf[:, :], start=True, stop=True), reads=[("arb",), ("ones",)], writes=[("ps", 4 * cp + 2)])
        A.op("tensor", lambda e, cp=cp: e.matmul(out=psum[4 * cp + 3][:, 0:NC], lhsT=ones1b[0:1, :],
             rhs=brow_bf[:, :], start=True, stop=True), reads=[("brb",), ("ones",)], writes=[("ps", 4 * cp + 3)])
        for k in range(KD):
            A.op("vector", lambda e, k=k, cp=cp: e.tensor_tensor(out=x3f[:, k, :], in0=x3f[:, k, :],
                 in1=psum[4 * cp + 2][:, 0:NC], op=OP.mult),
                 reads=[("x3f",), ("ps", 4 * cp + 2)], writes=[("x3f",)])
            A.op("vector", lambda e, k=k, cp=cp: e.tensor_tensor(out=x3f[:, k, :], in0=x3f[:, k, :],
                 in1=psum[4 * cp + 3][:, 0:NC], op=OP.add),
                 reads=[("x3f",), ("ps", 4 * cp + 3)], writes=[("x3f",)])
        for half in range(2):
            A.dma("gpsimd", dst_hbm[:, half * 4:(half + 1) * 4, t0:t0 + NC],
                  x3f[:, half * 4:(half + 1) * 4, :], reads=[("x3f",)],
                  writes=[("dsth",)], stream=("dst_stream",))


def emit_w2bf(au, es, nc, pairs, tag="w2", bufw=16384, defer=False):
    A = au
    bufs = [es.enter_context(nc.sbuf_tensor(f"{tag}{i}", [128, bufw], BF16)) for i in range(2)]
    units = []
    for src, dst in pairs:
        kk, _, F = src.shape
        step = max(1, bufw // F)
        for k0 in range(0, kk, step):
            units.append((src, dst, k0, min(step, kk - k0), F))

    state = {"i": 0}

    def feed(n=1):
        for _ in range(n):
            i = state["i"]
            if i >= len(units):
                return
            src, dst, k0, ks, F = units[i]
            b = bufs[i % 2]
            tok = (tag + "b", i % 2)
            A.dma("gpsimd", b[:, 0:ks * F].rearrange("p (k f) -> p k f", k=ks),
                  src[k0:k0 + ks, :, :].rearrange("k p f -> p k f"), writes=[tok])
            A.dma("sync", dst[k0:k0 + ks, :, :].rearrange("k p f -> p k f"),
                  b[:, 0:ks * F].rearrange("p (k f) -> p k f", k=ks),
                  reads=[tok], writes=[(tag + "st",)], stream=(tag + "st",))
            state["i"] += 1

    if defer:
        return feed
    feed(len(units))


def emit_proj(au, *, w_hbm, rhs_fn, kd, mt, evac, psum, wring, wrtok, kslab=4,
              group=4, eng="sync", bank_base=0, pingpong=True):
    A = au
    nbuf = len(wring)
    i = 0
    for g in range((mt + group - 1) // group):
        gm = min(group, mt - g * group)
        b0 = bank_base + ((g % 2) * group if pingpong else 0)
        for k0 in range(0, kd, kslab):
            ks = min(kslab, kd - k0)
            wb = wring[i % nbuf]
            tok = (wrtok, i % nbuf)
            A.dma(eng, wb[:, 0:ks * gm * 128].rearrange("p (k f) -> p k f", k=ks),
                  w_hbm[k0:k0 + ks, :, g * group * 128:g * group * 128 + gm * 128]
                  .rearrange("k p f -> p k f"), writes=[tok])
            for dk in range(ks):
                k = k0 + dk
                rhs, rtok = rhs_fn(k)
                for m8 in range(gm):
                    A.op("tensor", lambda e, wb=wb, dk=dk, m8=m8, k=k, rhs=rhs, gm=gm, ks=ks, b0=b0:
                         e.matmul(out=psum[b0 + m8][:, 0:NC],
                                  lhsT=wb[:, 0:ks * gm * 128].rearrange(
                                      "p (k f) -> p k f", k=ks)[:, dk, m8 * 128:(m8 + 1) * 128],
                                  rhs=rhs, start=(k == 0), stop=(k == kd - 1)),
                         reads=[tok, rtok], writes=[("ps", b0 + m8)])
            i += 1
        for m8 in range(gm):
            evac(g * group + m8, psum[b0 + m8], ("ps", b0 + m8))



from contextlib import ExitStack
import ml_dtypes
from concourse.bass_utils import run_bass_kernel_spmd

NCORES = 8
_BUILD_CACHE = {}


def _prep_shared(i):
    s = {}
    perm = rope_perm()
    wq = i["L0_attn_w_qkv"][0:D][perm]
    wk = i["L0_attn_w_qkv"][D:2 * D][perm]
    s["a_wqk"] = kpt(np.ascontiguousarray(np.concatenate([wq, wk], 0).T))
    s["a_bqk"] = pvec(np.concatenate([i["L0_attn_b_qkv"][0:D][perm],
                                      i["L0_attn_b_qkv"][D:2 * D][perm]]))
    s["a_wv"] = kpt(np.ascontiguousarray(i["L0_attn_w_qkv"][2 * D:3 * D].T))
    s["a_bv"] = i["L0_attn_b_qkv"][2 * D:3 * D].reshape(1, 1024).copy()
    s["a_wout"] = kpt(np.ascontiguousarray(i["L0_attn_w_out"].T))
    s["a_bout"] = pvec(i["L0_attn_b_out"])
    cos, sin = rope_tables()
    s["cosT"] = cos.astype(ml_dtypes.bfloat16)
    s["sinT"] = sin.astype(ml_dtypes.bfloat16)
    s["masks"] = causal_masks().astype(ml_dtypes.bfloat16)
    qw = i["L0_ffn_qkv_w"]
    s["f_wqk"] = kpt(np.ascontiguousarray(np.concatenate([qw[0:D], qw[D:2 * D]], 0).T))
    s["f_bqk"] = pvec(np.concatenate([i["L0_ffn_qkv_b"][0:D], i["L0_ffn_qkv_b"][D:2 * D]]))
    s["f_wv"] = kpt(np.ascontiguousarray(qw[2 * D:3 * D].T))
    s["f_bv"] = i["L0_ffn_qkv_b"][2 * D:3 * D].reshape(1, 1024).copy()
    s["f_wout"] = kpt(np.ascontiguousarray(i["L0_ffn_out_w"].T))
    s["f_bout"] = pvec(i["L0_ffn_out_b"])
    s["norm1"] = pvec(i["L0_ffn_norm1_w"])
    s["norm2"] = pvec(i["L0_ffn_norm2_w"])
    s["gate"] = kpt(np.ascontiguousarray(i["L0_ffn_gate_w"].T))
    s["w1"] = np.stack([kpt(np.ascontiguousarray(i["L0_ffn_w1"][e].T)) for e in range(8)])
    s["w2"] = np.stack([kpt(np.ascontiguousarray(i["L0_ffn_w2"][e].T)) for e in range(8)])
    s["w3"] = np.stack([kpt(np.ascontiguousarray(i["L0_ffn_w3"][e].T)) for e in range(8)])
    s["b1"] = np.stack([pvec(i["L0_ffn_b1"][e]) for e in range(8)], axis=1)
    s["b2"] = np.stack([pvec(i["L0_ffn_b2"][e]) for e in range(8)], axis=1)
    s["b3"] = i["L0_ffn_b3"].copy()
    esel = np.zeros((8, 1024), np.float32)
    for e in range(8):
        esel[e, e * 128:(e + 1) * 128] = 1.0
    s["esel"] = esel.astype(ml_dtypes.bfloat16)
    s["ident"] = np.eye(128, dtype=np.float32).astype(ml_dtypes.bfloat16)
    s["m_win"] = kpt(np.ascontiguousarray(i["L1_attn_W_in"]))
    s["m_cw"] = np.ascontiguousarray(i["L1_attn_conv_w"].reshape(KDI, 128, 4).transpose(1, 0, 2))
    s["m_cb"] = pvec(i["L1_attn_conv_b"])
    s["m_wx"] = kpt(np.ascontiguousarray(i["L1_attn_W_x"]))
    s["m_wdt"] = i["L1_attn_W_dt"].copy()
    s["m_bdt"] = pvec(i["L1_attn_b_dt"])
    s["m_alog"] = np.ascontiguousarray(i["L1_attn_A_log"].reshape(KDI, 128, DST).transpose(1, 0, 2))
    s["m_d"] = pvec(i["L1_attn_D"])
    s["m_wout"] = kpt(np.ascontiguousarray(i["L1_attn_W_out"]))
    esel32 = np.zeros((32, 32 * 128), np.float32)
    for n in range(16):
        esel32[n, n * 128:(n + 1) * 128] = 1.0
        esel32[16 + n, (16 + n) * 128:(17 + n) * 128] = 1.0
    s["esel32"] = esel32.astype(ml_dtypes.bfloat16)
    permB = bfly_perm_B()

    def coefpack(wstack, which):
        outs = []
        for ci in range(4):
            a_, b_ = divmod(ci, 2)
            cA, cB = bfly_coeffs(wstack[:, :, a_, b_])
            outs.append(cA if which == "A" else cB)
        arr = np.stack(outs, axis=2)
        arr = arr.transpose(1, 0, 2, 3).reshape(32, 1, -1)
        return np.ascontiguousarray(np.broadcast_to(arr, (32, 128, arr.shape[2]))).astype(ml_dtypes.bfloat16)
    s["d_wexp"] = kpt(np.ascontiguousarray(i["L1_ffn_W_exp"]))
    s["d_bexp"] = pvec(i["L1_ffn_b_exp"])
    s["coefAu"] = coefpack(i["L1_ffn_up_weights"], "A")
    s["coefBu"] = coefpack(i["L1_ffn_up_weights"], "B")
    s["coefAd"] = coefpack(i["L1_ffn_down_weights"], "A")
    s["coefBd"] = coefpack(i["L1_ffn_down_weights"], "B")
    s["d_ubias"] = np.ascontiguousarray(i["L1_ffn_up_bias"][permB].reshape(32, 128).T)
    s["d_wcon"] = kpt(np.ascontiguousarray(i["L1_ffn_W_con"][permB, :]))
    s["d_bcon"] = pvec(i["L1_ffn_b_con"])
    s["d_dbias"] = pvec(i["L1_ffn_down_bias"])
    return s


_BIGW = ("a_wqk", "a_wv", "a_wout", "f_wqk", "f_wv", "f_wout", "w1", "w2", "w3",
         "m_win", "m_wout", "d_wexp", "d_wcon")


_EAGER = ("a_wqk", "a_wv", "a_wout")


def _declare(nc, shared, au, es, defer_rest=False):
    ap = {}
    pairs = []
    dpairs = []
    for k, v in shared.items():
        dt = BF16 if v.dtype == ml_dtypes.bfloat16 else F32
        src_ap = nc.dram_tensor(k, list(v.shape), dt, kind="ExternalInput").ap()
        if k in _BIGW:
            mir = nc.dram_tensor(k + "_bf", list(v.shape), BF16, kind="Internal").ap()
            if len(v.shape) == 4:
                pr = (src_ap.rearrange("e k p f -> (e k) p f"),
                      mir.rearrange("e k p f -> (e k) p f"))
            else:
                pr = (src_ap, mir)
            (pairs if (k in _EAGER or not defer_rest) else dpairs).append(pr)
            ap[k] = mir
        else:
            ap[k] = src_ap
    emit_w2bf(au, es, nc, pairs)
    au.barrier()
    return ap, dpairs


def _common_prolog(au, es, nc, w):
    g = {}
    g["psum"] = [es.enter_context(nc.psum_tensor(f"pb{i}", [128, 512], F32)) for i in range(8)]
    g["ones128b"] = es.enter_context(nc.sbuf_tensor("ones128b", [128, 1], BF16))
    g["ones1b"] = es.enter_context(nc.sbuf_tensor("ones1b", [1, 128], BF16))
    g["ones128f32"] = es.enter_context(nc.sbuf_tensor("ones128f32", [128, 1], F32))
    g["shift_sb"] = es.enter_context(nc.sbuf_tensor("shift_sb", [128, 1], F32))
    g["eps1_sb"] = es.enter_context(nc.sbuf_tensor("eps1_sb", [1, 2], F32))
    au.op("vector", lambda e: e.memset(g["ones128b"][:], 1.0), writes=[("ones",)])
    au.op("vector", lambda e: e.memset(g["ones1b"][:], 1.0), writes=[("ones",)])
    au.op("vector", lambda e: e.memset(g["ones128f32"][:], 1.0), writes=[("onesf",)])
    au.op("vector", lambda e: e.memset(g["shift_sb"][:], -SHIFT), writes=[("const",)])
    au.op("vector", lambda e: e.memset(g["eps1_sb"][0:1, 0:1], LN_EPS), writes=[("const",)])
    au.op("vector", lambda e: e.memset(g["eps1_sb"][0:1, 1:2], RMS_EPS), writes=[("const",)])
    return g


def _emit_block0_attn(au, es, nc, w, g, src, dst, q_hbm, attn_hbm, feed=None):
    bqk_sb = es.enter_context(nc.sbuf_tensor("a_bqk_sb", [128, 16], F32))
    bv_sb = es.enter_context(nc.sbuf_tensor("a_bv_sb", [1, 1024], BF16))
    bout_sb = es.enter_context(nc.sbuf_tensor("a_bout_sb", [128, 8], F32))
    masks_sb = es.enter_context(nc.sbuf_tensor("a_masks_sb", [128, 4, NC], BF16))
    au.dma("sync", bqk_sb[:], w["a_bqk"][:, :], writes=[("bqk",)])
    au.dma("gpsimd", bv_sb[:], w["a_bv"][:, :], writes=[("bv",)])
    au.dma("sync", bout_sb[:], w["a_bout"][:, :], writes=[("bout",)])
    au.dma("sync", masks_sb[:], w["masks"][:, :, :], writes=[("msk",)])
    with ExitStack() as es1:
        K_sb = es1.enter_context(nc.sbuf_tensor("a_K", [128, KD, T], BF16))
        V65 = es1.enter_context(nc.sbuf_tensor("a_V65", [128, T // 128, NH, 65], BF16))
        emit_attn_phase(au, es1, nc, src_hbm=src, rms_gain_sb=None, wqk_hbm=w["a_wqk"],
                        bqk_sb=bqk_sb, wv_hbm=w["a_wv"], bv_sb=bv_sb, causal=True,
                        cos_hbm=w["cosT"], sin_hbm=w["sinT"], masks_sb=masks_sb,
                        q_hbm=q_hbm, attn_hbm=attn_hbm, K_sb=K_sb, V65=V65,
                        psum=g["psum"], ones128b=g["ones128b"], ones1b=g["ones1b"],
                        shift_sb=g["shift_sb"], eps1_sb=g["eps1_sb"], tag="a",
                        feed=feed)
        au.barrier()
    with ExitStack() as es2:
        emit_outproj_ln(au, es2, nc, attn_hbm=attn_hbm, src_hbm=src, dst_hbm=dst,
                        w_hbm=w["a_wout"], b_sb=bout_sb, psum=g["psum"],
                        ones128b=g["ones128b"], ones1b=g["ones1b"],
                        eps1_sb=g["eps1_sb"], ln=True, tag="ao")
        au.barrier()


def _emit_block0_ffn(au, es, nc, w, g, src, dst, q_hbm, attn_hbm, z_hbm):
    bqk_sb = es.enter_context(nc.sbuf_tensor("f_bqk_sb", [128, 16], F32))
    bv_sb = es.enter_context(nc.sbuf_tensor("f_bv_sb", [1, 1024], BF16))
    bout_sb = es.enter_context(nc.sbuf_tensor("f_bout_sb", [128, 8], F32))
    n1_sb = es.enter_context(nc.sbuf_tensor("n1_sb", [128, KD], F32))
    n2_sb = es.enter_context(nc.sbuf_tensor("n2_sb", [128, KD], F32))
    b1_sb = es.enter_context(nc.sbuf_tensor("b1_sb", [128, 8, 16], F32))
    b2_sb = es.enter_context(nc.sbuf_tensor("b2_sb", [128, 8, 16], F32))
    b3_sb = es.enter_context(nc.sbuf_tensor("b3_sb", [8, 1024], BF16))
    esel_sb = es.enter_context(nc.sbuf_tensor("esel_sb", [8, 1024], BF16))
    ident_sb = es.enter_context(nc.sbuf_tensor("ident_sb", [128, 128], BF16))
    au.dma("sync", bqk_sb[:], w["f_bqk"][:, :], writes=[("bqk",)])
    au.dma("gpsimd", bv_sb[:], w["f_bv"][:, :], writes=[("bv",)])
    au.dma("sync", bout_sb[:], w["f_bout"][:, :], writes=[("bout",)])
    au.dma("sync", n1_sb[:], w["norm1"][:, :], writes=[("gain",)])
    au.dma("sync", n2_sb[:], w["norm2"][:, :], writes=[("gain2",)])
    au.dma("sync", b1_sb[:], w["b1"][:, :, :], writes=[("b1",)])
    au.dma("sync", b2_sb[:], w["b2"][:, :, :], writes=[("b2",)])
    au.dma("gpsimd", b3_sb[:], w["b3"][:, :], writes=[("b3",)])
    au.dma("sync", esel_sb[:], w["esel"][:, :], writes=[("esel",)])
    au.dma("sync", ident_sb[:], w["ident"][:, :], writes=[("ident",)])
    with ExitStack() as es1:
        K_sb = es1.enter_context(nc.sbuf_tensor("f_K", [128, KD, T], BF16))
        V65 = es1.enter_context(nc.sbuf_tensor("f_V65", [128, T // 128, NH, 65], BF16))
        emit_attn_phase(au, es1, nc, src_hbm=src, rms_gain_sb=n1_sb, wqk_hbm=w["f_wqk"],
                        bqk_sb=bqk_sb, wv_hbm=w["f_wv"], bv_sb=bv_sb, causal=False,
                        cos_hbm=w["cosT"], sin_hbm=w["sinT"], masks_sb=None,
                        q_hbm=q_hbm, attn_hbm=attn_hbm, K_sb=K_sb, V65=V65,
                        psum=g["psum"], ones128b=g["ones128b"], ones1b=g["ones1b"],
                        shift_sb=g["shift_sb"], eps1_sb=g["eps1_sb"], tag="f")
        au.barrier()
    with ExitStack() as es2:
        emit_outproj_ln(au, es2, nc, attn_hbm=attn_hbm, src_hbm=src, dst_hbm=z_hbm,
                        w_hbm=w["f_wout"], b_sb=bout_sb, psum=g["psum"],
                        ones128b=g["ones128b"], ones1b=g["ones1b"],
                        eps1_sb=g["eps1_sb"], ln=False, tag="fo")
        au.barrier()
    emit_moe_phase(au, es, nc, z_hbm=z_hbm, x1_hbm=src, dst_hbm=dst,
                   rms_gain_sb=n2_sb, gate_hbm=w["gate"], w1_hbm=w["w1"],
                   w2_hbm=w["w2"], w3_hbm=w["w3"], b1_sb=b1_sb, b2_sb=b2_sb,
                   b3_sb=b3_sb, psum=g["psum"], ones128b=g["ones128b"],
                   ones1b=g["ones1b"], esel_sb=esel_sb, ident_bf=ident_sb,
                   eps1_sb=g["eps1_sb"], tag="m")
    au.barrier()


def _emit_block1_attn(au, es, nc, w, g, src, dst):
    cw_sb = es.enter_context(nc.sbuf_tensor("cw_sb", [128, KDI, 4], F32))
    cb_sb = es.enter_context(nc.sbuf_tensor("cb_sb", [128, KDI], F32))
    wx_sb = es.enter_context(nc.sbuf_tensor("wx_sb", [128, KDI, 96], BF16))
    wdt_sb = es.enter_context(nc.sbuf_tensor("wdt_sb", [64, 2048], BF16))
    bdt_sb = es.enter_context(nc.sbuf_tensor("bdt_sb", [128, KDI], F32))
    A_sb = es.enter_context(nc.sbuf_tensor("A_sb", [128, KDI, DST], F32))
    D_sb = es.enter_context(nc.sbuf_tensor("D_sb", [128, KDI], F32))
    esel_sb = es.enter_context(nc.sbuf_tensor("esel32_sb", [32, 32 * 128], BF16))
    au.dma("sync", cw_sb[:], w["m_cw"][:, :, :], writes=[("cw",)])
    au.dma("sync", cb_sb[:], w["m_cb"][:, :], writes=[("cb",)])
    for k in range(KDI):
        au.dma("gpsimd", wx_sb[:, k, :], w["m_wx"][k, :, :], writes=[("wx",)], stream=("wx",))
    au.dma("gpsimd", wdt_sb[:, :], w["m_wdt"][:, :], writes=[("wdt",)])
    au.dma("sync", bdt_sb[:], w["m_bdt"][:, :], writes=[("bdt",)])
    au.dma("sync", A_sb[:], w["m_alog"][:, :, :], writes=[("A",)])
    au.dma("sync", D_sb[:], w["m_d"][:, :], writes=[("D",)])
    au.dma("sync", esel_sb[:], w["esel32"][:, :], writes=[("esel32",)])
    au.op("scalar", lambda e: e.activation(out=A_sb[:, :, :], in_=A_sb[:, :, :],
         func=AF.Exp, scale=1.0), reads=[("A",)], writes=[("A",)])
    au.op("vector", lambda e: e.tensor_scalar(out=A_sb[:, :, :], in0=A_sb[:, :, :],
         scalar1=-1.0, scalar2=None, op0=OP.mult), reads=[("A",)], writes=[("A",)])
    emit_mamba_phase(au, es, nc, src_hbm=src, dst_hbm=dst, win_hbm=w["m_win"],
                     cw_sb=cw_sb, cb_sb=cb_sb, wx_sb=wx_sb, wdt_sb=wdt_sb,
                     bdt_sb=bdt_sb, A_sb=A_sb, D_sb=D_sb, wout_hbm=w["m_wout"],
                     esel32_sb=esel_sb, psum=g["psum"], ones128b=g["ones128b"],
                     ones1b=g["ones1b"], ones128f32=g["ones128f32"],
                     eps1_sb=g["eps1_sb"], tag="c")
    au.barrier()


def _emit_block1_ffn(au, es, nc, w, g, src, dst, shuf_hbm):
    ident_sb = es.enter_context(nc.sbuf_tensor("identd_sb", [128, 128], BF16))
    bexp_sb = es.enter_context(nc.sbuf_tensor("bexp_sb", [128, 32], F32))
    ubias_sb = es.enter_context(nc.sbuf_tensor("ubias_sb", [128, 32], F32))
    bcon_sb = es.enter_context(nc.sbuf_tensor("bcon_sb", [128, 8], F32))
    dbias_sb = es.enter_context(nc.sbuf_tensor("dbias_sb", [128, 8], F32))
    MAu = es.enter_context(nc.sbuf_tensor("MAu", [128, 32, 128], BF16))
    MBu = es.enter_context(nc.sbuf_tensor("MBu", [128, 32, 128], BF16))
    MAd = es.enter_context(nc.sbuf_tensor("MAd", [128, 32, 128], BF16))
    MBd = es.enter_context(nc.sbuf_tensor("MBd", [128, 32, 128], BF16))
    au.dma("sync", ident_sb[:], w["ident"][:, :], writes=[("ident",)])
    au.dma("sync", bexp_sb[:], w["d_bexp"][:, :], writes=[("bexp",)])
    au.dma("sync", ubias_sb[:], w["d_ubias"][:, :], writes=[("ubias",)])
    au.dma("sync", bcon_sb[:], w["d_bcon"][:, :], writes=[("bcon",)])
    au.dma("sync", dbias_sb[:], w["d_dbias"][:, :], writes=[("dbias",)])
    au.op("vector", lambda e: e.tensor_tensor(out=bcon_sb[:], in0=bcon_sb[:],
         in1=dbias_sb[:], op=OP.add), reads=[("bcon",), ("dbias",)], writes=[("bcon",)])
    with ExitStack() as esb:
        for nm, dr, M in (("Au", w["coefAu"], MAu), ("Bu", w["coefBu"], MBu),
                          ("Ad", w["coefAd"], MAd), ("Bd", w["coefBd"], MBd)):
            emit_bfly_build(au, esb, nc, coef_hbm=dr, which=nm[0], M_sb=M,
                            ident_bf=ident_sb, psum=g["psum"], ones1b=g["ones1b"], tag=nm)
        au.barrier()
    with ExitStack() as es1:
        emit_bfly_phase(au, es1, nc, src_hbm=src, dst_hbm=dst, wexp_hbm=w["d_wexp"],
                        bexp_sb=bexp_sb, MAu=MAu, MBu=MBu, MAd=MAd, MBd=MBd,
                        ubias_sb=ubias_sb, wcon_hbm=w["d_wcon"], bcon_sb=bcon_sb,
                        shuf_hbm=shuf_hbm, psum=g["psum"], ones128b=g["ones128b"],
                        ones1b=g["ones1b"], eps1_sb=g["eps1_sb"], tag="d")
        au.barrier()


_LAUNCH_W = {
    "b0a": ("a_wqk", "a_wv", "a_wout", "cosT", "sinT", "masks", "a_bqk", "a_bv", "a_bout"),
    "b0f": ("f_wqk", "f_wv", "f_wout", "w1", "w2", "w3", "cosT", "sinT", "norm1", "norm2",
             "gate", "b1", "b2", "b3", "esel", "ident", "f_bqk", "f_bv", "f_bout"),
    "b1a": ("m_win", "m_cw", "m_cb", "m_wx", "m_wdt", "m_bdt", "m_alog", "m_d", "m_wout", "esel32"),
    "b1f": ("d_wexp", "d_bexp", "coefAu", "coefBu", "coefAd", "coefBd", "d_ubias",
             "d_wcon", "d_bcon", "d_dbias", "ident"),
}


def _build_launch(which, shared):
    key = which
    shared = {k: v for k, v in shared.items() if k in _LAUNCH_W[which]}
    if key in _BUILD_CACHE:
        return _BUILD_CACHE[key]
    nc = bass.Bass("TRN2", target_bir_lowering=False)
    xin = nc.dram_tensor("xin", [KD, 128, T], F32, kind="ExternalInput").ap()
    xout = nc.dram_tensor("xout", [KD, 128, T], F32, kind="ExternalOutput").ap()
    src = xin.rearrange("k p t -> p k t")
    dst = xout.rearrange("k p t -> p k t")
    with ExitStack() as es:
        au = Auto(nc, es)
        with ExitStack() as esw:
            w, _ = _declare(nc, shared, au, esw)
        g = _common_prolog(au, es, nc, w)
        if which == "b0a":
            q_hbm = nc.dram_tensor("q_hbm", [128, KD, T], BF16, kind="Internal").ap()
            attn_hbm = nc.dram_tensor("attn_hbm", [128, KD, T], BF16, kind="Internal").ap()
            _emit_block0_attn(au, es, nc, w, g, src, dst, q_hbm, attn_hbm)
        elif which == "b0f":
            q_hbm = nc.dram_tensor("q_hbm", [128, KD, T], BF16, kind="Internal").ap()
            attn_hbm = nc.dram_tensor("attn_hbm", [128, KD, T], BF16, kind="Internal").ap()
            z_hbm = nc.dram_tensor("z_hbm", [128, KD, T], F32, kind="Internal").ap()
            _emit_block0_ffn(au, es, nc, w, g, src, dst, q_hbm, attn_hbm, z_hbm[:, :, :])
        elif which == "b1a":
            _emit_block1_attn(au, es, nc, w, g, src, dst)
        elif which == "b1f":
            shuf = nc.dram_tensor("shuf", [2, 32, 4, 32, NC], BF16, kind="Internal").ap()
            _emit_block1_ffn(au, es, nc, w, g, src, dst, shuf)
        with nc.Block() as block:
            au.finalize(block)
    _BUILD_CACHE[key] = nc
    return nc


def _build_fused(shared):
    key = "fused"
    if key in _BUILD_CACHE:
        return _BUILD_CACHE[key]
    nc = bass.Bass("TRN2", target_bir_lowering=False)
    xin = nc.dram_tensor("xin", [KD, 128, T], F32, kind="ExternalInput").ap()
    xout = nc.dram_tensor("xout", [KD, 128, T], F32, kind="ExternalOutput").ap()
    x1h = nc.dram_tensor("x1h", [128, KD, T], F32, kind="Internal").ap()
    x2h = nc.dram_tensor("x2h", [128, KD, T], F32, kind="Internal").ap()
    x3h = nc.dram_tensor("x3h", [128, KD, T], F32, kind="Internal").ap()
    q_hbm = nc.dram_tensor("q_hbm", [128, KD, T], BF16, kind="Internal").ap()
    attn_hbm = nc.dram_tensor("attn_hbm", [128, KD, T], BF16, kind="Internal").ap()
    z_hbm = nc.dram_tensor("z_hbm", [128, KD, T], F32, kind="Internal").ap()
    shuf = nc.dram_tensor("shuf", [2, 32, 4, 32, NC], BF16, kind="Internal").ap()
    src = xin.rearrange("k p t -> p k t")
    dst = xout.rearrange("k p t -> p k t")
    with ExitStack() as es:
        au = Auto(nc, es)
        with ExitStack() as esw:
            w, dpairs = _declare(nc, shared, au, esw, defer_rest=True)
        g = _common_prolog(au, es, nc, w)
        with ExitStack() as p1:
            feed = emit_w2bf(au, p1, nc, dpairs, tag="cv", bufw=4096, defer=True)
            _emit_block0_attn(au, p1, nc, w, g, src, x1h[:, :, :], q_hbm, attn_hbm,
                              feed=feed)
            feed(10000)
            au.barrier()
        with ExitStack() as p2:
            _emit_block0_ffn(au, p2, nc, w, g, x1h[:, :, :], x2h[:, :, :],
                             q_hbm, attn_hbm, z_hbm[:, :, :])
        with ExitStack() as p3:
            _emit_block1_attn(au, p3, nc, w, g, x2h[:, :, :], x3h[:, :, :])
        with ExitStack() as p4:
            _emit_block1_ffn(au, p4, nc, w, g, x3h[:, :, :], dst, shuf)
            au.barrier()
        with nc.Block() as block:
            au.finalize(block)
    _BUILD_CACHE[key] = nc
    return nc


FUSED = True


def kernel(**inputs):
    x = inputs["L0_attn_x"]
    shared = _prep_shared(inputs)
    xs = [np.ascontiguousarray(x[c].T).reshape(KD, 128, T) for c in range(NCORES)]
    if FUSED:
        nc = _build_fused(shared)
        in_maps = [dict(shared, xin=xs[c]) for c in range(NCORES)]
        res = run_bass_kernel_spmd(nc, in_maps, core_ids=list(range(NCORES)))
        outs = [r["xout"] for r in res.results]
    else:
        cur = xs
        for which in ("b0a", "b0f", "b1a", "b1f"):
            nc = _build_launch(which, shared)
            sh = {k: v for k, v in shared.items() if k in _LAUNCH_W[which]}
            in_maps = [dict(sh, xin=cur[c]) for c in range(NCORES)]
            res = run_bass_kernel_spmd(nc, in_maps, core_ids=list(range(NCORES)))
            cur = [r["xout"] for r in res.results]
        outs = cur
    full = np.stack([o.reshape(D, T).T for o in outs])
    return np.ascontiguousarray(full.astype(np.float32))
